# version 05
# baseline (speedup 1.0000x reference)
import contextlib
import numpy as np

import concourse.bass as bass
import concourse.mybir as mybir
from concourse.bass_utils import run_bass_kernel_spmd

F32 = mybir.dt.float32
BF16 = mybir.dt.bfloat16
I32 = mybir.dt.int32
ALU = mybir.AluOpType
ACT = mybir.ActivationFunctionType
AX = mybir.AxisListType

NCORE = 8
T = 1024
TT = 8
D = 2048
DC = 16
H0 = 16
M0 = 64
H1 = 16
KVH = 4
DH = 128
SM_SCALE = DH ** -0.5
NE0 = 16
K0_TOP = 512
E1 = 8
I1 = 1408
J1 = 5632
CAP1 = 2560
CAPL = 384
NEG = -1.0e30


class Prog:

    ENGS = ("sync", "tensor", "vector", "scalar", "gpsimd")

    def __init__(self, nc):
        self.nc = nc
        self.ops = {k: [] for k in self.ENGS}
        self.sems = {}
        self.cnt = {}
        self.names = {}
        self._sem_prod = {}
        self._rec = None
        self.stack = contextlib.ExitStack()

    def _uniq(self, name):
        n = self.names.get(name, 0)
        self.names[name] = n + 1
        return name if n == 0 else "%s_%d" % (name, n)

    def sb(self, name, shape, dtype, stack=None):
        return (stack or self.stack).enter_context(
            self.nc.sbuf_tensor(self._uniq(name), list(shape), dtype))

    def sba(self, name, shape, dtype, offset):
        return self.nc.alloc_sbuf_tensor_at(self._uniq(name), list(shape), dtype,
                                            offset=offset)

    def ps(self, name, shape, stack=None, dtype=F32):
        return (stack or self.stack).enter_context(
            self.nc.psum_tensor(self._uniq(name), list(shape), dtype))

    def sem(self, name):
        if name not in self.sems:
            self.sems[name] = self.stack.enter_context(self.nc.semaphore(name))
            self.cnt[name] = 0
        return self.sems[name]

    def inc(self, name, k=1):
        s = self.sem(name)
        self.cnt[name] += k
        return s

    def val(self, name):
        self.sem(name)
        return self.cnt[name]

    def on(self, eng, fn):
        self.ops[eng].append(fn)
        if getattr(self, "_rec", None) is not None:
            self._rec_engs.add(eng)

    def skip_begin(self, cond_ap):
        assert getattr(self, "_rec", None) is None
        self._rec = dict(self.cnt)
        self._rec_engs = set()
        self._rec_marks = {k: len(v) for k, v in self.ops.items()}
        self._rec_cond = cond_ap

    def skip_end(self, gpsimd_in=False):
        pre = self._rec
        engs = self._rec_engs
        marks = self._rec_marks
        cond = self._rec_cond
        self._rec = None
        deltas = {n: self.cnt[n] - pre.get(n, 0) for n in self.cnt
                  if self.cnt[n] != pre.get(n, 0)}
        prod = self._sem_prod
        for eng in engs:
            if eng == "gpsimd" and not gpsimd_in:
                continue
            body = self.ops[eng][marks[eng]:]
            del self.ops[eng][marks[eng]:]
            if eng == "gpsimd":
                comp = []
            else:
                comp = [(self.sems[n], pre.get(n, 0), d) for n, d in deltas.items()
                        if prod.get(n) == eng]

            def wrapped(e, body=body, comp=comp, cond=cond):
                with e.register() as reg:
                    e.reg_load(reg, cond)
                    with e.If(reg):
                        for f in body:
                            f(e)
                    if comp:
                        with e.Else():
                            for sh_, pv_, d_ in comp:
                                if pv_ > 0:
                                    e.wait_ge(sh_, pv_)
                                e.sem_inc(sh_, d_)
                    e.end_ifs()
            self.ops[eng].append(wrapped)

    def wait(self, eng, name, v=None):
        s = self.sem(name)
        v = self.cnt[name] if v is None else v
        if v > 0:
            self.on(eng, lambda e, s=s, v=v: e.wait_ge(s, v))

    def dma(self, eng, sem, out_ap, in_ap):
        self._sem_prod[sem] = eng
        s = self.inc(sem, 16)
        self.on(eng, lambda e, o=out_ap, i=in_ap, s=s:
                e.dma_start(out=o, in_=i).then_inc(s, 16))

    def mm(self, out_ap, lhsT_ap, rhs_ap, start, stop, sem=None):
        if sem is not None:
            self._sem_prod[sem] = "tensor"
        if sem is None:
            self.on("tensor", lambda e, o=out_ap, l=lhsT_ap, r=rhs_ap, st=start,
                    sp=stop: e.matmul(o, l, r, start=st, stop=sp))
        else:
            s = self.inc(sem)
            self.on("tensor", lambda e, o=out_ap, l=lhsT_ap, r=rhs_ap, st=start,
                    sp=stop, s=s: e.matmul(o, l, r, start=st, stop=sp).then_inc(s, 1))

    def op(self, eng, fn, sem=None):
        if sem is not None:
            self._sem_prod[sem] = eng
        if sem is None:
            self.on(eng, lambda e, f=fn: f(e))
        else:
            s = self.inc(sem)
            self.on(eng, lambda e, f=fn, s=s: f(e).then_inc(s, 1))

    def trans(self, ps_ap, in_ap, ident_ap, out_ap, sem, evac_eng="vector"):
        self.op("tensor", lambda e, o=ps_ap, i=in_ap, idn=ident_ap:
                e.transpose(o, i, idn), sem + "_t")
        self.wait(evac_eng, sem + "_t")
        self.op(evac_eng, lambda e, o=out_ap, i=ps_ap: e.tensor_copy(o, i), sem)

    def finish(self):
        nc = self.nc
        with nc.Block() as block:
            @block.sync
            def _(e):
                for f in self.ops["sync"]:
                    f(e)

            @block.tensor
            def _(e):
                for f in self.ops["tensor"]:
                    f(e)

            @block.vector
            def _(e):
                for f in self.ops["vector"]:
                    f(e)

            @block.scalar
            def _(e):
                for f in self.ops["scalar"]:
                    f(e)

            @block.gpsimd
            def _(e):
                for f in self.ops["gpsimd"]:
                    f(e)
        self.stack.close()
        return nc


def _new_prog(name, detect=True):
    nc = bass.Bass("TRN2", target_bir_lowering=False,
                   detect_race_conditions=detect)
    return Prog(nc)


def elu1(p, psum_ap, out_ap, scr1, sem):
    p.op("vector", lambda e, i=psum_ap, o=scr1: e.tensor_scalar_min(o, i, 0.0), sem)
    p.wait("scalar", sem)
    p.op("scalar", lambda e, i=scr1: e.activation(i, i, ACT.Exp), sem)
    p.wait("vector", sem)
    p.op("vector", lambda e, i=psum_ap, x=scr1, o=out_ap:
         e.scalar_tensor_tensor(o, i, 0.0, x, ALU.max, ALU.add), sem)


def layer_norm(p, x_ap, sq_scr, ssum, ssq, mean, m2, var, rstd, sem):
    p.op("vector", lambda e, o=ssum, i=x_ap: e.reduce_sum(o, i, AX.X), sem)
    p.wait("scalar", sem)
    p.op("scalar", lambda e, o=sq_scr, i=x_ap, a=ssq:
         e.activation(o, i, ACT.Square, accum_out=a), sem)
    p.wait("vector", sem)
    p.op("vector", lambda e, o=mean, i=ssum: e.tensor_scalar_mul(o, i, 1.0 / D), sem)
    p.wait("vector", sem)
    p.op("vector", lambda e, o=m2, i=mean, b=mean:
         e.tensor_tensor(o, i, b, ALU.mult), sem)
    p.wait("vector", sem)
    p.op("vector", lambda e, o=var, i=ssq, b=m2:
         e.tensor_scalar(o, i, 1.0 / D, b, ALU.mult, ALU.subtract), sem)
    p.wait("vector", sem)
    p.op("vector", lambda e, o=var, i=var: e.tensor_scalar_add(o, i, 1e-5), sem)
    p.wait("vector", sem)
    p.op("vector", lambda e, o=rstd, i=var: e.reciprocal(o, i), sem)
    p.wait("scalar", sem)
    p.op("scalar", lambda e, o=rstd, i=rstd: e.activation(o, i, ACT.Sqrt), sem)
    p.wait("vector", sem)
    p.op("vector", lambda e, o=x_ap, i=x_ap, a=mean, b=rstd:
         e.tensor_scalar(o, i, a, b, ALU.subtract, ALU.mult), sem)



def build_l1(upto=8):
    p = _new_prog("l1")
    nc = p.nc
    dt = lambda n, sh, ty=F32, kind="ExternalInput": nc.dram_tensor(
        n, list(sh), ty, kind=kind).ap()

    x_in = dt("x_in", [T, D])
    xT_in = dt("xT_in", [D, T])
    xoT_in = dt("xoT_in", [D, T])
    wq_in = dt("wq_in", [H0, 128, DC, DH])
    wk_in = dt("wk_in", [H0, 128, DC, DH])
    wv_in = dt("wv_in", [H0, 128, DC, DH])
    wo_in = dt("wo_in", [D, D])
    bq_in = dt("bq_in", [DH, DC])
    bk_in = dt("bk_in", [DH, DC])
    bv_in = dt("bv_in", [DH, DC])
    bo_in = dt("bo_in", [1, D])
    wmq_in = dt("wmq_in", [DH, M0])
    wmk_in = dt("wmk_in", [DH, M0])
    rw_in = dt("rw_in", [D, NE0])
    rb_in = dt("rb_in", [NE0, 1])
    ones_in = dt("ones_in", [1, 128])
    idf_in = dt("idf_in", [128, 128])

    x1_out = dt("x1_out", [T, D], F32, "ExternalOutput")
    s_out = dt("s_out", [T, NE0], F32, "ExternalOutput")

    x_tm = p.sb("x_tm", [128, TT, D], F32)
    qm = p.sb("qm", [128, H0 // 2, T], BF16)
    km = p.sb("km", [128, H0, 2 * T // 128 * M0], BF16)
    kvb = p.sb("kvb", [128, 8 * 256], BF16)
    ksblk = p.sb("ksblk", [128, H0, H0], BF16)
    rnorm = p.sb("rnorm", [128, TT, 16], F32)
    bq = p.sb("bq", [128, DC], F32)
    bk = p.sb("bk", [128, DC], F32)
    bv = p.sb("bv", [128, DC], F32)
    wmq = p.sb("wmq", [128, M0], BF16)
    wmk = p.sb("wmk", [128, M0], BF16)
    idf = p.sb("idf", [128, 128], F32)
    idb = p.sb("idb", [128, 128], BF16)
    scr1 = p.sb("scr1", [128, T], F32)
    ssum = p.sb("ssum", [128, 1], F32)
    ssq = p.sb("ssq", [128, 1], F32)
    mean = p.sb("mean", [128, 1], F32)
    m2 = p.sb("m2", [128, 1], F32)
    var = p.sb("var", [128, 1], F32)
    rstd = p.sb("rstd", [128, 1], F32)
    s_tm = p.sb("s_tm", [128, TT, NE0], F32)

    p.dma("sync", "ld", x_tm[:, :, :], x_in.rearrange("(tt p) d -> p tt d", p=128))
    p.dma("sync", "ld", bq[:, :], bq_in[:, :])
    p.dma("sync", "ld", bk[:, :], bk_in[:, :])
    p.dma("sync", "ld", bv[:, :], bv_in[:, :])
    p.dma("sync", "ld", idf[:, :], idf_in[:, :])
    p.dma("gpsimd", "ldg", wmq[:, :], wmq_in[:, :])
    p.dma("gpsimd", "ldg", wmk[:, :], wmk_in[:, :])
    p.wait("vector", "ld")
    p.op("vector", lambda e, o=idb[:, :], i=idf[:, :]: e.tensor_copy(o, i), "idb")

    with contextlib.ExitStack() as sc1:
      if upto >= 2:
        xT = p.sb("xT", [128, DC, T], BF16, sc1)
        xoT = p.sb("xoT", [128, DC, T], BF16, sc1)
        wbuf = p.sb("wbuf", [128, 2, DC, DH], BF16, sc1)
        kf = p.sb("kf", [128, 2 * T], BF16, sc1)

        p.dma("gpsimd", "ldg", xT[:, :, :], xT_in.rearrange("(c p) t -> p c t", p=128))
        p.dma("gpsimd", "ldg", xoT[:, :, :], xoT_in.rearrange("(c p) t -> p c t", p=128))
        ldg_all = p.val("ldg")

        qps = contextlib.ExitStack()
        ps_q = p.ps("ps_q", [128, T], qps)
        ps_qm = p.ps("ps_qm", [128, T], qps)

        for h in range(H0):
            buf = h % 2
            if h >= 2:
                p.wait("gpsimd", "qpr", h - 1)
            p.dma("gpsimd", "wq%d" % buf, wbuf[:, buf, :, :], wq_in[h])
            p.wait("tensor", "ld", 5 * 16)
            p.wait("tensor", "ldg", ldg_all)
            p.wait("tensor", "wq%d" % buf, p.val("wq%d" % buf))
            if h >= 1:
                p.wait("tensor", "qfe", h)
            for dc in range(DC):
                p.mm(ps_q[:, 0:512], wbuf[:, buf, dc, :], xT[:, dc, 0:512],
                     dc == 0, dc == DC - 1)
                p.mm(ps_q[:, 512:1024], wbuf[:, buf, dc, :], xT[:, dc, 512:1024],
                     dc == 0, dc == DC - 1,
                     sem=None if dc < DC - 1 else "qpr")
            p.wait("scalar", "qpr", h + 1)
            p.op("scalar", lambda e, o=kf[:, 0:T], i=ps_q[:, :], b=bq[:, h:h + 1]:
                 e.activation(o, i, ACT.Identity, bias=b), "qfe")
            base = (h % 2) * 64
            p.wait("tensor", "qfe", p.val("qfe"))
            if h % 2 == 0 and h >= 2:
                p.wait("tensor", "qel", 3 * (h // 2))
            p.mm(ps_qm[base:base + 64, 0:512], wmq[:, :], kf[:, 0:512], True, True)
            p.mm(ps_qm[base:base + 64, 512:1024], wmq[:, :], kf[:, 512:1024],
                 True, True, sem="qmm")
            if h % 2 == 1:
                pair = h // 2
                p.wait("vector", "qmm", p.val("qmm"))
                elu1(p, ps_qm[:, :], qm[:, pair, :], scr1[:, 0:T], "qel")

        qps.close()
        kps = contextlib.ExitStack()
        ps_k = p.ps("ps_k", [128, 2 * T], kps)
        ps_km = p.ps("ps_km", [128, H0 * M0], kps)

        p.wait("tensor", "qel", p.val("qel"))
        for h in range(H0):
            buf = h % 2
            if h < 2:
                p.wait("gpsimd", "qpr", H0)
            else:
                p.wait("gpsimd", "kpr", h - 1)
            p.dma("gpsimd", "wk%d" % buf, wbuf[:, buf, :, :], wk_in[h])
            p.wait("tensor", "wk%d" % buf, p.val("wk%d" % buf))
            if h >= 1:
                p.wait("tensor", "kfe", h)
            for dc in range(DC):
                for q4 in range(4):
                    src = xT if q4 < 2 else xoT
                    sl = slice((q4 % 2) * 512, (q4 % 2) * 512 + 512)
                    p.mm(ps_k[:, q4 * 512:(q4 + 1) * 512], wbuf[:, buf, dc, :],
                         src[:, dc, sl], dc == 0, dc == DC - 1,
                         sem=None if not (dc == DC - 1 and q4 == 3) else "kpr")
            p.wait("scalar", "kpr", h + 1)
            p.op("scalar", lambda e, o=kf[:, :], i=ps_k[:, :], b=bk[:, h:h + 1]:
                 e.activation(o, i, ACT.Identity, bias=b), "kfe")
            p.wait("tensor", "kfe", p.val("kfe"))
            if h >= 1:
                p.wait("tensor", "kel", 3 * h)
            for t16 in range(16):
                p.mm(ps_km[:, t16 * M0:(t16 + 1) * M0],
                     kf[:, t16 * 128:(t16 + 1) * 128], wmk[:, :], True, True,
                     sem=None if t16 < 15 else "kmm")
            p.wait("vector", "kmm", p.val("kmm"))
            elu1(p, ps_km[:, :], km[:, h, :], scr1[:, 0:H0 * M0], "kel")

        kps.close()

    with contextlib.ExitStack() as sc2:
        xT2 = p.sb("xT2", [128, DC, T], BF16, sc2)
        xoT2 = p.sb("xoT2", [128, DC, T], BF16, sc2)
        wvb = p.sb("wvb", [128, 2, DC, DH], BF16, sc2)
        vfm = p.sb("vfm", [128, 2 * T], BF16, sc2)
        vaug = p.sb("vaug", [128, 16, 129], BF16, sc2)

        p.wait("gpsimd", "kmm", p.val("kmm"))
        p.dma("gpsimd", "ldg2", xT2[:, :, :], xT_in.rearrange("(c p) t -> p c t", p=128))
        p.dma("gpsimd", "ldg2", xoT2[:, :, :], xoT_in.rearrange("(c p) t -> p c t", p=128))
        ldg2_all = p.val("ldg2")

        ps_v = p.ps("ps_v", [128, T], sc2)
        ps_vt = p.ps("ps_vt", [128, 128], sc2, BF16)
        ps_kv = p.ps("ps_kv", [128, 8 * 256], sc2)

        p.wait("tensor", "kel", p.val("kel"))
        for h in range(H0):
            buf = h % 2
            if h >= 2:
                p.wait("gpsimd", "vpr", 2 * (h - 1))
            p.dma("gpsimd", "wv%d" % buf, wvb[:, buf, :, :], wv_in[h])
            p.wait("tensor", "wv%d" % buf, p.val("wv%d" % buf))
            p.wait("tensor", "ldg2", ldg2_all)
            for vh in range(2):
                src = xT2 if vh == 0 else xoT2
                if vh == 1:
                    p.wait("tensor", "vfe", p.val("vfe"))
                for dc in range(DC):
                    for q2 in range(2):
                        sl = slice(q2 * 512, q2 * 512 + 512)
                        p.mm(ps_v[:, q2 * 512:(q2 + 1) * 512], wvb[:, buf, dc, :],
                             src[:, dc, sl], dc == 0, dc == DC - 1,
                             sem=None if not (dc == DC - 1 and q2 == 1) else "vpr")
                p.wait("scalar", "vpr", 2 * h + vh + 1)
                p.op("scalar", lambda e, o=vfm[:, vh * T:(vh + 1) * T], i=ps_v[:, :],
                     b=bv[:, h:h + 1]:
                     e.activation(o, i, ACT.Identity, bias=b), "vfe")
            p.wait("tensor", "vfe", p.val("vfe"))
            p.wait("tensor", "idb", 1)
            p.wait("vector", "kvm", p.val("kvm"))
            for t16 in range(16):
                p.wait("tensor", "vte", p.val("vte"))
                p.trans(ps_vt[:, :], vfm[:, t16 * 128:(t16 + 1) * 128], idb[:, :],
                        vaug[:, t16, 0:128], "vte")
            p.op("vector", lambda e, o=vaug[:, :, 128:129]: e.memset(o, 1.0), "vte")
            base = (h % 2) * 64
            fsl = slice((h // 2) * 256, (h // 2) * 256 + 129)
            p.wait("tensor", "vte", p.val("vte"))
            for t16 in range(16):
                p.mm(ps_kv[base:base + 64, fsl], km[:, h, t16 * M0:(t16 + 1) * M0],
                     vaug[:, t16, :], t16 == 0, t16 == 15,
                     sem=None if t16 < 15 else "kvm")
        p.wait("vector", "kvm", p.val("kvm"))
        p.op("vector", lambda e,
             o=kvb[:, :].rearrange("p (g s) -> p g s", g=8)[:, :, 0:129],
             i=ps_kv[:, :].rearrange("p (g s) -> p g s", g=8)[:, :, 0:129]:
             e.tensor_copy(o, i), "kve")

    p.wait("vector", "kve", p.val("kve"))
    p.op("vector", lambda e, o=ksblk[:, :, :]: e.memset(o, 0.0), "ksb")
    p.wait("vector", "ksb")
    for h in range(H0):
        base = (h % 2) * 64
        col = (h // 2) * 256 + 128
        p.op("vector", lambda e, o=ksblk[base:base + 64, h, h:h + 1],
             i=kvb[base:base + 64, col:col + 1]: e.tensor_copy(o, i), "ksb")

    with contextlib.ExitStack() as sc3:
        normf = p.sb("normf", [16, T], F32, sc3)
        eps6 = p.sb("eps6", [16, 1], F32, sc3)
        p.op("vector", lambda e, o=eps6[:, :]: e.memset(o, 1e-6), "eps6")
        ps_n = p.ps("ps_n", [16, T], sc3)
        ps_nt = p.ps("ps_nt", [128, 16], sc3)
        p.wait("tensor", "ksb", p.val("ksb"))
        p.wait("tensor", "qel", p.val("qel"))
        for h in range(H0):
            pair = h // 2
            for th in range(2):
                p.mm(ps_n[:, th * 512:(th + 1) * 512], ksblk[:, h, :],
                     qm[:, pair, th * 512:(th + 1) * 512],
                     h == 0, h == H0 - 1,
                     sem=None if not (h == H0 - 1 and th == 1) else "nrm")
        p.wait("scalar", "nrm", p.val("nrm"))
        p.wait("scalar", "eps6", p.val("eps6"))
        p.op("scalar", lambda e, o=normf[:, :], i=ps_n[:, :], b=eps6[:, :]: e.activation(o, i, ACT.Identity, bias=b), "nrme")
        p.wait("tensor", "nrme", p.val("nrme"))
        for tt in range(TT):
            p.wait("tensor", "nte", p.val("nte"))
            p.trans(ps_nt[:, :], normf[:, tt * 128:(tt + 1) * 128], idf[0:16, 0:16],
                    rnorm[:, tt, :], "nte")
        p.wait("vector", "nte", p.val("nte"))
        p.wait("vector", "nte", p.val("nte"))
        p.op("vector", lambda e, o=rnorm[:, :, :], i=rnorm[:, :, :]:
             e.reciprocal(o, i), "ntr")

    with contextlib.ExitStack() as sc4:
        woT = p.sb("woT", [128, DC, D], BF16, sc4)
        aoT = p.sb("aoT", [128, H0, 128], BF16, sc4)
        ao_s = p.sb("ao_s", [128, 128], BF16, sc4)
        sq_scr = p.sb("sq_scr", [128, D], F32, sc4)
        bo = p.sb("bo", [1, D], F32, sc4)
        onesf = p.sb("onesf", [1, 128], F32, sc4)
        p.wait("sync", "ntr", p.val("ntr"))
        p.dma("sync", "ld2", bo[:, :], bo_in[:, :])
        p.dma("sync", "ld2", onesf[:, :], ones_in[:, :])
        p.wait("gpsimd", "ntr", p.val("ntr"))
        p.dma("gpsimd", "wo", woT[:, :, :], wo_in.rearrange("(c p) o -> p c o", p=128))

        ps_ao = p.ps("ps_ao", [128, 128], sc4)
        ps_at = p.ps("ps_at", [128, 128], sc4, BF16)
        ps_o = p.ps("ps_o", [128, D], sc4)

        p.wait("tensor", "ntr", p.val("ntr"))
        p.wait("vector", "ntr", p.val("ntr"))
        for tt in range(TT):
            for h in range(H0):
                base = (h % 2) * 64
                pair = h // 2
                fsl = slice((h // 2) * 256, (h // 2) * 256 + 128)
                g = tt * H0 + h
                if g >= 1:
                    p.wait("tensor", "aoe", g)
                p.mm(ps_ao[:, :], qm[base:base + 64, pair, tt * 128:(tt + 1) * 128],
                     kvb[base:base + 64, fsl], True, True, sem="aom")
                p.wait("vector", "aom", p.val("aom"))
                p.op("vector", lambda e, o=ao_s[:, :], i=ps_ao[:, :],
                     r=rnorm[:, tt, h:h + 1]: e.tensor_scalar_mul(o, i, r), "aoe")
                p.wait("tensor", "aoe", p.val("aoe"))
                p.wait("tensor", "att", p.val("att"))
                p.trans(ps_at[:, :], ao_s[:, :], idb[:, :], aoT[:, h, :], "att")
            p.wait("tensor", "wo", 16)
            p.wait("tensor", "ld2", 32)
            p.wait("tensor", "att", p.val("att"))
            if tt >= 1:
                p.wait("tensor", "ore", tt)
            for oc in range(4):
                p.mm(ps_o[:, oc * 512:(oc + 1) * 512], onesf[:, :],
                     bo[:, oc * 512:(oc + 1) * 512], True, False)
            for dc in range(DC):
                for oc in range(4):
                    p.mm(ps_o[:, oc * 512:(oc + 1) * 512], aoT[:, dc, :],
                         woT[:, dc, oc * 512:(oc + 1) * 512], False, dc == DC - 1,
                         sem=None if not (dc == DC - 1 and oc == 3) else "opr")
            p.wait("vector", "opr", p.val("opr"))
            p.op("vector", lambda e, o=x_tm[:, tt, :], i=ps_o[:, :], x=x_tm[:, tt, :]:
                 e.tensor_tensor(o, i, x, ALU.add), "ore")

        for tt in range(TT):
            p.wait("vector", "ore", p.val("ore"))
            layer_norm(p, x_tm[:, tt, :], sq_scr[:, :], ssum[:, :], ssq[:, :],
                       mean[:, :], m2[:, :], var[:, :], rstd[:, :], "lne")

    with contextlib.ExitStack() as sc5:
        x1T = p.sb("x1T", [128, DC, T], BF16, sc5)
        rw = p.sb("rw", [128, DC, NE0], BF16, sc5)
        rb = p.sb("rb", [16, 1], F32, sc5)
        lgf = p.sb("lgf", [16, T], F32, sc5)
        lgt = p.sb("lgt", [128, TT, 16], F32, sc5)
        sumr = p.sb("sumr", [128, TT], F32, sc5)
        ps_t = p.ps("ps_t", [128, 128], sc5)
        ps_r = p.ps("ps_r", [16, T], sc5)
        ps_rt = p.ps("ps_rt", [128, 16], sc5)

        p.wait("gpsimd", "lne", p.val("lne"))
        p.wait("sync", "lne", p.val("lne"))
        p.dma("gpsimd", "rwl", rw[:, :, :], rw_in.rearrange("(c p) w -> p c w", p=128))
        p.dma("sync", "rbl", rb[:, :], rb_in[:, :])

        p.wait("tensor", "lne", p.val("lne"))
        for tt in range(TT):
            for dc in range(DC):
                p.wait("tensor", "x1e", p.val("x1e"))
                p.trans(ps_t[:, :], x_tm[:, tt, dc * 128:(dc + 1) * 128], idf[:, :],
                        x1T[:, dc, tt * 128:(tt + 1) * 128], "x1e")
        p.wait("tensor", "x1e", p.val("x1e"))
        p.wait("tensor", "rwl", p.val("rwl"))
        for dc in range(DC):
            for th in range(2):
                p.mm(ps_r[:, th * 512:(th + 1) * 512], rw[:, dc, :],
                     x1T[:, dc, th * 512:(th + 1) * 512], dc == 0, dc == DC - 1,
                     sem=None if not (dc == DC - 1 and th == 1) else "rmm")
        p.wait("scalar", "rmm", p.val("rmm"))
        p.wait("scalar", "rbl", 16)
        p.op("scalar", lambda e, o=lgf[:, :], i=ps_r[:, :], b=rb[:, :]:
             e.activation(o, i, ACT.Identity, bias=b), "rme")
        p.wait("tensor", "rme", p.val("rme"))
        for tt in range(TT):
            p.wait("tensor", "rte", p.val("rte"))
            p.trans(ps_rt[:, :], lgf[:, tt * 128:(tt + 1) * 128], idf[0:16, 0:16],
                    lgt[:, tt, :], "rte")
        p.wait("scalar", "rte", p.val("rte"))
        p.op("scalar", lambda e, o=lgt[:, :, :], i=lgt[:, :, :]:
             e.activation(o, i, ACT.Exp), "rex")
        p.wait("vector", "rex", p.val("rex"))
        p.op("vector", lambda e, o=sumr[:, :], i=lgt[:, :, :]:
             e.tensor_reduce(o, i, AX.X, ALU.add), "rsm")
        p.wait("vector", "rsm")
        p.op("vector", lambda e, o=sumr[:, :], i=sumr[:, :]: e.reciprocal(o, i), "rsm")
        p.wait("vector", "rsm")
        for tt in range(TT):
            p.op("vector", lambda e, o=s_tm[:, tt, :], i=lgt[:, tt, :],
                 r=sumr[:, tt:tt + 1]: e.tensor_scalar_mul(o, i, r), "rsc")

    p.wait("sync", "lne", p.val("lne"))
    p.dma("sync", "st", x1_out.rearrange("(tt p) d -> p tt d", p=128), x_tm[:, :, :])
    p.wait("sync", "rsc", p.val("rsc"))
    p.dma("sync", "st", s_out.rearrange("(tt p) w -> p tt w", p=128), s_tm[:, :, :])
    p.wait("sync", "st", p.val("st"))
    return p.finish()



def _prep_l1(inputs):
    x = np.ascontiguousarray(
        np.asarray(inputs["L0_attn_hidden_states"], np.float32).reshape(8 * T, D))
    shared = {
        "wq_in": np.ascontiguousarray(
            np.asarray(inputs["L0_attn_wq"], np.float32).T.reshape(DC, 128, H0, DH)
            .transpose(2, 1, 0, 3)),
        "wk_in": np.ascontiguousarray(
            np.asarray(inputs["L0_attn_wk"], np.float32).T.reshape(DC, 128, H0, DH)
            .transpose(2, 1, 0, 3)),
        "wv_in": np.ascontiguousarray(
            np.asarray(inputs["L0_attn_wv"], np.float32).T.reshape(DC, 128, H0, DH)
            .transpose(2, 1, 0, 3)),
        "wo_in": np.ascontiguousarray(np.asarray(inputs["L0_attn_wo"], np.float32).T),
        "bq_in": np.ascontiguousarray(np.asarray(inputs["L0_attn_bq"], np.float32).reshape(DC, DH).T),
        "bk_in": np.ascontiguousarray(np.asarray(inputs["L0_attn_bk"], np.float32).reshape(DC, DH).T),
        "bv_in": np.ascontiguousarray(np.asarray(inputs["L0_attn_bv"], np.float32).reshape(DC, DH).T),
        "bo_in": np.asarray(inputs["L0_attn_bo"], np.float32).reshape(1, D),
        "wmq_in": np.ascontiguousarray(np.asarray(inputs["L0_attn_wmap_q"], np.float32).T),
        "wmk_in": np.ascontiguousarray(np.asarray(inputs["L0_attn_wmap_k"], np.float32).T),
        "rw_in": np.asarray(inputs["L0_ffn_router_w"], np.float32),
        "rb_in": np.asarray(inputs["L0_ffn_router_b"], np.float32).reshape(NE0, 1),
        "ones_in": np.ones((1, 128), np.float32),
        "idf_in": np.eye(128, dtype=np.float32),
    }
    in_maps = []
    for c in range(NCORE):
        o = c ^ 1
        m = dict(shared)
        m["x_in"] = x[c * T:(c + 1) * T]
        m["xT_in"] = np.ascontiguousarray(x[c * T:(c + 1) * T].T)
        m["xoT_in"] = np.ascontiguousarray(x[o * T:(o + 1) * T].T)
        in_maps.append(m)
    return in_maps


def run_l1(inputs):
    nc = build_l1()
    in_maps = _prep_l1(inputs)
    res = run_bass_kernel_spmd(nc, in_maps, core_ids=list(range(NCORE)))
    x1 = np.concatenate([res.results[c]["x1_out"] for c in range(NCORE)], 0)
    s = np.concatenate([res.results[c]["s_out"] for c in range(NCORE)], 0)
    return x1, s





def build_l2():
    p = _new_prog("l2")
    nc = p.nc
    dt = lambda n, sh, ty=F32, kind="ExternalInput": nc.dram_tensor(
        n, list(sh), ty, kind=kind).ap()

    x_in = dt("x_in", [T, D])
    xT_in = dt("xT_in", [D, T])
    sall_in = dt("sall_in", [8 * T, NE0])
    sown_in = dt("sown_in", [T, NE0])
    ew_in = dt("ew_in", [D, D])
    eb_in = dt("eb_in", [1, D])
    ones_in = dt("ones_in", [1, 128])
    x2_out = dt("x2_out", [T, D], F32, "ExternalOutput")

    x_tm = p.sb("x_tm", [128, TT, D], F32)
    x1T = p.sb("x1T", [128, DC, T], BF16)
    ew = p.sb("ew", [128, DC, D], BF16)
    sthr = p.sb("sthr", [128, 64, NE0], F32)
    sown = p.sb("sown", [128, TT, NE0], F32)
    eb = p.sb("eb", [1, D], F32)
    onesf = p.sb("onesf", [1, 128], F32)
    onescol = p.sb("onescol", [128, 1], F32)
    lo = p.sb("lo", [1, NE0], F32)
    hi = p.sb("hi", [1, NE0], F32)
    mid = p.sb("mid", [1, NE0], F32)
    dd = p.sb("dd", [1, NE0], F32)
    sel = p.sb("sel", [1, NE0], F32)
    cntv = p.sb("cntv", [128, NE0], F32)
    wcol = p.sb("wcol", [128, TT], F32)
    scrA = p.sb("scrA", [128, NE0], F32)
    sq_scr = p.sb("sq_scr", [128, D], F32)
    ssum = p.sb("ssum", [128, 1], F32)
    ssq = p.sb("ssq", [128, 1], F32)
    mean = p.sb("mean", [128, 1], F32)
    m2 = p.sb("m2", [128, 1], F32)
    var = p.sb("var", [128, 1], F32)
    rstd = p.sb("rstd", [128, 1], F32)

    ps_b = p.ps("ps_b", [128, NE0])
    ps_c = p.ps("ps_c", [1, NE0])
    ps_tau = p.ps("ps_tau", [128, NE0])
    ps_z = p.ps("ps_z", [128, D])

    p.dma("sync", "ld", x_tm[:, :, :], x_in.rearrange("(tt p) d -> p tt d", p=128))
    p.dma("sync", "ld", onesf[:, :], ones_in[:, :])
    p.dma("gpsimd", "ldg", x1T[:, :, :], xT_in.rearrange("(c p) t -> p c t", p=128))
    p.dma("gpsimd", "ldg", ew[:, :, :], ew_in.rearrange("(c p) o -> p c o", p=128))

    p.wait("vector", "ld", p.val("ld"))
    p.op("vector", lambda e, o=lo[:, :]: e.memset(o, 0.0), "bi")
    p.op("vector", lambda e, o=hi[:, :]: e.memset(o, 1.0), "bi")
    p.op("vector", lambda e, o=onescol[:, :]: e.memset(o, 1.0), "bi")
    for it in range(42):
        p.wait("vector", "bi")
        p.op("vector", lambda e, o=mid[:, :], a=lo[:, :], b=hi[:, :]:
             e.tensor_tensor(o, a, b, ALU.add), "bi")
        p.wait("vector", "bi")
        p.op("vector", lambda e, o=mid[:, :]: e.tensor_scalar_mul(o, o, 0.5), "bi")
        p.wait("tensor", "bi", p.val("bi"))
        if it:
            p.wait("tensor", "gei", p.val("gei"))
        p.mm(ps_b[:, :], onesf[:, :], mid[:, :], True, True, sem="bmm")
        p.wait("vector", "bmm", p.val("bmm"))
        p.op("vector", lambda e, o=ge[:, :, :], a=sthr[:, :, :],
             b=ps_b[:, :].unsqueeze(1).broadcast_to([128, 64, NE0]):
             e.tensor_tensor(o, a, b, ALU.is_ge), "gei")
        p.wait("vector", "gei")
        p.op("vector", lambda e, o=cntv[:, :],
             i=ge[:, :, :].rearrange("p j e -> p e j"):
             e.tensor_reduce(o, i, AX.X, ALU.add), "gei")
        p.wait("tensor", "gei", p.val("gei"))
        if it:
            p.wait("tensor", "seli", p.val("seli"))
        p.mm(ps_c[:, :], onescol[:, :], cntv[:, :], True, True, sem="cmm")
        p.wait("vector", "cmm", p.val("cmm"))
        p.op("vector", lambda e, o=sel[:, :], i=ps_c[:, :]:
             e.tensor_single_scalar(o, i, float(K0_TOP), ALU.is_ge), "seli")
        p.wait("vector", "seli")
        p.op("vector", lambda e, o=dd[:, :], a=mid[:, :], b=lo[:, :]:
             e.tensor_tensor(o, a, b, ALU.subtract), "bi")
        p.wait("vector", "bi")
        p.op("vector", lambda e, o=dd[:, :], a=sel[:, :], b=dd[:, :]:
             e.tensor_tensor(o, a, b, ALU.mult), "bi")
        p.wait("vector", "bi")
        p.op("vector", lambda e, o=lo[:, :], a=lo[:, :], b=dd[:, :]:
             e.tensor_tensor(o, a, b, ALU.add), "bi")
        p.wait("vector", "bi")
        p.op("vector", lambda e, o=dd[:, :], a=hi[:, :], b=mid[:, :]:
             e.tensor_tensor(o, a, b, ALU.subtract), "bi")
        p.wait("vector", "bi")
        p.op("vector", lambda e, o=dd[:, :], a=sel[:, :], b=dd[:, :]:
             e.tensor_tensor(o, a, b, ALU.mult), "bi")
        p.wait("vector", "bi")
        p.op("vector", lambda e, o=hi[:, :], a=mid[:, :], b=dd[:, :]:
             e.tensor_tensor(o, a, b, ALU.add), "bi")
    p.wait("tensor", "bi", p.val("bi"))
    p.wait("tensor", "gei", p.val("gei"))
    p.mm(ps_tau[:, :], onesf[:, :], lo[:, :], True, True, sem="tmm")

    p.wait("vector", "tmm", p.val("tmm"))
    for tt in range(TT):
        p.op("vector", lambda e, o=scrA[:, :], a=sown[:, tt, :], b=ps_tau[:, :]:
             e.tensor_tensor(o, a, b, ALU.is_ge), "wt")
        p.wait("vector", "wt")
        p.op("vector", lambda e, o=scrA[:, :], a=sown[:, tt, :], b=scrA[:, :]:
             e.tensor_tensor(o, a, b, ALU.mult), "wt")
        p.wait("vector", "wt")
        p.op("vector", lambda e, o=wcol[:, tt:tt + 1], i=scrA[:, :]:
             e.reduce_sum(o, i, AX.X), "wt")
        p.wait("vector", "wt")

    for tt in range(TT):
        p.wait("tensor", "ld", p.val("ld"))
        p.wait("tensor", "ldg", p.val("ldg"))
        if tt:
            p.wait("tensor", "zev", p.val("zev"))
        for oc in range(4):
            p.mm(ps_z[:, oc * 512:(oc + 1) * 512], onesf[:, :],
                 eb[:, oc * 512:(oc + 1) * 512], True, False)
        for dc in range(DC):
            for oc in range(4):
                p.mm(ps_z[:, oc * 512:(oc + 1) * 512], x1T[:, dc, tt * 128:(tt + 1) * 128],
                     ew[:, dc, oc * 512:(oc + 1) * 512], False, dc == DC - 1,
                     sem=None if not (dc == DC - 1 and oc == 3) else "zmm")
        p.wait("vector", "zmm", p.val("zmm"))
        p.wait("vector", "wt", p.val("wt"))
        for oc in range(4):
            p.op("vector", lambda e, o=x_tm[:, tt, oc * 512:(oc + 1) * 512],
                 i=ps_z[:, oc * 512:(oc + 1) * 512], w=wcol[:, tt:tt + 1],
                 x=x_tm[:, tt, oc * 512:(oc + 1) * 512]:
                 e.scalar_tensor_tensor(o, i, w, x, ALU.mult, ALU.add), "zev")
        p.wait("vector", "zev")
        layer_norm(p, x_tm[:, tt, :], sq_scr[:, :], ssum[:, :], ssq[:, :],
                   mean[:, :], m2[:, :], var[:, :], rstd[:, :], "lne")

    p.wait("sync", "lne", p.val("lne"))
    p.dma("sync", "st", x2_out.rearrange("(tt p) d -> p tt d", p=128), x_tm[:, :, :])
    p.wait("sync", "st", p.val("st"))
    return p.finish()


def _prep_l2(inputs, x1, s_all):
    shared = {
        "sall_in": np.ascontiguousarray(s_all),
        "ew_in": np.asarray(inputs["L0_ffn_expert_w"], np.float32),
        "eb_in": np.asarray(inputs["L0_ffn_expert_b"], np.float32).reshape(1, D),
        "ones_in": np.ones((1, 128), np.float32),
    }
    in_maps = []
    for c in range(NCORE):
        m = dict(shared)
        m["x_in"] = x1[c * T:(c + 1) * T]
        m["xT_in"] = np.ascontiguousarray(x1[c * T:(c + 1) * T].T)
        m["sown_in"] = s_all[c * T:(c + 1) * T]
        in_maps.append(m)
    return in_maps


def run_l2(inputs, x1, s_all):
    nc = build_l2()
    in_maps = _prep_l2(inputs, x1, s_all)
    res = run_bass_kernel_spmd(nc, in_maps, core_ids=list(range(NCORE)))
    return np.concatenate([res.results[c]["x2_out"] for c in range(NCORE)], 0)



def build_l3():
    p = _new_prog("l3")
    nc = p.nc
    dt = lambda n, sh, ty=F32, kind="ExternalInput": nc.dram_tensor(
        n, list(sh), ty, kind=kind).ap()

    xT_in = dt("xT_in", [D, T])
    wq_in = dt("wq_in", [D, D])
    wk_in = dt("wk_in", [D, KVH * DH])
    wv_in = dt("wv_in", [D, KVH * DH])
    qnw_in = dt("qnw_in", [1, DH])
    knw_in = dt("knw_in", [1, DH])
    cos_in = dt("cos_in", [T, 64])
    sin_in = dt("sin_in", [T, 64])
    ones_in = dt("ones_in", [1, 128])
    q_out = dt("q_out", [T, D], BF16, "ExternalOutput")
    k_out = dt("k_out", [T, KVH * DH], BF16, "ExternalOutput")
    v_out = dt("v_out", [T, KVH * DH], BF16, "ExternalOutput")

    xT = p.sb("xT", [128, DC, T], BF16)
    wq = p.sb("wq", [128, DC, D], BF16)
    wk = p.sb("wk", [128, DC, KVH * DH], BF16)
    wv = p.sb("wv", [128, DC, KVH * DH], BF16)
    qnw = p.sb("qnw", [1, DH], F32)
    knw = p.sb("knw", [1, DH], F32)
    onesf = p.sb("onesf", [1, 128], F32)
    cosb = p.sb("cosb", [128, TT, 64], F32)
    sinb = p.sb("sinb", [128, TT, 64], F32)
    wbc = p.sb("wbc", [128, DH], F32)
    kwbc = p.sb("kwbc", [128, DH], F32)
    qtm = p.sb("qtm", [128, D], F32)
    ktm = p.sb("ktm", [128, KVH * DH], F32)
    sq_scr = p.sb("sq_scr", [128, D], F32)
    ssq = p.sb("ssq", [128, H1], F32)
    rr = p.sb("rr", [128, H1], F32)
    scrA = p.sb("scrA", [128, H1 * 64], F32)
    scrB = p.sb("scrB", [128, H1 * 64], F32)
    qbf = p.sb("qbf", [128, D], BF16)
    kbf = p.sb("kbf", [128, KVH * DH], BF16)
    vbf = p.sb("vbf", [128, KVH * DH], BF16)

    ps_q = p.ps("ps_q", [128, D])
    ps_k = p.ps("ps_k", [128, KVH * DH])
    ps_v = p.ps("ps_v", [128, KVH * DH])
    ps_w = p.ps("ps_w", [128, DH])

    p.dma("gpsimd", "ldg", xT[:, :, :], xT_in.rearrange("(c p) t -> p c t", p=128))
    p.dma("gpsimd", "ldg", wq[:, :, :], wq_in.rearrange("(c p) o -> p c o", p=128))
    p.dma("gpsimd", "ldg", wk[:, :, :], wk_in.rearrange("(c p) o -> p c o", p=128))
    p.dma("gpsimd", "ldg", wv[:, :, :], wv_in.rearrange("(c p) o -> p c o", p=128))
    p.dma("sync", "ld", qnw[:, :], qnw_in[:, :])
    p.dma("sync", "ld", knw[:, :], knw_in[:, :])
    p.dma("sync", "ld", onesf[:, :], ones_in[:, :])
    p.dma("sync", "ld", cosb[:, :, :], cos_in.rearrange("(tt p) f -> p tt f", p=128))
    p.dma("sync", "ld", sinb[:, :, :], sin_in.rearrange("(tt p) f -> p tt f", p=128))

    p.wait("tensor", "ld", p.val("ld"))
    p.mm(ps_w[:, :], onesf[:, :], qnw[:, :], True, True, sem="wb")
    p.wait("vector", "wb", p.val("wb"))
    p.op("vector", lambda e, o=wbc[:, :], i=ps_w[:, :]: e.tensor_copy(o, i), "wbe")
    p.wait("tensor", "wbe", p.val("wbe"))
    p.mm(ps_w[:, :], onesf[:, :], knw[:, :], True, True, sem="wb")
    p.wait("vector", "wb", p.val("wb"))
    p.op("vector", lambda e, o=kwbc[:, :], i=ps_w[:, :]: e.tensor_copy(o, i), "wbe")

    def rmsnorm_rope(tt, nh, xv, sqv, ssqv, rv, wv_bc, outbf, sem):
        p.op("scalar", lambda e, o=sqv, i=xv: e.activation(o, i, ACT.Square), sem)
        p.wait("vector", sem)
        p.op("vector", lambda e, o=ssqv, i=sqv.rearrange("p (h f) -> p h f", h=nh):
             e.tensor_reduce(o, i, AX.X, ALU.add), sem)
        p.wait("vector", sem)
        p.op("vector", lambda e, o=rv, i=ssqv:
             e.tensor_scalar(o, i, 1.0 / DH, 1e-6, ALU.mult, ALU.add), sem)
        p.wait("scalar", sem)
        p.op("scalar", lambda e, o=rv, i=rv: e.activation(o, i, ACT.Sqrt), sem)
        p.wait("vector", sem)
        p.op("vector", lambda e, o=rv, i=rv: e.reciprocal(o, i), sem)
        p.wait("vector", sem)
        for h in range(nh):
            p.op("vector", lambda e, o=xv[:, h * DH:(h + 1) * DH],
                 i=xv[:, h * DH:(h + 1) * DH], r=rv[:, h:h + 1], w=wv_bc:
                 e.scalar_tensor_tensor(o, i, r, w, ALU.mult, ALU.mult), sem)
        p.wait("vector", sem)
        x3 = xv.rearrange("p (h f) -> p h f", h=nh)
        o3 = outbf.rearrange("p (h f) -> p h f", h=nh)
        cb = cosb[:, tt, :].unsqueeze(1).broadcast_to([128, nh, 64])
        sb_ = sinb[:, tt, :].unsqueeze(1).broadcast_to([128, nh, 64])
        a3 = scrA.rearrange("p (h f) -> p h f", h=H1)[:, 0:nh, :]
        b3 = scrB.rearrange("p (h f) -> p h f", h=H1)[:, 0:nh, :]
        p.op("vector", lambda e, o=a3, i=x3[:, :, 0:64], c=cb:
             e.tensor_tensor(o, i, c, ALU.mult), sem)
        p.op("vector", lambda e, o=b3, i=x3[:, :, 64:128], c=sb_:
             e.tensor_tensor(o, i, c, ALU.mult), sem)
        p.wait("vector", sem)
        p.op("vector", lambda e, o=o3[:, :, 0:64], a=a3, b=b3:
             e.tensor_tensor(o, a, b, ALU.subtract), sem)
        p.wait("vector", sem)
        p.op("vector", lambda e, o=a3, i=x3[:, :, 0:64], c=sb_:
             e.tensor_tensor(o, i, c, ALU.mult), sem)
        p.op("vector", lambda e, o=b3, i=x3[:, :, 64:128], c=cb:
             e.tensor_tensor(o, i, c, ALU.mult), sem)
        p.wait("vector", sem)
        p.op("vector", lambda e, o=o3[:, :, 64:128], a=a3, b=b3:
             e.tensor_tensor(o, a, b, ALU.add), sem)

    for tt in range(TT):
        p.wait("tensor", "ldg", p.val("ldg"))
        p.wait("tensor", "wbe", p.val("wbe"))
        if tt:
            p.wait("tensor", "qe", p.val("qe"))
        for dc in range(DC):
            for oc in range(4):
                p.mm(ps_q[:, oc * 512:(oc + 1) * 512], xT[:, dc, tt * 128:(tt + 1) * 128],
                     wq[:, dc, oc * 512:(oc + 1) * 512], dc == 0, dc == DC - 1,
                     sem=None if not (dc == DC - 1 and oc == 3) else "qmm")
        p.wait("vector", "qmm", p.val("qmm"))
        if tt:
            p.wait("vector", "qst", p.val("qst"))
        p.op("vector", lambda e, o=qtm[:, :], i=ps_q[:, :]: e.tensor_copy(o, i), "qe")
        p.wait("vector", "qe")
        p.wait("scalar", "qe", p.val("qe"))
        rmsnorm_rope(tt, H1, qtm[:, :], sq_scr[:, :], ssq[:, :], rr[:, :],
                     wbc[:, :], qbf[:, :], "qe")
        p.wait("scalar", "qe", p.val("qe"))
        p.dma("scalar", "qst", q_out[tt * 128:(tt + 1) * 128, :], qbf[:, :])
        if tt:
            p.wait("tensor", "ke", p.val("ke"))
        for dc in range(DC):
            p.mm(ps_k[:, :], xT[:, dc, tt * 128:(tt + 1) * 128],
                 wk[:, dc, :], dc == 0, dc == DC - 1,
                 sem=None if dc < DC - 1 else "kmm")
        p.wait("vector", "kmm", p.val("kmm"))
        if tt:
            p.wait("vector", "kst", p.val("kst"))
        p.op("vector", lambda e, o=ktm[:, :], i=ps_k[:, :]: e.tensor_copy(o, i), "ke")
        p.wait("vector", "ke")
        p.wait("scalar", "ke", p.val("ke"))
        rmsnorm_rope(tt, KVH, ktm[:, :], sq_scr[:, 0:KVH * DH], ssq[:, 0:KVH],
                     rr[:, 0:KVH], kwbc[:, :], kbf[:, :], "ke")
        p.wait("scalar", "ke", p.val("ke"))
        p.dma("scalar", "kst", k_out[tt * 128:(tt + 1) * 128, :], kbf[:, :])
        if tt:
            p.wait("tensor", "ve", p.val("ve"))
        for dc in range(DC):
            p.mm(ps_v[:, :], xT[:, dc, tt * 128:(tt + 1) * 128],
                 wv[:, dc, :], dc == 0, dc == DC - 1,
                 sem=None if dc < DC - 1 else "vmm")
        p.wait("vector", "vmm", p.val("vmm"))
        if tt:
            p.wait("vector", "vst", p.val("vst"))
        p.op("vector", lambda e, o=vbf[:, :], i=ps_v[:, :]: e.tensor_copy(o, i), "ve")
        p.wait("scalar", "ve", p.val("ve"))
        p.dma("scalar", "vst", v_out[tt * 128:(tt + 1) * 128, :], vbf[:, :])
    p.wait("sync", "qst", p.val("qst"))
    p.wait("sync", "kst", p.val("kst"))
    p.wait("sync", "vst", p.val("vst"))
    return p.finish()


def _rope_tables(core):
    half = core % 2
    pos = np.arange(half * T, half * T + T, dtype=np.float64)
    inv = 1.0 / (10000.0 ** (np.arange(0, DH, 2, dtype=np.float64) / DH))
    ang = pos[:, None] * inv[None, :]
    return (np.cos(ang).astype(np.float32), np.sin(ang).astype(np.float32))


def _prep_l3(inputs, x2):
    shared = {
        "wq_in": np.ascontiguousarray(np.asarray(inputs["L1_attn_wq"], np.float32).T),
        "wk_in": np.ascontiguousarray(np.asarray(inputs["L1_attn_wk"], np.float32).T),
        "wv_in": np.ascontiguousarray(np.asarray(inputs["L1_attn_wv"], np.float32).T),
        "qnw_in": np.asarray(inputs["L1_attn_q_norm_w"], np.float32).reshape(1, DH),
        "knw_in": np.asarray(inputs["L1_attn_k_norm_w"], np.float32).reshape(1, DH),
        "ones_in": np.ones((1, 128), np.float32),
    }
    in_maps = []
    for c in range(NCORE):
        m = dict(shared)
        m["xT_in"] = np.ascontiguousarray(x2[c * T:(c + 1) * T].T)
        cos, sin = _rope_tables(c)
        m["cos_in"] = cos
        m["sin_in"] = sin
        in_maps.append(m)
    return in_maps


def run_l3(inputs, x2):
    nc = build_l3()
    in_maps = _prep_l3(inputs, x2)
    res = run_bass_kernel_spmd(nc, in_maps, core_ids=list(range(NCORE)))
    q = np.concatenate([res.results[c]["q_out"] for c in range(NCORE)], 0)
    k = np.concatenate([res.results[c]["k_out"] for c in range(NCORE)], 0)
    v = np.concatenate([res.results[c]["v_out"] for c in range(NCORE)], 0)
    return q, k, v



def build_l4():
    p = _new_prog("l4")
    nc = p.nc
    dt = lambda n, sh, ty=F32, kind="ExternalInput": nc.dram_tensor(
        n, list(sh), ty, kind=kind).ap()

    x_in = dt("x_in", [T, D])
    qT_in = dt("qT_in", [D, T], BF16)
    kT_in = dt("kT_in", [KVH * DH, 2 * T], BF16)
    v_in = dt("v_in", [2 * T, KVH * DH], BF16)
    vones_in = dt("vones_in", [128, 16], BF16)
    wo_in = dt("wo_in", [D, D])
    dmask_in = dt("dmask_in", [128, 128])
    bmask_in = dt("bmask_in", [128, 128])
    idb_in = dt("idb_in", [128, 128], BF16)
    rtr_in = dt("rtr_in", [D, E1])
    idf_in = dt("idf_in", [128, 128])
    ones_in = dt("ones_in", [1, 128])
    x3_out = dt("x3_out", [T, D], F32, "ExternalOutput")
    cnt_out = dt("cnt_out", [1, E1], F32, "ExternalOutput")

    x_tm = p.sb("x_tm", [128, TT, D], F32)
    aoT = p.sb("aoT", [128, H1, T], BF16)
    idb = p.sb("idb", [128, 128], BF16)
    onesf = p.sb("onesf", [1, 128], F32)
    onescol = p.sb("onescol", [128, 1], F32)
    mx = p.sb("mx", [128, 1], F32)
    mb = p.sb("mb", [128, 1], F32)
    rcol = p.sb("rcol", [128, 1], F32)
    sq_scr = p.sb("sq_scr", [128, D], F32)
    ssum = p.sb("ssum", [128, 1], F32)
    ssq = p.sb("ssq", [128, 1], F32)
    mean = p.sb("mean", [128, 1], F32)
    m2 = p.sb("m2", [128, 1], F32)
    var = p.sb("var", [128, 1], F32)
    rstd = p.sb("rstd", [128, 1], F32)

    p.dma("sync", "ld", x_tm[:, :, :], x_in.rearrange("(tt p) d -> p tt d", p=128))
    p.dma("sync", "ld", idb[:, :], idb_in[:, :])
    p.dma("sync", "ld", onesf[:, :], ones_in[:, :])
    p.wait("vector", "ld", p.val("ld"))
    p.op("vector", lambda e, o=onescol[:, :]: e.memset(o, 1.0), "oc")

    with contextlib.ExitStack() as scB:
        qT = p.sb("qT", [128, H1, T], BF16, scB)
        kT = p.sb("kT", [128, KVH, 2 * T], BF16, scB)
        vaug = p.sb("vaug", [128, 16, KVH, 132], BF16, scB)
        vo = p.sb("vo", [128, 16], BF16, scB)
        p_sb = p.sb("p_sb", [128, 1152], BF16, scB)
        pt_sb = p.sb("pt_sb", [128, 1152], BF16, scB)
        ao_s = p.sb("ao_s", [128, 128], BF16, scB)
        dmask = p.sb("dmask", [128, 128], F32, scB)
        bmask = p.sb("bmask", [128, 128], F32, scB)

        p.dma("sync", "ld2", qT[:, :, :], qT_in.rearrange("(h p) t -> p h t", p=128))
        p.dma("sync", "ld2", kT[:, :, :], kT_in.rearrange("(h p) t -> p h t", p=128))
        for h4 in range(KVH):
            p.dma("sync", "ld2", vaug[:, :, h4, 0:128],
                  v_in[:, h4 * DH:(h4 + 1) * DH].rearrange("(kbt p) f -> p kbt f", p=128))
        p.dma("sync", "ld2", vo[:, :], vones_in[:, :])
        p.dma("sync", "ld2", dmask[:, :], dmask_in[:, :])
        p.dma("sync", "ld2", bmask[:, :], bmask_in[:, :])
        p.wait("vector", "ld2", p.val("ld2"))
        for h4 in range(KVH):
            p.op("vector", lambda e, o=vaug[:, :, h4, 128:129], i=vo[:, :].unsqueeze(2):
                 e.tensor_copy(o, i), "vob")

        ps_s = p.ps("ps_s", [128, 1152], scB)
        ps_pt = p.ps("ps_pt", [128, 1152], scB, BF16)
        ps_av = p.ps("ps_av", [128, 129], scB)
        ps_at = p.ps("ps_at", [128, 128], scB, BF16)

        for qt in range(TT):
            for h in range(H1):
                kvh = h // KVH
                base_kb = qt * 128
                p.wait("tensor", "ld2", p.val("ld2"))
                p.wait("tensor", "pexp", p.val("pexp"))
                for ci, (off, n) in enumerate(((0, 512), (512, 512), (1024, 128))):
                    p.mm(ps_s[:, off:off + n], qT[:, h, qt * 128:(qt + 1) * 128],
                         kT[:, kvh, base_kb + off:base_kb + off + n], True, True,
                         sem=None if ci < 2 else "smm")
                p.wait("vector", "smm", p.val("smm"))
                p.op("vector", lambda e, o=ps_s[:, 0:128], i=ps_s[:, 0:128], m=bmask[:, :]:
                     e.tensor_tensor(o, i, m, ALU.add), "msk")
                p.op("vector", lambda e, o=ps_s[:, 1024:1152], i=ps_s[:, 1024:1152],
                     m=dmask[:, :]: e.tensor_tensor(o, i, m, ALU.add), "msk")
                p.wait("vector", "msk")
                p.op("vector", lambda e, o=mx[:, :], i=ps_s[:, :]:
                     e.reduce_max(o, i, AX.X), "msk")
                p.wait("vector", "msk")
                p.op("vector", lambda e, o=mb[:, :], i=mx[:, :]:
                     e.tensor_scalar_mul(o, i, -SM_SCALE), "msk")
                p.wait("scalar", "msk", p.val("msk"))
                p.op("scalar", lambda e, o=p_sb[:, :], i=ps_s[:, :], b=mb[:, :]:
                     e.activation(o, i, ACT.Exp, bias=b, scale=SM_SCALE), "pexp")
                p.wait("tensor", "pexp", p.val("pexp"))
                p.wait("tensor", "ptv", p.val("ptv"))
                for b9 in range(9):
                    p.op("tensor", lambda e, o=ps_pt[:, b9 * 128:(b9 + 1) * 128],
                         i=p_sb[:, b9 * 128:(b9 + 1) * 128], idn=idb[:, :]:
                         e.transpose(o, i, idn), "ptt")
                p.wait("vector", "ptt", p.val("ptt"))
                p.op("vector", lambda e, o=pt_sb[:, :], i=ps_pt[:, :]:
                     e.tensor_copy(o, i), "ptv")
                p.wait("tensor", "ptv", p.val("ptv"))
                p.wait("tensor", "ave", p.val("ave"))
                for b9 in range(9):
                    p.mm(ps_av[:, :], pt_sb[:, b9 * 128:(b9 + 1) * 128],
                         vaug[:, qt + b9, kvh, 0:129], b9 == 0, b9 == 8,
                         sem=None if b9 < 8 else "avm")
                p.wait("vector", "avm", p.val("avm"))
                p.op("vector", lambda e, o=rcol[:, :], i=ps_av[:, 128:129]:
                     e.reciprocal(o, i), "ave")
                p.wait("vector", "ave")
                p.op("vector", lambda e, o=ao_s[:, :], i=ps_av[:, 0:128], r=rcol[:, :]:
                     e.tensor_scalar_mul(o, i, r), "ave")
                p.wait("tensor", "ave", p.val("ave"))
                p.wait("tensor", "aoe", p.val("aoe"))
                p.op("tensor", lambda e, o=ps_at[:, :], i=ao_s[:, :], idn=idb[:, :]:
                     e.transpose(o, i, idn), "att")
                p.wait("vector", "att", p.val("att"))
                p.op("vector", lambda e, o=aoT[:, h, qt * 128:(qt + 1) * 128],
                     i=ps_at[:, :]: e.tensor_copy(o, i), "aoe")

    with contextlib.ExitStack() as scC:
        woT = p.sb("woT", [128, DC, D], BF16, scC)
        p.wait("gpsimd", "aoe", p.val("aoe"))
        p.dma("gpsimd", "wo", woT[:, :, :], wo_in.rearrange("(c p) o -> p c o", p=128))
        ps_o = p.ps("ps_o", [128, D], scC)
        p.wait("tensor", "wo", p.val("wo"))
        p.wait("tensor", "aoe", p.val("aoe"))
        for tt in range(TT):
            if tt:
                p.wait("tensor", "ore", p.val("ore"))
            for dc in range(DC):
                for oc in range(4):
                    p.mm(ps_o[:, oc * 512:(oc + 1) * 512], aoT[:, dc, tt * 128:(tt + 1) * 128],
                         woT[:, dc, oc * 512:(oc + 1) * 512], dc == 0, dc == DC - 1,
                         sem=None if not (dc == DC - 1 and oc == 3) else "opr")
            p.wait("vector", "opr", p.val("opr"))
            p.op("vector", lambda e, o=x_tm[:, tt, :], i=ps_o[:, :], x=x_tm[:, tt, :]:
                 e.tensor_tensor(o, i, x, ALU.add), "ore")
        for tt in range(TT):
            p.wait("vector", "ore", p.val("ore"))
            layer_norm(p, x_tm[:, tt, :], sq_scr[:, :], ssum[:, :], ssq[:, :],
                       mean[:, :], m2[:, :], var[:, :], rstd[:, :], "lne")

    with contextlib.ExitStack() as scD:
        x3T = p.sb("x3T", [128, DC, T], BF16, scD)
        rtr = p.sb("rtr", [128, DC, E1], BF16, scD)
        lgt = p.sb("lgt", [128, TT, E1], F32, scD)
        m1 = p.sb("m1", [128, 1], F32, scD)
        m2c = p.sb("m2c", [128, 1], F32, scD)
        oh1 = p.sb("oh1", [128, E1], F32, scD)
        oh2 = p.sb("oh2", [128, E1], F32, scD)
        scrM = p.sb("scrM", [128, E1], F32, scD)
        cnt_s = p.sb("cnt_s", [1, E1], F32, scD)
        idf = p.sb("idf", [128, 128], F32, scD)
        ps_t = p.ps("ps_t", [128, 128], scD)
        ps_lg = p.ps("ps_lg", [128, E1], scD)
        ps_cnt = p.ps("ps_cnt", [1, E1], scD)

        p.wait("gpsimd", "lne", p.val("lne"))
        p.dma("gpsimd", "rtl", rtr[:, :, :], rtr_in.rearrange("(c p) w -> p c w", p=128))
        p.wait("sync", "lne", p.val("lne"))
        p.dma("sync", "idl", idf[:, :], idf_in[:, :])
        p.wait("tensor", "lne", p.val("lne"))
        p.wait("tensor", "idl", p.val("idl"))
        for tt in range(TT):
            for dc in range(DC):
                p.wait("tensor", "x3e", p.val("x3e"))
                p.trans(ps_t[:, :], x_tm[:, tt, dc * 128:(dc + 1) * 128], idf[:, :],
                        x3T[:, dc, tt * 128:(tt + 1) * 128], "x3e")
        p.wait("tensor", "x3e", p.val("x3e"))
        p.wait("tensor", "rtl", p.val("rtl"))
        for tt in range(TT):
            if tt:
                p.wait("tensor", "lge", p.val("lge"))
            for dc in range(DC):
                p.mm(ps_lg[:, :], x3T[:, dc, tt * 128:(tt + 1) * 128], rtr[:, dc, :],
                     dc == 0, dc == DC - 1, sem=None if dc < DC - 1 else "lgm")
            p.wait("vector", "lgm", p.val("lgm"))
            p.op("vector", lambda e, o=lgt[:, tt, :], i=ps_lg[:, :]:
                 e.tensor_copy(o, i), "lge")
        for tt in range(TT):
            p.wait("vector", "lge", p.val("lge"))
            p.op("vector", lambda e, o=m1[:, :], i=lgt[:, tt, :]:
                 e.reduce_max(o, i, AX.X), "tp")
            p.wait("vector", "tp")
            p.op("vector", lambda e, o=oh1[:, :], i=lgt[:, tt, :], m=m1[:, :]:
                 e.tensor_scalar(o, i, m, None, ALU.is_equal), "tp")
            p.wait("vector", "tp")
            p.op("vector", lambda e, o=scrM[:, :], i=oh1[:, :], x=lgt[:, tt, :]:
                 e.scalar_tensor_tensor(o, i, NEG, x, ALU.mult, ALU.add), "tp")
            p.wait("vector", "tp")
            p.op("vector", lambda e, o=m2c[:, :], i=scrM[:, :]:
                 e.reduce_max(o, i, AX.X), "tp")
            p.wait("vector", "tp")
            p.op("vector", lambda e, o=oh2[:, :], i=scrM[:, :], m=m2c[:, :]:
                 e.tensor_scalar(o, i, m, None, ALU.is_equal), "tp")
            p.wait("vector", "tp")
            p.op("vector", lambda e, o=oh1[:, :], a=oh1[:, :], b=oh2[:, :]:
                 e.tensor_tensor(o, a, b, ALU.add), "tp")
            p.wait("tensor", "tp", p.val("tp"))
            p.mm(ps_cnt[:, :], onescol[:, :], oh1[:, :], tt == 0, tt == TT - 1,
                 sem="cm")
            p.wait("vector", "cm", p.val("cm"))
        p.wait("vector", "cm", p.val("cm"))
        p.op("vector", lambda e, o=cnt_s[:, :], i=ps_cnt[:, :]:
             e.tensor_copy(o, i), "cme")

    p.wait("sync", "lne", p.val("lne"))
    p.dma("sync", "st", x3_out.rearrange("(tt p) d -> p tt d", p=128), x_tm[:, :, :])
    p.wait("sync", "cme", p.val("cme"))
    p.dma("sync", "st", cnt_out[:, :], cnt_s[:, :])
    p.wait("sync", "st", p.val("st"))
    return p.finish()


def _prep_l4(inputs, x2, q, k, v):
    import ml_dtypes
    bfd = ml_dtypes.bfloat16
    neg = np.float32(NEG)
    ii, jj = np.meshgrid(np.arange(128), np.arange(128), indexing="ij")
    dmask = np.where(ii >= jj, 0.0, neg).astype(np.float32)
    bmask = np.where(jj > ii, 0.0, neg).astype(np.float32)
    shared = {
        "wo_in": np.ascontiguousarray(np.asarray(inputs["L1_attn_wo"], np.float32).T),
        "dmask_in": dmask,
        "bmask_in": bmask,
        "idb_in": np.eye(128, dtype=np.float32).astype(bfd),
        "rtr_in": np.asarray(inputs["L1_ffn_router"], np.float32),
        "idf_in": np.eye(128, dtype=np.float32),
        "ones_in": np.ones((1, 128), np.float32),
    }
    in_maps = []
    for c in range(NCORE):
        half = c % 2
        b0 = (c // 2) * 2 * T
        m = dict(shared)
        m["x_in"] = x2[c * T:(c + 1) * T]
        m["qT_in"] = np.ascontiguousarray(q[c * T:(c + 1) * T].T)
        kk = np.zeros((2 * T, KVH * DH), bfd)
        vv = np.zeros((2 * T, KVH * DH), bfd)
        von = np.zeros((2 * T,), np.float32)
        if half == 0:
            kk[T:] = k[b0:b0 + T]
            vv[T:] = v[b0:b0 + T]
            von[T:] = 1.0
        else:
            kk[:] = k[b0:b0 + 2 * T]
            vv[:] = v[b0:b0 + 2 * T]
            von[:] = 1.0
        m["kT_in"] = np.ascontiguousarray(kk.T)
        m["v_in"] = vv
        m["vones_in"] = np.ascontiguousarray(von.reshape(16, 128).T).astype(bfd)
        in_maps.append(m)
    return in_maps


def run_l4(inputs, x2, q, k, v):
    nc = build_l4()
    in_maps = _prep_l4(inputs, x2, q, k, v)
    res = run_bass_kernel_spmd(nc, in_maps, core_ids=list(range(NCORE)))
    x3 = np.concatenate([res.results[c]["x3_out"] for c in range(NCORE)], 0)
    cnt = np.stack([res.results[c]["cnt_out"][0] for c in range(NCORE)], 0)
    return x3, cnt



CAPL1 = 1024
IC1 = I1 // 128
ICS = J1 // 128


def build_l5():
    p = _new_prog("l5")
    nc = p.nc
    dt = lambda n, sh, ty=F32, kind="ExternalInput": nc.dram_tensor(
        n, list(sh), ty, kind=kind).ap()

    x_in = dt("x_in", [T, D])
    xT_in = dt("xT_in", [D, T])
    cnts_in = dt("cnts_in", [NCORE, E1])
    rowsel_in = dt("rowsel_in", [NCORE, 1])
    rtr_in = dt("rtr_in", [D, E1])
    u128_in = dt("u128_in", [128, 128])
    u8_in = dt("u8_in", [NCORE, NCORE])
    iota_in = dt("iota_in", [128, CAPL1])
    ones_in = dt("ones_in", [1, 128])
    idb_in = dt("idb_in", [128, 128], BF16)
    shup_in = dt("shup_in", [ICS, 2, 128, DC, 128])
    shdn_in = dt("shdn_in", [J1, D])
    wug_in = dt("wug_in", [E1, IC1, 2, 128, DC, 128])
    wd_in = dt("wd_in", [E1, I1, D])
    x4_out = dt("x4_out", [T, D], F32, "ExternalOutput")

    CL = 512
    JC = CL // 128

    x_tm = p.sb("x_tm", [128, TT, D], F32)
    x3tm = p.sb("x3tm", [128, TT, D], BF16)
    rtr = p.sb("rtr", [128, DC, E1], BF16)
    u128 = p.sb("u128", [128, 128], F32)
    u8 = p.sb("u8", [NCORE, NCORE], F32)
    cnts = p.sb("cnts", [NCORE, E1], F32)
    rowsel = p.sb("rowsel", [NCORE, 1], F32)
    iota = p.sb("iota", [128, CAPL1], F32)
    onesf = p.sb("onesf", [1, 128], F32)
    onescol = p.sb("onescol", [128, 1], F32)
    idb = p.sb("idb", [128, 128], BF16)
    bc_sb = p.sb("bc_sb", [NCORE, E1], F32)
    cm_sb = p.sb("cm_sb", [1, E1], F32)
    bvld = p.sb("bvld", [1, E1], F32)
    skipf = p.sb("skipf", [1, 2 * E1], F32)
    skipi = p.sb("skipi", [1, 2 * E1], I32)
    skipf2 = p.sb("skipf2", [1, 2 * E1], F32)
    skipi2 = p.sb("skipi2", [1, 2 * E1], I32)
    bme = p.sb("bme", [1, E1], F32)
    runb = p.sb("runb", [1, E1], F32)
    totb = p.sb("totb", [1, E1], F32)
    lgt = p.sb("lgt", [128, E1], F32)
    m1 = p.sb("m1", [128, 1], F32)
    m2c = p.sb("m2c", [128, 1], F32)
    cw1 = p.sb("cw1", [128, 1], F32)
    cw2 = p.sb("cw2", [128, 1], F32)
    oh1 = p.sb("oh1", [128, E1], F32)
    oh2 = p.sb("oh2", [128, E1], F32)
    ohsum = p.sb("ohsum", [128, E1], F32)
    scrM = p.sb("scrM", [128, E1], F32)
    pfl = p.sb("pfl", [128, E1], F32)
    pf_sb = p.sb("pf_sb", [128, E1], F32)
    validm = p.sb("validm", [128, E1], F32)
    posm = p.sb("posm", [128, TT, E1], F32)
    gatem = p.sb("gatem", [128, TT, E1], F32)
    p.dma("sync", "ld", x_tm[:, :, :], x_in.rearrange("(tt p) d -> p tt d", p=128))
    p.dma("gpsimd", "ldg", x3tm[:, :, :], x_in.rearrange("(tt p) d -> p tt d", p=128))
    p.dma("gpsimd", "ldg", rtr[:, :, :], rtr_in.rearrange("(c p) w -> p c w", p=128))
    p.dma("sync", "ld", u128[:, :], u128_in[:, :])
    p.dma("sync", "ld", u8[:, :], u8_in[:, :])
    p.dma("sync", "ld", cnts[:, :], cnts_in[:, :])
    p.dma("sync", "ld", rowsel[:, :], rowsel_in[:, :])
    p.dma("sync", "ld", iota[:, :], iota_in[:, :])
    p.dma("sync", "ld", onesf[:, :], ones_in[:, :])
    p.dma("sync", "ld", idb[:, :], idb_in[:, :])
    p.wait("vector", "ld", p.val("ld"))
    p.op("vector", lambda e, o=onescol[:, :]: e.memset(o, 1.0), "oc")

    with contextlib.ExitStack() as s0:
        ps_b = p.ps("ps_b", [NCORE, E1], s0)
        ps_m = p.ps("ps_m", [1, E1], s0)
        p.wait("tensor", "ld", p.val("ld"))
        p.mm(ps_b[:, :], u8[:, :], cnts[:, :], True, True, sem="b0")
        p.wait("vector", "b0", p.val("b0"))
        p.op("vector", lambda e, o=bc_sb[:, :], i=ps_b[:, :]: e.tensor_copy(o, i), "b1")
        p.wait("tensor", "b1", p.val("b1"))
        p.mm(ps_m[:, :], rowsel[:, :], bc_sb[:, :], True, True, sem="b2")
        p.wait("vector", "b2", p.val("b2"))
        p.op("vector", lambda e, o=bme[:, :], i=ps_m[:, :]: e.tensor_copy(o, i), "b3")
        p.op("vector", lambda e, o=runb[:, :]: e.memset(o, 0.0), "b3")
        p.wait("tensor", "b3", p.val("b3"))
        p.mm(ps_m[:, :], rowsel[:, :], cnts[:, :], True, True, sem="b4")
        p.wait("vector", "b4", p.val("b4"))
        p.op("vector", lambda e, o=cm_sb[:, :], i=ps_m[:, :]: e.tensor_copy(o, i), "skf")
        p.wait("vector", "skf")
        p.op("vector", lambda e, o=bvld[:, :], i=bme[:, :]:
             e.tensor_single_scalar(o, i, float(CAP1), ALU.is_lt), "skf")
        p.op("vector", lambda e, o=skipf[:, 0:E1], i=cm_sb[:, :]:
             e.tensor_single_scalar(o, i, 0.0, ALU.is_gt), "skf")
        p.op("vector", lambda e, o=skipf[:, E1:2 * E1], i=cm_sb[:, :]:
             e.tensor_single_scalar(o, i, 512.0, ALU.is_gt), "skf")
        p.op("vector", lambda e, o=skipf2[:, 0:E1], i=cm_sb[:, :]:
             e.tensor_single_scalar(o, i, 256.0, ALU.is_gt), "skf")
        p.op("vector", lambda e, o=skipf2[:, E1:2 * E1], i=cm_sb[:, :]:
             e.tensor_single_scalar(o, i, 768.0, ALU.is_gt), "skf")
        p.wait("vector", "skf")
        for shh in range(2):
            p.op("vector", lambda e, o=skipf[:, shh * E1:(shh + 1) * E1],
                 a=skipf[:, shh * E1:(shh + 1) * E1], b=bvld[:, :]:
                 e.tensor_tensor(o, a, b, ALU.mult), "skf")
            p.op("vector", lambda e, o=skipf2[:, shh * E1:(shh + 1) * E1],
                 a=skipf2[:, shh * E1:(shh + 1) * E1], b=bvld[:, :]:
                 e.tensor_tensor(o, a, b, ALU.mult), "skf")
        p.wait("vector", "skf")
        p.op("vector", lambda e, o=skipi[:, :], i=skipf[:, :]:
             e.tensor_copy(o, i), "skf")
        p.op("vector", lambda e, o=skipi2[:, :], i=skipf2[:, :]:
             e.tensor_copy(o, i), "skf")

    with contextlib.ExitStack() as s12:
        x3T = p.sb("x3T", [128, DC, T], BF16, s12)
        p.dma("gpsimd", "ldg", x3T[:, :, :], xT_in.rearrange("(c p) t -> p c t", p=128))

        with contextlib.ExitStack() as s1:
            ps_lg = p.ps("ps_lg", [128, E1], s1)
            ps_pf = p.ps("ps_pf", [128, E1], s1)
            ps_tc = p.ps("ps_tc", [1, E1], s1)
            ps_tb = p.ps("ps_tb", [128, E1], s1)
            p.wait("tensor", "ldg", p.val("ldg"))
            p.wait("tensor", "b3", p.val("b3"))
            for tt in range(TT):
                if tt:
                    p.wait("tensor", "lge", p.val("lge"))
                for dc in range(DC):
                    p.mm(ps_lg[:, :], x3T[:, dc, tt * 128:(tt + 1) * 128], rtr[:, dc, :],
                         dc == 0, dc == DC - 1, sem=None if dc < DC - 1 else "lgm")
                p.wait("vector", "lgm", p.val("lgm"))
                p.op("vector", lambda e, o=lgt[:, :], i=ps_lg[:, :]:
                     e.tensor_copy(o, i), "lge")
                p.wait("vector", "lge")
                p.op("vector", lambda e, o=m1[:, :], i=lgt[:, :]:
                     e.reduce_max(o, i, AX.X), "tp")
                p.wait("vector", "tp")
                p.op("vector", lambda e, o=oh1[:, :], i=lgt[:, :], m=m1[:, :]:
                     e.tensor_scalar(o, i, m, None, ALU.is_equal), "tp")
                p.wait("vector", "tp")
                p.op("vector", lambda e, o=scrM[:, :], i=oh1[:, :], x=lgt[:, :]:
                     e.scalar_tensor_tensor(o, i, NEG, x, ALU.mult, ALU.add), "tp")
                p.wait("vector", "tp")
                p.op("vector", lambda e, o=m2c[:, :], i=scrM[:, :]:
                     e.reduce_max(o, i, AX.X), "tp")
                p.wait("vector", "tp")
                p.op("vector", lambda e, o=oh2[:, :], i=scrM[:, :], m=m2c[:, :]:
                     e.tensor_scalar(o, i, m, None, ALU.is_equal), "tp")
                p.op("vector", lambda e, o=cw1[:, :], a=m1[:, :], b=m2c[:, :]:
                     e.tensor_tensor(o, a, b, ALU.subtract), "tp")
                p.wait("scalar", "tp", p.val("tp"))
                p.op("scalar", lambda e, o=cw1[:, :], i=cw1[:, :]:
                     e.activation(o, i, ACT.Sigmoid), "tps")
                p.wait("vector", "tps", p.val("tps"))
                p.op("vector", lambda e, o=cw2[:, :], i=cw1[:, :]:
                     e.tensor_scalar(o, i, -1.0, 1.0, ALU.mult, ALU.add), "tp")
                p.op("vector", lambda e, o=ohsum[:, :], a=oh1[:, :], b=oh2[:, :]:
                     e.tensor_tensor(o, a, b, ALU.add), "tp")
                p.wait("tensor", "tp", p.val("tp"))
                if tt:
                    p.wait("tensor", "bcw", p.val("bcw"))
                p.mm(ps_pf[:, :], u128[:, :], ohsum[:, :], True, True, sem="pfm")
                p.mm(ps_tc[:, :], onescol[:, :], ohsum[:, :], True, True, sem="pfm")
                p.op("vector", lambda e, o=totb[:, :], a=bme[:, :], b=runb[:, :]:
                     e.tensor_tensor(o, a, b, ALU.add), "tb")
                p.wait("tensor", "tb", p.val("tb"))
                p.mm(ps_tb[:, :], onesf[:, :], runb[:, :], True, True, sem="tbl")
                p.wait("vector", "pfm", p.val("pfm"))
                p.wait("vector", "tbl", p.val("tbl"))
                p.op("vector", lambda e, o=pf_sb[:, :], i=ps_pf[:, :]:
                     e.tensor_copy(o, i), "pfe")
                p.wait("vector", "pfe")
                p.op("vector", lambda e, o=pfl[:, :], a=pf_sb[:, :], b=ps_tb[:, :]:
                     e.tensor_tensor(o, a, b, ALU.add), "pfe")
                p.wait("tensor", "pfe", p.val("pfe"))
                p.mm(ps_tb[:, :], onesf[:, :], totb[:, :], True, True, sem="tb2")
                p.wait("vector", "tb2", p.val("tb2"))
                p.op("vector", lambda e, o=validm[:, :], a=pf_sb[:, :], b=ps_tb[:, :]:
                     e.tensor_tensor(o, a, b, ALU.add), "bcw")
                p.wait("vector", "bcw")
                p.op("vector", lambda e, o=validm[:, :], i=validm[:, :]:
                     e.tensor_single_scalar(o, i, float(CAP1), ALU.is_lt), "bcw")
                p.op("vector", lambda e, o=posm[:, tt, :], a=pfl[:, :], b=ohsum[:, :]:
                     e.tensor_tensor(o, a, b, ALU.mult), "bcw")
                p.wait("vector", "bcw")
                p.op("vector", lambda e, o=posm[:, tt, :], a=posm[:, tt, :], b=ohsum[:, :]:
                     e.tensor_tensor(o, a, b, ALU.add), "bcw")
                p.wait("vector", "bcw")
                p.op("vector", lambda e, o=posm[:, tt, :], i=posm[:, tt, :]:
                     e.tensor_scalar_add(o, i, -1.0), "bcw")
                p.op("vector", lambda e, o=gatem[:, tt, :], i=oh1[:, :], w=cw1[:, :]:
                     e.tensor_scalar_mul(o, i, w), "bcw")
                p.wait("vector", "bcw")
                p.op("vector", lambda e, o=gatem[:, tt, :], i=oh2[:, :], w=cw2[:, :],
                     x=gatem[:, tt, :]:
                     e.scalar_tensor_tensor(o, i, w, x, ALU.mult, ALU.add), "bcw")
                p.wait("vector", "bcw")
                p.op("vector", lambda e, o=gatem[:, tt, :], a=gatem[:, tt, :],
                     b=validm[:, :]:
                     e.tensor_tensor(o, a, b, ALU.mult), "bcw")
                p.wait("vector", "bcw")
                p.op("vector", lambda e, o=runb[:, :], a=runb[:, :], b=ps_tc[:, :]:
                     e.tensor_tensor(o, a, b, ALU.add), "bcw")
                p.wait("vector", "bcw")

        p.wait("tensor", "bcw", p.val("bcw"))
        with contextlib.ExitStack() as s2:
            wbuf = p.sb("wbuf", [128, 2, 2, DC, 128], BF16, s2)
            wdbuf = p.sb("wdbuf", [128, 2, D], BF16, s2)
            h_sh = p.sb("h_sh", [128, ICS, 512], BF16, s2)
            sil = p.sb("sil", [128, 512], F32, s2)
            silb = p.sb("silb", [128, 512], F32, s2)
            silL = [sil, silb]
            for th in range(2):
                tsl = slice(th * 512, (th + 1) * 512)
                p.wait("tensor", "dne", p.val("dne"))
                with contextlib.ExitStack() as s2a:
                    ps_guL = [p.ps("ps_gu", [128, 1024], s2a), p.ps("ps_gu", [128, 1024], s2a)]
                    for i in range(ICS):
                        buf = i % 2
                        gi = th * ICS + i
                        ps_gu = ps_guL[i % 2]
                        silx = silL[i % 2]
                        if gi >= 2:
                            p.wait("gpsimd", "gum", gi - 1)
                        p.dma("gpsimd", "wu%d" % buf, wbuf[:, buf, 0, :, :],
                              shup_in[i, 0])
                        p.dma("gpsimd", "wu%d" % buf, wbuf[:, buf, 1, :, :],
                              shup_in[i, 1])
                        p.wait("tensor", "wu%d" % buf, p.val("wu%d" % buf))
                        if gi >= 2:
                            p.wait("tensor", "sme", max(0, p.val("sme") - 1))
                        for dc in range(DC):
                            p.mm(ps_gu[:, 0:512], wbuf[:, buf, 0, dc, :],
                                 x3T[:, dc, tsl], dc == 0, dc == DC - 1)
                            p.mm(ps_gu[:, 512:1024], wbuf[:, buf, 1, dc, :],
                                 x3T[:, dc, tsl], dc == 0, dc == DC - 1,
                                 sem=None if dc < DC - 1 else "gum")
                        p.wait("scalar", "gum", p.val("gum"))
                        p.op("scalar", lambda e, o=silx[:, :], i=ps_gu[:, 0:512]:
                             e.activation(o, i, ACT.Silu), "sil")
                        p.wait("vector", "sil", p.val("sil"))
                        p.op("vector", lambda e, o=h_sh[:, i, :], a=silx[:, :],
                             b=ps_gu[:, 512:1024]:
                             e.tensor_tensor(o, a, b, ALU.mult), "sme")
                with contextlib.ExitStack() as s2b:
                    ps_dn = p.ps("ps_dn", [128, 4096], s2b)
                    for q in range(2):
                        for i in range(ICS):
                            buf = i % 2
                            di = (2 * th + q) * ICS + i
                            if di >= 2:
                                p.wait("gpsimd", "dnm", di - 1)
                            p.dma("gpsimd", "wd%d" % buf, wdbuf[:, buf, :],
                                  shdn_in[i * 128:(i + 1) * 128, :])
                            p.wait("tensor", "wd%d" % buf, p.val("wd%d" % buf))
                            p.wait("tensor", "sme", p.val("sme"))
                            if di >= 1:
                                p.wait("tensor", "dne", p.val("dne"))
                            got_sem = False
                            for t2 in range(2):
                                for oc in range(4):
                                    last = (i == ICS - 1 and t2 == 1 and oc == 3)
                                    p.mm(ps_dn[:, (t2 * 4 + oc) * 512:(t2 * 4 + oc + 1) * 512],
                                         h_sh[:, i, (q * 2 + t2) * 128:(q * 2 + t2 + 1) * 128],
                                         wdbuf[:, buf, oc * 512:(oc + 1) * 512],
                                         i == 0, i == ICS - 1,
                                         sem="dnm" if last or (i < ICS - 1 and t2 == 1 and oc == 3) else None)
                        p.wait("vector", "dnm", p.val("dnm"))
                        for t2 in range(2):
                            tti = th * 4 + q * 2 + t2
                            for oc in range(4):
                                p.op("vector", lambda e,
                                     o=x_tm[:, tti, oc * 512:(oc + 1) * 512],
                                     i=ps_dn[:, (t2 * 4 + oc) * 512:(t2 * 4 + oc + 1) * 512],
                                     x=x_tm[:, tti, oc * 512:(oc + 1) * 512]:
                                     e.tensor_tensor(o, i, x, ALU.add), "dne")
                        p.wait("vector", "dne")

    with contextlib.ExitStack() as s3:
        Gt = p.sb("Gt", [128, 4, T], BF16, s3)
        xg = p.sb("xg", [128, DC, 512], BF16, s3)
        h_e = p.sb("h_e", [128, IC1, 512], BF16, s3)
        yg = p.sb("yg", [128, 4, D], BF16, s3)
        wbufe = p.sb("wbufe", [128, 2, 2, DC, 128], BF16, s3)
        wdbufe = p.sb("wdbufe", [128, 2, D], BF16, s3)
        G = p.sb("G", [128, TT, 512], BF16, s3)
        sil2 = p.sb("sil2", [128, 512], F32, s3)
        sil2b = p.sb("sil2b", [128, 512], F32, s3)
        CL = 512
        JC = 4

        p.wait("tensor", "dne", p.val("dne"))
        p.wait("vector", "dnm", p.val("dnm"))
        p.wait("gpsimd", "dnm", p.val("dnm"))
        for eng_ in ("tensor", "vector", "scalar", "gpsimd"):
            p.wait(eng_, "skf", p.val("skf"))
        p.op("vector", lambda e, o=yg[:, :, :]: e.memset(o, 0.0), "ygz")
        p.wait("tensor", "ygz", 1)
        for ve in range(2 * E1):
            ex, sh = ve // 2, ve % 2
            isl = slice(sh * CL, (sh + 1) * CL)
            sgp = ["exg%d_0" % ex, "exg%d_1" % ex]
            sdp = ["exd%d_0" % ex, "exd%d_1" % ex]
            p.skip_begin(skipi[0:1, sh * E1 + ex:sh * E1 + ex + 1])
            p.wait("vector", "xge", p.val("xge"))
            for tt in range(TT):
                p.op("vector", lambda e, o=G[:, tt, :], i=iota[:, isl],
                     m=posm[:, tt, ex:ex + 1]:
                     e.tensor_scalar(o, i, m, None, ALU.is_equal), "gb")
            with contextlib.ExitStack() as sA:
                ps_bt = p.ps("ps_bt", [128, T], sA, BF16)
                p.wait("vector", "sce", p.val("sce"))
                p.wait("tensor", "gb", p.val("gb"))
                for jc in range(JC):
                    for tt in range(TT):
                        if tt + jc or ve:
                            p.wait("tensor", "gte", p.val("gte"))
                        p.op("tensor", lambda e, o=ps_bt[:, tt * 128:(tt + 1) * 128],
                             i=G[:, tt, jc * 128:(jc + 1) * 128], idn=idb[:, :]:
                             e.transpose(o, i, idn), "gtt")
                    p.wait("vector", "gtt", p.val("gtt"))
                    p.op("vector", lambda e, o=Gt[:, jc, :], i=ps_bt[:, :]:
                         e.tensor_copy(o, i), "gte")
            with contextlib.ExitStack() as sB:
                ps_gL = [p.ps("ps_g", [128, CL], sB), p.ps("ps_g", [128, CL], sB)]
                p.wait("tensor", "gte", p.val("gte"))
                for dc in range(DC):
                    ps_g = ps_gL[dc % 2]
                    if dc >= 2 or ve:
                        p.wait("tensor", "xge", max(0, p.val("xge") - 1))
                    for tt in range(TT):
                        p.mm(ps_g[:, :], x3tm[:, tt, dc * 128:(dc + 1) * 128],
                             G[:, tt, :], tt == 0, tt == TT - 1,
                             sem=None if tt < TT - 1 else "xgm")
                    p.wait("vector", "xgm", p.val("xgm"))
                    p.op("vector", lambda e, o=xg[:, dc, :], i=ps_g[:, :]:
                         e.tensor_copy(o, i), "xge")
            with contextlib.ExitStack() as sC:
                ps_gueL = [p.ps("ps_gue", [128, 1024], sC), p.ps("ps_gue", [128, 1024], sC)]
                sil2L = [sil2, sil2b]
                for i in range(IC1):
                    gi = ve * IC1 + i
                    buf = gi % 2
                    ps_gue = ps_gueL[i % 2]
                    sil2x = sil2L[i % 2]
                    sg = sgp[buf]
                    if gi >= 2:
                        p.wait("gpsimd", "egm", gi - 1)
                    p.dma("gpsimd", sg, wbufe[:, buf, 0, :, :],
                          wug_in[ex, i, 0])
                    p.dma("gpsimd", sg, wbufe[:, buf, 1, :, :],
                          wug_in[ex, i, 1])
                    p.wait("tensor", sg, p.val(sg))
                    p.wait("tensor", "xge", p.val("xge"))
                    if gi >= 2:
                        p.wait("tensor", "ehe", max(0, p.val("ehe") - 1))
                    for dc in range(DC):
                        p.mm(ps_gue[:, 0:512], wbufe[:, buf, 0, dc, :],
                             xg[:, dc, :], dc == 0, dc == DC - 1)
                        p.mm(ps_gue[:, 512:1024], wbufe[:, buf, 1, dc, :],
                             xg[:, dc, :], dc == 0, dc == DC - 1,
                             sem=None if dc < DC - 1 else "egm")
                    p.wait("scalar", "egm", p.val("egm"))
                    p.op("scalar", lambda e, o=sil2x[:, :], i=ps_gue[:, 0:512]:
                         e.activation(o, i, ACT.Silu), "esl")
                    p.wait("vector", "esl", p.val("esl"))
                    p.op("vector", lambda e, o=h_e[:, i, :], a=sil2x[:, :],
                         b=ps_gue[:, 512:1024]:
                         e.tensor_tensor(o, a, b, ALU.mult), "ehe")
            with contextlib.ExitStack() as sD:
                ps_dne = p.ps("ps_dne", [128, 4096], sD)
                for jh2 in range(2):
                    if jh2 == 1:
                        p.skip_end(gpsimd_in=True)
                        p.skip_begin(skipi2[0:1, sh * E1 + ex:sh * E1 + ex + 1])
                    for i in range(IC1):
                        di = (ve * 2 + jh2) * IC1 + i
                        buf = di % 2
                        sd = sdp[buf]
                        if di >= 2:
                            p.wait("gpsimd", "edm", di - 1)
                        p.dma("gpsimd", sd, wdbufe[:, buf, :],
                              wd_in[ex, i * 128:(i + 1) * 128, :])
                        p.wait("tensor", sd, p.val(sd))
                        p.wait("tensor", "ehe", p.val("ehe"))
                        if di >= 1:
                            p.wait("tensor", "yge", p.val("yge"))
                        for j2 in range(2):
                            for oc in range(4):
                                last = (j2 == 1 and oc == 3)
                                p.mm(ps_dne[:, (j2 * 4 + oc) * 512:(j2 * 4 + oc + 1) * 512],
                                     h_e[:, i, (jh2 * 2 + j2) * 128:(jh2 * 2 + j2 + 1) * 128],
                                     wdbufe[:, buf, oc * 512:(oc + 1) * 512],
                                     i == 0, i == IC1 - 1,
                                     sem="edm" if last else None)
                    p.wait("vector", "edm", p.val("edm"))
                    for j2 in range(2):
                        p.op("vector", lambda e, o=yg[:, jh2 * 2 + j2, :],
                             i=ps_dne[:, j2 * 2048:(j2 + 1) * 2048]:
                             e.tensor_copy(o, i), "yge")
            p.skip_end(gpsimd_in=True)
            p.skip_begin(skipi[0:1, sh * E1 + ex:sh * E1 + ex + 1])
            with contextlib.ExitStack() as sE:
                ps_scL = [p.ps("ps_sc", [128, 1024], sE), p.ps("ps_sc", [128, 1024], sE)]
                p.wait("tensor", "yge", p.val("yge"))
                for tt in range(TT):
                    for ob in range(2):
                        ps_sc = ps_scL[(tt * 2 + ob) % 2]
                        if (tt * 2 + ob) >= 2 or ve:
                            p.wait("tensor", "sce", max(0, p.val("sce") - 2))
                        for jc in range(JC):
                            for o2 in range(2):
                                p.mm(ps_sc[:, o2 * 512:(o2 + 1) * 512],
                                     Gt[:, jc, tt * 128:(tt + 1) * 128],
                                     yg[:, jc, (ob * 2 + o2) * 512:(ob * 2 + o2 + 1) * 512],
                                     jc == 0, jc == JC - 1,
                                     sem=None if not (jc == JC - 1 and o2 == 1) else "scm")
                        p.wait("vector", "scm", p.val("scm"))
                        for o2 in range(2):
                            p.op("vector", lambda e,
                                 o=x_tm[:, tt, (ob * 2 + o2) * 512:(ob * 2 + o2 + 1) * 512],
                                 i=ps_sc[:, o2 * 512:(o2 + 1) * 512],
                                 g=gatem[:, tt, ex:ex + 1],
                                 x=x_tm[:, tt, (ob * 2 + o2) * 512:(ob * 2 + o2 + 1) * 512]:
                                 e.scalar_tensor_tensor(o, i, g, x, ALU.mult, ALU.add), "sce")
            p.skip_end(gpsimd_in=True)

    sq_scr = p.sb("sq_scr", [128, D], F32)
    ssum = p.sb("ssum", [128, 1], F32)
    ssq = p.sb("ssq", [128, 1], F32)
    mean = p.sb("mean", [128, 1], F32)
    m2 = p.sb("m2", [128, 1], F32)
    var = p.sb("var", [128, 1], F32)
    rstd = p.sb("rstd", [128, 1], F32)
    for tt in range(TT):
        p.wait("vector", "sce", p.val("sce"))
        p.wait("vector", "dne", p.val("dne"))
        layer_norm(p, x_tm[:, tt, :], sq_scr[:, :], ssum[:, :], ssq[:, :],
                   mean[:, :], m2[:, :], var[:, :], rstd[:, :], "lne")
        p.wait("sync", "lne", 9 * (tt + 1))
        p.dma("sync", "st", x4_out[tt * 128:(tt + 1) * 128, :], x_tm[:, tt, :])
    p.wait("sync", "st", p.val("st"))
    return p.finish()


def _prep_l5(inputs, x3, cnt):
    def _pchunk(a):
        return a.reshape(DC, 128, 128).transpose(1, 0, 2)
    sup = np.asarray(inputs["L1_ffn_shared_w_up_gate"], np.float32)
    shup = np.stack([np.stack([_pchunk(sup[:, i * 128:(i + 1) * 128]),
                               _pchunk(sup[:, J1 + i * 128:J1 + (i + 1) * 128])], 0)
                     for i in range(ICS)], 0)
    wug_raw = np.asarray(inputs["L1_ffn_w_up_gate"], np.float32)
    wug = np.stack([
        np.stack([np.stack([_pchunk(wug_raw[ex][:, i * 128:(i + 1) * 128]),
                            _pchunk(wug_raw[ex][:, I1 + i * 128:I1 + (i + 1) * 128])], 0)
                  for i in range(IC1)], 0)
        for ex in range(E1)], 0)
    ii, jj = np.meshgrid(np.arange(128), np.arange(128), indexing="ij")
    u128 = (ii < jj).astype(np.float32)
    i8, j8 = np.meshgrid(np.arange(NCORE), np.arange(NCORE), indexing="ij")
    u8 = (i8 < j8).astype(np.float32)
    import ml_dtypes
    shared = {
        "cnts_in": np.ascontiguousarray(cnt.astype(np.float32)),
        "rtr_in": np.asarray(inputs["L1_ffn_router"], np.float32),
        "u128_in": u128,
        "u8_in": u8,
        "iota_in": np.tile(np.arange(CAPL1, dtype=np.float32), (128, 1)),
        "ones_in": np.ones((1, 128), np.float32),
        "idb_in": np.eye(128, dtype=np.float32).astype(ml_dtypes.bfloat16),
        "shup_in": np.ascontiguousarray(shup),
        "shdn_in": np.asarray(inputs["L1_ffn_shared_w_down"], np.float32),
        "wug_in": np.ascontiguousarray(wug),
        "wd_in": np.asarray(inputs["L1_ffn_w_down"], np.float32),
    }
    in_maps = []
    for c in range(NCORE):
        m = dict(shared)
        m["x_in"] = x3[c * T:(c + 1) * T]
        m["xT_in"] = np.ascontiguousarray(x3[c * T:(c + 1) * T].T)
        rs = np.zeros((NCORE, 1), np.float32)
        rs[c, 0] = 1.0
        m["rowsel_in"] = rs
        in_maps.append(m)
    return in_maps


def run_l5(inputs, x3, cnt):
    nc = build_l5()
    in_maps = _prep_l5(inputs, x3, cnt)
    res = run_bass_kernel_spmd(nc, in_maps, core_ids=list(range(NCORE)))
    return np.concatenate([res.results[c]["x4_out"] for c in range(NCORE)], 0)



SB_TOP = 229344
OFF_XTM = SB_TOP - 65536
OFF_QT = OFF_XTM - 32768
OFF_KT = OFF_QT - 16384
OFF_VA = OFF_KT - 16896
OFF_SLOTA = OFF_VA - 32768
OFF_BIG = OFF_VA


def build_lB():
    p = _new_prog("lB", detect=False)
    nc = p.nc
    dt = lambda n, sh, ty=F32, kind="ExternalInput": nc.dram_tensor(
        n, list(sh), ty, kind=kind).ap()

    x_in = dt("x_in", [T, D])
    xT_in = dt("xT_in", [D, T])
    sall_in = dt("sall_in", [8 * T, NE0])
    sown_in = dt("sown_in", [T, NE0])
    ew_in = dt("ew_in", [D, D])
    eb_in = dt("eb_in", [1, D])
    wq1_in = dt("wq1_in", [D, D])
    wk1_in = dt("wk1_in", [D, KVH * DH])
    wv1_in = dt("wv1_in", [D, KVH * DH])
    qnw_in = dt("qnw_in", [1, DH])
    knw_in = dt("knw_in", [1, DH])
    cos_in = dt("cos_in", [T, 64])
    sin_in = dt("sin_in", [T, 64])
    wo1_in = dt("wo1_in", [D, D])
    dmask_in = dt("dmask_in", [128, 128])
    bmask_in = dt("bmask_in", [128, 128])
    idb_in = dt("idb_in", [128, 128], BF16)
    idf_in = dt("idf_in", [128, 128])
    rtr_in = dt("rtr_in", [D, E1])
    vones_in = dt("vones_in", [128, 16], BF16)
    ones_in = dt("ones_in", [1, 128])
    nonce_in = dt("nonce_in", [1, 1], I32)
    x3_out = dt("x3_out", [T, D], F32, "ExternalOutput")
    cnt_out = dt("cnt_out", [1, E1], F32, "ExternalOutput")

    xch_k = nc.dram_tensor("xch_k", [2, 128, 4096], BF16, kind="Internal",
                           addr_space="Shared")
    xch_v = nc.dram_tensor("xch_v", [2, 128, 4096], BF16, kind="Internal",
                           addr_space="Shared")
    xfl = nc.dram_tensor("xfl", [2, 1], I32, kind="Internal", addr_space="Shared")

    x_tm = p.sba("x_tm", [128, TT, D], F32, OFF_XTM)[...]
    qT = p.sba("qT", [128, H1, T], BF16, OFF_QT)[...]
    kT = p.sba("kT", [128, KVH, 2 * T], BF16, OFF_KT)[...]
    vaug = p.sba("vaug", [128, 16, KVH, 132], BF16, OFF_VA)[...]
    x1T = p.sba("x1T", [128, DC, T], BF16, OFF_SLOTA)[...]
    x2T = p.sba("x2T", [128, DC, T], BF16, OFF_SLOTA)[...]
    aoT = p.sba("aoT", [128, H1, T], BF16, OFF_SLOTA)[...]
    ew = p.sba("ew", [128, DC, D], BF16, OFF_BIG)[...]
    woQ = p.sba("woQ", [128, 2, 4, D], BF16, OFF_BIG)[...]
    x3T = p.sba("x3T", [128, DC, T], BF16, OFF_BIG)[...]

    onesf = p.sb("onesf", [1, 128], F32)
    onescol = p.sb("onescol", [128, 1], F32)
    idf = p.sb("idf", [128, 128], F32)
    idb = p.sb("idb", [128, 128], BF16)
    non = p.sb("non", [1, 1], I32)
    vones = p.sb("vones", [128, 16], BF16)
    vonesf = p.sb("vonesf", [128, 16], F32)
    lo = p.sb("lo", [1, NE0], F32)
    hi = p.sb("hi", [1, NE0], F32)
    mid = p.sb("mid", [1, NE0], F32)
    dd = p.sb("dd", [1, NE0], F32)
    sel = p.sb("sel", [1, NE0], F32)
    cntv = p.sb("cntv", [128, NE0], F32)
    wcol = p.sb("wcol", [128, TT], F32)
    scrA8 = p.sb("scrA8", [128, NE0], F32)
    sq_scr = p.sb("sq_scr", [128, D], F32)
    ssum = p.sb("ssum", [128, 1], F32)
    ssq = p.sb("ssq", [128, 1], F32)
    mean = p.sb("mean", [128, 1], F32)
    m2 = p.sb("m2", [128, 1], F32)
    var = p.sb("var", [128, 1], F32)
    rstd = p.sb("rstd", [128, 1], F32)

    p.dma("sync", "ld", x_tm[:, :, :], x_in.rearrange("(tt p) d -> p tt d", p=128))
    p.dma("sync", "ld", onesf[:, :], ones_in[:, :])
    p.dma("sync", "ld", idf[:, :], idf_in[:, :])
    p.dma("sync", "ld", idb[:, :], idb_in[:, :])
    p.dma("sync", "ld", non[:, :], nonce_in[:, :])
    p.dma("gpsimd", "ldg", x1T[:, :, :], xT_in.rearrange("(c p) t -> p c t", p=128))
    p.dma("gpsimd", "ldg", ew[:, :, :], ew_in.rearrange("(c p) o -> p c o", p=128))
    p.wait("vector", "ld", p.val("ld"))
    p.op("vector", lambda e, o=onescol[:, :]: e.memset(o, 1.0), "oc")

    with contextlib.ExitStack() as s1:
        sthr = p.sb("sthr", [128, 64, NE0], F32, s1)
        sown = p.sb("sown", [128, TT, NE0], F32, s1)
        eb = p.sb("eb", [1, D], F32, s1)
        ge = p.sb("ge", [128, 64, NE0], F32, s1)
        ps_b = p.ps("ps_b", [128, NE0], s1)
        ps_c = p.ps("ps_c", [1, NE0], s1)
        ps_tau = p.ps("ps_tau", [128, NE0], s1)
        ps_z = p.ps("ps_z", [128, D], s1)
        p.dma("sync", "lds1", sthr[:, :, :], sall_in.rearrange("(pp j) e -> pp j e", j=64))
        p.dma("sync", "lds1", sown[:, :, :], sown_in.rearrange("(tt p) e -> p tt e", p=128))
        p.dma("sync", "lds1", eb[:, :], eb_in[:, :])
        p.wait("vector", "lds1", p.val("lds1"))
        p.op("vector", lambda e, o=lo[:, :]: e.memset(o, 0.0), "bi")
        p.op("vector", lambda e, o=hi[:, :]: e.memset(o, 1.0), "bi")
        for it in range(42):
            p.wait("vector", "bi")
            p.op("vector", lambda e, o=mid[:, :], a=lo[:, :], b=hi[:, :]:
                 e.tensor_tensor(o, a, b, ALU.add), "bi")
            p.wait("vector", "bi")
            p.op("vector", lambda e, o=mid[:, :]: e.tensor_scalar_mul(o, o, 0.5), "bi")
            p.wait("tensor", "bi", p.val("bi"))
            if it:
                p.wait("tensor", "gei", p.val("gei"))
            p.mm(ps_b[:, :], onesf[:, :], mid[:, :], True, True, sem="bmm")
            p.wait("vector", "bmm", p.val("bmm"))
            p.op("vector", lambda e, o=ge[:, :, :], a=sthr[:, :, :],
                 b=ps_b[:, :].unsqueeze(1).broadcast_to([128, 64, NE0]):
                 e.tensor_tensor(o, a, b, ALU.is_ge), "gei")
            p.wait("vector", "gei")
            p.op("vector", lambda e, o=cntv[:, :],
                 i=ge[:, :, :].rearrange("p j e -> p e j"):
                 e.tensor_reduce(o, i, AX.X, ALU.add), "gei")
            p.wait("tensor", "gei", p.val("gei"))
            if it:
                p.wait("tensor", "seli", p.val("seli"))
            p.mm(ps_c[:, :], onescol[:, :], cntv[:, :], True, True, sem="cmm")
            p.wait("vector", "cmm", p.val("cmm"))
            p.op("vector", lambda e, o=sel[:, :], i=ps_c[:, :]:
                 e.tensor_single_scalar(o, i, float(K0_TOP), ALU.is_ge), "seli")
            p.wait("vector", "seli")
            p.op("vector", lambda e, o=dd[:, :], a=mid[:, :], b=lo[:, :]:
                 e.tensor_tensor(o, a, b, ALU.subtract), "bi")
            p.wait("vector", "bi")
            p.op("vector", lambda e, o=dd[:, :], a=sel[:, :], b=dd[:, :]:
                 e.tensor_tensor(o, a, b, ALU.mult), "bi")
            p.wait("vector", "bi")
            p.op("vector", lambda e, o=lo[:, :], a=lo[:, :], b=dd[:, :]:
                 e.tensor_tensor(o, a, b, ALU.add), "bi")
            p.wait("vector", "bi")
            p.op("vector", lambda e, o=dd[:, :], a=hi[:, :], b=mid[:, :]:
                 e.tensor_tensor(o, a, b, ALU.subtract), "bi")
            p.wait("vector", "bi")
            p.op("vector", lambda e, o=dd[:, :], a=sel[:, :], b=dd[:, :]:
                 e.tensor_tensor(o, a, b, ALU.mult), "bi")
            p.wait("vector", "bi")
            p.op("vector", lambda e, o=hi[:, :], a=mid[:, :], b=dd[:, :]:
                 e.tensor_tensor(o, a, b, ALU.add), "bi")
        p.wait("tensor", "bi", p.val("bi"))
        p.wait("tensor", "gei", p.val("gei"))
        p.mm(ps_tau[:, :], onesf[:, :], lo[:, :], True, True, sem="tmm")
        p.wait("vector", "tmm", p.val("tmm"))
        for tt in range(TT):
            p.op("vector", lambda e, o=scrA8[:, :], a=sown[:, tt, :], b=ps_tau[:, :]:
                 e.tensor_tensor(o, a, b, ALU.is_ge), "wt")
            p.wait("vector", "wt")
            p.op("vector", lambda e, o=scrA8[:, :], a=sown[:, tt, :], b=scrA8[:, :]:
                 e.tensor_tensor(o, a, b, ALU.mult), "wt")
            p.wait("vector", "wt")
            p.op("vector", lambda e, o=wcol[:, tt:tt + 1], i=scrA8[:, :]:
                 e.reduce_sum(o, i, AX.X), "wt")
            p.wait("vector", "wt")
        for tt in range(TT):
            p.wait("tensor", "ld", p.val("ld"))
            p.wait("tensor", "lds1", p.val("lds1"))
            p.wait("tensor", "ldg", p.val("ldg"))
            if tt:
                p.wait("tensor", "zev", p.val("zev"))
            for oc in range(4):
                p.mm(ps_z[:, oc * 512:(oc + 1) * 512], onesf[:, :],
                     eb[:, oc * 512:(oc + 1) * 512], True, False)
            for dc in range(DC):
                for oc in range(4):
                    p.mm(ps_z[:, oc * 512:(oc + 1) * 512],
                         x1T[:, dc, tt * 128:(tt + 1) * 128],
                         ew[:, dc, oc * 512:(oc + 1) * 512], False, dc == DC - 1,
                         sem=None if not (dc == DC - 1 and oc == 3) else "zmm")
            p.wait("vector", "zmm", p.val("zmm"))
            p.wait("vector", "wt", p.val("wt"))
            for oc in range(4):
                p.op("vector", lambda e, o=x_tm[:, tt, oc * 512:(oc + 1) * 512],
                     i=ps_z[:, oc * 512:(oc + 1) * 512], w=wcol[:, tt:tt + 1],
                     x=x_tm[:, tt, oc * 512:(oc + 1) * 512]:
                     e.scalar_tensor_tensor(o, i, w, x, ALU.mult, ALU.add), "zev")
            p.wait("vector", "zev")
            layer_norm(p, x_tm[:, tt, :], sq_scr[:, :], ssum[:, :], ssq[:, :],
                       mean[:, :], m2[:, :], var[:, :], rstd[:, :], "ln2")

    with contextlib.ExitStack() as s2:
        ps_t = p.ps("ps_t", [128, 128], s2)
        p.wait("tensor", "ln2", p.val("ln2"))
        for tt in range(TT):
            for dc in range(DC):
                p.wait("tensor", "x2e", p.val("x2e"))
                p.trans(ps_t[:, :], x_tm[:, tt, dc * 128:(dc + 1) * 128], idf[:, :],
                        x2T[:, dc, tt * 128:(tt + 1) * 128], "x2e")

    with contextlib.ExitStack() as s3:
        qnw = p.sb("qnw", [1, DH], F32, s3)
        knw = p.sb("knw", [1, DH], F32, s3)
        cosb = p.sb("cosb", [128, TT, 64], F32, s3)
        sinb = p.sb("sinb", [128, TT, 64], F32, s3)
        wbc = p.sb("wbc", [128, DH], F32, s3)
        kwbc = p.sb("kwbc", [128, DH], F32, s3)
        qtm = p.sb("qtm", [128, 512], F32, s3)
        ktm = p.sb("ktm", [128, 2 * DH], F32, s3)
        ssq4 = p.sb("ssq4", [128, KVH], F32, s3)
        rr4 = p.sb("rr4", [128, KVH], F32, s3)
        scr1 = p.sb("scr1", [128, 4 * 64], F32, s3)
        scr2 = p.sb("scr2", [128, 4 * 64], F32, s3)
        qbf = p.sb("qbf", [128, 512], BF16, s3)
        kbf = p.sb("kbf", [128, 2 * DH], BF16, s3)

        p.wait("sync", "ln2", p.val("ln2"))
        p.dma("sync", "ld2", qnw[:, :], qnw_in[:, :])
        p.dma("sync", "ld2", knw[:, :], knw_in[:, :])
        p.dma("sync", "ld2", cosb[:, :, :], cos_in.rearrange("(tt p) f -> p tt f", p=128))
        p.dma("sync", "ld2", sinb[:, :, :], sin_in.rearrange("(tt p) f -> p tt f", p=128))
        p.dma("sync", "ld2", vones[:, :], vones_in[:, :])

        ps_w = p.ps("ps_w", [128, DH], s3)
        p.wait("tensor", "ld2", p.val("ld2"))
        p.wait("tensor", "x2e", p.val("x2e"))
        p.mm(ps_w[:, :], onesf[:, :], qnw[:, :], True, True, sem="wb")
        p.wait("vector", "wb", p.val("wb"))
        p.op("vector", lambda e, o=wbc[:, :], i=ps_w[:, :]: e.tensor_copy(o, i), "wbe")
        p.wait("tensor", "wbe", p.val("wbe"))
        p.mm(ps_w[:, :], onesf[:, :], knw[:, :], True, True, sem="wb")
        p.wait("vector", "wb", p.val("wb"))
        p.op("vector", lambda e, o=kwbc[:, :], i=ps_w[:, :]: e.tensor_copy(o, i), "wbe")

        def rms_rope(tt, nh, xv, sqv, ssqv, rv, wv_bc, outbf, sem):
            p.op("scalar", lambda e, o=sqv, i=xv: e.activation(o, i, ACT.Square), sem)
            p.wait("vector", sem)
            p.op("vector", lambda e, o=ssqv, i=sqv.rearrange("p (h f) -> p h f", h=nh):
                 e.tensor_reduce(o, i, AX.X, ALU.add), sem)
            p.wait("vector", sem)
            p.op("vector", lambda e, o=rv, i=ssqv:
                 e.tensor_scalar(o, i, 1.0 / DH, 1e-6, ALU.mult, ALU.add), sem)
            p.wait("scalar", sem)
            p.op("scalar", lambda e, o=rv, i=rv: e.activation(o, i, ACT.Sqrt), sem)
            p.wait("vector", sem)
            p.op("vector", lambda e, o=rv, i=rv: e.reciprocal(o, i), sem)
            p.wait("vector", sem)
            for h in range(nh):
                p.op("vector", lambda e, o=xv[:, h * DH:(h + 1) * DH],
                     i=xv[:, h * DH:(h + 1) * DH], r=rv[:, h:h + 1], w=wv_bc:
                     e.scalar_tensor_tensor(o, i, r, w, ALU.mult, ALU.mult), sem)
            p.wait("vector", sem)
            x3v = xv.rearrange("p (h f) -> p h f", h=nh)
            o3 = outbf.rearrange("p (h f) -> p h f", h=nh)
            cb = cosb[:, tt, :].unsqueeze(1).broadcast_to([128, nh, 64])
            sb_ = sinb[:, tt, :].unsqueeze(1).broadcast_to([128, nh, 64])
            a3 = scr1.rearrange("p (h f) -> p h f", h=4)[:, 0:nh, :]
            b3 = scr2.rearrange("p (h f) -> p h f", h=4)[:, 0:nh, :]
            p.op("vector", lambda e, o=a3, i=x3v[:, :, 0:64], c=cb:
                 e.tensor_tensor(o, i, c, ALU.mult), sem)
            p.op("vector", lambda e, o=b3, i=x3v[:, :, 64:128], c=sb_:
                 e.tensor_tensor(o, i, c, ALU.mult), sem)
            p.wait("vector", sem)
            p.op("vector", lambda e, o=o3[:, :, 0:64], a=a3, b=b3:
                 e.tensor_tensor(o, a, b, ALU.subtract), sem)
            p.wait("vector", sem)
            p.op("vector", lambda e, o=a3, i=x3v[:, :, 0:64], c=sb_:
                 e.tensor_tensor(o, i, c, ALU.mult), sem)
            p.op("vector", lambda e, o=b3, i=x3v[:, :, 64:128], c=cb:
                 e.tensor_tensor(o, i, c, ALU.mult), sem)
            p.wait("vector", sem)
            p.op("vector", lambda e, o=o3[:, :, 64:128], a=a3, b=b3:
                 e.tensor_tensor(o, a, b, ALU.add), sem)

        with contextlib.ExitStack() as s3kv:
            wkb = p.sb("wkb", [128, DC, 2 * DH], BF16, s3kv)
            wvb2 = p.sb("wvb2", [128, DC, 2 * DH], BF16, s3kv)
            vbf2 = p.sb("vbf2", [128, 2 * DH], BF16, s3kv)
            ps_k = p.ps("ps_k", [128, 2 * DH], s3kv)
            ps_v = p.ps("ps_v", [128, 2 * DH], s3kv)
            ps_kt = p.ps("ps_kt", [128, 128], s3kv, BF16)
            p.wait("gpsimd", "ln2", p.val("ln2"))
            for hp in range(2):
                if hp:
                    p.wait("gpsimd", "kmm", p.val("kmm"))
                p.dma("gpsimd", "wk%d" % (hp % 2), wkb[:, :, :],
                      wk1_in[:, hp * 256:(hp + 1) * 256].rearrange(
                          "(c p) o -> p c o", p=128))
                p.dma("gpsimd", "wv%d" % (hp % 2), wvb2[:, :, :],
                      wv1_in[:, hp * 256:(hp + 1) * 256].rearrange(
                          "(c p) o -> p c o", p=128))
                p.wait("tensor", "wk%d" % (hp % 2), p.val("wk%d" % (hp % 2)))
                p.wait("tensor", "wv%d" % (hp % 2), p.val("wv%d" % (hp % 2)))
                for tt in range(TT):
                    if tt or hp:
                        p.wait("tensor", "kfe", p.val("kfe"))
                    for dc in range(DC):
                        p.mm(ps_k[:, :], x2T[:, dc, tt * 128:(tt + 1) * 128],
                             wkb[:, dc, :], dc == 0, dc == DC - 1,
                             sem=None if dc < DC - 1 else "kmm")
                        p.mm(ps_v[:, :], x2T[:, dc, tt * 128:(tt + 1) * 128],
                             wvb2[:, dc, :], dc == 0, dc == DC - 1,
                             sem=None if dc < DC - 1 else "kmm")
                    p.wait("vector", "kmm", p.val("kmm"))
                    p.wait("vector", "vst", p.val("vst"))
                    p.op("vector", lambda e, o=ktm[:, :], i=ps_k[:, :]:
                         e.tensor_copy(o, i), "kfe")
                    p.op("vector", lambda e, o=vbf2[:, :], i=ps_v[:, :]:
                         e.tensor_copy(o, i), "kfe")
                    p.wait("vector", "kfe")
                    p.wait("scalar", "kfe", p.val("kfe"))
                    rms_rope(tt, 2, ktm[:, :], sq_scr[:, 0:256], ssq4[:, 0:2],
                             rr4[:, 0:2], kwbc[:, :], kbf[:, :], "kfe")
                    p.wait("vector", "kfe", p.val("kfe"))
                    for h2 in range(2):
                        h4 = hp * 2 + h2
                        p.op("vector", lambda e, o=vaug[:, 8 + tt, h4, 0:128],
                             i=vbf2[:, h2 * 128:(h2 + 1) * 128]:
                             e.tensor_copy(o, i), "vst")
                        p.wait("tensor", "kfe", p.val("kfe"))
                        p.wait("tensor", "ktt", p.val("ktt"))
                        p.trans(ps_kt[:, :], kbf[:, h2 * 128:(h2 + 1) * 128],
                                idb[:, :],
                                kT[:, h4, T + tt * 128:T + (tt + 1) * 128], "ktt")
        p.wait("vector", "ld2", p.val("ld2"))
        for h4 in range(KVH):
            p.op("vector", lambda e, o=vaug[:, :, h4, 128:129],
                 i=vones[:, :].unsqueeze(2): e.tensor_copy(o, i), "vst")

        p.wait("gpsimd", "ktt", p.val("ktt"))
        p.wait("gpsimd", "vst", p.val("vst"))
        def _xch(g):
            half = g.bass.partition_id()
            myh = g.compute_val(half % 2)
            oth = g.compute_val(1 - (half % 2))
            g.dma_start(out=xch_k.ap()[bass.ds(myh, 1), :, :].squeeze(0),
                        in_=kT[:, :, T:2 * T]).then_inc(p.sem("xwr"), 16)
            g.dma_start(out=xch_v.ap()[bass.ds(myh, 1), :, :].squeeze(0),
                        in_=vaug[:, 8:16, :, 0:128]).then_inc(p.sem("xwr"), 16)
            g.wait_ge(p.sem("xwr"), 32)
            g.dma_start(out=xfl.ap()[bass.ds(myh, 1), :].squeeze(0),
                        in_=non[:, :]).then_inc(p.sem("xwr"), 16)
            g.wait_ge(p.sem("xwr"), 48)
            nval = g.value_load(non[0:1, 0:1])
            cntr = g.alloc_register("pollcnt")
            g.reg_mov(cntr, 0)
            fval = g.alloc_register("fval")
            g.reg_mov(fval, 0)
            ctmp = g.alloc_register("ctmp")
            flg = p._flg
            def cond():
                g.reg_alu(ctmp, fval, nval, ALU.not_equal)
                return ctmp
            with g.While(cond):
                g.reg_alu(cntr, cntr, 16, ALU.add)
                g.dma_start(out=flg[:, :],
                            in_=xfl.ap()[bass.ds(oth, 1), :].squeeze(0)
                            ).then_inc(p.sem("xpl"), 16)
                g.wait_ge(p.sem("xpl"), cntr)
                g.reg_load(fval, flg[0:1, 0:1])
            g.dma_start(out=kT[:, :, 0:T],
                        in_=xch_k.ap()[bass.ds(oth, 1), :, :].squeeze(0)
                        ).then_inc(p.sem("xrd"), 16)
            g.dma_start(out=vaug[:, 0:8, :, 0:128],
                        in_=xch_v.ap()[bass.ds(oth, 1), :, :].squeeze(0)
                        ).then_inc(p.sem("xrd"), 16)
        p._flg = p.sb("flg", [1, 1], I32, s3)
        p.inc("xrd", 32)
        p.on("gpsimd", _xch)

        with contextlib.ExitStack() as s3q:
            wqb = p.sb("wqb", [128, DC, 512], BF16, s3q)
            ps_q = p.ps("ps_q", [128, 512], s3q)
            ps_qt = p.ps("ps_qt", [128, 128], s3q, BF16)
            for oq in range(4):
                if oq:
                    p.wait("gpsimd", "qmm", p.val("qmm"))
                p.dma("gpsimd", "wq%d" % (oq % 2), wqb[:, :, :],
                      wq1_in[:, oq * 512:(oq + 1) * 512].rearrange(
                          "(c p) o -> p c o", p=128))
                p.wait("tensor", "wq%d" % (oq % 2), p.val("wq%d" % (oq % 2)))
                for tt in range(TT):
                    if tt or oq:
                        p.wait("tensor", "qfe", p.val("qfe"))
                    for dc in range(DC):
                        p.mm(ps_q[:, :], x2T[:, dc, tt * 128:(tt + 1) * 128],
                             wqb[:, dc, :], dc == 0, dc == DC - 1,
                             sem=None if dc < DC - 1 else "qmm")
                    p.wait("vector", "qmm", p.val("qmm"))
                    p.op("vector", lambda e, o=qtm[:, :], i=ps_q[:, :]:
                         e.tensor_copy(o, i), "qfe")
                    p.wait("vector", "qfe")
                    p.wait("scalar", "qfe", p.val("qfe"))
                    rms_rope(tt, 4, qtm[:, :], sq_scr[:, 0:512], ssq4[:, :],
                             rr4[:, :], wbc[:, :], qbf[:, :], "qfe")
                    p.wait("tensor", "qfe", p.val("qfe"))
                    for h4 in range(4):
                        h = oq * 4 + h4
                        p.wait("tensor", "qtt", p.val("qtt"))
                        p.trans(ps_qt[:, :], qbf[:, h4 * 128:(h4 + 1) * 128],
                                idb[:, :], qT[:, h, tt * 128:(tt + 1) * 128], "qtt")

    p.wait("vector", "xrd", 32)
    p.op("vector", lambda e, o=vonesf[:, :], i=vones[:, :]: e.tensor_copy(o, i), "vzm")
    p.wait("vector", "vzm")
    for kbt in range(8):
        p.op("vector", lambda e, o=vaug[:, kbt, :, 0:128], i=vaug[:, kbt, :, 0:128],
             m=vonesf[:, kbt:kbt + 1]: e.tensor_scalar_mul(o, i, m), "vzm")

    with contextlib.ExitStack() as s4:
        p_sb = p.sb("p_sb", [128, 2, 1152], BF16, s4)
        dmask = p.sb("dmask", [128, 128], F32, s4)
        bmask = p.sb("bmask", [128, 128], F32, s4)
        rden = p.sb("rden", [1, 2, 128], F32, s4)
        rb_sb = p.sb("rb_sb", [128, 2, 128], F32, s4)
        mbias = p.sb("mbias", [128, 128], F32, s4)
        onescb = p.sb("onescb", [128, 1], BF16, s4)
        p.wait("sync", "qtt", p.val("qtt"))
        p.dma("sync", "ld3", dmask[:, :], dmask_in[:, :])
        p.dma("sync", "ld3", bmask[:, :], bmask_in[:, :])

        with contextlib.ExitStack() as s4m:
            qsq = p.sb("qsq", [128, T], BF16, s4m)
            ksq = p.sb("ksq", [128, 2 * T], BF16, s4m)
            mqrow = p.sb("mqrow", [1, 128], F32, s4m)
            mkrow = p.sb("mkrow", [1, KVH, 16], F32, s4m)
            mkwin = p.sb("mkwin", [1, KVH, TT], F32, s4m)
            mrow = p.sb("mrow", [1, 128], F32, s4m)
            ps_nq = p.ps("ps_nq", [1, T], s4m)
            ps_mb = p.ps("ps_mb", [128, 128], s4m)
            for eng in ("scalar", "tensor"):
                p.wait(eng, "xrd", 32)
                p.wait(eng, "qtt", p.val("qtt"))
                p.wait(eng, "vst", p.val("vst"))
                p.wait(eng, "vzm", p.val("vzm"))
            p.wait("vector", "ld", p.val("ld"))
            p.op("vector", lambda e, o=onescb[:, :]: e.memset(o, 1.0), "ocb")
            p.wait("tensor", "ocb", 1)
            for h in range(H1):
                if h:
                    p.wait("scalar", "nqm", 8 * h)
                p.op("scalar", lambda e, o=qsq[:, :], i=qT[:, h, :]:
                     e.activation(o, i, ACT.Square), "sqq")
                p.wait("tensor", "sqq", h + 1)
                if h:
                    p.wait("tensor", "mqr", 8 * h)
                for qt in range(TT):
                    p.mm(ps_nq[0:1, qt * 128:(qt + 1) * 128], onescb[:, :],
                         qsq[:, qt * 128:(qt + 1) * 128], True, True, sem="nqm")
                p.wait("vector", "nqm", 8 * (h + 1))
                for qt in range(TT):
                    p.op("vector", lambda e,
                         o=mqrow[0:1, h * 8 + qt:h * 8 + qt + 1],
                         i=ps_nq[0:1, qt * 128:(qt + 1) * 128]:
                         e.reduce_max(o, i, AX.X), "mqr")
            for kvh in range(KVH):
                if kvh:
                    p.wait("scalar", "nkm", 16 * kvh)
                p.op("scalar", lambda e, o=ksq[:, :], i=kT[:, kvh, :]:
                     e.activation(o, i, ACT.Square), "sqk")
                p.wait("tensor", "sqk", kvh + 1)
                p.wait("tensor", "mqr", 8 * H1 if kvh == 0 else 0)
                for kt in range(16):
                    p.mm(ps_nq[0:1, (kt % 8) * 128:(kt % 8 + 1) * 128],
                         onescb[:, :], ksq[:, kt * 128:(kt + 1) * 128],
                         True, True, sem="nkm")
                    p.wait("vector", "nkm", 16 * kvh + kt + 1)
                    p.op("vector", lambda e,
                         o=mkrow[0:1, kvh, kt:kt + 1],
                         i=ps_nq[0:1, (kt % 8) * 128:(kt % 8 + 1) * 128]:
                         e.reduce_max(o, i, AX.X), "mkr")
                    if kt < 15:
                        p.wait("tensor", "mkr", 16 * kvh + kt + 1)
            p.wait("vector", "mkr", 64)
            for kvh in range(KVH):
                for qt in range(TT):
                    p.op("vector", lambda e, o=mkwin[0:1, kvh, qt:qt + 1],
                         i=mkrow[0:1, kvh, qt:qt + 9]:
                         e.reduce_max(o, i, AX.X), "mkw")
            p.wait("vector", "mkw")
            p.wait("vector", "mqr")
            for h in range(H1):
                p.op("vector", lambda e, o=mrow[0:1, h * 8:(h + 1) * 8],
                     a=mqrow[0:1, h * 8:(h + 1) * 8], b=mkwin[0:1, h // KVH, :]:
                     e.tensor_tensor(o, a, b, ALU.mult), "mcr")
            p.wait("scalar", "mcr", p.val("mcr"))
            p.op("scalar", lambda e, o=mrow[0:1, :], i=mrow[0:1, :]:
                 e.activation(o, i, ACT.Sqrt), "msq")
            p.wait("vector", "msq", 1)
            p.op("vector", lambda e, o=mrow[0:1, :], i=mrow[0:1, :]:
                 e.tensor_scalar_mul(o, i, -SM_SCALE), "mns")
            p.wait("tensor", "mns", 1)
            p.wait("tensor", "ld3", p.val("ld3"))
            p.mm(ps_mb[:, :], onesf[:, :], mrow[0:1, :], True, True, sem="mbm")
            p.wait("vector", "mbm", 1)
            p.op("vector", lambda e, o=mbias[:, :], i=ps_mb[:, :]:
                 e.tensor_copy(o, i), "mbb")

        ps_sL = [p.ps("ps_s", [128, 1152], s4), p.ps("ps_s", [128, 1152], s4)]
        ps_ao = p.ps("ps_ao", [128, 128], s4)
        ps_den = p.ps("ps_den", [1, 128], s4)

        p.wait("scalar", "mbb", 1)
        p.wait("tensor", "mbb", 1)
        NT4 = TT * H1
        tiles = [(qt, h) for qt in range(TT) for h in range(H1)]

        def _qk(ti):
            qt, h = tiles[ti]
            kvh = h // KVH
            ps_s = ps_sL[ti % 2]
            if ti >= 2:
                p.wait("tensor", "pexp", ti - 1)
                p.wait("tensor", "rbc", ti - 1)
            for b9 in range(9):
                p.mm(ps_s[:, b9 * 128:(b9 + 1) * 128],
                     kT[:, kvh, (qt + b9) * 128:(qt + b9 + 1) * 128],
                     qT[:, h, qt * 128:(qt + 1) * 128], True, True,
                     sem=None if b9 < 8 else "smm")

        def _pv(ti):
            qt, h = tiles[ti]
            kvh = h // KVH
            p.wait("tensor", "pexp", ti + 1)
            if ti >= 1:
                p.wait("tensor", "aomul", ti)
                p.wait("tensor", "rcp", ti)
            for b9 in range(9):
                p.mm(ps_den[:, :], vaug[:, qt + b9, kvh, 128:129],
                     p_sb[:, ti % 2, b9 * 128:(b9 + 1) * 128], b9 == 0, b9 == 8)
                p.mm(ps_ao[:, :], vaug[:, qt + b9, kvh, 0:128],
                     p_sb[:, ti % 2, b9 * 128:(b9 + 1) * 128], b9 == 0, b9 == 8,
                     sem=None if b9 < 8 else "avm")
            p.wait("tensor", "rcp", ti + 1)
            p.mm(ps_sL[ti % 2][:, 0:128], onesf[:, :], rden[0:1, ti % 2, :],
                 True, True, sem="rbm")

        def _msks(ti):
            ps_s = ps_sL[ti % 2]
            p.wait("vector", "smm", ti + 1)
            p.op("vector", lambda e, o=ps_s[:, 0:128], i=ps_s[:, 0:128],
                 m=bmask[:, :]: e.tensor_tensor(o, i, m, ALU.add), "msk")
            p.op("vector", lambda e, o=ps_s[:, 1024:1152], i=ps_s[:, 1024:1152],
                 m=dmask[:, :]: e.tensor_tensor(o, i, m, ALU.add), "mk2")

        def _recip(ti):
            p.wait("vector", "avm", ti + 1)
            p.op("vector", lambda e, o=rden[0:1, ti % 2, :], i=ps_den[:, :]:
                 e.reciprocal(o, i), "rcp")

        def _mult(ti):
            qt, h = tiles[ti]
            p.wait("vector", "rbc", ti + 1)
            p.op("vector", lambda e, o=aoT[:, h, qt * 128:(qt + 1) * 128],
                 a=ps_ao[:, :], b=rb_sb[:, ti % 2, :]:
                 e.tensor_tensor(o, a, b, ALU.mult), "aomul")

        def _exp(ti):
            qt, h = tiles[ti]
            p.wait("scalar", "mk2", ti + 1)
            if ti >= 2:
                p.wait("scalar", "avm", ti - 1)
            p.op("scalar", lambda e, o=p_sb[:, ti % 2, :], i=ps_sL[ti % 2][:, :],
                 b=mbias[:, h * 8 + qt:h * 8 + qt + 1]:
                 e.activation(o, i, ACT.Exp, bias=b, scale=SM_SCALE), "pexp")

        def _copy(ti):
            p.wait("scalar", "rbm", ti + 1)
            if ti >= 2:
                p.wait("scalar", "aomul", ti - 1)
            p.op("scalar", lambda e, o=rb_sb[:, ti % 2, :],
                 i=ps_sL[ti % 2][:, 0:128]: e.copy(o, i), "rbc")

        for ti in range(NT4):
            _qk(ti)
            if ti >= 1:
                _pv(ti - 1)
            _msks(ti)
            if ti >= 2:
                _mult(ti - 2)
            if ti >= 1:
                _recip(ti - 1)
            _exp(ti)
            if ti >= 1:
                _copy(ti - 1)
        _pv(NT4 - 1)
        _mult(NT4 - 2)
        _recip(NT4 - 1)
        _copy(NT4 - 1)
        _mult(NT4 - 1)
        p.wait("gpsimd", "smm", NT4)
        p.wait("gpsimd", "avm", NT4)
        for q in range(4):
            if q >= 2:
                p.wait("gpsimd", "opr", (q - 1) * TT)
            for ch in range(4):
                r0 = (q * 4 + ch) * 128
                for hp in range(2):
                    p.dma("gpsimd", "wo",
                          woQ[hp * 64:(hp + 1) * 64, q % 2, ch, :],
                          wo1_in[r0 + hp * 64:r0 + (hp + 1) * 64, :])

    with contextlib.ExitStack() as s5:
        s5o = contextlib.ExitStack()
        ps_oL = [p.ps("ps_o", [128, D], s5o), p.ps("ps_o", [128, D], s5o)]
        p.wait("tensor", "aomul", p.val("aomul"))
        for q in range(4):
            p.wait("tensor", "wo", 128 * (q + 1))
            for tt in range(TT):
                gi = q * TT + tt
                ps_o = ps_oL[gi % 2]
                if gi >= 2:
                    p.wait("tensor", "ore", gi - 1)
                for dc4 in range(4):
                    for oc in range(4):
                        p.mm(ps_o[:, oc * 512:(oc + 1) * 512],
                             aoT[:, q * 4 + dc4, tt * 128:(tt + 1) * 128],
                             woQ[:, q % 2, dc4, oc * 512:(oc + 1) * 512],
                             dc4 == 0, dc4 == 3,
                             sem=None if not (dc4 == 3 and oc == 3) else "opr")
                p.wait("vector", "opr", gi + 1)
                p.op("vector", lambda e, o=x_tm[:, tt, :], i=ps_o[:, :],
                     x=x_tm[:, tt, :]:
                     e.tensor_tensor(o, i, x, ALU.add), "ore")
        s5o.close()
        for tt in range(TT):
            p.wait("vector", "ore", p.val("ore"))
            layer_norm(p, x_tm[:, tt, :], sq_scr[:, :], ssum[:, :], ssq[:, :],
                       mean[:, :], m2[:, :], var[:, :], rstd[:, :], "lne")

        rtr = p.sb("rtr", [128, DC, E1], BF16, s5)
        lgt = p.sb("lgt", [128, TT, E1], F32, s5)
        m1c = p.sb("m1c", [128, 1], F32, s5)
        m2cc = p.sb("m2cc", [128, 1], F32, s5)
        oh1 = p.sb("oh1", [128, E1], F32, s5)
        oh2 = p.sb("oh2", [128, E1], F32, s5)
        scrM = p.sb("scrM", [128, E1], F32, s5)
        cnt_s = p.sb("cnt_s", [1, E1], F32, s5)
        ps_t2 = p.ps("ps_t2", [128, 128], s5)
        ps_lg = p.ps("ps_lg", [128, E1], s5)
        ps_cnt = p.ps("ps_cnt", [1, E1], s5)
        p.wait("gpsimd", "lne", p.val("lne"))
        p.dma("gpsimd", "rtl", rtr[:, :, :], rtr_in.rearrange("(c p) w -> p c w", p=128))
        p.wait("tensor", "lne", p.val("lne"))
        for tt in range(TT):
            for dc in range(DC):
                p.wait("tensor", "x3e", p.val("x3e"))
                p.trans(ps_t2[:, :], x_tm[:, tt, dc * 128:(dc + 1) * 128], idf[:, :],
                        x3T[:, dc, tt * 128:(tt + 1) * 128], "x3e")
        p.wait("tensor", "x3e", p.val("x3e"))
        p.wait("tensor", "rtl", p.val("rtl"))
        for tt in range(TT):
            if tt:
                p.wait("tensor", "lge", p.val("lge"))
            for dc in range(DC):
                p.mm(ps_lg[:, :], x3T[:, dc, tt * 128:(tt + 1) * 128], rtr[:, dc, :],
                     dc == 0, dc == DC - 1, sem=None if dc < DC - 1 else "lgm")
            p.wait("vector", "lgm", p.val("lgm"))
            p.op("vector", lambda e, o=lgt[:, tt, :], i=ps_lg[:, :]:
                 e.tensor_copy(o, i), "lge")
        for tt in range(TT):
            p.wait("vector", "lge", p.val("lge"))
            p.op("vector", lambda e, o=m1c[:, :], i=lgt[:, tt, :]:
                 e.reduce_max(o, i, AX.X), "tp")
            p.wait("vector", "tp")
            p.op("vector", lambda e, o=oh1[:, :], i=lgt[:, tt, :], m=m1c[:, :]:
                 e.tensor_scalar(o, i, m, None, ALU.is_equal), "tp")
            p.wait("vector", "tp")
            p.op("vector", lambda e, o=scrM[:, :], i=oh1[:, :], x=lgt[:, tt, :]:
                 e.scalar_tensor_tensor(o, i, NEG, x, ALU.mult, ALU.add), "tp")
            p.wait("vector", "tp")
            p.op("vector", lambda e, o=m2cc[:, :], i=scrM[:, :]:
                 e.reduce_max(o, i, AX.X), "tp")
            p.wait("vector", "tp")
            p.op("vector", lambda e, o=oh2[:, :], i=scrM[:, :], m=m2cc[:, :]:
                 e.tensor_scalar(o, i, m, None, ALU.is_equal), "tp")
            p.wait("vector", "tp")
            p.op("vector", lambda e, o=oh1[:, :], a=oh1[:, :], b=oh2[:, :]:
                 e.tensor_tensor(o, a, b, ALU.add), "tp")
            p.wait("tensor", "tp", p.val("tp"))
            p.mm(ps_cnt[:, :], onescol[:, :], oh1[:, :], tt == 0, tt == TT - 1,
                 sem="cm")
            p.wait("vector", "cm", p.val("cm"))
        p.wait("vector", "cm", p.val("cm"))
        p.op("vector", lambda e, o=cnt_s[:, :], i=ps_cnt[:, :]:
             e.tensor_copy(o, i), "cme")

    for tt in range(TT):
        p.wait("sync", "lne", 9 * (tt + 1))
        p.dma("sync", "st", x3_out[tt * 128:(tt + 1) * 128, :], x_tm[:, tt, :])
    p.wait("sync", "cme", p.val("cme"))
    p.dma("sync", "st", cnt_out[:, :], cnt_s[:, :])
    p.wait("sync", "st", p.val("st"))
    return p.finish()


def _prep_lB(inputs, x1, s_all, nonce):
    import ml_dtypes
    bfd = ml_dtypes.bfloat16
    neg = np.float32(NEG)
    ii, jj = np.meshgrid(np.arange(128), np.arange(128), indexing="ij")
    shared = {
        "sall_in": np.ascontiguousarray(s_all),
        "ew_in": np.asarray(inputs["L0_ffn_expert_w"], np.float32),
        "eb_in": np.asarray(inputs["L0_ffn_expert_b"], np.float32).reshape(1, D),
        "wq1_in": np.ascontiguousarray(np.asarray(inputs["L1_attn_wq"], np.float32).T),
        "wk1_in": np.ascontiguousarray(np.asarray(inputs["L1_attn_wk"], np.float32).T),
        "wv1_in": np.ascontiguousarray(np.asarray(inputs["L1_attn_wv"], np.float32).T),
        "qnw_in": np.asarray(inputs["L1_attn_q_norm_w"], np.float32).reshape(1, DH),
        "knw_in": np.asarray(inputs["L1_attn_k_norm_w"], np.float32).reshape(1, DH),
        "wo1_in": np.ascontiguousarray(np.asarray(inputs["L1_attn_wo"], np.float32).T),
        "dmask_in": np.ascontiguousarray(
            np.where(ii >= jj, 0.0, neg).astype(np.float32).T),
        "bmask_in": np.ascontiguousarray(
            np.where(jj > ii, 0.0, neg).astype(np.float32).T),
        "idb_in": np.eye(128, dtype=np.float32).astype(bfd),
        "idf_in": np.eye(128, dtype=np.float32),
        "rtr_in": np.asarray(inputs["L1_ffn_router"], np.float32),
        "ones_in": np.ones((1, 128), np.float32),
        "nonce_in": np.array([[nonce]], np.int32),
    }
    in_maps = []
    for c in range(NCORE):
        half = c % 2
        m = dict(shared)
        m["x_in"] = x1[c * T:(c + 1) * T]
        m["xT_in"] = np.ascontiguousarray(x1[c * T:(c + 1) * T].T)
        m["sown_in"] = s_all[c * T:(c + 1) * T]
        cos, sin = _rope_tables(c)
        m["cos_in"] = cos
        m["sin_in"] = sin
        von = np.zeros((2 * T,), np.float32)
        if half == 0:
            von[T:] = 1.0
        else:
            von[:] = 1.0
        m["vones_in"] = np.ascontiguousarray(von.reshape(16, 128).T).astype(bfd)
        in_maps.append(m)
    return in_maps


def run_lB(inputs, x1, s_all, nonce=1234567):
    nc = build_lB()
    in_maps = _prep_lB(inputs, x1, s_all, nonce)
    res = run_bass_kernel_spmd(nc, in_maps, core_ids=list(range(NCORE)))
    x3 = np.concatenate([res.results[c]["x3_out"] for c in range(NCORE)], 0)
    cnt = np.stack([res.results[c]["cnt_out"][0] for c in range(NCORE)], 0)
    return x3, cnt


import time as _time


def kernel(**inputs):
    x1, s = run_l1(inputs)
    x3, cnt = run_lB(inputs, x1, s, nonce=(int(_time.time() * 1000) & 0x7FFFFFFF) | 1)
    x4 = run_l5(inputs, x3, cnt)
    return x4.reshape(4, 2048, D)


# version 06
# speedup vs baseline: 1.0539x; 1.0539x over previous
import contextlib
import numpy as np

import concourse.bass as bass
import concourse.mybir as mybir
from concourse.bass_utils import run_bass_kernel_spmd

F32 = mybir.dt.float32
BF16 = mybir.dt.bfloat16
I32 = mybir.dt.int32
ALU = mybir.AluOpType
ACT = mybir.ActivationFunctionType
AX = mybir.AxisListType

NCORE = 8
T = 1024
TT = 8
D = 2048
DC = 16
H0 = 16
M0 = 64
H1 = 16
KVH = 4
DH = 128
SM_SCALE = DH ** -0.5
NE0 = 16
K0_TOP = 512
E1 = 8
I1 = 1408
J1 = 5632
CAP1 = 2560
CAPL = 384
NEG = -1.0e30


class Prog:

    ENGS = ("sync", "tensor", "vector", "scalar", "gpsimd")

    def __init__(self, nc):
        self.nc = nc
        self.ops = {k: [] for k in self.ENGS}
        self.sems = {}
        self.cnt = {}
        self.names = {}
        self._sem_prod = {}
        self._rec = None
        self.stack = contextlib.ExitStack()

    def _uniq(self, name):
        n = self.names.get(name, 0)
        self.names[name] = n + 1
        return name if n == 0 else "%s_%d" % (name, n)

    def sb(self, name, shape, dtype, stack=None):
        return (stack or self.stack).enter_context(
            self.nc.sbuf_tensor(self._uniq(name), list(shape), dtype))

    def sba(self, name, shape, dtype, offset):
        return self.nc.alloc_sbuf_tensor_at(self._uniq(name), list(shape), dtype,
                                            offset=offset)

    def ps(self, name, shape, stack=None, dtype=F32):
        return (stack or self.stack).enter_context(
            self.nc.psum_tensor(self._uniq(name), list(shape), dtype))

    def sem(self, name):
        if name not in self.sems:
            self.sems[name] = self.stack.enter_context(self.nc.semaphore(name))
            self.cnt[name] = 0
        return self.sems[name]

    def inc(self, name, k=1):
        s = self.sem(name)
        self.cnt[name] += k
        return s

    def val(self, name):
        self.sem(name)
        return self.cnt[name]

    def on(self, eng, fn):
        self.ops[eng].append(fn)
        if getattr(self, "_rec", None) is not None:
            self._rec_engs.add(eng)

    def skip_begin(self, cond_ap):
        assert getattr(self, "_rec", None) is None
        self._rec = dict(self.cnt)
        self._rec_engs = set()
        self._rec_marks = {k: len(v) for k, v in self.ops.items()}
        self._rec_cond = cond_ap

    def skip_end(self, gpsimd_in=False):
        pre = self._rec
        engs = self._rec_engs
        marks = self._rec_marks
        cond = self._rec_cond
        self._rec = None
        deltas = {n: self.cnt[n] - pre.get(n, 0) for n in self.cnt
                  if self.cnt[n] != pre.get(n, 0)}
        prod = self._sem_prod
        for eng in engs:
            if eng == "gpsimd" and not gpsimd_in:
                continue
            body = self.ops[eng][marks[eng]:]
            del self.ops[eng][marks[eng]:]
            if eng == "gpsimd":
                comp = []
            else:
                comp = [(self.sems[n], pre.get(n, 0), d) for n, d in deltas.items()
                        if prod.get(n) == eng]

            def wrapped(e, body=body, comp=comp, cond=cond):
                with e.register() as reg:
                    e.reg_load(reg, cond)
                    with e.If(reg):
                        for f in body:
                            f(e)
                    if comp:
                        with e.Else():
                            for sh_, pv_, d_ in comp:
                                if pv_ > 0:
                                    e.wait_ge(sh_, pv_)
                                e.sem_inc(sh_, d_)
                    e.end_ifs()
            self.ops[eng].append(wrapped)

    def wait(self, eng, name, v=None):
        s = self.sem(name)
        v = self.cnt[name] if v is None else v
        if v > 0:
            self.on(eng, lambda e, s=s, v=v: e.wait_ge(s, v))

    def dma(self, eng, sem, out_ap, in_ap):
        self._sem_prod[sem] = eng
        s = self.inc(sem, 16)
        self.on(eng, lambda e, o=out_ap, i=in_ap, s=s:
                e.dma_start(out=o, in_=i).then_inc(s, 16))

    def mm(self, out_ap, lhsT_ap, rhs_ap, start, stop, sem=None):
        if sem is not None:
            self._sem_prod[sem] = "tensor"
        if sem is None:
            self.on("tensor", lambda e, o=out_ap, l=lhsT_ap, r=rhs_ap, st=start,
                    sp=stop: e.matmul(o, l, r, start=st, stop=sp))
        else:
            s = self.inc(sem)
            self.on("tensor", lambda e, o=out_ap, l=lhsT_ap, r=rhs_ap, st=start,
                    sp=stop, s=s: e.matmul(o, l, r, start=st, stop=sp).then_inc(s, 1))

    def op(self, eng, fn, sem=None):
        if sem is not None:
            self._sem_prod[sem] = eng
        if sem is None:
            self.on(eng, lambda e, f=fn: f(e))
        else:
            s = self.inc(sem)
            self.on(eng, lambda e, f=fn, s=s: f(e).then_inc(s, 1))

    def trans(self, ps_ap, in_ap, ident_ap, out_ap, sem, evac_eng="vector"):
        self.op("tensor", lambda e, o=ps_ap, i=in_ap, idn=ident_ap:
                e.transpose(o, i, idn), sem + "_t")
        self.wait(evac_eng, sem + "_t")
        self.op(evac_eng, lambda e, o=out_ap, i=ps_ap: e.tensor_copy(o, i), sem)

    def finish(self):
        nc = self.nc
        with nc.Block() as block:
            @block.sync
            def _(e):
                for f in self.ops["sync"]:
                    f(e)

            @block.tensor
            def _(e):
                for f in self.ops["tensor"]:
                    f(e)

            @block.vector
            def _(e):
                for f in self.ops["vector"]:
                    f(e)

            @block.scalar
            def _(e):
                for f in self.ops["scalar"]:
                    f(e)

            @block.gpsimd
            def _(e):
                for f in self.ops["gpsimd"]:
                    f(e)
        self.stack.close()
        return nc


def _new_prog(name, detect=True):
    nc = bass.Bass("TRN2", target_bir_lowering=False,
                   detect_race_conditions=detect)
    return Prog(nc)


def elu1(p, psum_ap, out_ap, scr1, sem):
    p.op("vector", lambda e, i=psum_ap, o=scr1: e.tensor_scalar_min(o, i, 0.0), sem)
    p.wait("scalar", sem)
    p.op("scalar", lambda e, i=scr1: e.activation(i, i, ACT.Exp), sem)
    p.wait("vector", sem)
    p.op("vector", lambda e, i=psum_ap, x=scr1, o=out_ap:
         e.scalar_tensor_tensor(o, i, 0.0, x, ALU.max, ALU.add), sem)


def layer_norm(p, x_ap, sq_scr, ssum, ssq, mean, m2, var, rstd, sem):
    p.op("vector", lambda e, o=ssum, i=x_ap: e.reduce_sum(o, i, AX.X), sem)
    p.wait("scalar", sem)
    p.op("scalar", lambda e, o=sq_scr, i=x_ap, a=ssq:
         e.activation(o, i, ACT.Square, accum_out=a), sem)
    p.wait("vector", sem)
    p.op("vector", lambda e, o=mean, i=ssum: e.tensor_scalar_mul(o, i, 1.0 / D), sem)
    p.wait("vector", sem)
    p.op("vector", lambda e, o=m2, i=mean, b=mean:
         e.tensor_tensor(o, i, b, ALU.mult), sem)
    p.wait("vector", sem)
    p.op("vector", lambda e, o=var, i=ssq, b=m2:
         e.tensor_scalar(o, i, 1.0 / D, b, ALU.mult, ALU.subtract), sem)
    p.wait("vector", sem)
    p.op("vector", lambda e, o=var, i=var: e.tensor_scalar_add(o, i, 1e-5), sem)
    p.wait("vector", sem)
    p.op("vector", lambda e, o=rstd, i=var: e.reciprocal(o, i), sem)
    p.wait("scalar", sem)
    p.op("scalar", lambda e, o=rstd, i=rstd: e.activation(o, i, ACT.Sqrt), sem)
    p.wait("vector", sem)
    p.op("vector", lambda e, o=x_ap, i=x_ap, a=mean, b=rstd:
         e.tensor_scalar(o, i, a, b, ALU.subtract, ALU.mult), sem)



def build_l1(upto=8):
    p = _new_prog("l1")
    nc = p.nc
    dt = lambda n, sh, ty=F32, kind="ExternalInput": nc.dram_tensor(
        n, list(sh), ty, kind=kind).ap()

    x_in = dt("x_in", [T, D])
    xT_in = dt("xT_in", [D, T])
    xoT_in = dt("xoT_in", [D, T])
    wq_in = dt("wq_in", [H0, 128, DC, DH])
    wk_in = dt("wk_in", [H0, 128, DC, DH])
    wv_in = dt("wv_in", [H0, 128, DC, DH])
    wo_in = dt("wo_in", [D, D])
    bq_in = dt("bq_in", [DH, DC])
    bk_in = dt("bk_in", [DH, DC])
    bv_in = dt("bv_in", [DH, DC])
    bo_in = dt("bo_in", [1, D])
    wmq_in = dt("wmq_in", [DH, M0])
    wmk_in = dt("wmk_in", [DH, M0])
    rw_in = dt("rw_in", [D, NE0])
    rb_in = dt("rb_in", [NE0, 1])
    ones_in = dt("ones_in", [1, 128])
    idf_in = dt("idf_in", [128, 128])

    x1_out = dt("x1_out", [T, D], F32, "ExternalOutput")
    s_out = dt("s_out", [T, NE0], F32, "ExternalOutput")

    x_tm = p.sb("x_tm", [128, TT, D], F32)
    qm = p.sb("qm", [128, H0 // 2, T], BF16)
    km = p.sb("km", [128, H0, 2 * T // 128 * M0], BF16)
    kvb = p.sb("kvb", [128, 8 * 256], BF16)
    ksblk = p.sb("ksblk", [128, H0, H0], BF16)
    rnorm = p.sb("rnorm", [128, TT, 16], F32)
    bq = p.sb("bq", [128, DC], F32)
    bk = p.sb("bk", [128, DC], F32)
    bv = p.sb("bv", [128, DC], F32)
    wmq = p.sb("wmq", [128, M0], BF16)
    wmk = p.sb("wmk", [128, M0], BF16)
    idf = p.sb("idf", [128, 128], F32)
    idb = p.sb("idb", [128, 128], BF16)
    scr1 = p.sb("scr1", [128, T], F32)
    ssum = p.sb("ssum", [128, 1], F32)
    ssq = p.sb("ssq", [128, 1], F32)
    mean = p.sb("mean", [128, 1], F32)
    m2 = p.sb("m2", [128, 1], F32)
    var = p.sb("var", [128, 1], F32)
    rstd = p.sb("rstd", [128, 1], F32)
    s_tm = p.sb("s_tm", [128, TT, NE0], F32)

    p.dma("sync", "ld", x_tm[:, :, :], x_in.rearrange("(tt p) d -> p tt d", p=128))
    p.dma("sync", "ld", bq[:, :], bq_in[:, :])
    p.dma("sync", "ld", bk[:, :], bk_in[:, :])
    p.dma("sync", "ld", bv[:, :], bv_in[:, :])
    p.dma("sync", "ld", idf[:, :], idf_in[:, :])
    p.dma("gpsimd", "ldg", wmq[:, :], wmq_in[:, :])
    p.dma("gpsimd", "ldg", wmk[:, :], wmk_in[:, :])
    p.wait("vector", "ld")
    p.op("vector", lambda e, o=idb[:, :], i=idf[:, :]: e.tensor_copy(o, i), "idb")

    with contextlib.ExitStack() as sc1:
      if upto >= 2:
        xT = p.sb("xT", [128, DC, T], BF16, sc1)
        xoT = p.sb("xoT", [128, DC, T], BF16, sc1)
        wbuf = p.sb("wbuf", [128, 2, DC, DH], BF16, sc1)
        kf = p.sb("kf", [128, 2 * T], BF16, sc1)

        p.dma("gpsimd", "ldg", xT[:, :, :], xT_in.rearrange("(c p) t -> p c t", p=128))
        p.dma("gpsimd", "ldg", xoT[:, :, :], xoT_in.rearrange("(c p) t -> p c t", p=128))
        ldg_all = p.val("ldg")

        qps = contextlib.ExitStack()
        ps_q = p.ps("ps_q", [128, T], qps)
        ps_qm = p.ps("ps_qm", [128, T], qps)

        for h in range(H0):
            buf = h % 2
            if h >= 2:
                p.wait("gpsimd", "qpr", h - 1)
            p.dma("gpsimd", "wq%d" % buf, wbuf[:, buf, :, :], wq_in[h])
            p.wait("tensor", "ld", 5 * 16)
            p.wait("tensor", "ldg", ldg_all)
            p.wait("tensor", "wq%d" % buf, p.val("wq%d" % buf))
            if h >= 1:
                p.wait("tensor", "qfe", h)
            for dc in range(DC):
                p.mm(ps_q[:, 0:512], wbuf[:, buf, dc, :], xT[:, dc, 0:512],
                     dc == 0, dc == DC - 1)
                p.mm(ps_q[:, 512:1024], wbuf[:, buf, dc, :], xT[:, dc, 512:1024],
                     dc == 0, dc == DC - 1,
                     sem=None if dc < DC - 1 else "qpr")
            p.wait("scalar", "qpr", h + 1)
            p.op("scalar", lambda e, o=kf[:, 0:T], i=ps_q[:, :], b=bq[:, h:h + 1]:
                 e.activation(o, i, ACT.Identity, bias=b), "qfe")
            base = (h % 2) * 64
            p.wait("tensor", "qfe", p.val("qfe"))
            if h % 2 == 0 and h >= 2:
                p.wait("tensor", "qel", 3 * (h // 2))
            p.mm(ps_qm[base:base + 64, 0:512], wmq[:, :], kf[:, 0:512], True, True)
            p.mm(ps_qm[base:base + 64, 512:1024], wmq[:, :], kf[:, 512:1024],
                 True, True, sem="qmm")
            if h % 2 == 1:
                pair = h // 2
                p.wait("vector", "qmm", p.val("qmm"))
                elu1(p, ps_qm[:, :], qm[:, pair, :], scr1[:, 0:T], "qel")

        qps.close()
        kps = contextlib.ExitStack()
        ps_k = p.ps("ps_k", [128, 2 * T], kps)
        ps_km = p.ps("ps_km", [128, H0 * M0], kps)

        p.wait("tensor", "qel", p.val("qel"))
        for h in range(H0):
            buf = h % 2
            if h < 2:
                p.wait("gpsimd", "qpr", H0)
            else:
                p.wait("gpsimd", "kpr", h - 1)
            p.dma("gpsimd", "wk%d" % buf, wbuf[:, buf, :, :], wk_in[h])
            p.wait("tensor", "wk%d" % buf, p.val("wk%d" % buf))
            if h >= 1:
                p.wait("tensor", "kfe", h)
            for dc in range(DC):
                for q4 in range(4):
                    src = xT if q4 < 2 else xoT
                    sl = slice((q4 % 2) * 512, (q4 % 2) * 512 + 512)
                    p.mm(ps_k[:, q4 * 512:(q4 + 1) * 512], wbuf[:, buf, dc, :],
                         src[:, dc, sl], dc == 0, dc == DC - 1,
                         sem=None if not (dc == DC - 1 and q4 == 3) else "kpr")
            p.wait("scalar", "kpr", h + 1)
            p.op("scalar", lambda e, o=kf[:, :], i=ps_k[:, :], b=bk[:, h:h + 1]:
                 e.activation(o, i, ACT.Identity, bias=b), "kfe")
            p.wait("tensor", "kfe", p.val("kfe"))
            if h >= 1:
                p.wait("tensor", "kel", 3 * h)
            for t16 in range(16):
                p.mm(ps_km[:, t16 * M0:(t16 + 1) * M0],
                     kf[:, t16 * 128:(t16 + 1) * 128], wmk[:, :], True, True,
                     sem=None if t16 < 15 else "kmm")
            p.wait("vector", "kmm", p.val("kmm"))
            elu1(p, ps_km[:, :], km[:, h, :], scr1[:, 0:H0 * M0], "kel")

        kps.close()

    with contextlib.ExitStack() as sc2:
        xT2 = p.sb("xT2", [128, DC, T], BF16, sc2)
        xoT2 = p.sb("xoT2", [128, DC, T], BF16, sc2)
        wvb = p.sb("wvb", [128, 2, DC, DH], BF16, sc2)
        vfm = p.sb("vfm", [128, 2 * T], BF16, sc2)
        vaug = p.sb("vaug", [128, 16, 129], BF16, sc2)

        p.wait("gpsimd", "kmm", p.val("kmm"))
        p.dma("gpsimd", "ldg2", xT2[:, :, :], xT_in.rearrange("(c p) t -> p c t", p=128))
        p.dma("gpsimd", "ldg2", xoT2[:, :, :], xoT_in.rearrange("(c p) t -> p c t", p=128))
        ldg2_all = p.val("ldg2")

        ps_v = p.ps("ps_v", [128, T], sc2)
        ps_vt = p.ps("ps_vt", [128, 128], sc2, BF16)
        ps_kv = p.ps("ps_kv", [128, 8 * 256], sc2)

        p.wait("tensor", "kel", p.val("kel"))
        for h in range(H0):
            buf = h % 2
            if h >= 2:
                p.wait("gpsimd", "vpr", 2 * (h - 1))
            p.dma("gpsimd", "wv%d" % buf, wvb[:, buf, :, :], wv_in[h])
            p.wait("tensor", "wv%d" % buf, p.val("wv%d" % buf))
            p.wait("tensor", "ldg2", ldg2_all)
            for vh in range(2):
                src = xT2 if vh == 0 else xoT2
                if vh == 1:
                    p.wait("tensor", "vfe", p.val("vfe"))
                for dc in range(DC):
                    for q2 in range(2):
                        sl = slice(q2 * 512, q2 * 512 + 512)
                        p.mm(ps_v[:, q2 * 512:(q2 + 1) * 512], wvb[:, buf, dc, :],
                             src[:, dc, sl], dc == 0, dc == DC - 1,
                             sem=None if not (dc == DC - 1 and q2 == 1) else "vpr")
                p.wait("scalar", "vpr", 2 * h + vh + 1)
                p.op("scalar", lambda e, o=vfm[:, vh * T:(vh + 1) * T], i=ps_v[:, :],
                     b=bv[:, h:h + 1]:
                     e.activation(o, i, ACT.Identity, bias=b), "vfe")
            p.wait("tensor", "vfe", p.val("vfe"))
            p.wait("tensor", "idb", 1)
            p.wait("vector", "kvm", p.val("kvm"))
            for t16 in range(16):
                p.wait("tensor", "vte", p.val("vte"))
                p.trans(ps_vt[:, :], vfm[:, t16 * 128:(t16 + 1) * 128], idb[:, :],
                        vaug[:, t16, 0:128], "vte")
            p.op("vector", lambda e, o=vaug[:, :, 128:129]: e.memset(o, 1.0), "vte")
            base = (h % 2) * 64
            fsl = slice((h // 2) * 256, (h // 2) * 256 + 129)
            p.wait("tensor", "vte", p.val("vte"))
            for t16 in range(16):
                p.mm(ps_kv[base:base + 64, fsl], km[:, h, t16 * M0:(t16 + 1) * M0],
                     vaug[:, t16, :], t16 == 0, t16 == 15,
                     sem=None if t16 < 15 else "kvm")
        p.wait("vector", "kvm", p.val("kvm"))
        p.op("vector", lambda e,
             o=kvb[:, :].rearrange("p (g s) -> p g s", g=8)[:, :, 0:129],
             i=ps_kv[:, :].rearrange("p (g s) -> p g s", g=8)[:, :, 0:129]:
             e.tensor_copy(o, i), "kve")

    p.wait("vector", "kve", p.val("kve"))
    p.op("vector", lambda e, o=ksblk[:, :, :]: e.memset(o, 0.0), "ksb")
    p.wait("vector", "ksb")
    for h in range(H0):
        base = (h % 2) * 64
        col = (h // 2) * 256 + 128
        p.op("vector", lambda e, o=ksblk[base:base + 64, h, h:h + 1],
             i=kvb[base:base + 64, col:col + 1]: e.tensor_copy(o, i), "ksb")

    with contextlib.ExitStack() as sc3:
        normf = p.sb("normf", [16, T], F32, sc3)
        eps6 = p.sb("eps6", [16, 1], F32, sc3)
        p.op("vector", lambda e, o=eps6[:, :]: e.memset(o, 1e-6), "eps6")
        ps_n = p.ps("ps_n", [16, T], sc3)
        ps_nt = p.ps("ps_nt", [128, 16], sc3)
        p.wait("tensor", "ksb", p.val("ksb"))
        p.wait("tensor", "qel", p.val("qel"))
        for h in range(H0):
            pair = h // 2
            for th in range(2):
                p.mm(ps_n[:, th * 512:(th + 1) * 512], ksblk[:, h, :],
                     qm[:, pair, th * 512:(th + 1) * 512],
                     h == 0, h == H0 - 1,
                     sem=None if not (h == H0 - 1 and th == 1) else "nrm")
        p.wait("scalar", "nrm", p.val("nrm"))
        p.wait("scalar", "eps6", p.val("eps6"))
        p.op("scalar", lambda e, o=normf[:, :], i=ps_n[:, :], b=eps6[:, :]: e.activation(o, i, ACT.Identity, bias=b), "nrme")
        p.wait("tensor", "nrme", p.val("nrme"))
        for tt in range(TT):
            p.wait("tensor", "nte", p.val("nte"))
            p.trans(ps_nt[:, :], normf[:, tt * 128:(tt + 1) * 128], idf[0:16, 0:16],
                    rnorm[:, tt, :], "nte")
        p.wait("vector", "nte", p.val("nte"))
        p.wait("vector", "nte", p.val("nte"))
        p.op("vector", lambda e, o=rnorm[:, :, :], i=rnorm[:, :, :]:
             e.reciprocal(o, i), "ntr")

    with contextlib.ExitStack() as sc4:
        woT = p.sb("woT", [128, DC, D], BF16, sc4)
        aoT = p.sb("aoT", [128, H0, 128], BF16, sc4)
        ao_s = p.sb("ao_s", [128, 128], BF16, sc4)
        sq_scr = p.sb("sq_scr", [128, D], F32, sc4)
        bo = p.sb("bo", [1, D], F32, sc4)
        onesf = p.sb("onesf", [1, 128], F32, sc4)
        p.wait("sync", "ntr", p.val("ntr"))
        p.dma("sync", "ld2", bo[:, :], bo_in[:, :])
        p.dma("sync", "ld2", onesf[:, :], ones_in[:, :])
        p.wait("gpsimd", "ntr", p.val("ntr"))
        p.dma("gpsimd", "wo", woT[:, :, :], wo_in.rearrange("(c p) o -> p c o", p=128))

        ps_ao = p.ps("ps_ao", [128, 128], sc4)
        ps_at = p.ps("ps_at", [128, 128], sc4, BF16)
        ps_o = p.ps("ps_o", [128, D], sc4)

        p.wait("tensor", "ntr", p.val("ntr"))
        p.wait("vector", "ntr", p.val("ntr"))
        for tt in range(TT):
            for h in range(H0):
                base = (h % 2) * 64
                pair = h // 2
                fsl = slice((h // 2) * 256, (h // 2) * 256 + 128)
                g = tt * H0 + h
                if g >= 1:
                    p.wait("tensor", "aoe", g)
                p.mm(ps_ao[:, :], qm[base:base + 64, pair, tt * 128:(tt + 1) * 128],
                     kvb[base:base + 64, fsl], True, True, sem="aom")
                p.wait("vector", "aom", p.val("aom"))
                p.op("vector", lambda e, o=ao_s[:, :], i=ps_ao[:, :],
                     r=rnorm[:, tt, h:h + 1]: e.tensor_scalar_mul(o, i, r), "aoe")
                p.wait("tensor", "aoe", p.val("aoe"))
                p.wait("tensor", "att", p.val("att"))
                p.trans(ps_at[:, :], ao_s[:, :], idb[:, :], aoT[:, h, :], "att")
            p.wait("tensor", "wo", 16)
            p.wait("tensor", "ld2", 32)
            p.wait("tensor", "att", p.val("att"))
            if tt >= 1:
                p.wait("tensor", "ore", tt)
            for oc in range(4):
                p.mm(ps_o[:, oc * 512:(oc + 1) * 512], onesf[:, :],
                     bo[:, oc * 512:(oc + 1) * 512], True, False)
            for dc in range(DC):
                for oc in range(4):
                    p.mm(ps_o[:, oc * 512:(oc + 1) * 512], aoT[:, dc, :],
                         woT[:, dc, oc * 512:(oc + 1) * 512], False, dc == DC - 1,
                         sem=None if not (dc == DC - 1 and oc == 3) else "opr")
            p.wait("vector", "opr", p.val("opr"))
            p.op("vector", lambda e, o=x_tm[:, tt, :], i=ps_o[:, :], x=x_tm[:, tt, :]:
                 e.tensor_tensor(o, i, x, ALU.add), "ore")

        for tt in range(TT):
            p.wait("vector", "ore", p.val("ore"))
            layer_norm(p, x_tm[:, tt, :], sq_scr[:, :], ssum[:, :], ssq[:, :],
                       mean[:, :], m2[:, :], var[:, :], rstd[:, :], "lne")

    with contextlib.ExitStack() as sc5:
        x1T = p.sb("x1T", [128, DC, T], BF16, sc5)
        rw = p.sb("rw", [128, DC, NE0], BF16, sc5)
        rb = p.sb("rb", [16, 1], F32, sc5)
        lgf = p.sb("lgf", [16, T], F32, sc5)
        lgt = p.sb("lgt", [128, TT, 16], F32, sc5)
        sumr = p.sb("sumr", [128, TT], F32, sc5)
        ps_t = p.ps("ps_t", [128, 128], sc5)
        ps_r = p.ps("ps_r", [16, T], sc5)
        ps_rt = p.ps("ps_rt", [128, 16], sc5)

        p.wait("gpsimd", "lne", p.val("lne"))
        p.wait("sync", "lne", p.val("lne"))
        p.dma("gpsimd", "rwl", rw[:, :, :], rw_in.rearrange("(c p) w -> p c w", p=128))
        p.dma("sync", "rbl", rb[:, :], rb_in[:, :])

        p.wait("tensor", "lne", p.val("lne"))
        for tt in range(TT):
            for dc in range(DC):
                p.wait("tensor", "x1e", p.val("x1e"))
                p.trans(ps_t[:, :], x_tm[:, tt, dc * 128:(dc + 1) * 128], idf[:, :],
                        x1T[:, dc, tt * 128:(tt + 1) * 128], "x1e")
        p.wait("tensor", "x1e", p.val("x1e"))
        p.wait("tensor", "rwl", p.val("rwl"))
        for dc in range(DC):
            for th in range(2):
                p.mm(ps_r[:, th * 512:(th + 1) * 512], rw[:, dc, :],
                     x1T[:, dc, th * 512:(th + 1) * 512], dc == 0, dc == DC - 1,
                     sem=None if not (dc == DC - 1 and th == 1) else "rmm")
        p.wait("scalar", "rmm", p.val("rmm"))
        p.wait("scalar", "rbl", 16)
        p.op("scalar", lambda e, o=lgf[:, :], i=ps_r[:, :], b=rb[:, :]:
             e.activation(o, i, ACT.Identity, bias=b), "rme")
        p.wait("tensor", "rme", p.val("rme"))
        for tt in range(TT):
            p.wait("tensor", "rte", p.val("rte"))
            p.trans(ps_rt[:, :], lgf[:, tt * 128:(tt + 1) * 128], idf[0:16, 0:16],
                    lgt[:, tt, :], "rte")
        p.wait("scalar", "rte", p.val("rte"))
        p.op("scalar", lambda e, o=lgt[:, :, :], i=lgt[:, :, :]:
             e.activation(o, i, ACT.Exp), "rex")
        p.wait("vector", "rex", p.val("rex"))
        p.op("vector", lambda e, o=sumr[:, :], i=lgt[:, :, :]:
             e.tensor_reduce(o, i, AX.X, ALU.add), "rsm")
        p.wait("vector", "rsm")
        p.op("vector", lambda e, o=sumr[:, :], i=sumr[:, :]: e.reciprocal(o, i), "rsm")
        p.wait("vector", "rsm")
        for tt in range(TT):
            p.op("vector", lambda e, o=s_tm[:, tt, :], i=lgt[:, tt, :],
                 r=sumr[:, tt:tt + 1]: e.tensor_scalar_mul(o, i, r), "rsc")

    p.wait("sync", "lne", p.val("lne"))
    p.dma("sync", "st", x1_out.rearrange("(tt p) d -> p tt d", p=128), x_tm[:, :, :])
    p.wait("sync", "rsc", p.val("rsc"))
    p.dma("sync", "st", s_out.rearrange("(tt p) w -> p tt w", p=128), s_tm[:, :, :])
    p.wait("sync", "st", p.val("st"))
    return p.finish()



def _prep_l1(inputs):
    x = np.ascontiguousarray(
        np.asarray(inputs["L0_attn_hidden_states"], np.float32).reshape(8 * T, D))
    shared = {
        "wq_in": np.ascontiguousarray(
            np.asarray(inputs["L0_attn_wq"], np.float32).T.reshape(DC, 128, H0, DH)
            .transpose(2, 1, 0, 3)),
        "wk_in": np.ascontiguousarray(
            np.asarray(inputs["L0_attn_wk"], np.float32).T.reshape(DC, 128, H0, DH)
            .transpose(2, 1, 0, 3)),
        "wv_in": np.ascontiguousarray(
            np.asarray(inputs["L0_attn_wv"], np.float32).T.reshape(DC, 128, H0, DH)
            .transpose(2, 1, 0, 3)),
        "wo_in": np.ascontiguousarray(np.asarray(inputs["L0_attn_wo"], np.float32).T),
        "bq_in": np.ascontiguousarray(np.asarray(inputs["L0_attn_bq"], np.float32).reshape(DC, DH).T),
        "bk_in": np.ascontiguousarray(np.asarray(inputs["L0_attn_bk"], np.float32).reshape(DC, DH).T),
        "bv_in": np.ascontiguousarray(np.asarray(inputs["L0_attn_bv"], np.float32).reshape(DC, DH).T),
        "bo_in": np.asarray(inputs["L0_attn_bo"], np.float32).reshape(1, D),
        "wmq_in": np.ascontiguousarray(np.asarray(inputs["L0_attn_wmap_q"], np.float32).T),
        "wmk_in": np.ascontiguousarray(np.asarray(inputs["L0_attn_wmap_k"], np.float32).T),
        "rw_in": np.asarray(inputs["L0_ffn_router_w"], np.float32),
        "rb_in": np.asarray(inputs["L0_ffn_router_b"], np.float32).reshape(NE0, 1),
        "ones_in": np.ones((1, 128), np.float32),
        "idf_in": np.eye(128, dtype=np.float32),
    }
    in_maps = []
    for c in range(NCORE):
        o = c ^ 1
        m = dict(shared)
        m["x_in"] = x[c * T:(c + 1) * T]
        m["xT_in"] = np.ascontiguousarray(x[c * T:(c + 1) * T].T)
        m["xoT_in"] = np.ascontiguousarray(x[o * T:(o + 1) * T].T)
        in_maps.append(m)
    return in_maps


def run_l1(inputs):
    nc = build_l1()
    in_maps = _prep_l1(inputs)
    res = run_bass_kernel_spmd(nc, in_maps, core_ids=list(range(NCORE)))
    x1 = np.concatenate([res.results[c]["x1_out"] for c in range(NCORE)], 0)
    s = np.concatenate([res.results[c]["s_out"] for c in range(NCORE)], 0)
    return x1, s





def build_l2():
    p = _new_prog("l2")
    nc = p.nc
    dt = lambda n, sh, ty=F32, kind="ExternalInput": nc.dram_tensor(
        n, list(sh), ty, kind=kind).ap()

    x_in = dt("x_in", [T, D])
    xT_in = dt("xT_in", [D, T])
    sall_in = dt("sall_in", [8 * T, NE0])
    sown_in = dt("sown_in", [T, NE0])
    ew_in = dt("ew_in", [D, D])
    eb_in = dt("eb_in", [1, D])
    ones_in = dt("ones_in", [1, 128])
    x2_out = dt("x2_out", [T, D], F32, "ExternalOutput")

    x_tm = p.sb("x_tm", [128, TT, D], F32)
    x1T = p.sb("x1T", [128, DC, T], BF16)
    ew = p.sb("ew", [128, DC, D], BF16)
    sthr = p.sb("sthr", [128, 64, NE0], F32)
    sown = p.sb("sown", [128, TT, NE0], F32)
    eb = p.sb("eb", [1, D], F32)
    onesf = p.sb("onesf", [1, 128], F32)
    onescol = p.sb("onescol", [128, 1], F32)
    lo = p.sb("lo", [1, NE0], F32)
    hi = p.sb("hi", [1, NE0], F32)
    mid = p.sb("mid", [1, NE0], F32)
    dd = p.sb("dd", [1, NE0], F32)
    sel = p.sb("sel", [1, NE0], F32)
    cntv = p.sb("cntv", [128, NE0], F32)
    wcol = p.sb("wcol", [128, TT], F32)
    scrA = p.sb("scrA", [128, NE0], F32)
    sq_scr = p.sb("sq_scr", [128, D], F32)
    ssum = p.sb("ssum", [128, 1], F32)
    ssq = p.sb("ssq", [128, 1], F32)
    mean = p.sb("mean", [128, 1], F32)
    m2 = p.sb("m2", [128, 1], F32)
    var = p.sb("var", [128, 1], F32)
    rstd = p.sb("rstd", [128, 1], F32)

    ps_b = p.ps("ps_b", [128, NE0])
    ps_c = p.ps("ps_c", [1, NE0])
    ps_tau = p.ps("ps_tau", [128, NE0])
    ps_z = p.ps("ps_z", [128, D])

    p.dma("sync", "ld", x_tm[:, :, :], x_in.rearrange("(tt p) d -> p tt d", p=128))
    p.dma("sync", "ld", onesf[:, :], ones_in[:, :])
    p.dma("gpsimd", "ldg", x1T[:, :, :], xT_in.rearrange("(c p) t -> p c t", p=128))
    p.dma("gpsimd", "ldg", ew[:, :, :], ew_in.rearrange("(c p) o -> p c o", p=128))

    p.wait("vector", "ld", p.val("ld"))
    p.op("vector", lambda e, o=lo[:, :]: e.memset(o, 0.0), "bi")
    p.op("vector", lambda e, o=hi[:, :]: e.memset(o, 1.0), "bi")
    p.op("vector", lambda e, o=onescol[:, :]: e.memset(o, 1.0), "bi")
    for it in range(42):
        p.wait("vector", "bi")
        p.op("vector", lambda e, o=mid[:, :], a=lo[:, :], b=hi[:, :]:
             e.tensor_tensor(o, a, b, ALU.add), "bi")
        p.wait("vector", "bi")
        p.op("vector", lambda e, o=mid[:, :]: e.tensor_scalar_mul(o, o, 0.5), "bi")
        p.wait("tensor", "bi", p.val("bi"))
        if it:
            p.wait("tensor", "gei", p.val("gei"))
        p.mm(ps_b[:, :], onesf[:, :], mid[:, :], True, True, sem="bmm")
        p.wait("vector", "bmm", p.val("bmm"))
        p.op("vector", lambda e, o=ge[:, :, :], a=sthr[:, :, :],
             b=ps_b[:, :].unsqueeze(1).broadcast_to([128, 64, NE0]):
             e.tensor_tensor(o, a, b, ALU.is_ge), "gei")
        p.wait("vector", "gei")
        p.op("vector", lambda e, o=cntv[:, :],
             i=ge[:, :, :].rearrange("p j e -> p e j"):
             e.tensor_reduce(o, i, AX.X, ALU.add), "gei")
        p.wait("tensor", "gei", p.val("gei"))
        if it:
            p.wait("tensor", "seli", p.val("seli"))
        p.mm(ps_c[:, :], onescol[:, :], cntv[:, :], True, True, sem="cmm")
        p.wait("vector", "cmm", p.val("cmm"))
        p.op("vector", lambda e, o=sel[:, :], i=ps_c[:, :]:
             e.tensor_single_scalar(o, i, float(K0_TOP), ALU.is_ge), "seli")
        p.wait("vector", "seli")
        p.op("vector", lambda e, o=dd[:, :], a=mid[:, :], b=lo[:, :]:
             e.tensor_tensor(o, a, b, ALU.subtract), "bi")
        p.wait("vector", "bi")
        p.op("vector", lambda e, o=dd[:, :], a=sel[:, :], b=dd[:, :]:
             e.tensor_tensor(o, a, b, ALU.mult), "bi")
        p.wait("vector", "bi")
        p.op("vector", lambda e, o=lo[:, :], a=lo[:, :], b=dd[:, :]:
             e.tensor_tensor(o, a, b, ALU.add), "bi")
        p.wait("vector", "bi")
        p.op("vector", lambda e, o=dd[:, :], a=hi[:, :], b=mid[:, :]:
             e.tensor_tensor(o, a, b, ALU.subtract), "bi")
        p.wait("vector", "bi")
        p.op("vector", lambda e, o=dd[:, :], a=sel[:, :], b=dd[:, :]:
             e.tensor_tensor(o, a, b, ALU.mult), "bi")
        p.wait("vector", "bi")
        p.op("vector", lambda e, o=hi[:, :], a=mid[:, :], b=dd[:, :]:
             e.tensor_tensor(o, a, b, ALU.add), "bi")
    p.wait("tensor", "bi", p.val("bi"))
    p.wait("tensor", "gei", p.val("gei"))
    p.mm(ps_tau[:, :], onesf[:, :], lo[:, :], True, True, sem="tmm")

    p.wait("vector", "tmm", p.val("tmm"))
    for tt in range(TT):
        p.op("vector", lambda e, o=scrA[:, :], a=sown[:, tt, :], b=ps_tau[:, :]:
             e.tensor_tensor(o, a, b, ALU.is_ge), "wt")
        p.wait("vector", "wt")
        p.op("vector", lambda e, o=scrA[:, :], a=sown[:, tt, :], b=scrA[:, :]:
             e.tensor_tensor(o, a, b, ALU.mult), "wt")
        p.wait("vector", "wt")
        p.op("vector", lambda e, o=wcol[:, tt:tt + 1], i=scrA[:, :]:
             e.reduce_sum(o, i, AX.X), "wt")
        p.wait("vector", "wt")

    for tt in range(TT):
        p.wait("tensor", "ld", p.val("ld"))
        p.wait("tensor", "ldg", p.val("ldg"))
        if tt:
            p.wait("tensor", "zev", p.val("zev"))
        for oc in range(4):
            p.mm(ps_z[:, oc * 512:(oc + 1) * 512], onesf[:, :],
                 eb[:, oc * 512:(oc + 1) * 512], True, False)
        for dc in range(DC):
            for oc in range(4):
                p.mm(ps_z[:, oc * 512:(oc + 1) * 512], x1T[:, dc, tt * 128:(tt + 1) * 128],
                     ew[:, dc, oc * 512:(oc + 1) * 512], False, dc == DC - 1,
                     sem=None if not (dc == DC - 1 and oc == 3) else "zmm")
        p.wait("vector", "zmm", p.val("zmm"))
        p.wait("vector", "wt", p.val("wt"))
        for oc in range(4):
            p.op("vector", lambda e, o=x_tm[:, tt, oc * 512:(oc + 1) * 512],
                 i=ps_z[:, oc * 512:(oc + 1) * 512], w=wcol[:, tt:tt + 1],
                 x=x_tm[:, tt, oc * 512:(oc + 1) * 512]:
                 e.scalar_tensor_tensor(o, i, w, x, ALU.mult, ALU.add), "zev")
        p.wait("vector", "zev")
        layer_norm(p, x_tm[:, tt, :], sq_scr[:, :], ssum[:, :], ssq[:, :],
                   mean[:, :], m2[:, :], var[:, :], rstd[:, :], "lne")

    p.wait("sync", "lne", p.val("lne"))
    p.dma("sync", "st", x2_out.rearrange("(tt p) d -> p tt d", p=128), x_tm[:, :, :])
    p.wait("sync", "st", p.val("st"))
    return p.finish()


def _prep_l2(inputs, x1, s_all):
    shared = {
        "sall_in": np.ascontiguousarray(s_all),
        "ew_in": np.asarray(inputs["L0_ffn_expert_w"], np.float32),
        "eb_in": np.asarray(inputs["L0_ffn_expert_b"], np.float32).reshape(1, D),
        "ones_in": np.ones((1, 128), np.float32),
    }
    in_maps = []
    for c in range(NCORE):
        m = dict(shared)
        m["x_in"] = x1[c * T:(c + 1) * T]
        m["xT_in"] = np.ascontiguousarray(x1[c * T:(c + 1) * T].T)
        m["sown_in"] = s_all[c * T:(c + 1) * T]
        in_maps.append(m)
    return in_maps


def run_l2(inputs, x1, s_all):
    nc = build_l2()
    in_maps = _prep_l2(inputs, x1, s_all)
    res = run_bass_kernel_spmd(nc, in_maps, core_ids=list(range(NCORE)))
    return np.concatenate([res.results[c]["x2_out"] for c in range(NCORE)], 0)



def build_l3():
    p = _new_prog("l3")
    nc = p.nc
    dt = lambda n, sh, ty=F32, kind="ExternalInput": nc.dram_tensor(
        n, list(sh), ty, kind=kind).ap()

    xT_in = dt("xT_in", [D, T])
    wq_in = dt("wq_in", [D, D])
    wk_in = dt("wk_in", [D, KVH * DH])
    wv_in = dt("wv_in", [D, KVH * DH])
    qnw_in = dt("qnw_in", [1, DH])
    knw_in = dt("knw_in", [1, DH])
    cos_in = dt("cos_in", [T, 64])
    sin_in = dt("sin_in", [T, 64])
    ones_in = dt("ones_in", [1, 128])
    q_out = dt("q_out", [T, D], BF16, "ExternalOutput")
    k_out = dt("k_out", [T, KVH * DH], BF16, "ExternalOutput")
    v_out = dt("v_out", [T, KVH * DH], BF16, "ExternalOutput")

    xT = p.sb("xT", [128, DC, T], BF16)
    wq = p.sb("wq", [128, DC, D], BF16)
    wk = p.sb("wk", [128, DC, KVH * DH], BF16)
    wv = p.sb("wv", [128, DC, KVH * DH], BF16)
    qnw = p.sb("qnw", [1, DH], F32)
    knw = p.sb("knw", [1, DH], F32)
    onesf = p.sb("onesf", [1, 128], F32)
    cosb = p.sb("cosb", [128, TT, 64], F32)
    sinb = p.sb("sinb", [128, TT, 64], F32)
    wbc = p.sb("wbc", [128, DH], F32)
    kwbc = p.sb("kwbc", [128, DH], F32)
    qtm = p.sb("qtm", [128, D], F32)
    ktm = p.sb("ktm", [128, KVH * DH], F32)
    sq_scr = p.sb("sq_scr", [128, D], F32)
    ssq = p.sb("ssq", [128, H1], F32)
    rr = p.sb("rr", [128, H1], F32)
    scrA = p.sb("scrA", [128, H1 * 64], F32)
    scrB = p.sb("scrB", [128, H1 * 64], F32)
    qbf = p.sb("qbf", [128, D], BF16)
    kbf = p.sb("kbf", [128, KVH * DH], BF16)
    vbf = p.sb("vbf", [128, KVH * DH], BF16)

    ps_q = p.ps("ps_q", [128, D])
    ps_k = p.ps("ps_k", [128, KVH * DH])
    ps_v = p.ps("ps_v", [128, KVH * DH])
    ps_w = p.ps("ps_w", [128, DH])

    p.dma("gpsimd", "ldg", xT[:, :, :], xT_in.rearrange("(c p) t -> p c t", p=128))
    p.dma("gpsimd", "ldg", wq[:, :, :], wq_in.rearrange("(c p) o -> p c o", p=128))
    p.dma("gpsimd", "ldg", wk[:, :, :], wk_in.rearrange("(c p) o -> p c o", p=128))
    p.dma("gpsimd", "ldg", wv[:, :, :], wv_in.rearrange("(c p) o -> p c o", p=128))
    p.dma("sync", "ld", qnw[:, :], qnw_in[:, :])
    p.dma("sync", "ld", knw[:, :], knw_in[:, :])
    p.dma("sync", "ld", onesf[:, :], ones_in[:, :])
    p.dma("sync", "ld", cosb[:, :, :], cos_in.rearrange("(tt p) f -> p tt f", p=128))
    p.dma("sync", "ld", sinb[:, :, :], sin_in.rearrange("(tt p) f -> p tt f", p=128))

    p.wait("tensor", "ld", p.val("ld"))
    p.mm(ps_w[:, :], onesf[:, :], qnw[:, :], True, True, sem="wb")
    p.wait("vector", "wb", p.val("wb"))
    p.op("vector", lambda e, o=wbc[:, :], i=ps_w[:, :]: e.tensor_copy(o, i), "wbe")
    p.wait("tensor", "wbe", p.val("wbe"))
    p.mm(ps_w[:, :], onesf[:, :], knw[:, :], True, True, sem="wb")
    p.wait("vector", "wb", p.val("wb"))
    p.op("vector", lambda e, o=kwbc[:, :], i=ps_w[:, :]: e.tensor_copy(o, i), "wbe")

    def rmsnorm_rope(tt, nh, xv, sqv, ssqv, rv, wv_bc, outbf, sem):
        p.op("scalar", lambda e, o=sqv, i=xv: e.activation(o, i, ACT.Square), sem)
        p.wait("vector", sem)
        p.op("vector", lambda e, o=ssqv, i=sqv.rearrange("p (h f) -> p h f", h=nh):
             e.tensor_reduce(o, i, AX.X, ALU.add), sem)
        p.wait("vector", sem)
        p.op("vector", lambda e, o=rv, i=ssqv:
             e.tensor_scalar(o, i, 1.0 / DH, 1e-6, ALU.mult, ALU.add), sem)
        p.wait("scalar", sem)
        p.op("scalar", lambda e, o=rv, i=rv: e.activation(o, i, ACT.Sqrt), sem)
        p.wait("vector", sem)
        p.op("vector", lambda e, o=rv, i=rv: e.reciprocal(o, i), sem)
        p.wait("vector", sem)
        for h in range(nh):
            p.op("vector", lambda e, o=xv[:, h * DH:(h + 1) * DH],
                 i=xv[:, h * DH:(h + 1) * DH], r=rv[:, h:h + 1], w=wv_bc:
                 e.scalar_tensor_tensor(o, i, r, w, ALU.mult, ALU.mult), sem)
        p.wait("vector", sem)
        x3 = xv.rearrange("p (h f) -> p h f", h=nh)
        o3 = outbf.rearrange("p (h f) -> p h f", h=nh)
        cb = cosb[:, tt, :].unsqueeze(1).broadcast_to([128, nh, 64])
        sb_ = sinb[:, tt, :].unsqueeze(1).broadcast_to([128, nh, 64])
        a3 = scrA.rearrange("p (h f) -> p h f", h=H1)[:, 0:nh, :]
        b3 = scrB.rearrange("p (h f) -> p h f", h=H1)[:, 0:nh, :]
        p.op("vector", lambda e, o=a3, i=x3[:, :, 0:64], c=cb:
             e.tensor_tensor(o, i, c, ALU.mult), sem)
        p.op("vector", lambda e, o=b3, i=x3[:, :, 64:128], c=sb_:
             e.tensor_tensor(o, i, c, ALU.mult), sem)
        p.wait("vector", sem)
        p.op("vector", lambda e, o=o3[:, :, 0:64], a=a3, b=b3:
             e.tensor_tensor(o, a, b, ALU.subtract), sem)
        p.wait("vector", sem)
        p.op("vector", lambda e, o=a3, i=x3[:, :, 0:64], c=sb_:
             e.tensor_tensor(o, i, c, ALU.mult), sem)
        p.op("vector", lambda e, o=b3, i=x3[:, :, 64:128], c=cb:
             e.tensor_tensor(o, i, c, ALU.mult), sem)
        p.wait("vector", sem)
        p.op("vector", lambda e, o=o3[:, :, 64:128], a=a3, b=b3:
             e.tensor_tensor(o, a, b, ALU.add), sem)

    for tt in range(TT):
        p.wait("tensor", "ldg", p.val("ldg"))
        p.wait("tensor", "wbe", p.val("wbe"))
        if tt:
            p.wait("tensor", "qe", p.val("qe"))
        for dc in range(DC):
            for oc in range(4):
                p.mm(ps_q[:, oc * 512:(oc + 1) * 512], xT[:, dc, tt * 128:(tt + 1) * 128],
                     wq[:, dc, oc * 512:(oc + 1) * 512], dc == 0, dc == DC - 1,
                     sem=None if not (dc == DC - 1 and oc == 3) else "qmm")
        p.wait("vector", "qmm", p.val("qmm"))
        if tt:
            p.wait("vector", "qst", p.val("qst"))
        p.op("vector", lambda e, o=qtm[:, :], i=ps_q[:, :]: e.tensor_copy(o, i), "qe")
        p.wait("vector", "qe")
        p.wait("scalar", "qe", p.val("qe"))
        rmsnorm_rope(tt, H1, qtm[:, :], sq_scr[:, :], ssq[:, :], rr[:, :],
                     wbc[:, :], qbf[:, :], "qe")
        p.wait("scalar", "qe", p.val("qe"))
        p.dma("scalar", "qst", q_out[tt * 128:(tt + 1) * 128, :], qbf[:, :])
        if tt:
            p.wait("tensor", "ke", p.val("ke"))
        for dc in range(DC):
            p.mm(ps_k[:, :], xT[:, dc, tt * 128:(tt + 1) * 128],
                 wk[:, dc, :], dc == 0, dc == DC - 1,
                 sem=None if dc < DC - 1 else "kmm")
        p.wait("vector", "kmm", p.val("kmm"))
        if tt:
            p.wait("vector", "kst", p.val("kst"))
        p.op("vector", lambda e, o=ktm[:, :], i=ps_k[:, :]: e.tensor_copy(o, i), "ke")
        p.wait("vector", "ke")
        p.wait("scalar", "ke", p.val("ke"))
        rmsnorm_rope(tt, KVH, ktm[:, :], sq_scr[:, 0:KVH * DH], ssq[:, 0:KVH],
                     rr[:, 0:KVH], kwbc[:, :], kbf[:, :], "ke")
        p.wait("scalar", "ke", p.val("ke"))
        p.dma("scalar", "kst", k_out[tt * 128:(tt + 1) * 128, :], kbf[:, :])
        if tt:
            p.wait("tensor", "ve", p.val("ve"))
        for dc in range(DC):
            p.mm(ps_v[:, :], xT[:, dc, tt * 128:(tt + 1) * 128],
                 wv[:, dc, :], dc == 0, dc == DC - 1,
                 sem=None if dc < DC - 1 else "vmm")
        p.wait("vector", "vmm", p.val("vmm"))
        if tt:
            p.wait("vector", "vst", p.val("vst"))
        p.op("vector", lambda e, o=vbf[:, :], i=ps_v[:, :]: e.tensor_copy(o, i), "ve")
        p.wait("scalar", "ve", p.val("ve"))
        p.dma("scalar", "vst", v_out[tt * 128:(tt + 1) * 128, :], vbf[:, :])
    p.wait("sync", "qst", p.val("qst"))
    p.wait("sync", "kst", p.val("kst"))
    p.wait("sync", "vst", p.val("vst"))
    return p.finish()


def _rope_tables(core):
    half = core % 2
    pos = np.arange(half * T, half * T + T, dtype=np.float64)
    inv = 1.0 / (10000.0 ** (np.arange(0, DH, 2, dtype=np.float64) / DH))
    ang = pos[:, None] * inv[None, :]
    return (np.cos(ang).astype(np.float32), np.sin(ang).astype(np.float32))


def _prep_l3(inputs, x2):
    shared = {
        "wq_in": np.ascontiguousarray(np.asarray(inputs["L1_attn_wq"], np.float32).T),
        "wk_in": np.ascontiguousarray(np.asarray(inputs["L1_attn_wk"], np.float32).T),
        "wv_in": np.ascontiguousarray(np.asarray(inputs["L1_attn_wv"], np.float32).T),
        "qnw_in": np.asarray(inputs["L1_attn_q_norm_w"], np.float32).reshape(1, DH),
        "knw_in": np.asarray(inputs["L1_attn_k_norm_w"], np.float32).reshape(1, DH),
        "ones_in": np.ones((1, 128), np.float32),
    }
    in_maps = []
    for c in range(NCORE):
        m = dict(shared)
        m["xT_in"] = np.ascontiguousarray(x2[c * T:(c + 1) * T].T)
        cos, sin = _rope_tables(c)
        m["cos_in"] = cos
        m["sin_in"] = sin
        in_maps.append(m)
    return in_maps


def run_l3(inputs, x2):
    nc = build_l3()
    in_maps = _prep_l3(inputs, x2)
    res = run_bass_kernel_spmd(nc, in_maps, core_ids=list(range(NCORE)))
    q = np.concatenate([res.results[c]["q_out"] for c in range(NCORE)], 0)
    k = np.concatenate([res.results[c]["k_out"] for c in range(NCORE)], 0)
    v = np.concatenate([res.results[c]["v_out"] for c in range(NCORE)], 0)
    return q, k, v



def build_l4():
    p = _new_prog("l4")
    nc = p.nc
    dt = lambda n, sh, ty=F32, kind="ExternalInput": nc.dram_tensor(
        n, list(sh), ty, kind=kind).ap()

    x_in = dt("x_in", [T, D])
    qT_in = dt("qT_in", [D, T], BF16)
    kT_in = dt("kT_in", [KVH * DH, 2 * T], BF16)
    v_in = dt("v_in", [2 * T, KVH * DH], BF16)
    vones_in = dt("vones_in", [128, 16], BF16)
    wo_in = dt("wo_in", [D, D])
    dmask_in = dt("dmask_in", [128, 512])
    bmask_in = dt("bmask_in", [128, 512])
    idb_in = dt("idb_in", [128, 128], BF16)
    rtr_in = dt("rtr_in", [D, E1])
    idf_in = dt("idf_in", [128, 128])
    ones_in = dt("ones_in", [1, 128])
    x3_out = dt("x3_out", [T, D], F32, "ExternalOutput")
    cnt_out = dt("cnt_out", [1, E1], F32, "ExternalOutput")

    x_tm = p.sb("x_tm", [128, TT, D], F32)
    aoT = p.sb("aoT", [128, H1, T], BF16)
    idb = p.sb("idb", [128, 128], BF16)
    onesf = p.sb("onesf", [1, 128], F32)
    onescol = p.sb("onescol", [128, 1], F32)
    mx = p.sb("mx", [128, 1], F32)
    mb = p.sb("mb", [128, 1], F32)
    rcol = p.sb("rcol", [128, 1], F32)
    sq_scr = p.sb("sq_scr", [128, D], F32)
    ssum = p.sb("ssum", [128, 1], F32)
    ssq = p.sb("ssq", [128, 1], F32)
    mean = p.sb("mean", [128, 1], F32)
    m2 = p.sb("m2", [128, 1], F32)
    var = p.sb("var", [128, 1], F32)
    rstd = p.sb("rstd", [128, 1], F32)

    p.dma("sync", "ld", x_tm[:, :, :], x_in.rearrange("(tt p) d -> p tt d", p=128))
    p.dma("sync", "ld", idb[:, :], idb_in[:, :])
    p.dma("sync", "ld", onesf[:, :], ones_in[:, :])
    p.wait("vector", "ld", p.val("ld"))
    p.op("vector", lambda e, o=onescol[:, :]: e.memset(o, 1.0), "oc")

    with contextlib.ExitStack() as scB:
        qT = p.sb("qT", [128, H1, T], BF16, scB)
        kT = p.sb("kT", [128, KVH, 2 * T], BF16, scB)
        vaug = p.sb("vaug", [128, 16, KVH, 132], BF16, scB)
        vo = p.sb("vo", [128, 16], BF16, scB)
        p_sb = p.sb("p_sb", [128, 1152], BF16, scB)
        pt_sb = p.sb("pt_sb", [128, 1152], BF16, scB)
        ao_s = p.sb("ao_s", [128, 128], BF16, scB)
        dmask = p.sb("dmask", [128, 128], F32, scB)
        bmask = p.sb("bmask", [128, 128], F32, scB)

        p.dma("sync", "ld2", qT[:, :, :], qT_in.rearrange("(h p) t -> p h t", p=128))
        p.dma("sync", "ld2", kT[:, :, :], kT_in.rearrange("(h p) t -> p h t", p=128))
        for h4 in range(KVH):
            p.dma("sync", "ld2", vaug[:, :, h4, 0:128],
                  v_in[:, h4 * DH:(h4 + 1) * DH].rearrange("(kbt p) f -> p kbt f", p=128))
        p.dma("sync", "ld2", vo[:, :], vones_in[:, :])
        p.dma("sync", "ld2", dmask[:, :], dmask_in[:, :])
        p.dma("sync", "ld2", bmask[:, :], bmask_in[:, :])
        p.wait("vector", "ld2", p.val("ld2"))
        for h4 in range(KVH):
            p.op("vector", lambda e, o=vaug[:, :, h4, 128:129], i=vo[:, :].unsqueeze(2):
                 e.tensor_copy(o, i), "vob")

        ps_s = p.ps("ps_s", [128, 1152], scB)
        ps_pt = p.ps("ps_pt", [128, 1152], scB, BF16)
        ps_av = p.ps("ps_av", [128, 129], scB)
        ps_at = p.ps("ps_at", [128, 128], scB, BF16)

        for qt in range(TT):
            for h in range(H1):
                kvh = h // KVH
                base_kb = qt * 128
                p.wait("tensor", "ld2", p.val("ld2"))
                p.wait("tensor", "pexp", p.val("pexp"))
                for ci, (off, n) in enumerate(((0, 512), (512, 512), (1024, 128))):
                    p.mm(ps_s[:, off:off + n], qT[:, h, qt * 128:(qt + 1) * 128],
                         kT[:, kvh, base_kb + off:base_kb + off + n], True, True,
                         sem=None if ci < 2 else "smm")
                p.wait("vector", "smm", p.val("smm"))
                p.op("vector", lambda e, o=ps_s[:, 0:128], i=ps_s[:, 0:128], m=bmask[:, :]:
                     e.tensor_tensor(o, i, m, ALU.add), "msk")
                p.op("vector", lambda e, o=ps_s[:, 1024:1152], i=ps_s[:, 1024:1152],
                     m=dmask[:, :]: e.tensor_tensor(o, i, m, ALU.add), "msk")
                p.wait("vector", "msk")
                p.op("vector", lambda e, o=mx[:, :], i=ps_s[:, :]:
                     e.reduce_max(o, i, AX.X), "msk")
                p.wait("vector", "msk")
                p.op("vector", lambda e, o=mb[:, :], i=mx[:, :]:
                     e.tensor_scalar_mul(o, i, -SM_SCALE), "msk")
                p.wait("scalar", "msk", p.val("msk"))
                p.op("scalar", lambda e, o=p_sb[:, :], i=ps_s[:, :], b=mb[:, :]:
                     e.activation(o, i, ACT.Exp, bias=b, scale=SM_SCALE), "pexp")
                p.wait("tensor", "pexp", p.val("pexp"))
                p.wait("tensor", "ptv", p.val("ptv"))
                for b9 in range(9):
                    p.op("tensor", lambda e, o=ps_pt[:, b9 * 128:(b9 + 1) * 128],
                         i=p_sb[:, b9 * 128:(b9 + 1) * 128], idn=idb[:, :]:
                         e.transpose(o, i, idn), "ptt")
                p.wait("vector", "ptt", p.val("ptt"))
                p.op("vector", lambda e, o=pt_sb[:, :], i=ps_pt[:, :]:
                     e.tensor_copy(o, i), "ptv")
                p.wait("tensor", "ptv", p.val("ptv"))
                p.wait("tensor", "ave", p.val("ave"))
                for b9 in range(9):
                    p.mm(ps_av[:, :], pt_sb[:, b9 * 128:(b9 + 1) * 128],
                         vaug[:, qt + b9, kvh, 0:129], b9 == 0, b9 == 8,
                         sem=None if b9 < 8 else "avm")
                p.wait("vector", "avm", p.val("avm"))
                p.op("vector", lambda e, o=rcol[:, :], i=ps_av[:, 128:129]:
                     e.reciprocal(o, i), "ave")
                p.wait("vector", "ave")
                p.op("vector", lambda e, o=ao_s[:, :], i=ps_av[:, 0:128], r=rcol[:, :]:
                     e.tensor_scalar_mul(o, i, r), "ave")
                p.wait("tensor", "ave", p.val("ave"))
                p.wait("tensor", "aoe", p.val("aoe"))
                p.op("tensor", lambda e, o=ps_at[:, :], i=ao_s[:, :], idn=idb[:, :]:
                     e.transpose(o, i, idn), "att")
                p.wait("vector", "att", p.val("att"))
                p.op("vector", lambda e, o=aoT[:, h, qt * 128:(qt + 1) * 128],
                     i=ps_at[:, :]: e.tensor_copy(o, i), "aoe")

    with contextlib.ExitStack() as scC:
        woT = p.sb("woT", [128, DC, D], BF16, scC)
        p.wait("gpsimd", "aoe", p.val("aoe"))
        p.dma("gpsimd", "wo", woT[:, :, :], wo_in.rearrange("(c p) o -> p c o", p=128))
        ps_o = p.ps("ps_o", [128, D], scC)
        p.wait("tensor", "wo", p.val("wo"))
        p.wait("tensor", "aoe", p.val("aoe"))
        for tt in range(TT):
            if tt:
                p.wait("tensor", "ore", p.val("ore"))
            for dc in range(DC):
                for oc in range(4):
                    p.mm(ps_o[:, oc * 512:(oc + 1) * 512], aoT[:, dc, tt * 128:(tt + 1) * 128],
                         woT[:, dc, oc * 512:(oc + 1) * 512], dc == 0, dc == DC - 1,
                         sem=None if not (dc == DC - 1 and oc == 3) else "opr")
            p.wait("vector", "opr", p.val("opr"))
            p.op("vector", lambda e, o=x_tm[:, tt, :], i=ps_o[:, :], x=x_tm[:, tt, :]:
                 e.tensor_tensor(o, i, x, ALU.add), "ore")
        for tt in range(TT):
            p.wait("vector", "ore", p.val("ore"))
            layer_norm(p, x_tm[:, tt, :], sq_scr[:, :], ssum[:, :], ssq[:, :],
                       mean[:, :], m2[:, :], var[:, :], rstd[:, :], "lne")

    with contextlib.ExitStack() as scD:
        x3T = p.sb("x3T", [128, DC, T], BF16, scD)
        rtr = p.sb("rtr", [128, DC, E1], BF16, scD)
        lgt = p.sb("lgt", [128, TT, E1], F32, scD)
        m1 = p.sb("m1", [128, 1], F32, scD)
        m2c = p.sb("m2c", [128, 1], F32, scD)
        oh1 = p.sb("oh1", [128, E1], F32, scD)
        oh2 = p.sb("oh2", [128, E1], F32, scD)
        scrM = p.sb("scrM", [128, E1], F32, scD)
        cnt_s = p.sb("cnt_s", [1, E1], F32, scD)
        idf = p.sb("idf", [128, 128], F32, scD)
        ps_t = p.ps("ps_t", [128, 128], scD)
        ps_lg = p.ps("ps_lg", [128, E1], scD)
        ps_cnt = p.ps("ps_cnt", [1, E1], scD)

        p.wait("gpsimd", "lne", p.val("lne"))
        p.dma("gpsimd", "rtl", rtr[:, :, :], rtr_in.rearrange("(c p) w -> p c w", p=128))
        p.wait("sync", "lne", p.val("lne"))
        p.dma("sync", "idl", idf[:, :], idf_in[:, :])
        p.wait("tensor", "lne", p.val("lne"))
        p.wait("tensor", "idl", p.val("idl"))
        for tt in range(TT):
            for dc in range(DC):
                p.wait("tensor", "x3e", p.val("x3e"))
                p.trans(ps_t[:, :], x_tm[:, tt, dc * 128:(dc + 1) * 128], idf[:, :],
                        x3T[:, dc, tt * 128:(tt + 1) * 128], "x3e")
        p.wait("tensor", "x3e", p.val("x3e"))
        p.wait("tensor", "rtl", p.val("rtl"))
        for tt in range(TT):
            if tt:
                p.wait("tensor", "lge", p.val("lge"))
            for dc in range(DC):
                p.mm(ps_lg[:, :], x3T[:, dc, tt * 128:(tt + 1) * 128], rtr[:, dc, :],
                     dc == 0, dc == DC - 1, sem=None if dc < DC - 1 else "lgm")
            p.wait("vector", "lgm", p.val("lgm"))
            p.op("vector", lambda e, o=lgt[:, tt, :], i=ps_lg[:, :]:
                 e.tensor_copy(o, i), "lge")
        for tt in range(TT):
            p.wait("vector", "lge", p.val("lge"))
            p.op("vector", lambda e, o=m1[:, :], i=lgt[:, tt, :]:
                 e.reduce_max(o, i, AX.X), "tp")
            p.wait("vector", "tp")
            p.op("vector", lambda e, o=oh1[:, :], i=lgt[:, tt, :], m=m1[:, :]:
                 e.tensor_scalar(o, i, m, None, ALU.is_equal), "tp")
            p.wait("vector", "tp")
            p.op("vector", lambda e, o=scrM[:, :], i=oh1[:, :], x=lgt[:, tt, :]:
                 e.scalar_tensor_tensor(o, i, NEG, x, ALU.mult, ALU.add), "tp")
            p.wait("vector", "tp")
            p.op("vector", lambda e, o=m2c[:, :], i=scrM[:, :]:
                 e.reduce_max(o, i, AX.X), "tp")
            p.wait("vector", "tp")
            p.op("vector", lambda e, o=oh2[:, :], i=scrM[:, :], m=m2c[:, :]:
                 e.tensor_scalar(o, i, m, None, ALU.is_equal), "tp")
            p.wait("vector", "tp")
            p.op("vector", lambda e, o=oh1[:, :], a=oh1[:, :], b=oh2[:, :]:
                 e.tensor_tensor(o, a, b, ALU.add), "tp")
            p.wait("tensor", "tp", p.val("tp"))
            p.mm(ps_cnt[:, :], onescol[:, :], oh1[:, :], tt == 0, tt == TT - 1,
                 sem="cm")
            p.wait("vector", "cm", p.val("cm"))
        p.wait("vector", "cm", p.val("cm"))
        p.op("vector", lambda e, o=cnt_s[:, :], i=ps_cnt[:, :]:
             e.tensor_copy(o, i), "cme")

    p.wait("sync", "lne", p.val("lne"))
    p.dma("sync", "st", x3_out.rearrange("(tt p) d -> p tt d", p=128), x_tm[:, :, :])
    p.wait("sync", "cme", p.val("cme"))
    p.dma("sync", "st", cnt_out[:, :], cnt_s[:, :])
    p.wait("sync", "st", p.val("st"))
    return p.finish()


def _prep_l4(inputs, x2, q, k, v):
    import ml_dtypes
    bfd = ml_dtypes.bfloat16
    neg = np.float32(NEG)
    ii, jj = np.meshgrid(np.arange(128), np.arange(128), indexing="ij")
    dmask = np.where(ii >= jj, 0.0, neg).astype(np.float32)
    bmask = np.where(jj > ii, 0.0, neg).astype(np.float32)
    shared = {
        "wo_in": np.ascontiguousarray(np.asarray(inputs["L1_attn_wo"], np.float32).T),
        "dmask_in": dmask,
        "bmask_in": bmask,
        "idb_in": np.eye(128, dtype=np.float32).astype(bfd),
        "rtr_in": np.asarray(inputs["L1_ffn_router"], np.float32),
        "idf_in": np.eye(128, dtype=np.float32),
        "ones_in": np.ones((1, 128), np.float32),
    }
    in_maps = []
    for c in range(NCORE):
        half = c % 2
        b0 = (c // 2) * 2 * T
        m = dict(shared)
        m["x_in"] = x2[c * T:(c + 1) * T]
        m["qT_in"] = np.ascontiguousarray(q[c * T:(c + 1) * T].T)
        kk = np.zeros((2 * T, KVH * DH), bfd)
        vv = np.zeros((2 * T, KVH * DH), bfd)
        von = np.zeros((2 * T,), np.float32)
        if half == 0:
            kk[T:] = k[b0:b0 + T]
            vv[T:] = v[b0:b0 + T]
            von[T:] = 1.0
        else:
            kk[:] = k[b0:b0 + 2 * T]
            vv[:] = v[b0:b0 + 2 * T]
            von[:] = 1.0
        m["kT_in"] = np.ascontiguousarray(kk.T)
        m["v_in"] = vv
        m["vones_in"] = np.ascontiguousarray(von.reshape(16, 128).T).astype(bfd)
        in_maps.append(m)
    return in_maps


def run_l4(inputs, x2, q, k, v):
    nc = build_l4()
    in_maps = _prep_l4(inputs, x2, q, k, v)
    res = run_bass_kernel_spmd(nc, in_maps, core_ids=list(range(NCORE)))
    x3 = np.concatenate([res.results[c]["x3_out"] for c in range(NCORE)], 0)
    cnt = np.stack([res.results[c]["cnt_out"][0] for c in range(NCORE)], 0)
    return x3, cnt



CAPL1 = 1024
IC1 = I1 // 128
ICS = J1 // 128


def build_l5():
    p = _new_prog("l5")
    nc = p.nc
    dt = lambda n, sh, ty=F32, kind="ExternalInput": nc.dram_tensor(
        n, list(sh), ty, kind=kind).ap()

    x_in = dt("x_in", [T, D])
    xT_in = dt("xT_in", [D, T])
    cnts_in = dt("cnts_in", [NCORE, E1])
    rowsel_in = dt("rowsel_in", [NCORE, 1])
    rtr_in = dt("rtr_in", [D, E1])
    u128_in = dt("u128_in", [128, 128])
    u8_in = dt("u8_in", [NCORE, NCORE])
    iota_in = dt("iota_in", [128, CAPL1])
    ones_in = dt("ones_in", [1, 128])
    idb_in = dt("idb_in", [128, 128], BF16)
    shup_in = dt("shup_in", [ICS, 2, 128, DC, 128])
    shdn_in = dt("shdn_in", [J1, D])
    wug_in = dt("wug_in", [E1, IC1, 2, 128, DC, 128])
    wd_in = dt("wd_in", [E1, I1, D])
    x4_out = dt("x4_out", [T, D], F32, "ExternalOutput")

    CL = 512
    JC = CL // 128

    x_tm = p.sb("x_tm", [128, TT, D], F32)
    x3tm = p.sb("x3tm", [128, TT, D], BF16)
    rtr = p.sb("rtr", [128, DC, E1], BF16)
    u128 = p.sb("u128", [128, 128], F32)
    u8 = p.sb("u8", [NCORE, NCORE], F32)
    cnts = p.sb("cnts", [NCORE, E1], F32)
    rowsel = p.sb("rowsel", [NCORE, 1], F32)
    iota = p.sb("iota", [128, CAPL1], F32)
    onesf = p.sb("onesf", [1, 128], F32)
    onescol = p.sb("onescol", [128, 1], F32)
    idb = p.sb("idb", [128, 128], BF16)
    bc_sb = p.sb("bc_sb", [NCORE, E1], F32)
    cm_sb = p.sb("cm_sb", [1, E1], F32)
    bvld = p.sb("bvld", [1, E1], F32)
    skipf = p.sb("skipf", [1, 2 * E1], F32)
    skipi = p.sb("skipi", [1, 2 * E1], I32)
    skipf2 = p.sb("skipf2", [1, 2 * E1], F32)
    skipi2 = p.sb("skipi2", [1, 2 * E1], I32)
    bme = p.sb("bme", [1, E1], F32)
    runb = p.sb("runb", [1, E1], F32)
    totb = p.sb("totb", [1, E1], F32)
    lgt = p.sb("lgt", [128, E1], F32)
    m1 = p.sb("m1", [128, 1], F32)
    m2c = p.sb("m2c", [128, 1], F32)
    cw1 = p.sb("cw1", [128, 1], F32)
    cw2 = p.sb("cw2", [128, 1], F32)
    oh1 = p.sb("oh1", [128, E1], F32)
    oh2 = p.sb("oh2", [128, E1], F32)
    ohsum = p.sb("ohsum", [128, E1], F32)
    scrM = p.sb("scrM", [128, E1], F32)
    pfl = p.sb("pfl", [128, E1], F32)
    pf_sb = p.sb("pf_sb", [128, E1], F32)
    validm = p.sb("validm", [128, E1], F32)
    posm = p.sb("posm", [128, TT, E1], F32)
    gatem = p.sb("gatem", [128, TT, E1], F32)
    p.dma("sync", "ld", x_tm[:, :, :], x_in.rearrange("(tt p) d -> p tt d", p=128))
    p.dma("gpsimd", "ldg", x3tm[:, :, :], x_in.rearrange("(tt p) d -> p tt d", p=128))
    p.dma("gpsimd", "ldg", rtr[:, :, :], rtr_in.rearrange("(c p) w -> p c w", p=128))
    p.dma("sync", "ld", u128[:, :], u128_in[:, :])
    p.dma("sync", "ld", u8[:, :], u8_in[:, :])
    p.dma("sync", "ld", cnts[:, :], cnts_in[:, :])
    p.dma("sync", "ld", rowsel[:, :], rowsel_in[:, :])
    p.dma("sync", "ld", iota[:, :], iota_in[:, :])
    p.dma("sync", "ld", onesf[:, :], ones_in[:, :])
    p.dma("sync", "ld", idb[:, :], idb_in[:, :])
    p.wait("vector", "ld", p.val("ld"))
    p.op("vector", lambda e, o=onescol[:, :]: e.memset(o, 1.0), "oc")

    with contextlib.ExitStack() as s0:
        ps_b = p.ps("ps_b", [NCORE, E1], s0)
        ps_m = p.ps("ps_m", [1, E1], s0)
        p.wait("tensor", "ld", p.val("ld"))
        p.mm(ps_b[:, :], u8[:, :], cnts[:, :], True, True, sem="b0")
        p.wait("vector", "b0", p.val("b0"))
        p.op("vector", lambda e, o=bc_sb[:, :], i=ps_b[:, :]: e.tensor_copy(o, i), "b1")
        p.wait("tensor", "b1", p.val("b1"))
        p.mm(ps_m[:, :], rowsel[:, :], bc_sb[:, :], True, True, sem="b2")
        p.wait("vector", "b2", p.val("b2"))
        p.op("vector", lambda e, o=bme[:, :], i=ps_m[:, :]: e.tensor_copy(o, i), "b3")
        p.op("vector", lambda e, o=runb[:, :]: e.memset(o, 0.0), "b3")
        p.wait("tensor", "b3", p.val("b3"))
        p.mm(ps_m[:, :], rowsel[:, :], cnts[:, :], True, True, sem="b4")
        p.wait("vector", "b4", p.val("b4"))
        p.op("vector", lambda e, o=cm_sb[:, :], i=ps_m[:, :]: e.tensor_copy(o, i), "skf")
        p.wait("vector", "skf")
        p.op("vector", lambda e, o=bvld[:, :], i=bme[:, :]:
             e.tensor_single_scalar(o, i, float(CAP1), ALU.is_lt), "skf")
        p.op("vector", lambda e, o=skipf[:, 0:E1], i=cm_sb[:, :]:
             e.tensor_single_scalar(o, i, 0.0, ALU.is_gt), "skf")
        p.op("vector", lambda e, o=skipf[:, E1:2 * E1], i=cm_sb[:, :]:
             e.tensor_single_scalar(o, i, 512.0, ALU.is_gt), "skf")
        p.op("vector", lambda e, o=skipf2[:, 0:E1], i=cm_sb[:, :]:
             e.tensor_single_scalar(o, i, 256.0, ALU.is_gt), "skf")
        p.op("vector", lambda e, o=skipf2[:, E1:2 * E1], i=cm_sb[:, :]:
             e.tensor_single_scalar(o, i, 768.0, ALU.is_gt), "skf")
        p.wait("vector", "skf")
        for shh in range(2):
            p.op("vector", lambda e, o=skipf[:, shh * E1:(shh + 1) * E1],
                 a=skipf[:, shh * E1:(shh + 1) * E1], b=bvld[:, :]:
                 e.tensor_tensor(o, a, b, ALU.mult), "skf")
            p.op("vector", lambda e, o=skipf2[:, shh * E1:(shh + 1) * E1],
                 a=skipf2[:, shh * E1:(shh + 1) * E1], b=bvld[:, :]:
                 e.tensor_tensor(o, a, b, ALU.mult), "skf")
        p.wait("vector", "skf")
        p.op("vector", lambda e, o=skipi[:, :], i=skipf[:, :]:
             e.tensor_copy(o, i), "skf")
        p.op("vector", lambda e, o=skipi2[:, :], i=skipf2[:, :]:
             e.tensor_copy(o, i), "skf")

    with contextlib.ExitStack() as s12:
        x3T = p.sb("x3T", [128, DC, T], BF16, s12)
        p.dma("gpsimd", "ldg", x3T[:, :, :], xT_in.rearrange("(c p) t -> p c t", p=128))

        with contextlib.ExitStack() as s1:
            ps_lg = p.ps("ps_lg", [128, E1], s1)
            ps_pf = p.ps("ps_pf", [128, E1], s1)
            ps_tc = p.ps("ps_tc", [1, E1], s1)
            ps_tb = p.ps("ps_tb", [128, E1], s1)
            p.wait("tensor", "ldg", p.val("ldg"))
            p.wait("tensor", "b3", p.val("b3"))
            for tt in range(TT):
                if tt:
                    p.wait("tensor", "lge", p.val("lge"))
                for dc in range(DC):
                    p.mm(ps_lg[:, :], x3T[:, dc, tt * 128:(tt + 1) * 128], rtr[:, dc, :],
                         dc == 0, dc == DC - 1, sem=None if dc < DC - 1 else "lgm")
                p.wait("vector", "lgm", p.val("lgm"))
                p.op("vector", lambda e, o=lgt[:, :], i=ps_lg[:, :]:
                     e.tensor_copy(o, i), "lge")
                p.wait("vector", "lge")
                p.op("vector", lambda e, o=m1[:, :], i=lgt[:, :]:
                     e.reduce_max(o, i, AX.X), "tp")
                p.wait("vector", "tp")
                p.op("vector", lambda e, o=oh1[:, :], i=lgt[:, :], m=m1[:, :]:
                     e.tensor_scalar(o, i, m, None, ALU.is_equal), "tp")
                p.wait("vector", "tp")
                p.op("vector", lambda e, o=scrM[:, :], i=oh1[:, :], x=lgt[:, :]:
                     e.scalar_tensor_tensor(o, i, NEG, x, ALU.mult, ALU.add), "tp")
                p.wait("vector", "tp")
                p.op("vector", lambda e, o=m2c[:, :], i=scrM[:, :]:
                     e.reduce_max(o, i, AX.X), "tp")
                p.wait("vector", "tp")
                p.op("vector", lambda e, o=oh2[:, :], i=scrM[:, :], m=m2c[:, :]:
                     e.tensor_scalar(o, i, m, None, ALU.is_equal), "tp")
                p.op("vector", lambda e, o=cw1[:, :], a=m1[:, :], b=m2c[:, :]:
                     e.tensor_tensor(o, a, b, ALU.subtract), "tp")
                p.wait("scalar", "tp", p.val("tp"))
                p.op("scalar", lambda e, o=cw1[:, :], i=cw1[:, :]:
                     e.activation(o, i, ACT.Sigmoid), "tps")
                p.wait("vector", "tps", p.val("tps"))
                p.op("vector", lambda e, o=cw2[:, :], i=cw1[:, :]:
                     e.tensor_scalar(o, i, -1.0, 1.0, ALU.mult, ALU.add), "tp")
                p.op("vector", lambda e, o=ohsum[:, :], a=oh1[:, :], b=oh2[:, :]:
                     e.tensor_tensor(o, a, b, ALU.add), "tp")
                p.wait("tensor", "tp", p.val("tp"))
                if tt:
                    p.wait("tensor", "bcw", p.val("bcw"))
                p.mm(ps_pf[:, :], u128[:, :], ohsum[:, :], True, True, sem="pfm")
                p.mm(ps_tc[:, :], onescol[:, :], ohsum[:, :], True, True, sem="pfm")
                p.op("vector", lambda e, o=totb[:, :], a=bme[:, :], b=runb[:, :]:
                     e.tensor_tensor(o, a, b, ALU.add), "tb")
                p.wait("tensor", "tb", p.val("tb"))
                p.mm(ps_tb[:, :], onesf[:, :], runb[:, :], True, True, sem="tbl")
                p.wait("vector", "pfm", p.val("pfm"))
                p.wait("vector", "tbl", p.val("tbl"))
                p.op("vector", lambda e, o=pf_sb[:, :], i=ps_pf[:, :]:
                     e.tensor_copy(o, i), "pfe")
                p.wait("vector", "pfe")
                p.op("vector", lambda e, o=pfl[:, :], a=pf_sb[:, :], b=ps_tb[:, :]:
                     e.tensor_tensor(o, a, b, ALU.add), "pfe")
                p.wait("tensor", "pfe", p.val("pfe"))
                p.mm(ps_tb[:, :], onesf[:, :], totb[:, :], True, True, sem="tb2")
                p.wait("vector", "tb2", p.val("tb2"))
                p.op("vector", lambda e, o=validm[:, :], a=pf_sb[:, :], b=ps_tb[:, :]:
                     e.tensor_tensor(o, a, b, ALU.add), "bcw")
                p.wait("vector", "bcw")
                p.op("vector", lambda e, o=validm[:, :], i=validm[:, :]:
                     e.tensor_single_scalar(o, i, float(CAP1), ALU.is_lt), "bcw")
                p.op("vector", lambda e, o=posm[:, tt, :], a=pfl[:, :], b=ohsum[:, :]:
                     e.tensor_tensor(o, a, b, ALU.mult), "bcw")
                p.wait("vector", "bcw")
                p.op("vector", lambda e, o=posm[:, tt, :], a=posm[:, tt, :], b=ohsum[:, :]:
                     e.tensor_tensor(o, a, b, ALU.add), "bcw")
                p.wait("vector", "bcw")
                p.op("vector", lambda e, o=posm[:, tt, :], i=posm[:, tt, :]:
                     e.tensor_scalar_add(o, i, -1.0), "bcw")
                p.op("vector", lambda e, o=gatem[:, tt, :], i=oh1[:, :], w=cw1[:, :]:
                     e.tensor_scalar_mul(o, i, w), "bcw")
                p.wait("vector", "bcw")
                p.op("vector", lambda e, o=gatem[:, tt, :], i=oh2[:, :], w=cw2[:, :],
                     x=gatem[:, tt, :]:
                     e.scalar_tensor_tensor(o, i, w, x, ALU.mult, ALU.add), "bcw")
                p.wait("vector", "bcw")
                p.op("vector", lambda e, o=gatem[:, tt, :], a=gatem[:, tt, :],
                     b=validm[:, :]:
                     e.tensor_tensor(o, a, b, ALU.mult), "bcw")
                p.wait("vector", "bcw")
                p.op("vector", lambda e, o=runb[:, :], a=runb[:, :], b=ps_tc[:, :]:
                     e.tensor_tensor(o, a, b, ALU.add), "bcw")
                p.wait("vector", "bcw")

        p.wait("tensor", "bcw", p.val("bcw"))
        with contextlib.ExitStack() as s2:
            wbuf = p.sb("wbuf", [128, 2, 2, DC, 128], BF16, s2)
            wdbuf = p.sb("wdbuf", [128, 2, D], BF16, s2)
            h_sh = p.sb("h_sh", [128, ICS, 512], BF16, s2)
            sil = p.sb("sil", [128, 512], F32, s2)
            silb = p.sb("silb", [128, 512], F32, s2)
            silL = [sil, silb]
            for th in range(2):
                tsl = slice(th * 512, (th + 1) * 512)
                p.wait("tensor", "dne", p.val("dne"))
                with contextlib.ExitStack() as s2a:
                    ps_guL = [p.ps("ps_gu", [128, 1024], s2a), p.ps("ps_gu", [128, 1024], s2a)]
                    for i in range(ICS):
                        buf = i % 2
                        gi = th * ICS + i
                        ps_gu = ps_guL[i % 2]
                        silx = silL[i % 2]
                        if gi >= 2:
                            p.wait("gpsimd", "gum", gi - 1)
                        p.dma("gpsimd", "wu%d" % buf, wbuf[:, buf, 0, :, :],
                              shup_in[i, 0])
                        p.dma("gpsimd", "wu%d" % buf, wbuf[:, buf, 1, :, :],
                              shup_in[i, 1])
                        p.wait("tensor", "wu%d" % buf, p.val("wu%d" % buf))
                        if gi >= 2:
                            p.wait("tensor", "sme", max(0, p.val("sme") - 1))
                        for dc in range(DC):
                            p.mm(ps_gu[:, 0:512], wbuf[:, buf, 0, dc, :],
                                 x3T[:, dc, tsl], dc == 0, dc == DC - 1)
                            p.mm(ps_gu[:, 512:1024], wbuf[:, buf, 1, dc, :],
                                 x3T[:, dc, tsl], dc == 0, dc == DC - 1,
                                 sem=None if dc < DC - 1 else "gum")
                        p.wait("scalar", "gum", p.val("gum"))
                        p.op("scalar", lambda e, o=silx[:, :], i=ps_gu[:, 0:512]:
                             e.activation(o, i, ACT.Silu), "sil")
                        p.wait("vector", "sil", p.val("sil"))
                        p.op("vector", lambda e, o=h_sh[:, i, :], a=silx[:, :],
                             b=ps_gu[:, 512:1024]:
                             e.tensor_tensor(o, a, b, ALU.mult), "sme")
                with contextlib.ExitStack() as s2b:
                    ps_dn = p.ps("ps_dn", [128, 4096], s2b)
                    for q in range(2):
                        for i in range(ICS):
                            buf = i % 2
                            di = (2 * th + q) * ICS + i
                            if di >= 2:
                                p.wait("gpsimd", "dnm", di - 1)
                            p.dma("gpsimd", "wd%d" % buf, wdbuf[:, buf, :],
                                  shdn_in[i * 128:(i + 1) * 128, :])
                            p.wait("tensor", "wd%d" % buf, p.val("wd%d" % buf))
                            p.wait("tensor", "sme", p.val("sme"))
                            if di >= 1:
                                p.wait("tensor", "dne", p.val("dne"))
                            got_sem = False
                            for t2 in range(2):
                                for oc in range(4):
                                    last = (i == ICS - 1 and t2 == 1 and oc == 3)
                                    p.mm(ps_dn[:, (t2 * 4 + oc) * 512:(t2 * 4 + oc + 1) * 512],
                                         h_sh[:, i, (q * 2 + t2) * 128:(q * 2 + t2 + 1) * 128],
                                         wdbuf[:, buf, oc * 512:(oc + 1) * 512],
                                         i == 0, i == ICS - 1,
                                         sem="dnm" if last or (i < ICS - 1 and t2 == 1 and oc == 3) else None)
                        p.wait("vector", "dnm", p.val("dnm"))
                        for t2 in range(2):
                            tti = th * 4 + q * 2 + t2
                            for oc in range(4):
                                p.op("vector", lambda e,
                                     o=x_tm[:, tti, oc * 512:(oc + 1) * 512],
                                     i=ps_dn[:, (t2 * 4 + oc) * 512:(t2 * 4 + oc + 1) * 512],
                                     x=x_tm[:, tti, oc * 512:(oc + 1) * 512]:
                                     e.tensor_tensor(o, i, x, ALU.add), "dne")
                        p.wait("vector", "dne")

    with contextlib.ExitStack() as s3:
        Gt = p.sb("Gt", [128, 4, T], BF16, s3)
        xg = p.sb("xg", [128, DC, 512], BF16, s3)
        h_e = p.sb("h_e", [128, IC1, 512], BF16, s3)
        yg = p.sb("yg", [128, 4, D], BF16, s3)
        wbufe = p.sb("wbufe", [128, 2, 2, DC, 128], BF16, s3)
        wdbufe = p.sb("wdbufe", [128, 2, D], BF16, s3)
        G = p.sb("G", [128, TT, 512], BF16, s3)
        sil2 = p.sb("sil2", [128, 512], F32, s3)
        sil2b = p.sb("sil2b", [128, 512], F32, s3)
        CL = 512
        JC = 4

        p.wait("tensor", "dne", p.val("dne"))
        p.wait("vector", "dnm", p.val("dnm"))
        p.wait("gpsimd", "dnm", p.val("dnm"))
        for eng_ in ("tensor", "vector", "scalar", "gpsimd"):
            p.wait(eng_, "skf", p.val("skf"))
        p.op("vector", lambda e, o=yg[:, :, :]: e.memset(o, 0.0), "ygz")
        p.wait("tensor", "ygz", 1)
        for ve in range(2 * E1):
            ex, sh = ve // 2, ve % 2
            isl = slice(sh * CL, (sh + 1) * CL)
            sgp = ["exg%d_0" % ex, "exg%d_1" % ex]
            sdp = ["exd%d_0" % ex, "exd%d_1" % ex]
            p.skip_begin(skipi[0:1, sh * E1 + ex:sh * E1 + ex + 1])
            p.wait("vector", "xge", p.val("xge"))
            for tt in range(TT):
                p.op("vector", lambda e, o=G[:, tt, :], i=iota[:, isl],
                     m=posm[:, tt, ex:ex + 1]:
                     e.tensor_scalar(o, i, m, None, ALU.is_equal), "gb")
            with contextlib.ExitStack() as sA:
                ps_bt = p.ps("ps_bt", [128, T], sA, BF16)
                p.wait("vector", "sce", p.val("sce"))
                p.wait("tensor", "gb", p.val("gb"))
                for jc in range(JC):
                    for tt in range(TT):
                        if tt + jc or ve:
                            p.wait("tensor", "gte", p.val("gte"))
                        p.op("tensor", lambda e, o=ps_bt[:, tt * 128:(tt + 1) * 128],
                             i=G[:, tt, jc * 128:(jc + 1) * 128], idn=idb[:, :]:
                             e.transpose(o, i, idn), "gtt")
                    p.wait("vector", "gtt", p.val("gtt"))
                    p.op("vector", lambda e, o=Gt[:, jc, :], i=ps_bt[:, :]:
                         e.tensor_copy(o, i), "gte")
            with contextlib.ExitStack() as sB:
                ps_gL = [p.ps("ps_g", [128, CL], sB), p.ps("ps_g", [128, CL], sB)]
                p.wait("tensor", "gte", p.val("gte"))
                for dc in range(DC):
                    ps_g = ps_gL[dc % 2]
                    if dc >= 2 or ve:
                        p.wait("tensor", "xge", max(0, p.val("xge") - 1))
                    for tt in range(TT):
                        p.mm(ps_g[:, :], x3tm[:, tt, dc * 128:(dc + 1) * 128],
                             G[:, tt, :], tt == 0, tt == TT - 1,
                             sem=None if tt < TT - 1 else "xgm")
                    p.wait("vector", "xgm", p.val("xgm"))
                    p.op("vector", lambda e, o=xg[:, dc, :], i=ps_g[:, :]:
                         e.tensor_copy(o, i), "xge")
            with contextlib.ExitStack() as sC:
                ps_gueL = [p.ps("ps_gue", [128, 1024], sC), p.ps("ps_gue", [128, 1024], sC)]
                sil2L = [sil2, sil2b]
                for i in range(IC1):
                    gi = ve * IC1 + i
                    buf = gi % 2
                    ps_gue = ps_gueL[i % 2]
                    sil2x = sil2L[i % 2]
                    sg = sgp[buf]
                    if gi >= 2:
                        p.wait("gpsimd", "egm", gi - 1)
                    p.dma("gpsimd", sg, wbufe[:, buf, 0, :, :],
                          wug_in[ex, i, 0])
                    p.dma("gpsimd", sg, wbufe[:, buf, 1, :, :],
                          wug_in[ex, i, 1])
                    p.wait("tensor", sg, p.val(sg))
                    p.wait("tensor", "xge", p.val("xge"))
                    if gi >= 2:
                        p.wait("tensor", "ehe", max(0, p.val("ehe") - 1))
                    for dc in range(DC):
                        p.mm(ps_gue[:, 0:512], wbufe[:, buf, 0, dc, :],
                             xg[:, dc, :], dc == 0, dc == DC - 1)
                        p.mm(ps_gue[:, 512:1024], wbufe[:, buf, 1, dc, :],
                             xg[:, dc, :], dc == 0, dc == DC - 1,
                             sem=None if dc < DC - 1 else "egm")
                    p.wait("scalar", "egm", p.val("egm"))
                    p.op("scalar", lambda e, o=sil2x[:, :], i=ps_gue[:, 0:512]:
                         e.activation(o, i, ACT.Silu), "esl")
                    p.wait("vector", "esl", p.val("esl"))
                    p.op("vector", lambda e, o=h_e[:, i, :], a=sil2x[:, :],
                         b=ps_gue[:, 512:1024]:
                         e.tensor_tensor(o, a, b, ALU.mult), "ehe")
            with contextlib.ExitStack() as sD:
                ps_dne = p.ps("ps_dne", [128, 4096], sD)
                for jh2 in range(2):
                    if jh2 == 1:
                        p.skip_end(gpsimd_in=True)
                        p.skip_begin(skipi2[0:1, sh * E1 + ex:sh * E1 + ex + 1])
                    for i in range(IC1):
                        di = (ve * 2 + jh2) * IC1 + i
                        buf = di % 2
                        sd = sdp[buf]
                        if di >= 2:
                            p.wait("gpsimd", "edm", di - 1)
                        p.dma("gpsimd", sd, wdbufe[:, buf, :],
                              wd_in[ex, i * 128:(i + 1) * 128, :])
                        p.wait("tensor", sd, p.val(sd))
                        p.wait("tensor", "ehe", p.val("ehe"))
                        if di >= 1:
                            p.wait("tensor", "yge", p.val("yge"))
                        for j2 in range(2):
                            for oc in range(4):
                                last = (j2 == 1 and oc == 3)
                                p.mm(ps_dne[:, (j2 * 4 + oc) * 512:(j2 * 4 + oc + 1) * 512],
                                     h_e[:, i, (jh2 * 2 + j2) * 128:(jh2 * 2 + j2 + 1) * 128],
                                     wdbufe[:, buf, oc * 512:(oc + 1) * 512],
                                     i == 0, i == IC1 - 1,
                                     sem="edm" if last else None)
                    p.wait("vector", "edm", p.val("edm"))
                    for j2 in range(2):
                        p.op("vector", lambda e, o=yg[:, jh2 * 2 + j2, :],
                             i=ps_dne[:, j2 * 2048:(j2 + 1) * 2048]:
                             e.tensor_copy(o, i), "yge")
            p.skip_end(gpsimd_in=True)
            p.skip_begin(skipi[0:1, sh * E1 + ex:sh * E1 + ex + 1])
            with contextlib.ExitStack() as sE:
                ps_scL = [p.ps("ps_sc", [128, 1024], sE), p.ps("ps_sc", [128, 1024], sE)]
                p.wait("tensor", "yge", p.val("yge"))
                for tt in range(TT):
                    for ob in range(2):
                        ps_sc = ps_scL[(tt * 2 + ob) % 2]
                        if (tt * 2 + ob) >= 2 or ve:
                            p.wait("tensor", "sce", max(0, p.val("sce") - 2))
                        for jc in range(JC):
                            for o2 in range(2):
                                p.mm(ps_sc[:, o2 * 512:(o2 + 1) * 512],
                                     Gt[:, jc, tt * 128:(tt + 1) * 128],
                                     yg[:, jc, (ob * 2 + o2) * 512:(ob * 2 + o2 + 1) * 512],
                                     jc == 0, jc == JC - 1,
                                     sem=None if not (jc == JC - 1 and o2 == 1) else "scm")
                        p.wait("vector", "scm", p.val("scm"))
                        for o2 in range(2):
                            p.op("vector", lambda e,
                                 o=x_tm[:, tt, (ob * 2 + o2) * 512:(ob * 2 + o2 + 1) * 512],
                                 i=ps_sc[:, o2 * 512:(o2 + 1) * 512],
                                 g=gatem[:, tt, ex:ex + 1],
                                 x=x_tm[:, tt, (ob * 2 + o2) * 512:(ob * 2 + o2 + 1) * 512]:
                                 e.scalar_tensor_tensor(o, i, g, x, ALU.mult, ALU.add), "sce")
            p.skip_end(gpsimd_in=True)

    sq_scr = p.sb("sq_scr", [128, D], F32)
    ssum = p.sb("ssum", [128, 1], F32)
    ssq = p.sb("ssq", [128, 1], F32)
    mean = p.sb("mean", [128, 1], F32)
    m2 = p.sb("m2", [128, 1], F32)
    var = p.sb("var", [128, 1], F32)
    rstd = p.sb("rstd", [128, 1], F32)
    for tt in range(TT):
        p.wait("vector", "sce", p.val("sce"))
        p.wait("vector", "dne", p.val("dne"))
        layer_norm(p, x_tm[:, tt, :], sq_scr[:, :], ssum[:, :], ssq[:, :],
                   mean[:, :], m2[:, :], var[:, :], rstd[:, :], "lne")
        p.wait("sync", "lne", 9 * (tt + 1))
        p.dma("sync", "st", x4_out[tt * 128:(tt + 1) * 128, :], x_tm[:, tt, :])
    p.wait("sync", "st", p.val("st"))
    return p.finish()


def _prep_l5(inputs, x3, cnt):
    def _pchunk(a):
        return a.reshape(DC, 128, 128).transpose(1, 0, 2)
    sup = np.asarray(inputs["L1_ffn_shared_w_up_gate"], np.float32)
    shup = np.stack([np.stack([_pchunk(sup[:, i * 128:(i + 1) * 128]),
                               _pchunk(sup[:, J1 + i * 128:J1 + (i + 1) * 128])], 0)
                     for i in range(ICS)], 0)
    wug_raw = np.asarray(inputs["L1_ffn_w_up_gate"], np.float32)
    wug = np.stack([
        np.stack([np.stack([_pchunk(wug_raw[ex][:, i * 128:(i + 1) * 128]),
                            _pchunk(wug_raw[ex][:, I1 + i * 128:I1 + (i + 1) * 128])], 0)
                  for i in range(IC1)], 0)
        for ex in range(E1)], 0)
    ii, jj = np.meshgrid(np.arange(128), np.arange(128), indexing="ij")
    u128 = (ii < jj).astype(np.float32)
    i8, j8 = np.meshgrid(np.arange(NCORE), np.arange(NCORE), indexing="ij")
    u8 = (i8 < j8).astype(np.float32)
    import ml_dtypes
    shared = {
        "cnts_in": np.ascontiguousarray(cnt.astype(np.float32)),
        "rtr_in": np.asarray(inputs["L1_ffn_router"], np.float32),
        "u128_in": u128,
        "u8_in": u8,
        "iota_in": np.tile(np.arange(CAPL1, dtype=np.float32), (128, 1)),
        "ones_in": np.ones((1, 128), np.float32),
        "idb_in": np.eye(128, dtype=np.float32).astype(ml_dtypes.bfloat16),
        "shup_in": np.ascontiguousarray(shup),
        "shdn_in": np.asarray(inputs["L1_ffn_shared_w_down"], np.float32),
        "wug_in": np.ascontiguousarray(wug),
        "wd_in": np.asarray(inputs["L1_ffn_w_down"], np.float32),
    }
    in_maps = []
    for c in range(NCORE):
        m = dict(shared)
        m["x_in"] = x3[c * T:(c + 1) * T]
        m["xT_in"] = np.ascontiguousarray(x3[c * T:(c + 1) * T].T)
        rs = np.zeros((NCORE, 1), np.float32)
        rs[c, 0] = 1.0
        m["rowsel_in"] = rs
        in_maps.append(m)
    return in_maps


def run_l5(inputs, x3, cnt):
    nc = build_l5()
    in_maps = _prep_l5(inputs, x3, cnt)
    res = run_bass_kernel_spmd(nc, in_maps, core_ids=list(range(NCORE)))
    return np.concatenate([res.results[c]["x4_out"] for c in range(NCORE)], 0)



SB_TOP = 229344
OFF_XTM = SB_TOP - 65536
OFF_QT = OFF_XTM - 32768
OFF_KT = OFF_QT - 16384
OFF_VA = OFF_KT - 16896
OFF_SLOTA = OFF_VA - 32768
OFF_BIG = OFF_VA


def build_lB():
    p = _new_prog("lB", detect=False)
    nc = p.nc
    dt = lambda n, sh, ty=F32, kind="ExternalInput": nc.dram_tensor(
        n, list(sh), ty, kind=kind).ap()

    x_in = dt("x_in", [T, D])
    xT_in = dt("xT_in", [D, T])
    sall_in = dt("sall_in", [8 * T, NE0])
    sown_in = dt("sown_in", [T, NE0])
    ew_in = dt("ew_in", [D, D])
    eb_in = dt("eb_in", [1, D])
    wq1_in = dt("wq1_in", [D, D])
    wk1_in = dt("wk1_in", [D, KVH * DH])
    wv1_in = dt("wv1_in", [D, KVH * DH])
    qnw_in = dt("qnw_in", [1, DH])
    knw_in = dt("knw_in", [1, DH])
    cos_in = dt("cos_in", [T, 64])
    sin_in = dt("sin_in", [T, 64])
    wo1_in = dt("wo1_in", [D, D])
    dmask_in = dt("dmask_in", [128, 512])
    bmask_in = dt("bmask_in", [128, 512])
    idb_in = dt("idb_in", [128, 128], BF16)
    idf_in = dt("idf_in", [128, 128])
    rtr_in = dt("rtr_in", [D, E1])
    vones_in = dt("vones_in", [128, 16], BF16)
    ones_in = dt("ones_in", [1, 128])
    nonce_in = dt("nonce_in", [1, 1], I32)
    x3_out = dt("x3_out", [T, D], F32, "ExternalOutput")
    cnt_out = dt("cnt_out", [1, E1], F32, "ExternalOutput")

    xch_k = nc.dram_tensor("xch_k", [2, 128, 4096], BF16, kind="Internal",
                           addr_space="Shared")
    xch_v = nc.dram_tensor("xch_v", [2, 128, 4096], BF16, kind="Internal",
                           addr_space="Shared")
    xfl = nc.dram_tensor("xfl", [2, 1], I32, kind="Internal", addr_space="Shared")

    x_tm = p.sba("x_tm", [128, TT, D], F32, OFF_XTM)[...]
    qT = p.sba("qT", [128, TT, H1 * 128], BF16, OFF_QT)[...]
    kT = p.sba("kT", [128, KVH, 2 * T], BF16, OFF_KT)[...]
    vaug = p.sba("vaug", [128, 16, KVH, 132], BF16, OFF_VA)[...]
    x1T = p.sba("x1T", [128, DC, T], BF16, OFF_SLOTA)[...]
    x2T = p.sba("x2T", [128, DC, T], BF16, OFF_SLOTA)[...]
    aoT = p.sba("aoT", [128, TT, H1 * 128], BF16, OFF_SLOTA)[...]
    ew = p.sba("ew", [128, DC, D], BF16, OFF_BIG)[...]
    woQ = p.sba("woQ", [128, 2, 4, D], BF16, OFF_BIG)[...]
    x3T = p.sba("x3T", [128, DC, T], BF16, OFF_BIG)[...]

    onesf = p.sb("onesf", [1, 128], F32)
    onescol = p.sb("onescol", [128, 1], F32)
    idf = p.sb("idf", [128, 128], F32)
    idb = p.sb("idb", [128, 128], BF16)
    non = p.sb("non", [1, 1], I32)
    vones = p.sb("vones", [128, 16], BF16)
    vonesf = p.sb("vonesf", [128, 16], F32)
    lo = p.sb("lo", [1, NE0], F32)
    hi = p.sb("hi", [1, NE0], F32)
    mid = p.sb("mid", [1, NE0], F32)
    dd = p.sb("dd", [1, NE0], F32)
    sel = p.sb("sel", [1, NE0], F32)
    cntv = p.sb("cntv", [128, NE0], F32)
    wcol = p.sb("wcol", [128, TT], F32)
    scrA8 = p.sb("scrA8", [128, NE0], F32)
    sq_scr = p.sb("sq_scr", [128, D], F32)
    ssum = p.sb("ssum", [128, 1], F32)
    ssq = p.sb("ssq", [128, 1], F32)
    mean = p.sb("mean", [128, 1], F32)
    m2 = p.sb("m2", [128, 1], F32)
    var = p.sb("var", [128, 1], F32)
    rstd = p.sb("rstd", [128, 1], F32)

    p.dma("sync", "ld", x_tm[:, :, :], x_in.rearrange("(tt p) d -> p tt d", p=128))
    p.dma("sync", "ld", onesf[:, :], ones_in[:, :])
    p.dma("sync", "ld", idf[:, :], idf_in[:, :])
    p.dma("sync", "ld", idb[:, :], idb_in[:, :])
    p.dma("sync", "ld", non[:, :], nonce_in[:, :])
    p.dma("gpsimd", "ldg", x1T[:, :, :], xT_in.rearrange("(c p) t -> p c t", p=128))
    p.dma("gpsimd", "ldg", ew[:, :, :], ew_in.rearrange("(c p) o -> p c o", p=128))
    p.wait("vector", "ld", p.val("ld"))
    p.op("vector", lambda e, o=onescol[:, :]: e.memset(o, 1.0), "oc")

    with contextlib.ExitStack() as s1:
        sthr = p.sb("sthr", [128, 64, NE0], F32, s1)
        sown = p.sb("sown", [128, TT, NE0], F32, s1)
        eb = p.sb("eb", [1, D], F32, s1)
        ge = p.sb("ge", [128, 64, NE0], F32, s1)
        ps_b = p.ps("ps_b", [128, NE0], s1)
        ps_c = p.ps("ps_c", [1, NE0], s1)
        ps_tau = p.ps("ps_tau", [128, NE0], s1)
        ps_z = p.ps("ps_z", [128, D], s1)
        p.dma("sync", "lds1", sthr[:, :, :], sall_in.rearrange("(pp j) e -> pp j e", j=64))
        p.dma("sync", "lds1", sown[:, :, :], sown_in.rearrange("(tt p) e -> p tt e", p=128))
        p.dma("sync", "lds1", eb[:, :], eb_in[:, :])
        p.wait("vector", "lds1", p.val("lds1"))
        p.op("vector", lambda e, o=lo[:, :]: e.memset(o, 0.0), "bi")
        p.op("vector", lambda e, o=hi[:, :]: e.memset(o, 1.0), "bi")
        for it in range(42):
            p.wait("vector", "bi")
            p.op("vector", lambda e, o=mid[:, :], a=lo[:, :], b=hi[:, :]:
                 e.tensor_tensor(o, a, b, ALU.add), "bi")
            p.wait("vector", "bi")
            p.op("vector", lambda e, o=mid[:, :]: e.tensor_scalar_mul(o, o, 0.5), "bi")
            p.wait("tensor", "bi", p.val("bi"))
            if it:
                p.wait("tensor", "gei", p.val("gei"))
            p.mm(ps_b[:, :], onesf[:, :], mid[:, :], True, True, sem="bmm")
            p.wait("vector", "bmm", p.val("bmm"))
            p.op("vector", lambda e, o=ge[:, :, :], a=sthr[:, :, :],
                 b=ps_b[:, :].unsqueeze(1).broadcast_to([128, 64, NE0]):
                 e.tensor_tensor(o, a, b, ALU.is_ge), "gei")
            p.wait("vector", "gei")
            p.op("vector", lambda e, o=cntv[:, :],
                 i=ge[:, :, :].rearrange("p j e -> p e j"):
                 e.tensor_reduce(o, i, AX.X, ALU.add), "gei")
            p.wait("tensor", "gei", p.val("gei"))
            if it:
                p.wait("tensor", "seli", p.val("seli"))
            p.mm(ps_c[:, :], onescol[:, :], cntv[:, :], True, True, sem="cmm")
            p.wait("vector", "cmm", p.val("cmm"))
            p.op("vector", lambda e, o=sel[:, :], i=ps_c[:, :]:
                 e.tensor_single_scalar(o, i, float(K0_TOP), ALU.is_ge), "seli")
            p.wait("vector", "seli")
            p.op("vector", lambda e, o=dd[:, :], a=mid[:, :], b=lo[:, :]:
                 e.tensor_tensor(o, a, b, ALU.subtract), "bi")
            p.wait("vector", "bi")
            p.op("vector", lambda e, o=dd[:, :], a=sel[:, :], b=dd[:, :]:
                 e.tensor_tensor(o, a, b, ALU.mult), "bi")
            p.wait("vector", "bi")
            p.op("vector", lambda e, o=lo[:, :], a=lo[:, :], b=dd[:, :]:
                 e.tensor_tensor(o, a, b, ALU.add), "bi")
            p.wait("vector", "bi")
            p.op("vector", lambda e, o=dd[:, :], a=hi[:, :], b=mid[:, :]:
                 e.tensor_tensor(o, a, b, ALU.subtract), "bi")
            p.wait("vector", "bi")
            p.op("vector", lambda e, o=dd[:, :], a=sel[:, :], b=dd[:, :]:
                 e.tensor_tensor(o, a, b, ALU.mult), "bi")
            p.wait("vector", "bi")
            p.op("vector", lambda e, o=hi[:, :], a=mid[:, :], b=dd[:, :]:
                 e.tensor_tensor(o, a, b, ALU.add), "bi")
        p.wait("tensor", "bi", p.val("bi"))
        p.wait("tensor", "gei", p.val("gei"))
        p.mm(ps_tau[:, :], onesf[:, :], lo[:, :], True, True, sem="tmm")
        p.wait("vector", "tmm", p.val("tmm"))
        for tt in range(TT):
            p.op("vector", lambda e, o=scrA8[:, :], a=sown[:, tt, :], b=ps_tau[:, :]:
                 e.tensor_tensor(o, a, b, ALU.is_ge), "wt")
            p.wait("vector", "wt")
            p.op("vector", lambda e, o=scrA8[:, :], a=sown[:, tt, :], b=scrA8[:, :]:
                 e.tensor_tensor(o, a, b, ALU.mult), "wt")
            p.wait("vector", "wt")
            p.op("vector", lambda e, o=wcol[:, tt:tt + 1], i=scrA8[:, :]:
                 e.reduce_sum(o, i, AX.X), "wt")
            p.wait("vector", "wt")
        for tt in range(TT):
            p.wait("tensor", "ld", p.val("ld"))
            p.wait("tensor", "lds1", p.val("lds1"))
            p.wait("tensor", "ldg", p.val("ldg"))
            if tt:
                p.wait("tensor", "zev", p.val("zev"))
            for oc in range(4):
                p.mm(ps_z[:, oc * 512:(oc + 1) * 512], onesf[:, :],
                     eb[:, oc * 512:(oc + 1) * 512], True, False)
            for dc in range(DC):
                for oc in range(4):
                    p.mm(ps_z[:, oc * 512:(oc + 1) * 512],
                         x1T[:, dc, tt * 128:(tt + 1) * 128],
                         ew[:, dc, oc * 512:(oc + 1) * 512], False, dc == DC - 1,
                         sem=None if not (dc == DC - 1 and oc == 3) else "zmm")
            p.wait("vector", "zmm", p.val("zmm"))
            p.wait("vector", "wt", p.val("wt"))
            for oc in range(4):
                p.op("vector", lambda e, o=x_tm[:, tt, oc * 512:(oc + 1) * 512],
                     i=ps_z[:, oc * 512:(oc + 1) * 512], w=wcol[:, tt:tt + 1],
                     x=x_tm[:, tt, oc * 512:(oc + 1) * 512]:
                     e.scalar_tensor_tensor(o, i, w, x, ALU.mult, ALU.add), "zev")
            p.wait("vector", "zev")
            layer_norm(p, x_tm[:, tt, :], sq_scr[:, :], ssum[:, :], ssq[:, :],
                       mean[:, :], m2[:, :], var[:, :], rstd[:, :], "ln2")

    with contextlib.ExitStack() as s2:
        ps_t = p.ps("ps_t", [128, 128], s2)
        p.wait("tensor", "ln2", p.val("ln2"))
        for tt in range(TT):
            for dc in range(DC):
                p.wait("tensor", "x2e", p.val("x2e"))
                p.trans(ps_t[:, :], x_tm[:, tt, dc * 128:(dc + 1) * 128], idf[:, :],
                        x2T[:, dc, tt * 128:(tt + 1) * 128], "x2e")

    with contextlib.ExitStack() as s3:
        qnw = p.sb("qnw", [1, DH], F32, s3)
        knw = p.sb("knw", [1, DH], F32, s3)
        cosb = p.sb("cosb", [128, TT, 64], F32, s3)
        sinb = p.sb("sinb", [128, TT, 64], F32, s3)
        wbc = p.sb("wbc", [128, DH], F32, s3)
        kwbc = p.sb("kwbc", [128, DH], F32, s3)
        qtm = p.sb("qtm", [128, 512], F32, s3)
        ktm = p.sb("ktm", [128, 2 * DH], F32, s3)
        ssq4 = p.sb("ssq4", [128, KVH], F32, s3)
        rr4 = p.sb("rr4", [128, KVH], F32, s3)
        scr1 = p.sb("scr1", [128, 4 * 64], F32, s3)
        scr2 = p.sb("scr2", [128, 4 * 64], F32, s3)
        qbf = p.sb("qbf", [128, 512], BF16, s3)
        kbf = p.sb("kbf", [128, 2 * DH], BF16, s3)

        p.wait("sync", "ln2", p.val("ln2"))
        p.dma("sync", "ld2", qnw[:, :], qnw_in[:, :])
        p.dma("sync", "ld2", knw[:, :], knw_in[:, :])
        p.dma("sync", "ld2", cosb[:, :, :], cos_in.rearrange("(tt p) f -> p tt f", p=128))
        p.dma("sync", "ld2", sinb[:, :, :], sin_in.rearrange("(tt p) f -> p tt f", p=128))
        p.dma("sync", "ld2", vones[:, :], vones_in[:, :])

        ps_w = p.ps("ps_w", [128, DH], s3)
        p.wait("tensor", "ld2", p.val("ld2"))
        p.wait("tensor", "x2e", p.val("x2e"))
        p.mm(ps_w[:, :], onesf[:, :], qnw[:, :], True, True, sem="wb")
        p.wait("vector", "wb", p.val("wb"))
        p.op("vector", lambda e, o=wbc[:, :], i=ps_w[:, :]: e.tensor_copy(o, i), "wbe")
        p.wait("tensor", "wbe", p.val("wbe"))
        p.mm(ps_w[:, :], onesf[:, :], knw[:, :], True, True, sem="wb")
        p.wait("vector", "wb", p.val("wb"))
        p.op("vector", lambda e, o=kwbc[:, :], i=ps_w[:, :]: e.tensor_copy(o, i), "wbe")

        def rms_rope(tt, nh, xv, sqv, ssqv, rv, wv_bc, outbf, sem):
            p.op("scalar", lambda e, o=sqv, i=xv: e.activation(o, i, ACT.Square), sem)
            p.wait("vector", sem)
            p.op("vector", lambda e, o=ssqv, i=sqv.rearrange("p (h f) -> p h f", h=nh):
                 e.tensor_reduce(o, i, AX.X, ALU.add), sem)
            p.wait("vector", sem)
            p.op("vector", lambda e, o=rv, i=ssqv:
                 e.tensor_scalar(o, i, 1.0 / DH, 1e-6, ALU.mult, ALU.add), sem)
            p.wait("scalar", sem)
            p.op("scalar", lambda e, o=rv, i=rv: e.activation(o, i, ACT.Sqrt), sem)
            p.wait("vector", sem)
            p.op("vector", lambda e, o=rv, i=rv: e.reciprocal(o, i), sem)
            p.wait("vector", sem)
            for h in range(nh):
                p.op("vector", lambda e, o=xv[:, h * DH:(h + 1) * DH],
                     i=xv[:, h * DH:(h + 1) * DH], r=rv[:, h:h + 1], w=wv_bc:
                     e.scalar_tensor_tensor(o, i, r, w, ALU.mult, ALU.mult), sem)
            p.wait("vector", sem)
            x3v = xv.rearrange("p (h f) -> p h f", h=nh)
            o3 = outbf.rearrange("p (h f) -> p h f", h=nh)
            cb = cosb[:, tt, :].unsqueeze(1).broadcast_to([128, nh, 64])
            sb_ = sinb[:, tt, :].unsqueeze(1).broadcast_to([128, nh, 64])
            a3 = scr1.rearrange("p (h f) -> p h f", h=4)[:, 0:nh, :]
            b3 = scr2.rearrange("p (h f) -> p h f", h=4)[:, 0:nh, :]
            p.op("vector", lambda e, o=a3, i=x3v[:, :, 0:64], c=cb:
                 e.tensor_tensor(o, i, c, ALU.mult), sem)
            p.op("vector", lambda e, o=b3, i=x3v[:, :, 64:128], c=sb_:
                 e.tensor_tensor(o, i, c, ALU.mult), sem)
            p.wait("vector", sem)
            p.op("vector", lambda e, o=o3[:, :, 0:64], a=a3, b=b3:
                 e.tensor_tensor(o, a, b, ALU.subtract), sem)
            p.wait("vector", sem)
            p.op("vector", lambda e, o=a3, i=x3v[:, :, 0:64], c=sb_:
                 e.tensor_tensor(o, i, c, ALU.mult), sem)
            p.op("vector", lambda e, o=b3, i=x3v[:, :, 64:128], c=cb:
                 e.tensor_tensor(o, i, c, ALU.mult), sem)
            p.wait("vector", sem)
            p.op("vector", lambda e, o=o3[:, :, 64:128], a=a3, b=b3:
                 e.tensor_tensor(o, a, b, ALU.add), sem)

        with contextlib.ExitStack() as s3kv:
            wkb = p.sb("wkb", [128, DC, 2 * DH], BF16, s3kv)
            wvb2 = p.sb("wvb2", [128, DC, 2 * DH], BF16, s3kv)
            vbf2 = p.sb("vbf2", [128, 2 * DH], BF16, s3kv)
            ps_k = p.ps("ps_k", [128, 2 * DH], s3kv)
            ps_v = p.ps("ps_v", [128, 2 * DH], s3kv)
            ps_kt = p.ps("ps_kt", [128, 128], s3kv, BF16)
            p.wait("gpsimd", "ln2", p.val("ln2"))
            for hp in range(2):
                if hp:
                    p.wait("gpsimd", "kmm", p.val("kmm"))
                p.dma("gpsimd", "wk%d" % (hp % 2), wkb[:, :, :],
                      wk1_in[:, hp * 256:(hp + 1) * 256].rearrange(
                          "(c p) o -> p c o", p=128))
                p.dma("gpsimd", "wv%d" % (hp % 2), wvb2[:, :, :],
                      wv1_in[:, hp * 256:(hp + 1) * 256].rearrange(
                          "(c p) o -> p c o", p=128))
                p.wait("tensor", "wk%d" % (hp % 2), p.val("wk%d" % (hp % 2)))
                p.wait("tensor", "wv%d" % (hp % 2), p.val("wv%d" % (hp % 2)))
                for tt in range(TT):
                    if tt or hp:
                        p.wait("tensor", "kfe", p.val("kfe"))
                    for dc in range(DC):
                        p.mm(ps_k[:, :], x2T[:, dc, tt * 128:(tt + 1) * 128],
                             wkb[:, dc, :], dc == 0, dc == DC - 1,
                             sem=None if dc < DC - 1 else "kmm")
                        p.mm(ps_v[:, :], x2T[:, dc, tt * 128:(tt + 1) * 128],
                             wvb2[:, dc, :], dc == 0, dc == DC - 1,
                             sem=None if dc < DC - 1 else "kmm")
                    p.wait("vector", "kmm", p.val("kmm"))
                    p.wait("vector", "vst", p.val("vst"))
                    p.op("vector", lambda e, o=ktm[:, :], i=ps_k[:, :]:
                         e.tensor_copy(o, i), "kfe")
                    p.op("vector", lambda e, o=vbf2[:, :], i=ps_v[:, :]:
                         e.tensor_copy(o, i), "kfe")
                    p.wait("vector", "kfe")
                    p.wait("scalar", "kfe", p.val("kfe"))
                    rms_rope(tt, 2, ktm[:, :], sq_scr[:, 0:256], ssq4[:, 0:2],
                             rr4[:, 0:2], kwbc[:, :], kbf[:, :], "kfe")
                    p.wait("vector", "kfe", p.val("kfe"))
                    for h2 in range(2):
                        h4 = hp * 2 + h2
                        p.op("vector", lambda e, o=vaug[:, 8 + tt, h4, 0:128],
                             i=vbf2[:, h2 * 128:(h2 + 1) * 128]:
                             e.tensor_copy(o, i), "vst")
                        p.wait("tensor", "kfe", p.val("kfe"))
                        p.wait("tensor", "ktt", p.val("ktt"))
                        p.trans(ps_kt[:, :], kbf[:, h2 * 128:(h2 + 1) * 128],
                                idb[:, :],
                                kT[:, h4, T + tt * 128:T + (tt + 1) * 128], "ktt")
        p.wait("vector", "ld2", p.val("ld2"))
        for h4 in range(KVH):
            p.op("vector", lambda e, o=vaug[:, :, h4, 128:129],
                 i=vones[:, :].unsqueeze(2): e.tensor_copy(o, i), "vst")

        p.wait("gpsimd", "ktt", p.val("ktt"))
        p.wait("gpsimd", "vst", p.val("vst"))
        def _xch(g):
            half = g.bass.partition_id()
            myh = g.compute_val(half % 2)
            oth = g.compute_val(1 - (half % 2))
            g.dma_start(out=xch_k.ap()[bass.ds(myh, 1), :, :].squeeze(0),
                        in_=kT[:, :, T:2 * T]).then_inc(p.sem("xwr"), 16)
            g.dma_start(out=xch_v.ap()[bass.ds(myh, 1), :, :].squeeze(0),
                        in_=vaug[:, 8:16, :, 0:128]).then_inc(p.sem("xwr"), 16)
            g.wait_ge(p.sem("xwr"), 32)
            g.dma_start(out=xfl.ap()[bass.ds(myh, 1), :].squeeze(0),
                        in_=non[:, :]).then_inc(p.sem("xwr"), 16)
            g.wait_ge(p.sem("xwr"), 48)
            nval = g.value_load(non[0:1, 0:1])
            cntr = g.alloc_register("pollcnt")
            g.reg_mov(cntr, 0)
            fval = g.alloc_register("fval")
            g.reg_mov(fval, 0)
            ctmp = g.alloc_register("ctmp")
            flg = p._flg
            def cond():
                g.reg_alu(ctmp, fval, nval, ALU.not_equal)
                return ctmp
            with g.While(cond):
                g.reg_alu(cntr, cntr, 16, ALU.add)
                g.dma_start(out=flg[:, :],
                            in_=xfl.ap()[bass.ds(oth, 1), :].squeeze(0)
                            ).then_inc(p.sem("xpl"), 16)
                g.wait_ge(p.sem("xpl"), cntr)
                g.reg_load(fval, flg[0:1, 0:1])
            g.dma_start(out=kT[:, :, 0:T],
                        in_=xch_k.ap()[bass.ds(oth, 1), :, :].squeeze(0)
                        ).then_inc(p.sem("xrd"), 16)
            g.dma_start(out=vaug[:, 0:8, :, 0:128],
                        in_=xch_v.ap()[bass.ds(oth, 1), :, :].squeeze(0)
                        ).then_inc(p.sem("xrd"), 16)
        p._flg = p.sb("flg", [1, 1], I32, s3)
        p.inc("xrd", 32)
        p.on("gpsimd", _xch)

        with contextlib.ExitStack() as s3q:
            wqb = p.sb("wqb", [128, DC, 512], BF16, s3q)
            ps_q = p.ps("ps_q", [128, 512], s3q)
            ps_qt = p.ps("ps_qt", [128, 128], s3q, BF16)
            for oq in range(4):
                if oq:
                    p.wait("gpsimd", "qmm", p.val("qmm"))
                p.dma("gpsimd", "wq%d" % (oq % 2), wqb[:, :, :],
                      wq1_in[:, oq * 512:(oq + 1) * 512].rearrange(
                          "(c p) o -> p c o", p=128))
                p.wait("tensor", "wq%d" % (oq % 2), p.val("wq%d" % (oq % 2)))
                for tt in range(TT):
                    if tt or oq:
                        p.wait("tensor", "qfe", p.val("qfe"))
                    for dc in range(DC):
                        p.mm(ps_q[:, :], x2T[:, dc, tt * 128:(tt + 1) * 128],
                             wqb[:, dc, :], dc == 0, dc == DC - 1,
                             sem=None if dc < DC - 1 else "qmm")
                    p.wait("vector", "qmm", p.val("qmm"))
                    p.op("vector", lambda e, o=qtm[:, :], i=ps_q[:, :]:
                         e.tensor_copy(o, i), "qfe")
                    p.wait("vector", "qfe")
                    p.wait("scalar", "qfe", p.val("qfe"))
                    rms_rope(tt, 4, qtm[:, :], sq_scr[:, 0:512], ssq4[:, :],
                             rr4[:, :], wbc[:, :], qbf[:, :], "qfe")
                    p.wait("tensor", "qfe", p.val("qfe"))
                    for h4 in range(4):
                        h = oq * 4 + h4
                        p.wait("tensor", "qtt", p.val("qtt"))
                        p.trans(ps_qt[:, :], qbf[:, h4 * 128:(h4 + 1) * 128],
                                idb[:, :], qT[:, tt, h * 128:(h + 1) * 128], "qtt")

    p.wait("vector", "xrd", 32)
    p.op("vector", lambda e, o=vonesf[:, :], i=vones[:, :]: e.tensor_copy(o, i), "vzm")
    p.wait("vector", "vzm")
    for kbt in range(8):
        p.op("vector", lambda e, o=vaug[:, kbt, :, 0:128], i=vaug[:, kbt, :, 0:128],
             m=vonesf[:, kbt:kbt + 1]: e.tensor_scalar_mul(o, i, m), "vzm")

    with contextlib.ExitStack() as s4:
        p_sb = p.sb("p_sb", [128, 2, 9, 512], BF16, s4)
        dmask = p.sb("dmask", [128, 512], F32, s4)
        bmask = p.sb("bmask", [128, 512], F32, s4)
        rden4 = p.sb("rden4", [1, 2, 512], F32, s4)
        rb4 = p.sb("rb4", [128, 2, 512], F32, s4)
        mbias4 = p.sb("mbias4", [128, 32], F32, s4)
        onescb = p.sb("onescb", [128, 1], BF16, s4)
        p.wait("sync", "qtt", p.val("qtt"))
        p.dma("sync", "ld3", dmask[:, :], dmask_in[:, :])
        p.dma("sync", "ld3", bmask[:, :], bmask_in[:, :])

        with contextlib.ExitStack() as s4m:
            qsq = p.sb("qsq", [128, TT, 128], BF16, s4m)
            m4row = p.sb("m4row", [1, 32], F32, s4m)
            ksq = p.sb("ksq", [128, 2 * T], BF16, s4m)
            mqrow = p.sb("mqrow", [1, 128], F32, s4m)
            mkrow = p.sb("mkrow", [1, KVH, 16], F32, s4m)
            mkwin = p.sb("mkwin", [1, KVH, TT], F32, s4m)
            mrow = p.sb("mrow", [1, 128], F32, s4m)
            ps_nq = p.ps("ps_nq", [1, T], s4m)
            ps_mb = p.ps("ps_mb", [128, 128], s4m)
            for eng in ("scalar", "tensor"):
                p.wait(eng, "xrd", 32)
                p.wait(eng, "qtt", p.val("qtt"))
                p.wait(eng, "vst", p.val("vst"))
                p.wait(eng, "vzm", p.val("vzm"))
            p.wait("vector", "ld", p.val("ld"))
            p.op("vector", lambda e, o=onescb[:, :]: e.memset(o, 1.0), "ocb")
            p.wait("tensor", "ocb", 1)
            for h in range(H1):
                if h:
                    p.wait("scalar", "nqm", 8 * h)
                p.op("scalar", lambda e, o=qsq[:, :, :],
                     i=qT[:, :, h * 128:(h + 1) * 128]:
                     e.activation(o, i, ACT.Square), "sqq")
                p.wait("tensor", "sqq", h + 1)
                if h:
                    p.wait("tensor", "mqr", 8 * h)
                for qt in range(TT):
                    p.mm(ps_nq[0:1, qt * 128:(qt + 1) * 128], onescb[:, :],
                         qsq[:, qt, :], True, True, sem="nqm")
                p.wait("vector", "nqm", 8 * (h + 1))
                for qt in range(TT):
                    p.op("vector", lambda e,
                         o=mqrow[0:1, h * 8 + qt:h * 8 + qt + 1],
                         i=ps_nq[0:1, qt * 128:(qt + 1) * 128]:
                         e.reduce_max(o, i, AX.X), "mqr")
            for kvh in range(KVH):
                if kvh:
                    p.wait("scalar", "nkm", 16 * kvh)
                p.op("scalar", lambda e, o=ksq[:, :], i=kT[:, kvh, :]:
                     e.activation(o, i, ACT.Square), "sqk")
                p.wait("tensor", "sqk", kvh + 1)
                p.wait("tensor", "mqr", 8 * H1 if kvh == 0 else 0)
                for kt in range(16):
                    p.mm(ps_nq[0:1, (kt % 8) * 128:(kt % 8 + 1) * 128],
                         onescb[:, :], ksq[:, kt * 128:(kt + 1) * 128],
                         True, True, sem="nkm")
                    p.wait("vector", "nkm", 16 * kvh + kt + 1)
                    p.op("vector", lambda e,
                         o=mkrow[0:1, kvh, kt:kt + 1],
                         i=ps_nq[0:1, (kt % 8) * 128:(kt % 8 + 1) * 128]:
                         e.reduce_max(o, i, AX.X), "mkr")
                    if kt < 15:
                        p.wait("tensor", "mkr", 16 * kvh + kt + 1)
            p.wait("vector", "mkr", 64)
            for kvh in range(KVH):
                for qt in range(TT):
                    p.op("vector", lambda e, o=mkwin[0:1, kvh, qt:qt + 1],
                         i=mkrow[0:1, kvh, qt:qt + 9]:
                         e.reduce_max(o, i, AX.X), "mkw")
            p.wait("vector", "mkw")
            p.wait("vector", "mqr")
            for h in range(H1):
                p.op("vector", lambda e, o=mrow[0:1, h * 8:(h + 1) * 8],
                     a=mqrow[0:1, h * 8:(h + 1) * 8], b=mkwin[0:1, h // KVH, :]:
                     e.tensor_tensor(o, a, b, ALU.mult), "mcr")
            p.wait("vector", "mcr")
            for kvh in range(KVH):
                for qt in range(TT):
                    j0 = (4 * kvh) * 8 + qt
                    p.op("vector", lambda e,
                         o=m4row[0:1, kvh * 8 + qt:kvh * 8 + qt + 1],
                         i=mrow[0:1, j0:j0 + 25:8]:
                         e.reduce_max(o, i, AX.X), "m4r")
            p.wait("scalar", "m4r", p.val("m4r"))
            p.op("scalar", lambda e, o=m4row[0:1, :], i=m4row[0:1, :]:
                 e.activation(o, i, ACT.Sqrt), "msq")
            p.wait("vector", "msq", 1)
            p.op("vector", lambda e, o=m4row[0:1, :], i=m4row[0:1, :]:
                 e.tensor_scalar_mul(o, i, -SM_SCALE), "mns")
            p.wait("tensor", "mns", 1)
            p.wait("tensor", "ld3", p.val("ld3"))
            p.mm(ps_mb[:, 0:32], onesf[:, :], m4row[0:1, :], True, True, sem="mbm")
            p.wait("vector", "mbm", 1)
            p.op("vector", lambda e, o=mbias4[:, :], i=ps_mb[:, 0:32]:
                 e.tensor_copy(o, i), "mbb")

        ps_qkL = [p.ps("ps_qk", [128, 512], s4), p.ps("ps_qk", [128, 512], s4)]
        ps_aoL = [p.ps("ps_ao", [128, 512], s4), p.ps("ps_ao", [128, 512], s4)]
        ps_denL = [p.ps("ps_den", [1, 512], s4), p.ps("ps_den", [1, 512], s4)]
        ps_rb = p.ps("ps_rb", [128, 512], s4)

        p.wait("scalar", "mbb", 1)
        p.wait("tensor", "mbb", 1)
        NG = TT * KVH
        groups = [(qt, kvh) for qt in range(TT) for kvh in range(KVH)]

        def _qkg(gi):
            qt, kvh = groups[gi]
            for b9 in range(9):
                qb = gi * 9 + b9
                if qb >= 2:
                    p.wait("tensor", "pexp", qb - 1)
                p.mm(ps_qkL[qb % 2][:, :],
                     kT[:, kvh, (qt + b9) * 128:(qt + b9 + 1) * 128],
                     qT[:, qt, kvh * 512:(kvh + 1) * 512], True, True, sem="smm")

        def _pvg(gi):
            qt, kvh = groups[gi]
            p.wait("tensor", "pexp", gi * 9 + 9)
            if gi >= 2:
                p.wait("tensor", "aomul", gi - 1)
                p.wait("tensor", "rcp", gi - 1)
            for b9 in range(9):
                p.mm(ps_denL[gi % 2][:, :], vaug[:, qt + b9, kvh, 128:129],
                     p_sb[:, gi % 2, b9, :], b9 == 0, b9 == 8)
                p.mm(ps_aoL[gi % 2][:, :], vaug[:, qt + b9, kvh, 0:128],
                     p_sb[:, gi % 2, b9, :], b9 == 0, b9 == 8,
                     sem=None if b9 < 8 else "avm")
            p.wait("tensor", "rcp", gi + 1)
            if gi >= 1:
                p.wait("tensor", "rbc", gi)
            p.mm(ps_rb[:, :], onesf[:, :], rden4[0:1, gi % 2, :],
                 True, True, sem="rbm")

        def _mskg(gi):
            b0 = (gi * 9) % 2
            b8 = (gi * 9 + 8) % 2
            p.wait("vector", "smm", gi * 9 + 1)
            p.op("vector", lambda e, o=ps_qkL[b0][:, :], i=ps_qkL[b0][:, :],
                 m=bmask[:, :]: e.tensor_tensor(o, i, m, ALU.add), "mk0")
            p.wait("vector", "smm", gi * 9 + 9)
            p.op("vector", lambda e, o=ps_qkL[b8][:, :], i=ps_qkL[b8][:, :],
                 m=dmask[:, :]: e.tensor_tensor(o, i, m, ALU.add), "mk8")

        def _recipg(gi):
            p.wait("vector", "avm", gi + 1)
            p.op("vector", lambda e, o=rden4[0:1, gi % 2, :],
                 i=ps_denL[gi % 2][:, :]: e.reciprocal(o, i), "rcp")

        def _multg(gi):
            qt, kvh = groups[gi]
            p.wait("vector", "rbc", gi + 1)
            p.op("vector", lambda e,
                 o=aoT[:, qt, kvh * 512:(kvh + 1) * 512],
                 a=ps_aoL[gi % 2][:, :], b=rb4[:, gi % 2, :]:
                 e.tensor_tensor(o, a, b, ALU.mult), "aomul")

        def _expg(gi):
            qt, kvh = groups[gi]
            for b9 in range(9):
                qb = gi * 9 + b9
                if b9 == 0:
                    p.wait("scalar", "mk0", gi + 1)
                    if gi >= 2:
                        p.wait("scalar", "avm", gi - 1)
                elif b9 == 8:
                    p.wait("scalar", "mk8", gi + 1)
                else:
                    p.wait("scalar", "smm", qb + 1)
                p.op("scalar", lambda e, o=p_sb[:, gi % 2, b9, :],
                     i=ps_qkL[qb % 2][:, :],
                     b=mbias4[:, kvh * 8 + qt:kvh * 8 + qt + 1]:
                     e.activation(o, i, ACT.Exp, bias=b, scale=SM_SCALE), "pexp")

        def _copyg(gi):
            p.wait("scalar", "rbm", gi + 1)
            if gi >= 2:
                p.wait("scalar", "aomul", gi - 1)
            p.op("scalar", lambda e, o=rb4[:, gi % 2, :],
                 i=ps_rb[:, :]: e.copy(o, i), "rbc")

        for gi in range(NG):
            _qkg(gi)
            if gi >= 1:
                _pvg(gi - 1)
            _mskg(gi)
            if gi >= 2:
                _multg(gi - 2)
            if gi >= 1:
                _recipg(gi - 1)
            _expg(gi)
            if gi >= 1:
                _copyg(gi - 1)
        _pvg(NG - 1)
        _multg(NG - 2)
        _recipg(NG - 1)
        _copyg(NG - 1)
        _multg(NG - 1)
        p.wait("gpsimd", "smm", 9 * NG)
        p.wait("gpsimd", "avm", NG)
        for q in range(4):
            if q >= 2:
                p.wait("gpsimd", "opr", (q - 1) * TT)
            for ch in range(4):
                r0 = (q * 4 + ch) * 128
                for hp in range(2):
                    p.dma("gpsimd", "wo",
                          woQ[hp * 64:(hp + 1) * 64, q % 2, ch, :],
                          wo1_in[r0 + hp * 64:r0 + (hp + 1) * 64, :])

    with contextlib.ExitStack() as s5:
        s5o = contextlib.ExitStack()
        ps_oL = [p.ps("ps_o", [128, D], s5o), p.ps("ps_o", [128, D], s5o)]
        p.wait("tensor", "aomul", p.val("aomul"))
        for q in range(4):
            p.wait("tensor", "wo", 128 * (q + 1))
            for tt in range(TT):
                gi = q * TT + tt
                ps_o = ps_oL[gi % 2]
                if gi >= 2:
                    p.wait("tensor", "ore", gi - 1)
                for dc4 in range(4):
                    for oc in range(4):
                        p.mm(ps_o[:, oc * 512:(oc + 1) * 512],
                             aoT[:, tt, (q * 4 + dc4) * 128:(q * 4 + dc4 + 1) * 128],
                             woQ[:, q % 2, dc4, oc * 512:(oc + 1) * 512],
                             dc4 == 0, dc4 == 3,
                             sem=None if not (dc4 == 3 and oc == 3) else "opr")
                p.wait("vector", "opr", gi + 1)
                p.op("vector", lambda e, o=x_tm[:, tt, :], i=ps_o[:, :],
                     x=x_tm[:, tt, :]:
                     e.tensor_tensor(o, i, x, ALU.add), "ore")
        s5o.close()
        for tt in range(TT):
            p.wait("vector", "ore", p.val("ore"))
            layer_norm(p, x_tm[:, tt, :], sq_scr[:, :], ssum[:, :], ssq[:, :],
                       mean[:, :], m2[:, :], var[:, :], rstd[:, :], "lne")

        rtr = p.sb("rtr", [128, DC, E1], BF16, s5)
        lgt = p.sb("lgt", [128, TT, E1], F32, s5)
        m1c = p.sb("m1c", [128, 1], F32, s5)
        m2cc = p.sb("m2cc", [128, 1], F32, s5)
        oh1 = p.sb("oh1", [128, E1], F32, s5)
        oh2 = p.sb("oh2", [128, E1], F32, s5)
        scrM = p.sb("scrM", [128, E1], F32, s5)
        cnt_s = p.sb("cnt_s", [1, E1], F32, s5)
        ps_t2 = p.ps("ps_t2", [128, 128], s5)
        ps_lg = p.ps("ps_lg", [128, E1], s5)
        ps_cnt = p.ps("ps_cnt", [1, E1], s5)
        p.wait("gpsimd", "lne", p.val("lne"))
        p.dma("gpsimd", "rtl", rtr[:, :, :], rtr_in.rearrange("(c p) w -> p c w", p=128))
        p.wait("tensor", "lne", p.val("lne"))
        for tt in range(TT):
            for dc in range(DC):
                p.wait("tensor", "x3e", p.val("x3e"))
                p.trans(ps_t2[:, :], x_tm[:, tt, dc * 128:(dc + 1) * 128], idf[:, :],
                        x3T[:, dc, tt * 128:(tt + 1) * 128], "x3e")
        p.wait("tensor", "x3e", p.val("x3e"))
        p.wait("tensor", "rtl", p.val("rtl"))
        for tt in range(TT):
            if tt:
                p.wait("tensor", "lge", p.val("lge"))
            for dc in range(DC):
                p.mm(ps_lg[:, :], x3T[:, dc, tt * 128:(tt + 1) * 128], rtr[:, dc, :],
                     dc == 0, dc == DC - 1, sem=None if dc < DC - 1 else "lgm")
            p.wait("vector", "lgm", p.val("lgm"))
            p.op("vector", lambda e, o=lgt[:, tt, :], i=ps_lg[:, :]:
                 e.tensor_copy(o, i), "lge")
        for tt in range(TT):
            p.wait("vector", "lge", p.val("lge"))
            p.op("vector", lambda e, o=m1c[:, :], i=lgt[:, tt, :]:
                 e.reduce_max(o, i, AX.X), "tp")
            p.wait("vector", "tp")
            p.op("vector", lambda e, o=oh1[:, :], i=lgt[:, tt, :], m=m1c[:, :]:
                 e.tensor_scalar(o, i, m, None, ALU.is_equal), "tp")
            p.wait("vector", "tp")
            p.op("vector", lambda e, o=scrM[:, :], i=oh1[:, :], x=lgt[:, tt, :]:
                 e.scalar_tensor_tensor(o, i, NEG, x, ALU.mult, ALU.add), "tp")
            p.wait("vector", "tp")
            p.op("vector", lambda e, o=m2cc[:, :], i=scrM[:, :]:
                 e.reduce_max(o, i, AX.X), "tp")
            p.wait("vector", "tp")
            p.op("vector", lambda e, o=oh2[:, :], i=scrM[:, :], m=m2cc[:, :]:
                 e.tensor_scalar(o, i, m, None, ALU.is_equal), "tp")
            p.wait("vector", "tp")
            p.op("vector", lambda e, o=oh1[:, :], a=oh1[:, :], b=oh2[:, :]:
                 e.tensor_tensor(o, a, b, ALU.add), "tp")
            p.wait("tensor", "tp", p.val("tp"))
            p.mm(ps_cnt[:, :], onescol[:, :], oh1[:, :], tt == 0, tt == TT - 1,
                 sem="cm")
            p.wait("vector", "cm", p.val("cm"))
        p.wait("vector", "cm", p.val("cm"))
        p.op("vector", lambda e, o=cnt_s[:, :], i=ps_cnt[:, :]:
             e.tensor_copy(o, i), "cme")

    for tt in range(TT):
        p.wait("sync", "lne", 9 * (tt + 1))
        p.dma("sync", "st", x3_out[tt * 128:(tt + 1) * 128, :], x_tm[:, tt, :])
    p.wait("sync", "cme", p.val("cme"))
    p.dma("sync", "st", cnt_out[:, :], cnt_s[:, :])
    p.wait("sync", "st", p.val("st"))
    return p.finish()


def _prep_lB(inputs, x1, s_all, nonce):
    import ml_dtypes
    bfd = ml_dtypes.bfloat16
    neg = np.float32(NEG)
    ii, jj = np.meshgrid(np.arange(128), np.arange(128), indexing="ij")
    shared = {
        "sall_in": np.ascontiguousarray(s_all),
        "ew_in": np.asarray(inputs["L0_ffn_expert_w"], np.float32),
        "eb_in": np.asarray(inputs["L0_ffn_expert_b"], np.float32).reshape(1, D),
        "wq1_in": np.ascontiguousarray(np.asarray(inputs["L1_attn_wq"], np.float32).T),
        "wk1_in": np.ascontiguousarray(np.asarray(inputs["L1_attn_wk"], np.float32).T),
        "wv1_in": np.ascontiguousarray(np.asarray(inputs["L1_attn_wv"], np.float32).T),
        "qnw_in": np.asarray(inputs["L1_attn_q_norm_w"], np.float32).reshape(1, DH),
        "knw_in": np.asarray(inputs["L1_attn_k_norm_w"], np.float32).reshape(1, DH),
        "wo1_in": np.ascontiguousarray(np.asarray(inputs["L1_attn_wo"], np.float32).T),
        "dmask_in": np.ascontiguousarray(np.tile(
            np.where(ii >= jj, 0.0, neg).astype(np.float32).T, (1, 4))),
        "bmask_in": np.ascontiguousarray(np.tile(
            np.where(jj > ii, 0.0, neg).astype(np.float32).T, (1, 4))),
        "idb_in": np.eye(128, dtype=np.float32).astype(bfd),
        "idf_in": np.eye(128, dtype=np.float32),
        "rtr_in": np.asarray(inputs["L1_ffn_router"], np.float32),
        "ones_in": np.ones((1, 128), np.float32),
        "nonce_in": np.array([[nonce]], np.int32),
    }
    in_maps = []
    for c in range(NCORE):
        half = c % 2
        m = dict(shared)
        m["x_in"] = x1[c * T:(c + 1) * T]
        m["xT_in"] = np.ascontiguousarray(x1[c * T:(c + 1) * T].T)
        m["sown_in"] = s_all[c * T:(c + 1) * T]
        cos, sin = _rope_tables(c)
        m["cos_in"] = cos
        m["sin_in"] = sin
        von = np.zeros((2 * T,), np.float32)
        if half == 0:
            von[T:] = 1.0
        else:
            von[:] = 1.0
        m["vones_in"] = np.ascontiguousarray(von.reshape(16, 128).T).astype(bfd)
        in_maps.append(m)
    return in_maps


def run_lB(inputs, x1, s_all, nonce=1234567):
    nc = build_lB()
    in_maps = _prep_lB(inputs, x1, s_all, nonce)
    res = run_bass_kernel_spmd(nc, in_maps, core_ids=list(range(NCORE)))
    x3 = np.concatenate([res.results[c]["x3_out"] for c in range(NCORE)], 0)
    cnt = np.stack([res.results[c]["cnt_out"][0] for c in range(NCORE)], 0)
    return x3, cnt


import time as _time


def kernel(**inputs):
    x1, s = run_l1(inputs)
    x3, cnt = run_lB(inputs, x1, s, nonce=(int(_time.time() * 1000) & 0x7FFFFFFF) | 1)
    x4 = run_l5(inputs, x3, cnt)
    return x4.reshape(4, 2048, D)


# version 07 (final)
# speedup vs baseline: 1.3773x; 1.3069x over previous
import contextlib
import numpy as np

import concourse.bass as bass
import concourse.mybir as mybir
from concourse.bass_utils import run_bass_kernel_spmd

F32 = mybir.dt.float32
BF16 = mybir.dt.bfloat16
I32 = mybir.dt.int32
ALU = mybir.AluOpType
ACT = mybir.ActivationFunctionType
AX = mybir.AxisListType

NCORE = 8
T = 1024
TT = 8
D = 2048
DC = 16
H0 = 16
M0 = 64
H1 = 16
KVH = 4
DH = 128
SM_SCALE = DH ** -0.5
NE0 = 16
K0_TOP = 512
E1 = 8
I1 = 1408
J1 = 5632
CAP1 = 2560
CAPL = 384
NEG = -1.0e30


class Prog:

    ENGS = ("sync", "tensor", "vector", "scalar", "gpsimd")

    def __init__(self, nc):
        self.nc = nc
        self.ops = {k: [] for k in self.ENGS}
        self.sems = {}
        self.cnt = {}
        self.names = {}
        self._sem_prod = {}
        self._rec = None
        self.stack = contextlib.ExitStack()

    def _uniq(self, name):
        n = self.names.get(name, 0)
        self.names[name] = n + 1
        return name if n == 0 else "%s_%d" % (name, n)

    def sb(self, name, shape, dtype, stack=None):
        return (stack or self.stack).enter_context(
            self.nc.sbuf_tensor(self._uniq(name), list(shape), dtype))

    def sba(self, name, shape, dtype, offset):
        return self.nc.alloc_sbuf_tensor_at(self._uniq(name), list(shape), dtype,
                                            offset=offset)

    def ps(self, name, shape, stack=None, dtype=F32):
        return (stack or self.stack).enter_context(
            self.nc.psum_tensor(self._uniq(name), list(shape), dtype))

    def sem(self, name):
        if name not in self.sems:
            self.sems[name] = self.stack.enter_context(self.nc.semaphore(name))
            self.cnt[name] = 0
        return self.sems[name]

    def inc(self, name, k=1):
        s = self.sem(name)
        self.cnt[name] += k
        return s

    def val(self, name):
        self.sem(name)
        return self.cnt[name]

    def on(self, eng, fn):
        self.ops[eng].append(fn)
        if getattr(self, "_rec", None) is not None:
            self._rec_engs.add(eng)

    def skip_begin(self, cond_ap):
        assert getattr(self, "_rec", None) is None
        self._rec = dict(self.cnt)
        self._rec_engs = set()
        self._rec_marks = {k: len(v) for k, v in self.ops.items()}
        self._rec_cond = cond_ap

    def skip_end(self, gpsimd_in=False):
        pre = self._rec
        engs = self._rec_engs
        marks = self._rec_marks
        cond = self._rec_cond
        self._rec = None
        deltas = {n: self.cnt[n] - pre.get(n, 0) for n in self.cnt
                  if self.cnt[n] != pre.get(n, 0)}
        prod = self._sem_prod
        for eng in engs:
            if eng == "gpsimd" and not gpsimd_in:
                continue
            body = self.ops[eng][marks[eng]:]
            del self.ops[eng][marks[eng]:]
            if eng == "gpsimd":
                comp = []
            else:
                comp = [(self.sems[n], pre.get(n, 0), d) for n, d in deltas.items()
                        if prod.get(n) == eng]

            def wrapped(e, body=body, comp=comp, cond=cond):
                with e.register() as reg:
                    e.reg_load(reg, cond)
                    with e.If(reg):
                        for f in body:
                            f(e)
                    if comp:
                        with e.Else():
                            for sh_, pv_, d_ in comp:
                                if pv_ > 0:
                                    e.wait_ge(sh_, pv_)
                                e.sem_inc(sh_, d_)
                    e.end_ifs()
            self.ops[eng].append(wrapped)

    def wait(self, eng, name, v=None):
        s = self.sem(name)
        v = self.cnt[name] if v is None else v
        if v > 0:
            self.on(eng, lambda e, s=s, v=v: e.wait_ge(s, v))

    def dma(self, eng, sem, out_ap, in_ap):
        self._sem_prod[sem] = eng
        s = self.inc(sem, 16)
        self.on(eng, lambda e, o=out_ap, i=in_ap, s=s:
                e.dma_start(out=o, in_=i).then_inc(s, 16))

    def mm(self, out_ap, lhsT_ap, rhs_ap, start, stop, sem=None):
        if sem is not None:
            self._sem_prod[sem] = "tensor"
        if sem is None:
            self.on("tensor", lambda e, o=out_ap, l=lhsT_ap, r=rhs_ap, st=start,
                    sp=stop: e.matmul(o, l, r, start=st, stop=sp))
        else:
            s = self.inc(sem)
            self.on("tensor", lambda e, o=out_ap, l=lhsT_ap, r=rhs_ap, st=start,
                    sp=stop, s=s: e.matmul(o, l, r, start=st, stop=sp).then_inc(s, 1))

    def op(self, eng, fn, sem=None):
        if sem is not None:
            self._sem_prod[sem] = eng
        if sem is None:
            self.on(eng, lambda e, f=fn: f(e))
        else:
            s = self.inc(sem)
            self.on(eng, lambda e, f=fn, s=s: f(e).then_inc(s, 1))

    def trans(self, ps_ap, in_ap, ident_ap, out_ap, sem, evac_eng="vector"):
        self.op("tensor", lambda e, o=ps_ap, i=in_ap, idn=ident_ap:
                e.transpose(o, i, idn), sem + "_t")
        self.wait(evac_eng, sem + "_t")
        self.op(evac_eng, lambda e, o=out_ap, i=ps_ap: e.tensor_copy(o, i), sem)

    def finish(self):
        nc = self.nc
        with nc.Block() as block:
            @block.sync
            def _(e):
                for f in self.ops["sync"]:
                    f(e)

            @block.tensor
            def _(e):
                for f in self.ops["tensor"]:
                    f(e)

            @block.vector
            def _(e):
                for f in self.ops["vector"]:
                    f(e)

            @block.scalar
            def _(e):
                for f in self.ops["scalar"]:
                    f(e)

            @block.gpsimd
            def _(e):
                for f in self.ops["gpsimd"]:
                    f(e)
        self.stack.close()
        return nc


def _new_prog(name, detect=True):
    nc = bass.Bass("TRN2", target_bir_lowering=False,
                   detect_race_conditions=detect)
    return Prog(nc)


def elu1(p, psum_ap, out_ap, scr1, sem):
    p.op("vector", lambda e, i=psum_ap, o=scr1: e.tensor_scalar_min(o, i, 0.0), sem)
    p.wait("scalar", sem)
    p.op("scalar", lambda e, i=scr1: e.activation(i, i, ACT.Exp), sem)
    p.wait("vector", sem)
    p.op("vector", lambda e, i=psum_ap, x=scr1, o=out_ap:
         e.scalar_tensor_tensor(o, i, 0.0, x, ALU.max, ALU.add), sem)


def layer_norm(p, x_ap, sq_scr, ssum, ssq, mean, m2, var, rstd, sem):
    p.op("vector", lambda e, o=ssum, i=x_ap: e.reduce_sum(o, i, AX.X), sem)
    p.wait("scalar", sem)
    p.op("scalar", lambda e, o=sq_scr, i=x_ap, a=ssq:
         e.activation(o, i, ACT.Square, accum_out=a), sem)
    p.wait("vector", sem)
    p.op("vector", lambda e, o=mean, i=ssum: e.tensor_scalar_mul(o, i, 1.0 / D), sem)
    p.wait("vector", sem)
    p.op("vector", lambda e, o=m2, i=mean, b=mean:
         e.tensor_tensor(o, i, b, ALU.mult), sem)
    p.wait("vector", sem)
    p.op("vector", lambda e, o=var, i=ssq, b=m2:
         e.tensor_scalar(o, i, 1.0 / D, b, ALU.mult, ALU.subtract), sem)
    p.wait("vector", sem)
    p.op("vector", lambda e, o=var, i=var: e.tensor_scalar_add(o, i, 1e-5), sem)
    p.wait("vector", sem)
    p.op("vector", lambda e, o=rstd, i=var: e.reciprocal(o, i), sem)
    p.wait("scalar", sem)
    p.op("scalar", lambda e, o=rstd, i=rstd: e.activation(o, i, ACT.Sqrt), sem)
    p.wait("vector", sem)
    p.op("vector", lambda e, o=x_ap, i=x_ap, a=mean, b=rstd:
         e.tensor_scalar(o, i, a, b, ALU.subtract, ALU.mult), sem)



def build_l1(upto=8):
    p = _new_prog("l1")
    nc = p.nc
    dt = lambda n, sh, ty=F32, kind="ExternalInput": nc.dram_tensor(
        n, list(sh), ty, kind=kind).ap()

    x_in = dt("x_in", [T, D])
    xT_in = dt("xT_in", [D, T])
    xoT_in = dt("xoT_in", [D, T])
    wq_in = dt("wq_in", [H0, 128, DC, DH])
    wk_in = dt("wk_in", [H0, 128, DC, DH])
    wv_in = dt("wv_in", [H0, 128, DC, DH])
    wo_in = dt("wo_in", [D, D])
    bq_in = dt("bq_in", [DH, DC])
    bk_in = dt("bk_in", [DH, DC])
    bv_in = dt("bv_in", [DH, DC])
    bo_in = dt("bo_in", [1, D])
    wmq_in = dt("wmq_in", [DH, M0])
    wmk_in = dt("wmk_in", [DH, M0])
    rw_in = dt("rw_in", [D, NE0])
    rb_in = dt("rb_in", [NE0, 1])
    ones_in = dt("ones_in", [1, 128])
    idf_in = dt("idf_in", [128, 128])

    x1_out = dt("x1_out", [T, D], F32, "ExternalOutput")
    s_out = dt("s_out", [T, NE0], F32, "ExternalOutput")

    x_tm = p.sb("x_tm", [128, TT, D], F32)
    qm = p.sb("qm", [128, H0 // 2, T], BF16)
    km = p.sb("km", [128, H0, 2 * T // 128 * M0], BF16)
    kvb = p.sb("kvb", [128, 8 * 256], BF16)
    ksblk = p.sb("ksblk", [128, H0, H0], BF16)
    rnorm = p.sb("rnorm", [128, TT, 16], F32)
    bq = p.sb("bq", [128, DC], F32)
    bk = p.sb("bk", [128, DC], F32)
    bv = p.sb("bv", [128, DC], F32)
    wmq = p.sb("wmq", [128, M0], BF16)
    wmk = p.sb("wmk", [128, M0], BF16)
    idf = p.sb("idf", [128, 128], F32)
    idb = p.sb("idb", [128, 128], BF16)
    scr1 = p.sb("scr1", [128, T], F32)
    ssum = p.sb("ssum", [128, 1], F32)
    ssq = p.sb("ssq", [128, 1], F32)
    mean = p.sb("mean", [128, 1], F32)
    m2 = p.sb("m2", [128, 1], F32)
    var = p.sb("var", [128, 1], F32)
    rstd = p.sb("rstd", [128, 1], F32)
    s_tm = p.sb("s_tm", [128, TT, NE0], F32)

    p.dma("sync", "ld", x_tm[:, :, :], x_in.rearrange("(tt p) d -> p tt d", p=128))
    p.dma("sync", "ld", bq[:, :], bq_in[:, :])
    p.dma("sync", "ld", bk[:, :], bk_in[:, :])
    p.dma("sync", "ld", bv[:, :], bv_in[:, :])
    p.dma("sync", "ld", idf[:, :], idf_in[:, :])
    p.dma("gpsimd", "ldg", wmq[:, :], wmq_in[:, :])
    p.dma("gpsimd", "ldg", wmk[:, :], wmk_in[:, :])
    p.wait("vector", "ld")
    p.op("vector", lambda e, o=idb[:, :], i=idf[:, :]: e.tensor_copy(o, i), "idb")

    with contextlib.ExitStack() as sc1:
      if upto >= 2:
        xT = p.sb("xT", [128, DC, T], BF16, sc1)
        xoT = p.sb("xoT", [128, DC, T], BF16, sc1)
        wbuf = p.sb("wbuf", [128, 2, DC, DH], BF16, sc1)
        kf = p.sb("kf", [128, 2 * T], BF16, sc1)

        p.dma("gpsimd", "ldg", xT[:, :, :], xT_in.rearrange("(c p) t -> p c t", p=128))
        p.dma("gpsimd", "ldg", xoT[:, :, :], xoT_in.rearrange("(c p) t -> p c t", p=128))
        ldg_all = p.val("ldg")

        qps = contextlib.ExitStack()
        ps_q = p.ps("ps_q", [128, T], qps)
        ps_qm = p.ps("ps_qm", [128, T], qps)

        for h in range(H0):
            buf = h % 2
            if h >= 2:
                p.wait("gpsimd", "qpr", h - 1)
            p.dma("gpsimd", "wq%d" % buf, wbuf[:, buf, :, :], wq_in[h])
            p.wait("tensor", "ld", 5 * 16)
            p.wait("tensor", "ldg", ldg_all)
            p.wait("tensor", "wq%d" % buf, p.val("wq%d" % buf))
            if h >= 1:
                p.wait("tensor", "qfe", h)
            for dc in range(DC):
                p.mm(ps_q[:, 0:512], wbuf[:, buf, dc, :], xT[:, dc, 0:512],
                     dc == 0, dc == DC - 1)
                p.mm(ps_q[:, 512:1024], wbuf[:, buf, dc, :], xT[:, dc, 512:1024],
                     dc == 0, dc == DC - 1,
                     sem=None if dc < DC - 1 else "qpr")
            p.wait("scalar", "qpr", h + 1)
            p.op("scalar", lambda e, o=kf[:, 0:T], i=ps_q[:, :], b=bq[:, h:h + 1]:
                 e.activation(o, i, ACT.Identity, bias=b), "qfe")
            base = (h % 2) * 64
            p.wait("tensor", "qfe", p.val("qfe"))
            if h % 2 == 0 and h >= 2:
                p.wait("tensor", "qel", 3 * (h // 2))
            p.mm(ps_qm[base:base + 64, 0:512], wmq[:, :], kf[:, 0:512], True, True)
            p.mm(ps_qm[base:base + 64, 512:1024], wmq[:, :], kf[:, 512:1024],
                 True, True, sem="qmm")
            if h % 2 == 1:
                pair = h // 2
                p.wait("vector", "qmm", p.val("qmm"))
                elu1(p, ps_qm[:, :], qm[:, pair, :], scr1[:, 0:T], "qel")

        qps.close()
        kps = contextlib.ExitStack()
        ps_k = p.ps("ps_k", [128, 2 * T], kps)
        ps_km = p.ps("ps_km", [128, H0 * M0], kps)

        p.wait("tensor", "qel", p.val("qel"))
        for h in range(H0):
            buf = h % 2
            if h < 2:
                p.wait("gpsimd", "qpr", H0)
            else:
                p.wait("gpsimd", "kpr", h - 1)
            p.dma("gpsimd", "wk%d" % buf, wbuf[:, buf, :, :], wk_in[h])
            p.wait("tensor", "wk%d" % buf, p.val("wk%d" % buf))
            if h >= 1:
                p.wait("tensor", "kfe", h)
            for dc in range(DC):
                for q4 in range(4):
                    src = xT if q4 < 2 else xoT
                    sl = slice((q4 % 2) * 512, (q4 % 2) * 512 + 512)
                    p.mm(ps_k[:, q4 * 512:(q4 + 1) * 512], wbuf[:, buf, dc, :],
                         src[:, dc, sl], dc == 0, dc == DC - 1,
                         sem=None if not (dc == DC - 1 and q4 == 3) else "kpr")
            p.wait("scalar", "kpr", h + 1)
            p.op("scalar", lambda e, o=kf[:, :], i=ps_k[:, :], b=bk[:, h:h + 1]:
                 e.activation(o, i, ACT.Identity, bias=b), "kfe")
            p.wait("tensor", "kfe", p.val("kfe"))
            if h >= 1:
                p.wait("tensor", "kel", 3 * h)
            for t16 in range(16):
                p.mm(ps_km[:, t16 * M0:(t16 + 1) * M0],
                     kf[:, t16 * 128:(t16 + 1) * 128], wmk[:, :], True, True,
                     sem=None if t16 < 15 else "kmm")
            p.wait("vector", "kmm", p.val("kmm"))
            elu1(p, ps_km[:, :], km[:, h, :], scr1[:, 0:H0 * M0], "kel")

        kps.close()

    with contextlib.ExitStack() as sc2:
        xT2 = p.sb("xT2", [128, DC, T], BF16, sc2)
        xoT2 = p.sb("xoT2", [128, DC, T], BF16, sc2)
        wvb = p.sb("wvb", [128, 2, DC, DH], BF16, sc2)
        vfm = p.sb("vfm", [128, 2 * T], BF16, sc2)
        vaug = p.sb("vaug", [128, 16, 129], BF16, sc2)

        p.wait("gpsimd", "kmm", p.val("kmm"))
        p.dma("gpsimd", "ldg2", xT2[:, :, :], xT_in.rearrange("(c p) t -> p c t", p=128))
        p.dma("gpsimd", "ldg2", xoT2[:, :, :], xoT_in.rearrange("(c p) t -> p c t", p=128))
        ldg2_all = p.val("ldg2")

        ps_v = p.ps("ps_v", [128, T], sc2)
        ps_vt = p.ps("ps_vt", [128, 128], sc2, BF16)
        ps_kv = p.ps("ps_kv", [128, 8 * 256], sc2)

        p.wait("tensor", "kel", p.val("kel"))
        for h in range(H0):
            buf = h % 2
            if h >= 2:
                p.wait("gpsimd", "vpr", 2 * (h - 1))
            p.dma("gpsimd", "wv%d" % buf, wvb[:, buf, :, :], wv_in[h])
            p.wait("tensor", "wv%d" % buf, p.val("wv%d" % buf))
            p.wait("tensor", "ldg2", ldg2_all)
            for vh in range(2):
                src = xT2 if vh == 0 else xoT2
                if vh == 1:
                    p.wait("tensor", "vfe", p.val("vfe"))
                for dc in range(DC):
                    for q2 in range(2):
                        sl = slice(q2 * 512, q2 * 512 + 512)
                        p.mm(ps_v[:, q2 * 512:(q2 + 1) * 512], wvb[:, buf, dc, :],
                             src[:, dc, sl], dc == 0, dc == DC - 1,
                             sem=None if not (dc == DC - 1 and q2 == 1) else "vpr")
                p.wait("scalar", "vpr", 2 * h + vh + 1)
                p.op("scalar", lambda e, o=vfm[:, vh * T:(vh + 1) * T], i=ps_v[:, :],
                     b=bv[:, h:h + 1]:
                     e.activation(o, i, ACT.Identity, bias=b), "vfe")
            p.wait("tensor", "vfe", p.val("vfe"))
            p.wait("tensor", "idb", 1)
            p.wait("vector", "kvm", p.val("kvm"))
            for t16 in range(16):
                p.wait("tensor", "vte", p.val("vte"))
                p.trans(ps_vt[:, :], vfm[:, t16 * 128:(t16 + 1) * 128], idb[:, :],
                        vaug[:, t16, 0:128], "vte")
            p.op("vector", lambda e, o=vaug[:, :, 128:129]: e.memset(o, 1.0), "vte")
            base = (h % 2) * 64
            fsl = slice((h // 2) * 256, (h // 2) * 256 + 129)
            p.wait("tensor", "vte", p.val("vte"))
            for t16 in range(16):
                p.mm(ps_kv[base:base + 64, fsl], km[:, h, t16 * M0:(t16 + 1) * M0],
                     vaug[:, t16, :], t16 == 0, t16 == 15,
                     sem=None if t16 < 15 else "kvm")
        p.wait("vector", "kvm", p.val("kvm"))
        p.op("vector", lambda e,
             o=kvb[:, :].rearrange("p (g s) -> p g s", g=8)[:, :, 0:129],
             i=ps_kv[:, :].rearrange("p (g s) -> p g s", g=8)[:, :, 0:129]:
             e.tensor_copy(o, i), "kve")

    p.wait("vector", "kve", p.val("kve"))
    p.op("vector", lambda e, o=ksblk[:, :, :]: e.memset(o, 0.0), "ksb")
    p.wait("vector", "ksb")
    for h in range(H0):
        base = (h % 2) * 64
        col = (h // 2) * 256 + 128
        p.op("vector", lambda e, o=ksblk[base:base + 64, h, h:h + 1],
             i=kvb[base:base + 64, col:col + 1]: e.tensor_copy(o, i), "ksb")

    with contextlib.ExitStack() as sc3:
        normf = p.sb("normf", [16, T], F32, sc3)
        eps6 = p.sb("eps6", [16, 1], F32, sc3)
        p.op("vector", lambda e, o=eps6[:, :]: e.memset(o, 1e-6), "eps6")
        ps_n = p.ps("ps_n", [16, T], sc3)
        ps_nt = p.ps("ps_nt", [128, 16], sc3)
        p.wait("tensor", "ksb", p.val("ksb"))
        p.wait("tensor", "qel", p.val("qel"))
        for h in range(H0):
            pair = h // 2
            for th in range(2):
                p.mm(ps_n[:, th * 512:(th + 1) * 512], ksblk[:, h, :],
                     qm[:, pair, th * 512:(th + 1) * 512],
                     h == 0, h == H0 - 1,
                     sem=None if not (h == H0 - 1 and th == 1) else "nrm")
        p.wait("scalar", "nrm", p.val("nrm"))
        p.wait("scalar", "eps6", p.val("eps6"))
        p.op("scalar", lambda e, o=normf[:, :], i=ps_n[:, :], b=eps6[:, :]: e.activation(o, i, ACT.Identity, bias=b), "nrme")
        p.wait("tensor", "nrme", p.val("nrme"))
        for tt in range(TT):
            p.wait("tensor", "nte", p.val("nte"))
            p.trans(ps_nt[:, :], normf[:, tt * 128:(tt + 1) * 128], idf[0:16, 0:16],
                    rnorm[:, tt, :], "nte")
        p.wait("vector", "nte", p.val("nte"))
        p.wait("vector", "nte", p.val("nte"))
        p.op("vector", lambda e, o=rnorm[:, :, :], i=rnorm[:, :, :]:
             e.reciprocal(o, i), "ntr")

    with contextlib.ExitStack() as sc4:
        woT = p.sb("woT", [128, DC, D], BF16, sc4)
        aoT = p.sb("aoT", [128, H0, 128], BF16, sc4)
        ao_s = p.sb("ao_s", [128, 128], BF16, sc4)
        sq_scr = p.sb("sq_scr", [128, D], F32, sc4)
        bo = p.sb("bo", [1, D], F32, sc4)
        onesf = p.sb("onesf", [1, 128], F32, sc4)
        p.wait("sync", "ntr", p.val("ntr"))
        p.dma("sync", "ld2", bo[:, :], bo_in[:, :])
        p.dma("sync", "ld2", onesf[:, :], ones_in[:, :])
        p.wait("gpsimd", "ntr", p.val("ntr"))
        p.dma("gpsimd", "wo", woT[:, :, :], wo_in.rearrange("(c p) o -> p c o", p=128))

        ps_ao = p.ps("ps_ao", [128, 128], sc4)
        ps_at = p.ps("ps_at", [128, 128], sc4, BF16)
        ps_o = p.ps("ps_o", [128, D], sc4)

        p.wait("tensor", "ntr", p.val("ntr"))
        p.wait("vector", "ntr", p.val("ntr"))
        for tt in range(TT):
            for h in range(H0):
                base = (h % 2) * 64
                pair = h // 2
                fsl = slice((h // 2) * 256, (h // 2) * 256 + 128)
                g = tt * H0 + h
                if g >= 1:
                    p.wait("tensor", "aoe", g)
                p.mm(ps_ao[:, :], qm[base:base + 64, pair, tt * 128:(tt + 1) * 128],
                     kvb[base:base + 64, fsl], True, True, sem="aom")
                p.wait("vector", "aom", p.val("aom"))
                p.op("vector", lambda e, o=ao_s[:, :], i=ps_ao[:, :],
                     r=rnorm[:, tt, h:h + 1]: e.tensor_scalar_mul(o, i, r), "aoe")
                p.wait("tensor", "aoe", p.val("aoe"))
                p.wait("tensor", "att", p.val("att"))
                p.trans(ps_at[:, :], ao_s[:, :], idb[:, :], aoT[:, h, :], "att")
            p.wait("tensor", "wo", 16)
            p.wait("tensor", "ld2", 32)
            p.wait("tensor", "att", p.val("att"))
            if tt >= 1:
                p.wait("tensor", "ore", tt)
            for oc in range(4):
                p.mm(ps_o[:, oc * 512:(oc + 1) * 512], onesf[:, :],
                     bo[:, oc * 512:(oc + 1) * 512], True, False)
            for dc in range(DC):
                for oc in range(4):
                    p.mm(ps_o[:, oc * 512:(oc + 1) * 512], aoT[:, dc, :],
                         woT[:, dc, oc * 512:(oc + 1) * 512], False, dc == DC - 1,
                         sem=None if not (dc == DC - 1 and oc == 3) else "opr")
            p.wait("vector", "opr", p.val("opr"))
            p.op("vector", lambda e, o=x_tm[:, tt, :], i=ps_o[:, :], x=x_tm[:, tt, :]:
                 e.tensor_tensor(o, i, x, ALU.add), "ore")

        for tt in range(TT):
            p.wait("vector", "ore", p.val("ore"))
            layer_norm(p, x_tm[:, tt, :], sq_scr[:, :], ssum[:, :], ssq[:, :],
                       mean[:, :], m2[:, :], var[:, :], rstd[:, :], "lne")

    with contextlib.ExitStack() as sc5:
        x1T = p.sb("x1T", [128, DC, T], BF16, sc5)
        rw = p.sb("rw", [128, DC, NE0], BF16, sc5)
        rb = p.sb("rb", [16, 1], F32, sc5)
        lgf = p.sb("lgf", [16, T], F32, sc5)
        lgt = p.sb("lgt", [128, TT, 16], F32, sc5)
        sumr = p.sb("sumr", [128, TT], F32, sc5)
        ps_t = p.ps("ps_t", [128, 128], sc5)
        ps_r = p.ps("ps_r", [16, T], sc5)
        ps_rt = p.ps("ps_rt", [128, 16], sc5)

        p.wait("gpsimd", "lne", p.val("lne"))
        p.wait("sync", "lne", p.val("lne"))
        p.dma("gpsimd", "rwl", rw[:, :, :], rw_in.rearrange("(c p) w -> p c w", p=128))
        p.dma("sync", "rbl", rb[:, :], rb_in[:, :])

        p.wait("tensor", "lne", p.val("lne"))
        for tt in range(TT):
            for dc in range(DC):
                p.wait("tensor", "x1e", p.val("x1e"))
                p.trans(ps_t[:, :], x_tm[:, tt, dc * 128:(dc + 1) * 128], idf[:, :],
                        x1T[:, dc, tt * 128:(tt + 1) * 128], "x1e")
        p.wait("tensor", "x1e", p.val("x1e"))
        p.wait("tensor", "rwl", p.val("rwl"))
        for dc in range(DC):
            for th in range(2):
                p.mm(ps_r[:, th * 512:(th + 1) * 512], rw[:, dc, :],
                     x1T[:, dc, th * 512:(th + 1) * 512], dc == 0, dc == DC - 1,
                     sem=None if not (dc == DC - 1 and th == 1) else "rmm")
        p.wait("scalar", "rmm", p.val("rmm"))
        p.wait("scalar", "rbl", 16)
        p.op("scalar", lambda e, o=lgf[:, :], i=ps_r[:, :], b=rb[:, :]:
             e.activation(o, i, ACT.Identity, bias=b), "rme")
        p.wait("tensor", "rme", p.val("rme"))
        for tt in range(TT):
            p.wait("tensor", "rte", p.val("rte"))
            p.trans(ps_rt[:, :], lgf[:, tt * 128:(tt + 1) * 128], idf[0:16, 0:16],
                    lgt[:, tt, :], "rte")
        p.wait("scalar", "rte", p.val("rte"))
        p.op("scalar", lambda e, o=lgt[:, :, :], i=lgt[:, :, :]:
             e.activation(o, i, ACT.Exp), "rex")
        p.wait("vector", "rex", p.val("rex"))
        p.op("vector", lambda e, o=sumr[:, :], i=lgt[:, :, :]:
             e.tensor_reduce(o, i, AX.X, ALU.add), "rsm")
        p.wait("vector", "rsm")
        p.op("vector", lambda e, o=sumr[:, :], i=sumr[:, :]: e.reciprocal(o, i), "rsm")
        p.wait("vector", "rsm")
        for tt in range(TT):
            p.op("vector", lambda e, o=s_tm[:, tt, :], i=lgt[:, tt, :],
                 r=sumr[:, tt:tt + 1]: e.tensor_scalar_mul(o, i, r), "rsc")

    p.wait("sync", "lne", p.val("lne"))
    p.dma("sync", "st", x1_out.rearrange("(tt p) d -> p tt d", p=128), x_tm[:, :, :])
    p.wait("sync", "rsc", p.val("rsc"))
    p.dma("sync", "st", s_out.rearrange("(tt p) w -> p tt w", p=128), s_tm[:, :, :])
    p.wait("sync", "st", p.val("st"))
    return p.finish()



def _prep_l1(inputs):
    x = np.ascontiguousarray(
        np.asarray(inputs["L0_attn_hidden_states"], np.float32).reshape(8 * T, D))
    shared = {
        "wq_in": np.ascontiguousarray(
            np.asarray(inputs["L0_attn_wq"], np.float32).T.reshape(DC, 128, H0, DH)
            .transpose(2, 1, 0, 3)),
        "wk_in": np.ascontiguousarray(
            np.asarray(inputs["L0_attn_wk"], np.float32).T.reshape(DC, 128, H0, DH)
            .transpose(2, 1, 0, 3)),
        "wv_in": np.ascontiguousarray(
            np.asarray(inputs["L0_attn_wv"], np.float32).T.reshape(DC, 128, H0, DH)
            .transpose(2, 1, 0, 3)),
        "wo_in": np.ascontiguousarray(np.asarray(inputs["L0_attn_wo"], np.float32).T),
        "bq_in": np.ascontiguousarray(np.asarray(inputs["L0_attn_bq"], np.float32).reshape(DC, DH).T),
        "bk_in": np.ascontiguousarray(np.asarray(inputs["L0_attn_bk"], np.float32).reshape(DC, DH).T),
        "bv_in": np.ascontiguousarray(np.asarray(inputs["L0_attn_bv"], np.float32).reshape(DC, DH).T),
        "bo_in": np.asarray(inputs["L0_attn_bo"], np.float32).reshape(1, D),
        "wmq_in": np.ascontiguousarray(np.asarray(inputs["L0_attn_wmap_q"], np.float32).T),
        "wmk_in": np.ascontiguousarray(np.asarray(inputs["L0_attn_wmap_k"], np.float32).T),
        "rw_in": np.asarray(inputs["L0_ffn_router_w"], np.float32),
        "rb_in": np.asarray(inputs["L0_ffn_router_b"], np.float32).reshape(NE0, 1),
        "ones_in": np.ones((1, 128), np.float32),
        "idf_in": np.eye(128, dtype=np.float32),
    }
    in_maps = []
    for c in range(NCORE):
        o = c ^ 1
        m = dict(shared)
        m["x_in"] = x[c * T:(c + 1) * T]
        m["xT_in"] = np.ascontiguousarray(x[c * T:(c + 1) * T].T)
        m["xoT_in"] = np.ascontiguousarray(x[o * T:(o + 1) * T].T)
        in_maps.append(m)
    return in_maps


def run_l1(inputs):
    nc = build_l1()
    in_maps = _prep_l1(inputs)
    res = run_bass_kernel_spmd(nc, in_maps, core_ids=list(range(NCORE)))
    x1 = np.concatenate([res.results[c]["x1_out"] for c in range(NCORE)], 0)
    s = np.concatenate([res.results[c]["s_out"] for c in range(NCORE)], 0)
    return x1, s





def build_l2():
    p = _new_prog("l2")
    nc = p.nc
    dt = lambda n, sh, ty=F32, kind="ExternalInput": nc.dram_tensor(
        n, list(sh), ty, kind=kind).ap()

    x_in = dt("x_in", [T, D])
    xT_in = dt("xT_in", [D, T])
    sall_in = dt("sall_in", [8 * T, NE0])
    sown_in = dt("sown_in", [T, NE0])
    ew_in = dt("ew_in", [D, D])
    eb_in = dt("eb_in", [1, D])
    ones_in = dt("ones_in", [1, 128])
    x2_out = dt("x2_out", [T, D], F32, "ExternalOutput")

    x_tm = p.sb("x_tm", [128, TT, D], F32)
    x1T = p.sb("x1T", [128, DC, T], BF16)
    ew = p.sb("ew", [128, DC, D], BF16)
    sthr = p.sb("sthr", [128, 64, NE0], F32)
    sown = p.sb("sown", [128, TT, NE0], F32)
    eb = p.sb("eb", [1, D], F32)
    onesf = p.sb("onesf", [1, 128], F32)
    onescol = p.sb("onescol", [128, 1], F32)
    lo = p.sb("lo", [1, NE0], F32)
    hi = p.sb("hi", [1, NE0], F32)
    mid = p.sb("mid", [1, NE0], F32)
    dd = p.sb("dd", [1, NE0], F32)
    sel = p.sb("sel", [1, NE0], F32)
    cntv = p.sb("cntv", [128, NE0], F32)
    wcol = p.sb("wcol", [128, TT], F32)
    scrA = p.sb("scrA", [128, NE0], F32)
    sq_scr = p.sb("sq_scr", [128, D], F32)
    ssum = p.sb("ssum", [128, 1], F32)
    ssq = p.sb("ssq", [128, 1], F32)
    mean = p.sb("mean", [128, 1], F32)
    m2 = p.sb("m2", [128, 1], F32)
    var = p.sb("var", [128, 1], F32)
    rstd = p.sb("rstd", [128, 1], F32)

    ps_b = p.ps("ps_b", [128, NE0])
    ps_c = p.ps("ps_c", [1, NE0])
    ps_tau = p.ps("ps_tau", [128, NE0])
    ps_z = p.ps("ps_z", [128, D])

    p.dma("sync", "ld", x_tm[:, :, :], x_in.rearrange("(tt p) d -> p tt d", p=128))
    p.dma("sync", "ld", onesf[:, :], ones_in[:, :])
    p.dma("gpsimd", "ldg", x1T[:, :, :], xT_in.rearrange("(c p) t -> p c t", p=128))
    p.dma("gpsimd", "ldg", ew[:, :, :], ew_in.rearrange("(c p) o -> p c o", p=128))

    p.wait("vector", "ld", p.val("ld"))
    p.op("vector", lambda e, o=lo[:, :]: e.memset(o, 0.0), "bi")
    p.op("vector", lambda e, o=hi[:, :]: e.memset(o, 1.0), "bi")
    p.op("vector", lambda e, o=onescol[:, :]: e.memset(o, 1.0), "bi")
    for it in range(42):
        p.wait("vector", "bi")
        p.op("vector", lambda e, o=mid[:, :], a=lo[:, :], b=hi[:, :]:
             e.tensor_tensor(o, a, b, ALU.add), "bi")
        p.wait("vector", "bi")
        p.op("vector", lambda e, o=mid[:, :]: e.tensor_scalar_mul(o, o, 0.5), "bi")
        p.wait("tensor", "bi", p.val("bi"))
        if it:
            p.wait("tensor", "gei", p.val("gei"))
        p.mm(ps_b[:, :], onesf[:, :], mid[:, :], True, True, sem="bmm")
        p.wait("vector", "bmm", p.val("bmm"))
        p.op("vector", lambda e, o=ge[:, :, :], a=sthr[:, :, :],
             b=ps_b[:, :].unsqueeze(1).broadcast_to([128, 64, NE0]):
             e.tensor_tensor(o, a, b, ALU.is_ge), "gei")
        p.wait("vector", "gei")
        p.op("vector", lambda e, o=cntv[:, :],
             i=ge[:, :, :].rearrange("p j e -> p e j"):
             e.tensor_reduce(o, i, AX.X, ALU.add), "gei")
        p.wait("tensor", "gei", p.val("gei"))
        if it:
            p.wait("tensor", "seli", p.val("seli"))
        p.mm(ps_c[:, :], onescol[:, :], cntv[:, :], True, True, sem="cmm")
        p.wait("vector", "cmm", p.val("cmm"))
        p.op("vector", lambda e, o=sel[:, :], i=ps_c[:, :]:
             e.tensor_single_scalar(o, i, float(K0_TOP), ALU.is_ge), "seli")
        p.wait("vector", "seli")
        p.op("vector", lambda e, o=dd[:, :], a=mid[:, :], b=lo[:, :]:
             e.tensor_tensor(o, a, b, ALU.subtract), "bi")
        p.wait("vector", "bi")
        p.op("vector", lambda e, o=dd[:, :], a=sel[:, :], b=dd[:, :]:
             e.tensor_tensor(o, a, b, ALU.mult), "bi")
        p.wait("vector", "bi")
        p.op("vector", lambda e, o=lo[:, :], a=lo[:, :], b=dd[:, :]:
             e.tensor_tensor(o, a, b, ALU.add), "bi")
        p.wait("vector", "bi")
        p.op("vector", lambda e, o=dd[:, :], a=hi[:, :], b=mid[:, :]:
             e.tensor_tensor(o, a, b, ALU.subtract), "bi")
        p.wait("vector", "bi")
        p.op("vector", lambda e, o=dd[:, :], a=sel[:, :], b=dd[:, :]:
             e.tensor_tensor(o, a, b, ALU.mult), "bi")
        p.wait("vector", "bi")
        p.op("vector", lambda e, o=hi[:, :], a=mid[:, :], b=dd[:, :]:
             e.tensor_tensor(o, a, b, ALU.add), "bi")
    p.wait("tensor", "bi", p.val("bi"))
    p.wait("tensor", "gei", p.val("gei"))
    p.mm(ps_tau[:, :], onesf[:, :], lo[:, :], True, True, sem="tmm")

    p.wait("vector", "tmm", p.val("tmm"))
    for tt in range(TT):
        p.op("vector", lambda e, o=scrA[:, :], a=sown[:, tt, :], b=ps_tau[:, :]:
             e.tensor_tensor(o, a, b, ALU.is_ge), "wt")
        p.wait("vector", "wt")
        p.op("vector", lambda e, o=scrA[:, :], a=sown[:, tt, :], b=scrA[:, :]:
             e.tensor_tensor(o, a, b, ALU.mult), "wt")
        p.wait("vector", "wt")
        p.op("vector", lambda e, o=wcol[:, tt:tt + 1], i=scrA[:, :]:
             e.reduce_sum(o, i, AX.X), "wt")
        p.wait("vector", "wt")

    for tt in range(TT):
        p.wait("tensor", "ld", p.val("ld"))
        p.wait("tensor", "ldg", p.val("ldg"))
        if tt:
            p.wait("tensor", "zev", p.val("zev"))
        for oc in range(4):
            p.mm(ps_z[:, oc * 512:(oc + 1) * 512], onesf[:, :],
                 eb[:, oc * 512:(oc + 1) * 512], True, False)
        for dc in range(DC):
            for oc in range(4):
                p.mm(ps_z[:, oc * 512:(oc + 1) * 512], x1T[:, dc, tt * 128:(tt + 1) * 128],
                     ew[:, dc, oc * 512:(oc + 1) * 512], False, dc == DC - 1,
                     sem=None if not (dc == DC - 1 and oc == 3) else "zmm")
        p.wait("vector", "zmm", p.val("zmm"))
        p.wait("vector", "wt", p.val("wt"))
        for oc in range(4):
            p.op("vector", lambda e, o=x_tm[:, tt, oc * 512:(oc + 1) * 512],
                 i=ps_z[:, oc * 512:(oc + 1) * 512], w=wcol[:, tt:tt + 1],
                 x=x_tm[:, tt, oc * 512:(oc + 1) * 512]:
                 e.scalar_tensor_tensor(o, i, w, x, ALU.mult, ALU.add), "zev")
        p.wait("vector", "zev")
        layer_norm(p, x_tm[:, tt, :], sq_scr[:, :], ssum[:, :], ssq[:, :],
                   mean[:, :], m2[:, :], var[:, :], rstd[:, :], "lne")

    p.wait("sync", "lne", p.val("lne"))
    p.dma("sync", "st", x2_out.rearrange("(tt p) d -> p tt d", p=128), x_tm[:, :, :])
    p.wait("sync", "st", p.val("st"))
    return p.finish()


def _prep_l2(inputs, x1, s_all):
    shared = {
        "sall_in": np.ascontiguousarray(s_all),
        "ew_in": np.asarray(inputs["L0_ffn_expert_w"], np.float32),
        "eb_in": np.asarray(inputs["L0_ffn_expert_b"], np.float32).reshape(1, D),
        "ones_in": np.ones((1, 128), np.float32),
    }
    in_maps = []
    for c in range(NCORE):
        m = dict(shared)
        m["x_in"] = x1[c * T:(c + 1) * T]
        m["xT_in"] = np.ascontiguousarray(x1[c * T:(c + 1) * T].T)
        m["sown_in"] = s_all[c * T:(c + 1) * T]
        in_maps.append(m)
    return in_maps


def run_l2(inputs, x1, s_all):
    nc = build_l2()
    in_maps = _prep_l2(inputs, x1, s_all)
    res = run_bass_kernel_spmd(nc, in_maps, core_ids=list(range(NCORE)))
    return np.concatenate([res.results[c]["x2_out"] for c in range(NCORE)], 0)



def build_l3():
    p = _new_prog("l3")
    nc = p.nc
    dt = lambda n, sh, ty=F32, kind="ExternalInput": nc.dram_tensor(
        n, list(sh), ty, kind=kind).ap()

    xT_in = dt("xT_in", [D, T])
    wq_in = dt("wq_in", [D, D])
    wk_in = dt("wk_in", [D, KVH * DH])
    wv_in = dt("wv_in", [D, KVH * DH])
    qnw_in = dt("qnw_in", [1, DH])
    knw_in = dt("knw_in", [1, DH])
    cos_in = dt("cos_in", [T, 64])
    sin_in = dt("sin_in", [T, 64])
    ones_in = dt("ones_in", [1, 128])
    q_out = dt("q_out", [T, D], BF16, "ExternalOutput")
    k_out = dt("k_out", [T, KVH * DH], BF16, "ExternalOutput")
    v_out = dt("v_out", [T, KVH * DH], BF16, "ExternalOutput")

    xT = p.sb("xT", [128, DC, T], BF16)
    wq = p.sb("wq", [128, DC, D], BF16)
    wk = p.sb("wk", [128, DC, KVH * DH], BF16)
    wv = p.sb("wv", [128, DC, KVH * DH], BF16)
    qnw = p.sb("qnw", [1, DH], F32)
    knw = p.sb("knw", [1, DH], F32)
    onesf = p.sb("onesf", [1, 128], F32)
    cosb = p.sb("cosb", [128, TT, 64], F32)
    sinb = p.sb("sinb", [128, TT, 64], F32)
    wbc = p.sb("wbc", [128, DH], F32)
    kwbc = p.sb("kwbc", [128, DH], F32)
    qtm = p.sb("qtm", [128, D], F32)
    ktm = p.sb("ktm", [128, KVH * DH], F32)
    sq_scr = p.sb("sq_scr", [128, D], F32)
    ssq = p.sb("ssq", [128, H1], F32)
    rr = p.sb("rr", [128, H1], F32)
    scrA = p.sb("scrA", [128, H1 * 64], F32)
    scrB = p.sb("scrB", [128, H1 * 64], F32)
    qbf = p.sb("qbf", [128, D], BF16)
    kbf = p.sb("kbf", [128, KVH * DH], BF16)
    vbf = p.sb("vbf", [128, KVH * DH], BF16)

    ps_q = p.ps("ps_q", [128, D])
    ps_k = p.ps("ps_k", [128, KVH * DH])
    ps_v = p.ps("ps_v", [128, KVH * DH])
    ps_w = p.ps("ps_w", [128, DH])

    p.dma("gpsimd", "ldg", xT[:, :, :], xT_in.rearrange("(c p) t -> p c t", p=128))
    p.dma("gpsimd", "ldg", wq[:, :, :], wq_in.rearrange("(c p) o -> p c o", p=128))
    p.dma("gpsimd", "ldg", wk[:, :, :], wk_in.rearrange("(c p) o -> p c o", p=128))
    p.dma("gpsimd", "ldg", wv[:, :, :], wv_in.rearrange("(c p) o -> p c o", p=128))
    p.dma("sync", "ld", qnw[:, :], qnw_in[:, :])
    p.dma("sync", "ld", knw[:, :], knw_in[:, :])
    p.dma("sync", "ld", onesf[:, :], ones_in[:, :])
    p.dma("sync", "ld", cosb[:, :, :], cos_in.rearrange("(tt p) f -> p tt f", p=128))
    p.dma("sync", "ld", sinb[:, :, :], sin_in.rearrange("(tt p) f -> p tt f", p=128))

    p.wait("tensor", "ld", p.val("ld"))
    p.mm(ps_w[:, :], onesf[:, :], qnw[:, :], True, True, sem="wb")
    p.wait("vector", "wb", p.val("wb"))
    p.op("vector", lambda e, o=wbc[:, :], i=ps_w[:, :]: e.tensor_copy(o, i), "wbe")
    p.wait("tensor", "wbe", p.val("wbe"))
    p.mm(ps_w[:, :], onesf[:, :], knw[:, :], True, True, sem="wb")
    p.wait("vector", "wb", p.val("wb"))
    p.op("vector", lambda e, o=kwbc[:, :], i=ps_w[:, :]: e.tensor_copy(o, i), "wbe")

    def rmsnorm_rope(tt, nh, xv, sqv, ssqv, rv, wv_bc, outbf, sem):
        p.op("scalar", lambda e, o=sqv, i=xv: e.activation(o, i, ACT.Square), sem)
        p.wait("vector", sem)
        p.op("vector", lambda e, o=ssqv, i=sqv.rearrange("p (h f) -> p h f", h=nh):
             e.tensor_reduce(o, i, AX.X, ALU.add), sem)
        p.wait("vector", sem)
        p.op("vector", lambda e, o=rv, i=ssqv:
             e.tensor_scalar(o, i, 1.0 / DH, 1e-6, ALU.mult, ALU.add), sem)
        p.wait("scalar", sem)
        p.op("scalar", lambda e, o=rv, i=rv: e.activation(o, i, ACT.Sqrt), sem)
        p.wait("vector", sem)
        p.op("vector", lambda e, o=rv, i=rv: e.reciprocal(o, i), sem)
        p.wait("vector", sem)
        for h in range(nh):
            p.op("vector", lambda e, o=xv[:, h * DH:(h + 1) * DH],
                 i=xv[:, h * DH:(h + 1) * DH], r=rv[:, h:h + 1], w=wv_bc:
                 e.scalar_tensor_tensor(o, i, r, w, ALU.mult, ALU.mult), sem)
        p.wait("vector", sem)
        x3 = xv.rearrange("p (h f) -> p h f", h=nh)
        o3 = outbf.rearrange("p (h f) -> p h f", h=nh)
        cb = cosb[:, tt, :].unsqueeze(1).broadcast_to([128, nh, 64])
        sb_ = sinb[:, tt, :].unsqueeze(1).broadcast_to([128, nh, 64])
        a3 = scrA.rearrange("p (h f) -> p h f", h=H1)[:, 0:nh, :]
        b3 = scrB.rearrange("p (h f) -> p h f", h=H1)[:, 0:nh, :]
        p.op("vector", lambda e, o=a3, i=x3[:, :, 0:64], c=cb:
             e.tensor_tensor(o, i, c, ALU.mult), sem)
        p.op("vector", lambda e, o=b3, i=x3[:, :, 64:128], c=sb_:
             e.tensor_tensor(o, i, c, ALU.mult), sem)
        p.wait("vector", sem)
        p.op("vector", lambda e, o=o3[:, :, 0:64], a=a3, b=b3:
             e.tensor_tensor(o, a, b, ALU.subtract), sem)
        p.wait("vector", sem)
        p.op("vector", lambda e, o=a3, i=x3[:, :, 0:64], c=sb_:
             e.tensor_tensor(o, i, c, ALU.mult), sem)
        p.op("vector", lambda e, o=b3, i=x3[:, :, 64:128], c=cb:
             e.tensor_tensor(o, i, c, ALU.mult), sem)
        p.wait("vector", sem)
        p.op("vector", lambda e, o=o3[:, :, 64:128], a=a3, b=b3:
             e.tensor_tensor(o, a, b, ALU.add), sem)

    for tt in range(TT):
        p.wait("tensor", "ldg", p.val("ldg"))
        p.wait("tensor", "wbe", p.val("wbe"))
        if tt:
            p.wait("tensor", "qe", p.val("qe"))
        for dc in range(DC):
            for oc in range(4):
                p.mm(ps_q[:, oc * 512:(oc + 1) * 512], xT[:, dc, tt * 128:(tt + 1) * 128],
                     wq[:, dc, oc * 512:(oc + 1) * 512], dc == 0, dc == DC - 1,
                     sem=None if not (dc == DC - 1 and oc == 3) else "qmm")
        p.wait("vector", "qmm", p.val("qmm"))
        if tt:
            p.wait("vector", "qst", p.val("qst"))
        p.op("vector", lambda e, o=qtm[:, :], i=ps_q[:, :]: e.tensor_copy(o, i), "qe")
        p.wait("vector", "qe")
        p.wait("scalar", "qe", p.val("qe"))
        rmsnorm_rope(tt, H1, qtm[:, :], sq_scr[:, :], ssq[:, :], rr[:, :],
                     wbc[:, :], qbf[:, :], "qe")
        p.wait("scalar", "qe", p.val("qe"))
        p.dma("scalar", "qst", q_out[tt * 128:(tt + 1) * 128, :], qbf[:, :])
        if tt:
            p.wait("tensor", "ke", p.val("ke"))
        for dc in range(DC):
            p.mm(ps_k[:, :], xT[:, dc, tt * 128:(tt + 1) * 128],
                 wk[:, dc, :], dc == 0, dc == DC - 1,
                 sem=None if dc < DC - 1 else "kmm")
        p.wait("vector", "kmm", p.val("kmm"))
        if tt:
            p.wait("vector", "kst", p.val("kst"))
        p.op("vector", lambda e, o=ktm[:, :], i=ps_k[:, :]: e.tensor_copy(o, i), "ke")
        p.wait("vector", "ke")
        p.wait("scalar", "ke", p.val("ke"))
        rmsnorm_rope(tt, KVH, ktm[:, :], sq_scr[:, 0:KVH * DH], ssq[:, 0:KVH],
                     rr[:, 0:KVH], kwbc[:, :], kbf[:, :], "ke")
        p.wait("scalar", "ke", p.val("ke"))
        p.dma("scalar", "kst", k_out[tt * 128:(tt + 1) * 128, :], kbf[:, :])
        if tt:
            p.wait("tensor", "ve", p.val("ve"))
        for dc in range(DC):
            p.mm(ps_v[:, :], xT[:, dc, tt * 128:(tt + 1) * 128],
                 wv[:, dc, :], dc == 0, dc == DC - 1,
                 sem=None if dc < DC - 1 else "vmm")
        p.wait("vector", "vmm", p.val("vmm"))
        if tt:
            p.wait("vector", "vst", p.val("vst"))
        p.op("vector", lambda e, o=vbf[:, :], i=ps_v[:, :]: e.tensor_copy(o, i), "ve")
        p.wait("scalar", "ve", p.val("ve"))
        p.dma("scalar", "vst", v_out[tt * 128:(tt + 1) * 128, :], vbf[:, :])
    p.wait("sync", "qst", p.val("qst"))
    p.wait("sync", "kst", p.val("kst"))
    p.wait("sync", "vst", p.val("vst"))
    return p.finish()


def _rope_tables(core):
    half = core % 2
    pos = np.arange(half * T, half * T + T, dtype=np.float64)
    inv = 1.0 / (10000.0 ** (np.arange(0, DH, 2, dtype=np.float64) / DH))
    ang = pos[:, None] * inv[None, :]
    return (np.cos(ang).astype(np.float32), np.sin(ang).astype(np.float32))


def _prep_l3(inputs, x2):
    shared = {
        "wq_in": np.ascontiguousarray(np.asarray(inputs["L1_attn_wq"], np.float32).T),
        "wk_in": np.ascontiguousarray(np.asarray(inputs["L1_attn_wk"], np.float32).T),
        "wv_in": np.ascontiguousarray(np.asarray(inputs["L1_attn_wv"], np.float32).T),
        "qnw_in": np.asarray(inputs["L1_attn_q_norm_w"], np.float32).reshape(1, DH),
        "knw_in": np.asarray(inputs["L1_attn_k_norm_w"], np.float32).reshape(1, DH),
        "ones_in": np.ones((1, 128), np.float32),
    }
    in_maps = []
    for c in range(NCORE):
        m = dict(shared)
        m["xT_in"] = np.ascontiguousarray(x2[c * T:(c + 1) * T].T)
        cos, sin = _rope_tables(c)
        m["cos_in"] = cos
        m["sin_in"] = sin
        in_maps.append(m)
    return in_maps


def run_l3(inputs, x2):
    nc = build_l3()
    in_maps = _prep_l3(inputs, x2)
    res = run_bass_kernel_spmd(nc, in_maps, core_ids=list(range(NCORE)))
    q = np.concatenate([res.results[c]["q_out"] for c in range(NCORE)], 0)
    k = np.concatenate([res.results[c]["k_out"] for c in range(NCORE)], 0)
    v = np.concatenate([res.results[c]["v_out"] for c in range(NCORE)], 0)
    return q, k, v



def build_l4():
    p = _new_prog("l4")
    nc = p.nc
    dt = lambda n, sh, ty=F32, kind="ExternalInput": nc.dram_tensor(
        n, list(sh), ty, kind=kind).ap()

    x_in = dt("x_in", [T, D])
    qT_in = dt("qT_in", [D, T], BF16)
    kT_in = dt("kT_in", [KVH * DH, 2 * T], BF16)
    v_in = dt("v_in", [2 * T, KVH * DH], BF16)
    vones_in = dt("vones_in", [128, 16], BF16)
    wo_in = dt("wo_in", [D, D])
    dmask_in = dt("dmask_in", [128, 512])
    bmask_in = dt("bmask_in", [128, 512])
    idb_in = dt("idb_in", [128, 128], BF16)
    rtr_in = dt("rtr_in", [D, E1])
    idf_in = dt("idf_in", [128, 128])
    ones_in = dt("ones_in", [1, 128])
    x3_out = dt("x3_out", [T, D], F32, "ExternalOutput")
    cnt_out = dt("cnt_out", [1, E1], F32, "ExternalOutput")

    x_tm = p.sb("x_tm", [128, TT, D], F32)
    aoT = p.sb("aoT", [128, H1, T], BF16)
    idb = p.sb("idb", [128, 128], BF16)
    onesf = p.sb("onesf", [1, 128], F32)
    onescol = p.sb("onescol", [128, 1], F32)
    mx = p.sb("mx", [128, 1], F32)
    mb = p.sb("mb", [128, 1], F32)
    rcol = p.sb("rcol", [128, 1], F32)
    sq_scr = p.sb("sq_scr", [128, D], F32)
    ssum = p.sb("ssum", [128, 1], F32)
    ssq = p.sb("ssq", [128, 1], F32)
    mean = p.sb("mean", [128, 1], F32)
    m2 = p.sb("m2", [128, 1], F32)
    var = p.sb("var", [128, 1], F32)
    rstd = p.sb("rstd", [128, 1], F32)

    p.dma("sync", "ld", x_tm[:, :, :], x_in.rearrange("(tt p) d -> p tt d", p=128))
    p.dma("sync", "ld", idb[:, :], idb_in[:, :])
    p.dma("sync", "ld", onesf[:, :], ones_in[:, :])
    p.wait("vector", "ld", p.val("ld"))
    p.op("vector", lambda e, o=onescol[:, :]: e.memset(o, 1.0), "oc")

    with contextlib.ExitStack() as scB:
        qT = p.sb("qT", [128, H1, T], BF16, scB)
        kT = p.sb("kT", [128, KVH, 2 * T], BF16, scB)
        vaug = p.sb("vaug", [128, 16, KVH, 132], BF16, scB)
        vo = p.sb("vo", [128, 16], BF16, scB)
        p_sb = p.sb("p_sb", [128, 1152], BF16, scB)
        pt_sb = p.sb("pt_sb", [128, 1152], BF16, scB)
        ao_s = p.sb("ao_s", [128, 128], BF16, scB)
        dmask = p.sb("dmask", [128, 128], F32, scB)
        bmask = p.sb("bmask", [128, 128], F32, scB)

        p.dma("sync", "ld2", qT[:, :, :], qT_in.rearrange("(h p) t -> p h t", p=128))
        p.dma("sync", "ld2", kT[:, :, :], kT_in.rearrange("(h p) t -> p h t", p=128))
        for h4 in range(KVH):
            p.dma("sync", "ld2", vaug[:, :, h4, 0:128],
                  v_in[:, h4 * DH:(h4 + 1) * DH].rearrange("(kbt p) f -> p kbt f", p=128))
        p.dma("sync", "ld2", vo[:, :], vones_in[:, :])
        p.dma("sync", "ld2", dmask[:, :], dmask_in[:, :])
        p.dma("sync", "ld2", bmask[:, :], bmask_in[:, :])
        p.wait("vector", "ld2", p.val("ld2"))
        for h4 in range(KVH):
            p.op("vector", lambda e, o=vaug[:, :, h4, 128:129], i=vo[:, :].unsqueeze(2):
                 e.tensor_copy(o, i), "vob")

        ps_s = p.ps("ps_s", [128, 1152], scB)
        ps_pt = p.ps("ps_pt", [128, 1152], scB, BF16)
        ps_av = p.ps("ps_av", [128, 129], scB)
        ps_at = p.ps("ps_at", [128, 128], scB, BF16)

        for qt in range(TT):
            for h in range(H1):
                kvh = h // KVH
                base_kb = qt * 128
                p.wait("tensor", "ld2", p.val("ld2"))
                p.wait("tensor", "pexp", p.val("pexp"))
                for ci, (off, n) in enumerate(((0, 512), (512, 512), (1024, 128))):
                    p.mm(ps_s[:, off:off + n], qT[:, h, qt * 128:(qt + 1) * 128],
                         kT[:, kvh, base_kb + off:base_kb + off + n], True, True,
                         sem=None if ci < 2 else "smm")
                p.wait("vector", "smm", p.val("smm"))
                p.op("vector", lambda e, o=ps_s[:, 0:128], i=ps_s[:, 0:128], m=bmask[:, :]:
                     e.tensor_tensor(o, i, m, ALU.add), "msk")
                p.op("vector", lambda e, o=ps_s[:, 1024:1152], i=ps_s[:, 1024:1152],
                     m=dmask[:, :]: e.tensor_tensor(o, i, m, ALU.add), "msk")
                p.wait("vector", "msk")
                p.op("vector", lambda e, o=mx[:, :], i=ps_s[:, :]:
                     e.reduce_max(o, i, AX.X), "msk")
                p.wait("vector", "msk")
                p.op("vector", lambda e, o=mb[:, :], i=mx[:, :]:
                     e.tensor_scalar_mul(o, i, -SM_SCALE), "msk")
                p.wait("scalar", "msk", p.val("msk"))
                p.op("scalar", lambda e, o=p_sb[:, :], i=ps_s[:, :], b=mb[:, :]:
                     e.activation(o, i, ACT.Exp, bias=b, scale=SM_SCALE), "pexp")
                p.wait("tensor", "pexp", p.val("pexp"))
                p.wait("tensor", "ptv", p.val("ptv"))
                for b9 in range(9):
                    p.op("tensor", lambda e, o=ps_pt[:, b9 * 128:(b9 + 1) * 128],
                         i=p_sb[:, b9 * 128:(b9 + 1) * 128], idn=idb[:, :]:
                         e.transpose(o, i, idn), "ptt")
                p.wait("vector", "ptt", p.val("ptt"))
                p.op("vector", lambda e, o=pt_sb[:, :], i=ps_pt[:, :]:
                     e.tensor_copy(o, i), "ptv")
                p.wait("tensor", "ptv", p.val("ptv"))
                p.wait("tensor", "ave", p.val("ave"))
                for b9 in range(9):
                    p.mm(ps_av[:, :], pt_sb[:, b9 * 128:(b9 + 1) * 128],
                         vaug[:, qt + b9, kvh, 0:129], b9 == 0, b9 == 8,
                         sem=None if b9 < 8 else "avm")
                p.wait("vector", "avm", p.val("avm"))
                p.op("vector", lambda e, o=rcol[:, :], i=ps_av[:, 128:129]:
                     e.reciprocal(o, i), "ave")
                p.wait("vector", "ave")
                p.op("vector", lambda e, o=ao_s[:, :], i=ps_av[:, 0:128], r=rcol[:, :]:
                     e.tensor_scalar_mul(o, i, r), "ave")
                p.wait("tensor", "ave", p.val("ave"))
                p.wait("tensor", "aoe", p.val("aoe"))
                p.op("tensor", lambda e, o=ps_at[:, :], i=ao_s[:, :], idn=idb[:, :]:
                     e.transpose(o, i, idn), "att")
                p.wait("vector", "att", p.val("att"))
                p.op("vector", lambda e, o=aoT[:, h, qt * 128:(qt + 1) * 128],
                     i=ps_at[:, :]: e.tensor_copy(o, i), "aoe")

    with contextlib.ExitStack() as scC:
        woT = p.sb("woT", [128, DC, D], BF16, scC)
        p.wait("gpsimd", "aoe", p.val("aoe"))
        p.dma("gpsimd", "wo", woT[:, :, :], wo_in.rearrange("(c p) o -> p c o", p=128))
        ps_o = p.ps("ps_o", [128, D], scC)
        p.wait("tensor", "wo", p.val("wo"))
        p.wait("tensor", "aoe", p.val("aoe"))
        for tt in range(TT):
            if tt:
                p.wait("tensor", "ore", p.val("ore"))
            for dc in range(DC):
                for oc in range(4):
                    p.mm(ps_o[:, oc * 512:(oc + 1) * 512], aoT[:, dc, tt * 128:(tt + 1) * 128],
                         woT[:, dc, oc * 512:(oc + 1) * 512], dc == 0, dc == DC - 1,
                         sem=None if not (dc == DC - 1 and oc == 3) else "opr")
            p.wait("vector", "opr", p.val("opr"))
            p.op("vector", lambda e, o=x_tm[:, tt, :], i=ps_o[:, :], x=x_tm[:, tt, :]:
                 e.tensor_tensor(o, i, x, ALU.add), "ore")
        for tt in range(TT):
            p.wait("vector", "ore", p.val("ore"))
            layer_norm(p, x_tm[:, tt, :], sq_scr[:, :], ssum[:, :], ssq[:, :],
                       mean[:, :], m2[:, :], var[:, :], rstd[:, :], "lne")

    with contextlib.ExitStack() as scD:
        x3T = p.sb("x3T", [128, DC, T], BF16, scD)
        rtr = p.sb("rtr", [128, DC, E1], BF16, scD)
        lgt = p.sb("lgt", [128, TT, E1], F32, scD)
        m1 = p.sb("m1", [128, 1], F32, scD)
        m2c = p.sb("m2c", [128, 1], F32, scD)
        oh1 = p.sb("oh1", [128, E1], F32, scD)
        oh2 = p.sb("oh2", [128, E1], F32, scD)
        scrM = p.sb("scrM", [128, E1], F32, scD)
        cnt_s = p.sb("cnt_s", [1, E1], F32, scD)
        idf = p.sb("idf", [128, 128], F32, scD)
        ps_t = p.ps("ps_t", [128, 128], scD)
        ps_lg = p.ps("ps_lg", [128, E1], scD)
        ps_cnt = p.ps("ps_cnt", [1, E1], scD)

        p.wait("gpsimd", "lne", p.val("lne"))
        p.dma("gpsimd", "rtl", rtr[:, :, :], rtr_in.rearrange("(c p) w -> p c w", p=128))
        p.wait("sync", "lne", p.val("lne"))
        p.dma("sync", "idl", idf[:, :], idf_in[:, :])
        p.wait("tensor", "lne", p.val("lne"))
        p.wait("tensor", "idl", p.val("idl"))
        for tt in range(TT):
            for dc in range(DC):
                p.wait("tensor", "x3e", p.val("x3e"))
                p.trans(ps_t[:, :], x_tm[:, tt, dc * 128:(dc + 1) * 128], idf[:, :],
                        x3T[:, dc, tt * 128:(tt + 1) * 128], "x3e")
        p.wait("tensor", "x3e", p.val("x3e"))
        p.wait("tensor", "rtl", p.val("rtl"))
        for tt in range(TT):
            if tt:
                p.wait("tensor", "lge", p.val("lge"))
            for dc in range(DC):
                p.mm(ps_lg[:, :], x3T[:, dc, tt * 128:(tt + 1) * 128], rtr[:, dc, :],
                     dc == 0, dc == DC - 1, sem=None if dc < DC - 1 else "lgm")
            p.wait("vector", "lgm", p.val("lgm"))
            p.op("vector", lambda e, o=lgt[:, tt, :], i=ps_lg[:, :]:
                 e.tensor_copy(o, i), "lge")
        for tt in range(TT):
            p.wait("vector", "lge", p.val("lge"))
            p.op("vector", lambda e, o=m1[:, :], i=lgt[:, tt, :]:
                 e.reduce_max(o, i, AX.X), "tp")
            p.wait("vector", "tp")
            p.op("vector", lambda e, o=oh1[:, :], i=lgt[:, tt, :], m=m1[:, :]:
                 e.tensor_scalar(o, i, m, None, ALU.is_equal), "tp")
            p.wait("vector", "tp")
            p.op("vector", lambda e, o=scrM[:, :], i=oh1[:, :], x=lgt[:, tt, :]:
                 e.scalar_tensor_tensor(o, i, NEG, x, ALU.mult, ALU.add), "tp")
            p.wait("vector", "tp")
            p.op("vector", lambda e, o=m2c[:, :], i=scrM[:, :]:
                 e.reduce_max(o, i, AX.X), "tp")
            p.wait("vector", "tp")
            p.op("vector", lambda e, o=oh2[:, :], i=scrM[:, :], m=m2c[:, :]:
                 e.tensor_scalar(o, i, m, None, ALU.is_equal), "tp")
            p.wait("vector", "tp")
            p.op("vector", lambda e, o=oh1[:, :], a=oh1[:, :], b=oh2[:, :]:
                 e.tensor_tensor(o, a, b, ALU.add), "tp")
            p.wait("tensor", "tp", p.val("tp"))
            p.mm(ps_cnt[:, :], onescol[:, :], oh1[:, :], tt == 0, tt == TT - 1,
                 sem="cm")
            p.wait("vector", "cm", p.val("cm"))
        p.wait("vector", "cm", p.val("cm"))
        p.op("vector", lambda e, o=cnt_s[:, :], i=ps_cnt[:, :]:
             e.tensor_copy(o, i), "cme")

    p.wait("sync", "lne", p.val("lne"))
    p.dma("sync", "st", x3_out.rearrange("(tt p) d -> p tt d", p=128), x_tm[:, :, :])
    p.wait("sync", "cme", p.val("cme"))
    p.dma("sync", "st", cnt_out[:, :], cnt_s[:, :])
    p.wait("sync", "st", p.val("st"))
    return p.finish()


def _prep_l4(inputs, x2, q, k, v):
    import ml_dtypes
    bfd = ml_dtypes.bfloat16
    neg = np.float32(NEG)
    ii, jj = np.meshgrid(np.arange(128), np.arange(128), indexing="ij")
    dmask = np.where(ii >= jj, 0.0, neg).astype(np.float32)
    bmask = np.where(jj > ii, 0.0, neg).astype(np.float32)
    shared = {
        "wo_in": np.ascontiguousarray(np.asarray(inputs["L1_attn_wo"], np.float32).T),
        "dmask_in": dmask,
        "bmask_in": bmask,
        "idb_in": np.eye(128, dtype=np.float32).astype(bfd),
        "rtr_in": np.asarray(inputs["L1_ffn_router"], np.float32),
        "idf_in": np.eye(128, dtype=np.float32),
        "ones_in": np.ones((1, 128), np.float32),
    }
    in_maps = []
    for c in range(NCORE):
        half = c % 2
        b0 = (c // 2) * 2 * T
        m = dict(shared)
        m["x_in"] = x2[c * T:(c + 1) * T]
        m["qT_in"] = np.ascontiguousarray(q[c * T:(c + 1) * T].T)
        kk = np.zeros((2 * T, KVH * DH), bfd)
        vv = np.zeros((2 * T, KVH * DH), bfd)
        von = np.zeros((2 * T,), np.float32)
        if half == 0:
            kk[T:] = k[b0:b0 + T]
            vv[T:] = v[b0:b0 + T]
            von[T:] = 1.0
        else:
            kk[:] = k[b0:b0 + 2 * T]
            vv[:] = v[b0:b0 + 2 * T]
            von[:] = 1.0
        m["kT_in"] = np.ascontiguousarray(kk.T)
        m["v_in"] = vv
        m["vones_in"] = np.ascontiguousarray(von.reshape(16, 128).T).astype(bfd)
        in_maps.append(m)
    return in_maps


def run_l4(inputs, x2, q, k, v):
    nc = build_l4()
    in_maps = _prep_l4(inputs, x2, q, k, v)
    res = run_bass_kernel_spmd(nc, in_maps, core_ids=list(range(NCORE)))
    x3 = np.concatenate([res.results[c]["x3_out"] for c in range(NCORE)], 0)
    cnt = np.stack([res.results[c]["cnt_out"][0] for c in range(NCORE)], 0)
    return x3, cnt



CAPL1 = 1024
IC1 = I1 // 128
ICS = J1 // 128


def build_l5():
    p = _new_prog("l5")
    nc = p.nc
    dt = lambda n, sh, ty=F32, kind="ExternalInput": nc.dram_tensor(
        n, list(sh), ty, kind=kind).ap()

    x_in = dt("x_in", [T, D])
    xT_in = dt("xT_in", [D, T])
    cnts_in = dt("cnts_in", [NCORE, E1])
    rowsel_in = dt("rowsel_in", [NCORE, 1])
    rtr_in = dt("rtr_in", [D, E1])
    u128_in = dt("u128_in", [128, 128])
    u8_in = dt("u8_in", [NCORE, NCORE])
    iota_in = dt("iota_in", [128, CAPL1])
    ones_in = dt("ones_in", [1, 128])
    idb_in = dt("idb_in", [128, 128], BF16)
    shup_in = dt("shup_in", [ICS, 2, 128, DC, 128])
    shdn_in = dt("shdn_in", [J1, D])
    wug_in = dt("wug_in", [E1, IC1, 2, 128, DC, 128])
    wd_in = dt("wd_in", [E1, I1, D])
    x4_out = dt("x4_out", [T, D], F32, "ExternalOutput")

    CL = 512
    JC = CL // 128

    x_tm = p.sb("x_tm", [128, TT, D], F32)
    x3tm = p.sb("x3tm", [128, TT, D], BF16)
    rtr = p.sb("rtr", [128, DC, E1], BF16)
    u128 = p.sb("u128", [128, 128], F32)
    u8 = p.sb("u8", [NCORE, NCORE], F32)
    cnts = p.sb("cnts", [NCORE, E1], F32)
    rowsel = p.sb("rowsel", [NCORE, 1], F32)
    iota = p.sb("iota", [128, CAPL1], F32)
    onesf = p.sb("onesf", [1, 128], F32)
    onescol = p.sb("onescol", [128, 1], F32)
    idb = p.sb("idb", [128, 128], BF16)
    bc_sb = p.sb("bc_sb", [NCORE, E1], F32)
    cm_sb = p.sb("cm_sb", [1, E1], F32)
    bvld = p.sb("bvld", [1, E1], F32)
    skipf = p.sb("skipf", [1, 2 * E1], F32)
    skipi = p.sb("skipi", [1, 2 * E1], I32)
    skipf2 = p.sb("skipf2", [1, 2 * E1], F32)
    skipi2 = p.sb("skipi2", [1, 2 * E1], I32)
    bme = p.sb("bme", [1, E1], F32)
    runb = p.sb("runb", [1, E1], F32)
    totb = p.sb("totb", [1, E1], F32)
    lgt = p.sb("lgt", [128, E1], F32)
    m1 = p.sb("m1", [128, 1], F32)
    m2c = p.sb("m2c", [128, 1], F32)
    cw1 = p.sb("cw1", [128, 1], F32)
    cw2 = p.sb("cw2", [128, 1], F32)
    oh1 = p.sb("oh1", [128, E1], F32)
    oh2 = p.sb("oh2", [128, E1], F32)
    ohsum = p.sb("ohsum", [128, E1], F32)
    scrM = p.sb("scrM", [128, E1], F32)
    pfl = p.sb("pfl", [128, E1], F32)
    pf_sb = p.sb("pf_sb", [128, E1], F32)
    validm = p.sb("validm", [128, E1], F32)
    posm = p.sb("posm", [128, TT, E1], F32)
    gatem = p.sb("gatem", [128, TT, E1], F32)
    p.dma("sync", "ld", x_tm[:, :, :], x_in.rearrange("(tt p) d -> p tt d", p=128))
    p.dma("gpsimd", "ldg", x3tm[:, :, :], x_in.rearrange("(tt p) d -> p tt d", p=128))
    p.dma("gpsimd", "ldg", rtr[:, :, :], rtr_in.rearrange("(c p) w -> p c w", p=128))
    p.dma("sync", "ld", u128[:, :], u128_in[:, :])
    p.dma("sync", "ld", u8[:, :], u8_in[:, :])
    p.dma("sync", "ld", cnts[:, :], cnts_in[:, :])
    p.dma("sync", "ld", rowsel[:, :], rowsel_in[:, :])
    p.dma("sync", "ld", iota[:, :], iota_in[:, :])
    p.dma("sync", "ld", onesf[:, :], ones_in[:, :])
    p.dma("sync", "ld", idb[:, :], idb_in[:, :])
    p.wait("vector", "ld", p.val("ld"))
    p.op("vector", lambda e, o=onescol[:, :]: e.memset(o, 1.0), "oc")

    with contextlib.ExitStack() as s0:
        ps_b = p.ps("ps_b", [NCORE, E1], s0)
        ps_m = p.ps("ps_m", [1, E1], s0)
        p.wait("tensor", "ld", p.val("ld"))
        p.mm(ps_b[:, :], u8[:, :], cnts[:, :], True, True, sem="b0")
        p.wait("vector", "b0", p.val("b0"))
        p.op("vector", lambda e, o=bc_sb[:, :], i=ps_b[:, :]: e.tensor_copy(o, i), "b1")
        p.wait("tensor", "b1", p.val("b1"))
        p.mm(ps_m[:, :], rowsel[:, :], bc_sb[:, :], True, True, sem="b2")
        p.wait("vector", "b2", p.val("b2"))
        p.op("vector", lambda e, o=bme[:, :], i=ps_m[:, :]: e.tensor_copy(o, i), "b3")
        p.op("vector", lambda e, o=runb[:, :]: e.memset(o, 0.0), "b3")
        p.wait("tensor", "b3", p.val("b3"))
        p.mm(ps_m[:, :], rowsel[:, :], cnts[:, :], True, True, sem="b4")
        p.wait("vector", "b4", p.val("b4"))
        p.op("vector", lambda e, o=cm_sb[:, :], i=ps_m[:, :]: e.tensor_copy(o, i), "skf")
        p.wait("vector", "skf")
        p.op("vector", lambda e, o=bvld[:, :], i=bme[:, :]:
             e.tensor_single_scalar(o, i, float(CAP1), ALU.is_lt), "skf")
        p.op("vector", lambda e, o=skipf[:, 0:E1], i=cm_sb[:, :]:
             e.tensor_single_scalar(o, i, 0.0, ALU.is_gt), "skf")
        p.op("vector", lambda e, o=skipf[:, E1:2 * E1], i=cm_sb[:, :]:
             e.tensor_single_scalar(o, i, 512.0, ALU.is_gt), "skf")
        p.op("vector", lambda e, o=skipf2[:, 0:E1], i=cm_sb[:, :]:
             e.tensor_single_scalar(o, i, 256.0, ALU.is_gt), "skf")
        p.op("vector", lambda e, o=skipf2[:, E1:2 * E1], i=cm_sb[:, :]:
             e.tensor_single_scalar(o, i, 768.0, ALU.is_gt), "skf")
        p.wait("vector", "skf")
        for shh in range(2):
            p.op("vector", lambda e, o=skipf[:, shh * E1:(shh + 1) * E1],
                 a=skipf[:, shh * E1:(shh + 1) * E1], b=bvld[:, :]:
                 e.tensor_tensor(o, a, b, ALU.mult), "skf")
            p.op("vector", lambda e, o=skipf2[:, shh * E1:(shh + 1) * E1],
                 a=skipf2[:, shh * E1:(shh + 1) * E1], b=bvld[:, :]:
                 e.tensor_tensor(o, a, b, ALU.mult), "skf")
        p.wait("vector", "skf")
        p.op("vector", lambda e, o=skipi[:, :], i=skipf[:, :]:
             e.tensor_copy(o, i), "skf")
        p.op("vector", lambda e, o=skipi2[:, :], i=skipf2[:, :]:
             e.tensor_copy(o, i), "skf")

    with contextlib.ExitStack() as s12:
        x3T = p.sb("x3T", [128, DC, T], BF16, s12)
        p.dma("gpsimd", "ldg", x3T[:, :, :], xT_in.rearrange("(c p) t -> p c t", p=128))

        with contextlib.ExitStack() as s1:
            ps_lg = p.ps("ps_lg", [128, E1], s1)
            ps_pf = p.ps("ps_pf", [128, E1], s1)
            ps_tc = p.ps("ps_tc", [1, E1], s1)
            ps_tb = p.ps("ps_tb", [128, E1], s1)
            p.wait("tensor", "ldg", p.val("ldg"))
            p.wait("tensor", "b3", p.val("b3"))
            for tt in range(TT):
                if tt:
                    p.wait("tensor", "lge", p.val("lge"))
                for dc in range(DC):
                    p.mm(ps_lg[:, :], x3T[:, dc, tt * 128:(tt + 1) * 128], rtr[:, dc, :],
                         dc == 0, dc == DC - 1, sem=None if dc < DC - 1 else "lgm")
                p.wait("vector", "lgm", p.val("lgm"))
                p.op("vector", lambda e, o=lgt[:, :], i=ps_lg[:, :]:
                     e.tensor_copy(o, i), "lge")
                p.wait("vector", "lge")
                p.op("vector", lambda e, o=m1[:, :], i=lgt[:, :]:
                     e.reduce_max(o, i, AX.X), "tp")
                p.wait("vector", "tp")
                p.op("vector", lambda e, o=oh1[:, :], i=lgt[:, :], m=m1[:, :]:
                     e.tensor_scalar(o, i, m, None, ALU.is_equal), "tp")
                p.wait("vector", "tp")
                p.op("vector", lambda e, o=scrM[:, :], i=oh1[:, :], x=lgt[:, :]:
                     e.scalar_tensor_tensor(o, i, NEG, x, ALU.mult, ALU.add), "tp")
                p.wait("vector", "tp")
                p.op("vector", lambda e, o=m2c[:, :], i=scrM[:, :]:
                     e.reduce_max(o, i, AX.X), "tp")
                p.wait("vector", "tp")
                p.op("vector", lambda e, o=oh2[:, :], i=scrM[:, :], m=m2c[:, :]:
                     e.tensor_scalar(o, i, m, None, ALU.is_equal), "tp")
                p.op("vector", lambda e, o=cw1[:, :], a=m1[:, :], b=m2c[:, :]:
                     e.tensor_tensor(o, a, b, ALU.subtract), "tp")
                p.wait("scalar", "tp", p.val("tp"))
                p.op("scalar", lambda e, o=cw1[:, :], i=cw1[:, :]:
                     e.activation(o, i, ACT.Sigmoid), "tps")
                p.wait("vector", "tps", p.val("tps"))
                p.op("vector", lambda e, o=cw2[:, :], i=cw1[:, :]:
                     e.tensor_scalar(o, i, -1.0, 1.0, ALU.mult, ALU.add), "tp")
                p.op("vector", lambda e, o=ohsum[:, :], a=oh1[:, :], b=oh2[:, :]:
                     e.tensor_tensor(o, a, b, ALU.add), "tp")
                p.wait("tensor", "tp", p.val("tp"))
                if tt:
                    p.wait("tensor", "bcw", p.val("bcw"))
                p.mm(ps_pf[:, :], u128[:, :], ohsum[:, :], True, True, sem="pfm")
                p.mm(ps_tc[:, :], onescol[:, :], ohsum[:, :], True, True, sem="pfm")
                p.op("vector", lambda e, o=totb[:, :], a=bme[:, :], b=runb[:, :]:
                     e.tensor_tensor(o, a, b, ALU.add), "tb")
                p.wait("tensor", "tb", p.val("tb"))
                p.mm(ps_tb[:, :], onesf[:, :], runb[:, :], True, True, sem="tbl")
                p.wait("vector", "pfm", p.val("pfm"))
                p.wait("vector", "tbl", p.val("tbl"))
                p.op("vector", lambda e, o=pf_sb[:, :], i=ps_pf[:, :]:
                     e.tensor_copy(o, i), "pfe")
                p.wait("vector", "pfe")
                p.op("vector", lambda e, o=pfl[:, :], a=pf_sb[:, :], b=ps_tb[:, :]:
                     e.tensor_tensor(o, a, b, ALU.add), "pfe")
                p.wait("tensor", "pfe", p.val("pfe"))
                p.mm(ps_tb[:, :], onesf[:, :], totb[:, :], True, True, sem="tb2")
                p.wait("vector", "tb2", p.val("tb2"))
                p.op("vector", lambda e, o=validm[:, :], a=pf_sb[:, :], b=ps_tb[:, :]:
                     e.tensor_tensor(o, a, b, ALU.add), "bcw")
                p.wait("vector", "bcw")
                p.op("vector", lambda e, o=validm[:, :], i=validm[:, :]:
                     e.tensor_single_scalar(o, i, float(CAP1), ALU.is_lt), "bcw")
                p.op("vector", lambda e, o=posm[:, tt, :], a=pfl[:, :], b=ohsum[:, :]:
                     e.tensor_tensor(o, a, b, ALU.mult), "bcw")
                p.wait("vector", "bcw")
                p.op("vector", lambda e, o=posm[:, tt, :], a=posm[:, tt, :], b=ohsum[:, :]:
                     e.tensor_tensor(o, a, b, ALU.add), "bcw")
                p.wait("vector", "bcw")
                p.op("vector", lambda e, o=posm[:, tt, :], i=posm[:, tt, :]:
                     e.tensor_scalar_add(o, i, -1.0), "bcw")
                p.op("vector", lambda e, o=gatem[:, tt, :], i=oh1[:, :], w=cw1[:, :]:
                     e.tensor_scalar_mul(o, i, w), "bcw")
                p.wait("vector", "bcw")
                p.op("vector", lambda e, o=gatem[:, tt, :], i=oh2[:, :], w=cw2[:, :],
                     x=gatem[:, tt, :]:
                     e.scalar_tensor_tensor(o, i, w, x, ALU.mult, ALU.add), "bcw")
                p.wait("vector", "bcw")
                p.op("vector", lambda e, o=gatem[:, tt, :], a=gatem[:, tt, :],
                     b=validm[:, :]:
                     e.tensor_tensor(o, a, b, ALU.mult), "bcw")
                p.wait("vector", "bcw")
                p.op("vector", lambda e, o=runb[:, :], a=runb[:, :], b=ps_tc[:, :]:
                     e.tensor_tensor(o, a, b, ALU.add), "bcw")
                p.wait("vector", "bcw")

        p.wait("tensor", "bcw", p.val("bcw"))
        with contextlib.ExitStack() as s2:
            wbuf = p.sb("wbuf", [128, 2, 2, DC, 128], BF16, s2)
            wdbuf = p.sb("wdbuf", [128, 2, 1024], BF16, s2)
            h_sh = p.sb("h_sh", [128, ICS, 512], BF16, s2)
            sil = p.sb("sil", [128, 512], F32, s2)
            silb = p.sb("silb", [128, 512], F32, s2)
            silL = [sil, silb]
            for th in range(2):
                tsl = slice(th * 512, (th + 1) * 512)
                p.wait("tensor", "dne", p.val("dne"))
                with contextlib.ExitStack() as s2a:
                    ps_guL = [p.ps("ps_gu", [128, 1024], s2a), p.ps("ps_gu", [128, 1024], s2a)]
                    for i in range(ICS):
                        buf = i % 2
                        gi = th * ICS + i
                        ps_gu = ps_guL[i % 2]
                        silx = silL[i % 2]
                        if gi >= 2:
                            p.wait("gpsimd", "gum", gi - 1)
                        p.dma("gpsimd", "wu%d" % buf, wbuf[:, buf, 0, :, :],
                              shup_in[i, 0])
                        p.dma("gpsimd", "wu%d" % buf, wbuf[:, buf, 1, :, :],
                              shup_in[i, 1])
                        p.wait("tensor", "wu%d" % buf, p.val("wu%d" % buf))
                        if gi >= 2:
                            p.wait("tensor", "sme", max(0, p.val("sme") - 1))
                        for dc in range(DC):
                            p.mm(ps_gu[:, 0:512], wbuf[:, buf, 0, dc, :],
                                 x3T[:, dc, tsl], dc == 0, dc == DC - 1)
                            p.mm(ps_gu[:, 512:1024], wbuf[:, buf, 1, dc, :],
                                 x3T[:, dc, tsl], dc == 0, dc == DC - 1,
                                 sem=None if dc < DC - 1 else "gum")
                        p.wait("scalar", "gum", p.val("gum"))
                        p.op("scalar", lambda e, o=silx[:, :], i=ps_gu[:, 0:512]:
                             e.activation(o, i, ACT.Silu), "sil")
                        p.wait("vector", "sil", p.val("sil"))
                        p.op("vector", lambda e, o=h_sh[:, i, :], a=silx[:, :],
                             b=ps_gu[:, 512:1024]:
                             e.tensor_tensor(o, a, b, ALU.mult), "sme")
                with contextlib.ExitStack() as s2b:
                    ps_dn = p.ps("ps_dn", [128, 4096], s2b)
                    for oh in range(2):
                        for i in range(ICS):
                            buf = i % 2
                            di = (2 * th + oh) * ICS + i
                            if di >= 2:
                                p.wait("gpsimd", "dnm", di - 1)
                            p.dma("gpsimd", "wd%d" % buf, wdbuf[:, buf, :],
                                  shdn_in[i * 128:(i + 1) * 128,
                                          oh * 1024:(oh + 1) * 1024])
                            p.wait("tensor", "wd%d" % buf, p.val("wd%d" % buf))
                            p.wait("tensor", "sme", p.val("sme"))
                            if di >= 1:
                                p.wait("tensor", "dne", p.val("dne"))
                            for t2 in range(4):
                                for oc in range(2):
                                    p.mm(ps_dn[:, (t2 * 2 + oc) * 512:(t2 * 2 + oc + 1) * 512],
                                         h_sh[:, i, t2 * 128:(t2 + 1) * 128],
                                         wdbuf[:, buf, oc * 512:(oc + 1) * 512],
                                         i == 0, i == ICS - 1,
                                         sem="dnm" if (t2 == 3 and oc == 1) else None)
                        p.wait("vector", "dnm", p.val("dnm"))
                        for t2 in range(4):
                            tti = th * 4 + t2
                            for oc in range(2):
                                p.op("vector", lambda e,
                                     o=x_tm[:, tti, oh * 1024 + oc * 512:oh * 1024 + (oc + 1) * 512],
                                     i=ps_dn[:, (t2 * 2 + oc) * 512:(t2 * 2 + oc + 1) * 512],
                                     x=x_tm[:, tti, oh * 1024 + oc * 512:oh * 1024 + (oc + 1) * 512]:
                                     e.tensor_tensor(o, i, x, ALU.add), "dne")
                        p.wait("vector", "dne")

    with contextlib.ExitStack() as s3:
        Gt = p.sb("Gt", [128, 4, T], BF16, s3)
        xg = p.sb("xg", [128, DC, 512], BF16, s3)
        h_e = p.sb("h_e", [128, IC1, 512], BF16, s3)
        yg = p.sb("yg", [128, 4, D], BF16, s3)
        wbufe = p.sb("wbufe", [128, 2, 2, DC, 128], BF16, s3)
        wdbufe = p.sb("wdbufe", [128, 2, D], BF16, s3)
        G = p.sb("G", [128, TT, 512], BF16, s3)
        sil2 = p.sb("sil2", [128, 512], F32, s3)
        sil2b = p.sb("sil2b", [128, 512], F32, s3)
        CL = 512
        JC = 4

        p.wait("tensor", "dne", p.val("dne"))
        p.wait("vector", "dnm", p.val("dnm"))
        p.wait("gpsimd", "dnm", p.val("dnm"))
        for eng_ in ("tensor", "vector", "scalar", "gpsimd"):
            p.wait(eng_, "skf", p.val("skf"))
        p.op("vector", lambda e, o=yg[:, :, :]: e.memset(o, 0.0), "ygz")
        p.wait("tensor", "ygz", 1)
        for ve in range(2 * E1):
            ex, sh = ve // 2, ve % 2
            isl = slice(sh * CL, (sh + 1) * CL)
            sgp = ["exg%d_0" % ex, "exg%d_1" % ex]
            sdp = ["exd%d_0" % ex, "exd%d_1" % ex]
            p.skip_begin(skipi[0:1, sh * E1 + ex:sh * E1 + ex + 1])
            p.wait("vector", "xge", p.val("xge"))
            for tt in range(TT):
                p.op("vector", lambda e, o=G[:, tt, :], i=iota[:, isl],
                     m=posm[:, tt, ex:ex + 1]:
                     e.tensor_scalar(o, i, m, None, ALU.is_equal), "gb")
            with contextlib.ExitStack() as sA:
                ps_bt = p.ps("ps_bt", [128, T], sA, BF16)
                p.wait("vector", "sce", p.val("sce"))
                p.wait("tensor", "gb", p.val("gb"))
                for jc in range(JC):
                    for tt in range(TT):
                        if tt + jc or ve:
                            p.wait("tensor", "gte", p.val("gte"))
                        p.op("tensor", lambda e, o=ps_bt[:, tt * 128:(tt + 1) * 128],
                             i=G[:, tt, jc * 128:(jc + 1) * 128], idn=idb[:, :]:
                             e.transpose(o, i, idn), "gtt")
                    p.wait("vector", "gtt", p.val("gtt"))
                    p.op("vector", lambda e, o=Gt[:, jc, :], i=ps_bt[:, :]:
                         e.tensor_copy(o, i), "gte")
            with contextlib.ExitStack() as sB:
                ps_gL = [p.ps("ps_g", [128, CL], sB), p.ps("ps_g", [128, CL], sB)]
                p.wait("tensor", "gte", p.val("gte"))
                for dc in range(DC):
                    ps_g = ps_gL[dc % 2]
                    if dc >= 2 or ve:
                        p.wait("tensor", "xge", max(0, p.val("xge") - 1))
                    for tt in range(TT):
                        p.mm(ps_g[:, :], x3tm[:, tt, dc * 128:(dc + 1) * 128],
                             G[:, tt, :], tt == 0, tt == TT - 1,
                             sem=None if tt < TT - 1 else "xgm")
                    p.wait("vector", "xgm", p.val("xgm"))
                    p.op("vector", lambda e, o=xg[:, dc, :], i=ps_g[:, :]:
                         e.tensor_copy(o, i), "xge")
            with contextlib.ExitStack() as sC:
                ps_gueL = [p.ps("ps_gue", [128, 1024], sC), p.ps("ps_gue", [128, 1024], sC)]
                sil2L = [sil2, sil2b]
                for i in range(IC1):
                    gi = ve * IC1 + i
                    buf = gi % 2
                    ps_gue = ps_gueL[i % 2]
                    sil2x = sil2L[i % 2]
                    sg = sgp[buf]
                    if gi >= 2:
                        p.wait("gpsimd", "egm", gi - 1)
                    p.dma("gpsimd", sg, wbufe[:, buf, 0, :, :],
                          wug_in[ex, i, 0])
                    p.dma("gpsimd", sg, wbufe[:, buf, 1, :, :],
                          wug_in[ex, i, 1])
                    p.wait("tensor", sg, p.val(sg))
                    p.wait("tensor", "xge", p.val("xge"))
                    if gi >= 2:
                        p.wait("tensor", "ehe", max(0, p.val("ehe") - 1))
                    for dc in range(DC):
                        p.mm(ps_gue[:, 0:512], wbufe[:, buf, 0, dc, :],
                             xg[:, dc, :], dc == 0, dc == DC - 1)
                        p.mm(ps_gue[:, 512:1024], wbufe[:, buf, 1, dc, :],
                             xg[:, dc, :], dc == 0, dc == DC - 1,
                             sem=None if dc < DC - 1 else "egm")
                    p.wait("scalar", "egm", p.val("egm"))
                    p.op("scalar", lambda e, o=sil2x[:, :], i=ps_gue[:, 0:512]:
                         e.activation(o, i, ACT.Silu), "esl")
                    p.wait("vector", "esl", p.val("esl"))
                    p.op("vector", lambda e, o=h_e[:, i, :], a=sil2x[:, :],
                         b=ps_gue[:, 512:1024]:
                         e.tensor_tensor(o, a, b, ALU.mult), "ehe")
            with contextlib.ExitStack() as sD:
                ps_dne = p.ps("ps_dne", [128, 4096], sD)
                for jh2 in range(2):
                    if jh2 == 1:
                        p.skip_end(gpsimd_in=True)
                        p.skip_begin(skipi2[0:1, sh * E1 + ex:sh * E1 + ex + 1])
                    for i in range(IC1):
                        di = (ve * 2 + jh2) * IC1 + i
                        buf = di % 2
                        sd = sdp[buf]
                        if di >= 2:
                            p.wait("gpsimd", "edm", di - 1)
                        p.dma("gpsimd", sd, wdbufe[:, buf, :],
                              wd_in[ex, i * 128:(i + 1) * 128, :])
                        p.wait("tensor", sd, p.val(sd))
                        p.wait("tensor", "ehe", p.val("ehe"))
                        if di >= 1:
                            p.wait("tensor", "yge", p.val("yge"))
                        for j2 in range(2):
                            for oc in range(4):
                                last = (j2 == 1 and oc == 3)
                                p.mm(ps_dne[:, (j2 * 4 + oc) * 512:(j2 * 4 + oc + 1) * 512],
                                     h_e[:, i, (jh2 * 2 + j2) * 128:(jh2 * 2 + j2 + 1) * 128],
                                     wdbufe[:, buf, oc * 512:(oc + 1) * 512],
                                     i == 0, i == IC1 - 1,
                                     sem="edm" if last else None)
                    p.wait("vector", "edm", p.val("edm"))
                    for j2 in range(2):
                        p.op("vector", lambda e, o=yg[:, jh2 * 2 + j2, :],
                             i=ps_dne[:, j2 * 2048:(j2 + 1) * 2048]:
                             e.tensor_copy(o, i), "yge")
            p.skip_end(gpsimd_in=True)
            p.skip_begin(skipi[0:1, sh * E1 + ex:sh * E1 + ex + 1])
            with contextlib.ExitStack() as sE:
                ps_scL = [p.ps("ps_sc", [128, 1024], sE), p.ps("ps_sc", [128, 1024], sE)]
                p.wait("tensor", "yge", p.val("yge"))
                for tt in range(TT):
                    for ob in range(2):
                        ps_sc = ps_scL[(tt * 2 + ob) % 2]
                        if (tt * 2 + ob) >= 2 or ve:
                            p.wait("tensor", "sce", max(0, p.val("sce") - 2))
                        for jc in range(JC):
                            for o2 in range(2):
                                p.mm(ps_sc[:, o2 * 512:(o2 + 1) * 512],
                                     Gt[:, jc, tt * 128:(tt + 1) * 128],
                                     yg[:, jc, (ob * 2 + o2) * 512:(ob * 2 + o2 + 1) * 512],
                                     jc == 0, jc == JC - 1,
                                     sem=None if not (jc == JC - 1 and o2 == 1) else "scm")
                        p.wait("vector", "scm", p.val("scm"))
                        for o2 in range(2):
                            p.op("vector", lambda e,
                                 o=x_tm[:, tt, (ob * 2 + o2) * 512:(ob * 2 + o2 + 1) * 512],
                                 i=ps_sc[:, o2 * 512:(o2 + 1) * 512],
                                 g=gatem[:, tt, ex:ex + 1],
                                 x=x_tm[:, tt, (ob * 2 + o2) * 512:(ob * 2 + o2 + 1) * 512]:
                                 e.scalar_tensor_tensor(o, i, g, x, ALU.mult, ALU.add), "sce")
            p.skip_end(gpsimd_in=True)

    sq_scr = p.sb("sq_scr", [128, D], F32)
    ssum = p.sb("ssum", [128, 1], F32)
    ssq = p.sb("ssq", [128, 1], F32)
    mean = p.sb("mean", [128, 1], F32)
    m2 = p.sb("m2", [128, 1], F32)
    var = p.sb("var", [128, 1], F32)
    rstd = p.sb("rstd", [128, 1], F32)
    for tt in range(TT):
        p.wait("vector", "sce", p.val("sce"))
        p.wait("vector", "dne", p.val("dne"))
        layer_norm(p, x_tm[:, tt, :], sq_scr[:, :], ssum[:, :], ssq[:, :],
                   mean[:, :], m2[:, :], var[:, :], rstd[:, :], "lne")
        p.wait("sync", "lne", 9 * (tt + 1))
        p.dma("sync", "st", x4_out[tt * 128:(tt + 1) * 128, :], x_tm[:, tt, :])
    p.wait("sync", "st", p.val("st"))
    return p.finish()


def _prep_l5(inputs, x3, cnt):
    def _pchunk(a):
        return a.reshape(DC, 128, 128).transpose(1, 0, 2)
    sup = np.asarray(inputs["L1_ffn_shared_w_up_gate"], np.float32)
    shup = np.stack([np.stack([_pchunk(sup[:, i * 128:(i + 1) * 128]),
                               _pchunk(sup[:, J1 + i * 128:J1 + (i + 1) * 128])], 0)
                     for i in range(ICS)], 0)
    wug_raw = np.asarray(inputs["L1_ffn_w_up_gate"], np.float32)
    wug = np.stack([
        np.stack([np.stack([_pchunk(wug_raw[ex][:, i * 128:(i + 1) * 128]),
                            _pchunk(wug_raw[ex][:, I1 + i * 128:I1 + (i + 1) * 128])], 0)
                  for i in range(IC1)], 0)
        for ex in range(E1)], 0)
    ii, jj = np.meshgrid(np.arange(128), np.arange(128), indexing="ij")
    u128 = (ii < jj).astype(np.float32)
    i8, j8 = np.meshgrid(np.arange(NCORE), np.arange(NCORE), indexing="ij")
    u8 = (i8 < j8).astype(np.float32)
    import ml_dtypes
    shared = {
        "cnts_in": np.ascontiguousarray(cnt.astype(np.float32)),
        "rtr_in": np.asarray(inputs["L1_ffn_router"], np.float32),
        "u128_in": u128,
        "u8_in": u8,
        "iota_in": np.tile(np.arange(CAPL1, dtype=np.float32), (128, 1)),
        "ones_in": np.ones((1, 128), np.float32),
        "idb_in": np.eye(128, dtype=np.float32).astype(ml_dtypes.bfloat16),
        "shup_in": np.ascontiguousarray(shup),
        "shdn_in": np.asarray(inputs["L1_ffn_shared_w_down"], np.float32),
        "wug_in": np.ascontiguousarray(wug),
        "wd_in": np.asarray(inputs["L1_ffn_w_down"], np.float32),
    }
    in_maps = []
    for c in range(NCORE):
        m = dict(shared)
        m["x_in"] = x3[c * T:(c + 1) * T]
        m["xT_in"] = np.ascontiguousarray(x3[c * T:(c + 1) * T].T)
        rs = np.zeros((NCORE, 1), np.float32)
        rs[c, 0] = 1.0
        m["rowsel_in"] = rs
        in_maps.append(m)
    return in_maps


def run_l5(inputs, x3, cnt):
    nc = build_l5()
    in_maps = _prep_l5(inputs, x3, cnt)
    res = run_bass_kernel_spmd(nc, in_maps, core_ids=list(range(NCORE)))
    return np.concatenate([res.results[c]["x4_out"] for c in range(NCORE)], 0)



SB_TOP = 229344
OFF_XTM = SB_TOP - 65536
OFF_QT = OFF_XTM - 32768
OFF_KT = OFF_QT - 16384
OFF_VA = OFF_KT - 16896
OFF_SLOTA = OFF_VA - 32768
OFF_BIG = OFF_VA


def build_lB():
    p = _new_prog("lB", detect=False)
    nc = p.nc
    dt = lambda n, sh, ty=F32, kind="ExternalInput": nc.dram_tensor(
        n, list(sh), ty, kind=kind).ap()

    x_in = dt("x_in", [T, D])
    xT_in = dt("xT_in", [D, T])
    sall_in = dt("sall_in", [8 * T, NE0])
    sown_in = dt("sown_in", [T, NE0])
    ew_in = dt("ew_in", [D, D])
    eb_in = dt("eb_in", [1, D])
    wq1_in = dt("wq1_in", [D, D])
    wk1_in = dt("wk1_in", [D, KVH * DH])
    wv1_in = dt("wv1_in", [D, KVH * DH])
    qnw_in = dt("qnw_in", [1, DH])
    knw_in = dt("knw_in", [1, DH])
    cos_in = dt("cos_in", [T, 64])
    sin_in = dt("sin_in", [T, 64])
    wo1_in = dt("wo1_in", [D, D])
    dmask_in = dt("dmask_in", [128, 512])
    bmask_in = dt("bmask_in", [128, 512])
    idb_in = dt("idb_in", [128, 128], BF16)
    idf_in = dt("idf_in", [128, 128])
    rtr_in = dt("rtr_in", [D, E1])
    vones_in = dt("vones_in", [128, 16], BF16)
    ones_in = dt("ones_in", [1, 128])
    nonce_in = dt("nonce_in", [1, 1], I32)
    x3_out = dt("x3_out", [T, D], F32, "ExternalOutput")
    cnt_out = dt("cnt_out", [1, E1], F32, "ExternalOutput")

    xch_k = nc.dram_tensor("xch_k", [2, 128, 4096], BF16, kind="Internal",
                           addr_space="Shared")
    xch_v = nc.dram_tensor("xch_v", [2, 128, 4096], BF16, kind="Internal",
                           addr_space="Shared")
    xfl = nc.dram_tensor("xfl", [2, 1], I32, kind="Internal", addr_space="Shared")

    x_tm = p.sba("x_tm", [128, TT, D], F32, OFF_XTM)[...]
    qT = p.sba("qT", [128, TT, H1 * 128], BF16, OFF_QT)[...]
    kT = p.sba("kT", [128, KVH, 2 * T], BF16, OFF_KT)[...]
    vaug = p.sba("vaug", [128, 16, KVH, 132], BF16, OFF_VA)[...]
    x1T = p.sba("x1T", [128, DC, T], BF16, OFF_SLOTA)[...]
    x2T = p.sba("x2T", [128, DC, T], BF16, OFF_SLOTA)[...]
    aoT = p.sba("aoT", [128, TT, H1 * 128], BF16, OFF_SLOTA)[...]
    ew = p.sba("ew", [128, DC, D], BF16, OFF_BIG)[...]
    woQ = p.sba("woQ", [128, 2, 4, D], BF16, OFF_BIG)[...]
    x3T = p.sba("x3T", [128, DC, T], BF16, OFF_BIG)[...]

    onesf = p.sb("onesf", [1, 128], F32)
    onescol = p.sb("onescol", [128, 1], F32)
    idf = p.sb("idf", [128, 128], F32)
    idb = p.sb("idb", [128, 128], BF16)
    non = p.sb("non", [1, 1], I32)
    vones = p.sb("vones", [128, 16], BF16)
    vonesf = p.sb("vonesf", [128, 16], F32)
    lo = p.sb("lo", [1, NE0], F32)
    hi = p.sb("hi", [1, NE0], F32)
    mid = p.sb("mid", [1, NE0], F32)
    dd = p.sb("dd", [1, NE0], F32)
    sel = p.sb("sel", [1, NE0], F32)
    cntv = p.sb("cntv", [128, NE0], F32)
    wcol = p.sb("wcol", [128, TT], F32)
    scrA8 = p.sb("scrA8", [128, NE0], F32)
    sq_scr = p.sb("sq_scr", [128, D], F32)
    ssum = p.sb("ssum", [128, 1], F32)
    ssq = p.sb("ssq", [128, 1], F32)
    mean = p.sb("mean", [128, 1], F32)
    m2 = p.sb("m2", [128, 1], F32)
    var = p.sb("var", [128, 1], F32)
    rstd = p.sb("rstd", [128, 1], F32)

    p.dma("sync", "ld", x_tm[:, :, :], x_in.rearrange("(tt p) d -> p tt d", p=128))
    p.dma("sync", "ld", onesf[:, :], ones_in[:, :])
    p.dma("sync", "ld", idf[:, :], idf_in[:, :])
    p.dma("sync", "ld", idb[:, :], idb_in[:, :])
    p.dma("sync", "ld", non[:, :], nonce_in[:, :])
    p.dma("gpsimd", "ldg", x1T[:, :, :], xT_in.rearrange("(c p) t -> p c t", p=128))
    p.dma("gpsimd", "ldg", ew[:, :, :], ew_in.rearrange("(c p) o -> p c o", p=128))
    p.wait("vector", "ld", p.val("ld"))
    p.op("vector", lambda e, o=onescol[:, :]: e.memset(o, 1.0), "oc")

    with contextlib.ExitStack() as s1:
        sthr = p.sb("sthr", [128, 64, NE0], F32, s1)
        sown = p.sb("sown", [128, TT, NE0], F32, s1)
        eb = p.sb("eb", [1, D], F32, s1)
        ge = p.sb("ge", [128, 64, NE0], F32, s1)
        ps_b = p.ps("ps_b", [128, NE0], s1)
        ps_c = p.ps("ps_c", [1, NE0], s1)
        ps_tau = p.ps("ps_tau", [128, NE0], s1)
        ps_z = p.ps("ps_z", [128, D], s1)
        p.dma("sync", "lds1", sthr[:, :, :], sall_in.rearrange("(pp j) e -> pp j e", j=64))
        p.dma("sync", "lds1", sown[:, :, :], sown_in.rearrange("(tt p) e -> p tt e", p=128))
        p.dma("sync", "lds1", eb[:, :], eb_in[:, :])
        p.wait("vector", "lds1", p.val("lds1"))
        p.op("vector", lambda e, o=lo[:, :]: e.memset(o, 0.0), "bi")
        p.op("vector", lambda e, o=hi[:, :]: e.memset(o, 1.0), "bi")
        for it in range(42):
            p.wait("vector", "bi")
            p.op("vector", lambda e, o=mid[:, :], a=lo[:, :], b=hi[:, :]:
                 e.tensor_tensor(o, a, b, ALU.add), "bi")
            p.wait("vector", "bi")
            p.op("vector", lambda e, o=mid[:, :]: e.tensor_scalar_mul(o, o, 0.5), "bi")
            p.wait("tensor", "bi", p.val("bi"))
            if it:
                p.wait("tensor", "gei", p.val("gei"))
            p.mm(ps_b[:, :], onesf[:, :], mid[:, :], True, True, sem="bmm")
            p.wait("vector", "bmm", p.val("bmm"))
            p.op("vector", lambda e, o=ge[:, :, :], a=sthr[:, :, :],
                 b=ps_b[:, :].unsqueeze(1).broadcast_to([128, 64, NE0]):
                 e.tensor_tensor(o, a, b, ALU.is_ge), "gei")
            p.wait("vector", "gei")
            p.op("vector", lambda e, o=cntv[:, :],
                 i=ge[:, :, :].rearrange("p j e -> p e j"):
                 e.tensor_reduce(o, i, AX.X, ALU.add), "gei")
            p.wait("tensor", "gei", p.val("gei"))
            if it:
                p.wait("tensor", "seli", p.val("seli"))
            p.mm(ps_c[:, :], onescol[:, :], cntv[:, :], True, True, sem="cmm")
            p.wait("vector", "cmm", p.val("cmm"))
            p.op("vector", lambda e, o=sel[:, :], i=ps_c[:, :]:
                 e.tensor_single_scalar(o, i, float(K0_TOP), ALU.is_ge), "seli")
            p.wait("vector", "seli")
            p.op("vector", lambda e, o=dd[:, :], a=mid[:, :], b=lo[:, :]:
                 e.tensor_tensor(o, a, b, ALU.subtract), "bi")
            p.wait("vector", "bi")
            p.op("vector", lambda e, o=dd[:, :], a=sel[:, :], b=dd[:, :]:
                 e.tensor_tensor(o, a, b, ALU.mult), "bi")
            p.wait("vector", "bi")
            p.op("vector", lambda e, o=lo[:, :], a=lo[:, :], b=dd[:, :]:
                 e.tensor_tensor(o, a, b, ALU.add), "bi")
            p.wait("vector", "bi")
            p.op("vector", lambda e, o=dd[:, :], a=hi[:, :], b=mid[:, :]:
                 e.tensor_tensor(o, a, b, ALU.subtract), "bi")
            p.wait("vector", "bi")
            p.op("vector", lambda e, o=dd[:, :], a=sel[:, :], b=dd[:, :]:
                 e.tensor_tensor(o, a, b, ALU.mult), "bi")
            p.wait("vector", "bi")
            p.op("vector", lambda e, o=hi[:, :], a=mid[:, :], b=dd[:, :]:
                 e.tensor_tensor(o, a, b, ALU.add), "bi")
        p.wait("tensor", "bi", p.val("bi"))
        p.wait("tensor", "gei", p.val("gei"))
        p.mm(ps_tau[:, :], onesf[:, :], lo[:, :], True, True, sem="tmm")
        p.wait("vector", "tmm", p.val("tmm"))
        for tt in range(TT):
            p.op("vector", lambda e, o=scrA8[:, :], a=sown[:, tt, :], b=ps_tau[:, :]:
                 e.tensor_tensor(o, a, b, ALU.is_ge), "wt")
            p.wait("vector", "wt")
            p.op("vector", lambda e, o=scrA8[:, :], a=sown[:, tt, :], b=scrA8[:, :]:
                 e.tensor_tensor(o, a, b, ALU.mult), "wt")
            p.wait("vector", "wt")
            p.op("vector", lambda e, o=wcol[:, tt:tt + 1], i=scrA8[:, :]:
                 e.reduce_sum(o, i, AX.X), "wt")
            p.wait("vector", "wt")
        for tt in range(TT):
            p.wait("tensor", "ld", p.val("ld"))
            p.wait("tensor", "lds1", p.val("lds1"))
            p.wait("tensor", "ldg", p.val("ldg"))
            if tt:
                p.wait("tensor", "zev", p.val("zev"))
            for oc in range(4):
                p.mm(ps_z[:, oc * 512:(oc + 1) * 512], onesf[:, :],
                     eb[:, oc * 512:(oc + 1) * 512], True, False)
            for dc in range(DC):
                for oc in range(4):
                    p.mm(ps_z[:, oc * 512:(oc + 1) * 512],
                         x1T[:, dc, tt * 128:(tt + 1) * 128],
                         ew[:, dc, oc * 512:(oc + 1) * 512], False, dc == DC - 1,
                         sem=None if not (dc == DC - 1 and oc == 3) else "zmm")
            p.wait("vector", "zmm", p.val("zmm"))
            p.wait("vector", "wt", p.val("wt"))
            for oc in range(4):
                p.op("vector", lambda e, o=x_tm[:, tt, oc * 512:(oc + 1) * 512],
                     i=ps_z[:, oc * 512:(oc + 1) * 512], w=wcol[:, tt:tt + 1],
                     x=x_tm[:, tt, oc * 512:(oc + 1) * 512]:
                     e.scalar_tensor_tensor(o, i, w, x, ALU.mult, ALU.add), "zev")
            p.wait("vector", "zev")
            layer_norm(p, x_tm[:, tt, :], sq_scr[:, :], ssum[:, :], ssq[:, :],
                       mean[:, :], m2[:, :], var[:, :], rstd[:, :], "ln2")

    with contextlib.ExitStack() as s2:
        ps_t = p.ps("ps_t", [128, 128], s2)
        p.wait("tensor", "ln2", p.val("ln2"))
        for tt in range(TT):
            for dc in range(DC):
                p.wait("tensor", "x2e", p.val("x2e"))
                p.trans(ps_t[:, :], x_tm[:, tt, dc * 128:(dc + 1) * 128], idf[:, :],
                        x2T[:, dc, tt * 128:(tt + 1) * 128], "x2e")

    with contextlib.ExitStack() as s3:
        qnw = p.sb("qnw", [1, DH], F32, s3)
        knw = p.sb("knw", [1, DH], F32, s3)
        cosb = p.sb("cosb", [128, TT, 64], F32, s3)
        sinb = p.sb("sinb", [128, TT, 64], F32, s3)
        wbc = p.sb("wbc", [128, DH], F32, s3)
        kwbc = p.sb("kwbc", [128, DH], F32, s3)
        qtm = p.sb("qtm", [128, 512], F32, s3)
        ktm = p.sb("ktm", [128, 2 * DH], F32, s3)
        ssq4 = p.sb("ssq4", [128, KVH], F32, s3)
        rr4 = p.sb("rr4", [128, KVH], F32, s3)
        scr1 = p.sb("scr1", [128, 4 * 64], F32, s3)
        scr2 = p.sb("scr2", [128, 4 * 64], F32, s3)
        qbf = p.sb("qbf", [128, 512], BF16, s3)
        kbf = p.sb("kbf", [128, 2 * DH], BF16, s3)

        p.wait("sync", "ln2", p.val("ln2"))
        p.dma("sync", "ld2", qnw[:, :], qnw_in[:, :])
        p.dma("sync", "ld2", knw[:, :], knw_in[:, :])
        p.dma("sync", "ld2", cosb[:, :, :], cos_in.rearrange("(tt p) f -> p tt f", p=128))
        p.dma("sync", "ld2", sinb[:, :, :], sin_in.rearrange("(tt p) f -> p tt f", p=128))
        p.dma("sync", "ld2", vones[:, :], vones_in[:, :])

        ps_w = p.ps("ps_w", [128, DH], s3)
        p.wait("tensor", "ld2", p.val("ld2"))
        p.wait("tensor", "x2e", p.val("x2e"))
        p.mm(ps_w[:, :], onesf[:, :], qnw[:, :], True, True, sem="wb")
        p.wait("vector", "wb", p.val("wb"))
        p.op("vector", lambda e, o=wbc[:, :], i=ps_w[:, :]: e.tensor_copy(o, i), "wbe")
        p.wait("tensor", "wbe", p.val("wbe"))
        p.mm(ps_w[:, :], onesf[:, :], knw[:, :], True, True, sem="wb")
        p.wait("vector", "wb", p.val("wb"))
        p.op("vector", lambda e, o=kwbc[:, :], i=ps_w[:, :]: e.tensor_copy(o, i), "wbe")

        def rms_rope(tt, nh, xv, sqv, ssqv, rv, wv_bc, outbf, sem):
            p.op("scalar", lambda e, o=sqv, i=xv: e.activation(o, i, ACT.Square), sem)
            p.wait("vector", sem)
            p.op("vector", lambda e, o=ssqv, i=sqv.rearrange("p (h f) -> p h f", h=nh):
                 e.tensor_reduce(o, i, AX.X, ALU.add), sem)
            p.wait("vector", sem)
            p.op("vector", lambda e, o=rv, i=ssqv:
                 e.tensor_scalar(o, i, 1.0 / DH, 1e-6, ALU.mult, ALU.add), sem)
            p.wait("scalar", sem)
            p.op("scalar", lambda e, o=rv, i=rv: e.activation(o, i, ACT.Sqrt), sem)
            p.wait("vector", sem)
            p.op("vector", lambda e, o=rv, i=rv: e.reciprocal(o, i), sem)
            p.wait("vector", sem)
            for h in range(nh):
                p.op("vector", lambda e, o=xv[:, h * DH:(h + 1) * DH],
                     i=xv[:, h * DH:(h + 1) * DH], r=rv[:, h:h + 1], w=wv_bc:
                     e.scalar_tensor_tensor(o, i, r, w, ALU.mult, ALU.mult), sem)
            p.wait("vector", sem)
            x3v = xv.rearrange("p (h f) -> p h f", h=nh)
            o3 = outbf.rearrange("p (h f) -> p h f", h=nh)
            cb = cosb[:, tt, :].unsqueeze(1).broadcast_to([128, nh, 64])
            sb_ = sinb[:, tt, :].unsqueeze(1).broadcast_to([128, nh, 64])
            a3 = scr1.rearrange("p (h f) -> p h f", h=4)[:, 0:nh, :]
            b3 = scr2.rearrange("p (h f) -> p h f", h=4)[:, 0:nh, :]
            p.op("vector", lambda e, o=a3, i=x3v[:, :, 0:64], c=cb:
                 e.tensor_tensor(o, i, c, ALU.mult), sem)
            p.op("vector", lambda e, o=b3, i=x3v[:, :, 64:128], c=sb_:
                 e.tensor_tensor(o, i, c, ALU.mult), sem)
            p.wait("vector", sem)
            p.op("vector", lambda e, o=o3[:, :, 0:64], a=a3, b=b3:
                 e.tensor_tensor(o, a, b, ALU.subtract), sem)
            p.wait("vector", sem)
            p.op("vector", lambda e, o=a3, i=x3v[:, :, 0:64], c=sb_:
                 e.tensor_tensor(o, i, c, ALU.mult), sem)
            p.op("vector", lambda e, o=b3, i=x3v[:, :, 64:128], c=cb:
                 e.tensor_tensor(o, i, c, ALU.mult), sem)
            p.wait("vector", sem)
            p.op("vector", lambda e, o=o3[:, :, 64:128], a=a3, b=b3:
                 e.tensor_tensor(o, a, b, ALU.add), sem)

        with contextlib.ExitStack() as s3kv:
            wkb = p.sb("wkb", [128, DC, 2 * DH], BF16, s3kv)
            wvb2 = p.sb("wvb2", [128, DC, 2 * DH], BF16, s3kv)
            vbf2 = p.sb("vbf2", [128, 2 * DH], BF16, s3kv)
            ps_k = p.ps("ps_k", [128, 2 * DH], s3kv)
            ps_v = p.ps("ps_v", [128, 2 * DH], s3kv)
            ps_kt = p.ps("ps_kt", [128, 128], s3kv, BF16)
            p.wait("gpsimd", "ln2", p.val("ln2"))
            for hp in range(2):
                if hp:
                    p.wait("gpsimd", "kmm", p.val("kmm"))
                p.dma("gpsimd", "wk%d" % (hp % 2), wkb[:, :, :],
                      wk1_in[:, hp * 256:(hp + 1) * 256].rearrange(
                          "(c p) o -> p c o", p=128))
                p.dma("gpsimd", "wv%d" % (hp % 2), wvb2[:, :, :],
                      wv1_in[:, hp * 256:(hp + 1) * 256].rearrange(
                          "(c p) o -> p c o", p=128))
                p.wait("tensor", "wk%d" % (hp % 2), p.val("wk%d" % (hp % 2)))
                p.wait("tensor", "wv%d" % (hp % 2), p.val("wv%d" % (hp % 2)))
                for tt in range(TT):
                    if tt or hp:
                        p.wait("tensor", "kfe", p.val("kfe"))
                    for dc in range(DC):
                        p.mm(ps_k[:, :], x2T[:, dc, tt * 128:(tt + 1) * 128],
                             wkb[:, dc, :], dc == 0, dc == DC - 1,
                             sem=None if dc < DC - 1 else "kmm")
                        p.mm(ps_v[:, :], x2T[:, dc, tt * 128:(tt + 1) * 128],
                             wvb2[:, dc, :], dc == 0, dc == DC - 1,
                             sem=None if dc < DC - 1 else "kmm")
                    p.wait("vector", "kmm", p.val("kmm"))
                    p.wait("vector", "vst", p.val("vst"))
                    p.op("vector", lambda e, o=ktm[:, :], i=ps_k[:, :]:
                         e.tensor_copy(o, i), "kfe")
                    p.op("vector", lambda e, o=vbf2[:, :], i=ps_v[:, :]:
                         e.tensor_copy(o, i), "kfe")
                    p.wait("vector", "kfe")
                    p.wait("scalar", "kfe", p.val("kfe"))
                    rms_rope(tt, 2, ktm[:, :], sq_scr[:, 0:256], ssq4[:, 0:2],
                             rr4[:, 0:2], kwbc[:, :], kbf[:, :], "kfe")
                    p.wait("vector", "kfe", p.val("kfe"))
                    for h2 in range(2):
                        h4 = hp * 2 + h2
                        p.op("vector", lambda e, o=vaug[:, 8 + tt, h4, 0:128],
                             i=vbf2[:, h2 * 128:(h2 + 1) * 128]:
                             e.tensor_copy(o, i), "vst")
                        p.wait("tensor", "kfe", p.val("kfe"))
                        p.wait("tensor", "ktt", p.val("ktt"))
                        p.trans(ps_kt[:, :], kbf[:, h2 * 128:(h2 + 1) * 128],
                                idb[:, :],
                                kT[:, h4, T + tt * 128:T + (tt + 1) * 128], "ktt")
        p.wait("vector", "ld2", p.val("ld2"))
        for h4 in range(KVH):
            p.op("vector", lambda e, o=vaug[:, :, h4, 128:129],
                 i=vones[:, :].unsqueeze(2): e.tensor_copy(o, i), "vst")

        p.wait("gpsimd", "ktt", p.val("ktt"))
        p.wait("gpsimd", "vst", p.val("vst"))
        def _xch(g):
            half = g.bass.partition_id()
            myh = g.compute_val(half % 2)
            oth = g.compute_val(1 - (half % 2))
            g.dma_start(out=xch_k.ap()[bass.ds(myh, 1), :, :].squeeze(0),
                        in_=kT[:, :, T:2 * T]).then_inc(p.sem("xwr"), 16)
            g.dma_start(out=xch_v.ap()[bass.ds(myh, 1), :, :].squeeze(0),
                        in_=vaug[:, 8:16, :, 0:128]).then_inc(p.sem("xwr"), 16)
            g.wait_ge(p.sem("xwr"), 32)
            g.dma_start(out=xfl.ap()[bass.ds(myh, 1), :].squeeze(0),
                        in_=non[:, :]).then_inc(p.sem("xwr"), 16)
            g.wait_ge(p.sem("xwr"), 48)
            nval = g.value_load(non[0:1, 0:1])
            cntr = g.alloc_register("pollcnt")
            g.reg_mov(cntr, 0)
            fval = g.alloc_register("fval")
            g.reg_mov(fval, 0)
            ctmp = g.alloc_register("ctmp")
            flg = p._flg
            def cond():
                g.reg_alu(ctmp, fval, nval, ALU.not_equal)
                return ctmp
            with g.While(cond):
                g.reg_alu(cntr, cntr, 16, ALU.add)
                g.dma_start(out=flg[:, :],
                            in_=xfl.ap()[bass.ds(oth, 1), :].squeeze(0)
                            ).then_inc(p.sem("xpl"), 16)
                g.wait_ge(p.sem("xpl"), cntr)
                g.reg_load(fval, flg[0:1, 0:1])
            g.dma_start(out=kT[:, :, 0:T],
                        in_=xch_k.ap()[bass.ds(oth, 1), :, :].squeeze(0)
                        ).then_inc(p.sem("xrd"), 16)
            g.dma_start(out=vaug[:, 0:8, :, 0:128],
                        in_=xch_v.ap()[bass.ds(oth, 1), :, :].squeeze(0)
                        ).then_inc(p.sem("xrd"), 16)
        p._flg = p.sb("flg", [1, 1], I32, s3)
        p.inc("xrd", 32)
        p.on("gpsimd", _xch)

        with contextlib.ExitStack() as s3q:
            wqb = p.sb("wqb", [128, DC, 512], BF16, s3q)
            ps_q = p.ps("ps_q", [128, 512], s3q)
            ps_qt = p.ps("ps_qt", [128, 128], s3q, BF16)
            for oq in range(4):
                if oq:
                    p.wait("gpsimd", "qmm", p.val("qmm"))
                p.dma("gpsimd", "wq%d" % (oq % 2), wqb[:, :, :],
                      wq1_in[:, oq * 512:(oq + 1) * 512].rearrange(
                          "(c p) o -> p c o", p=128))
                p.wait("tensor", "wq%d" % (oq % 2), p.val("wq%d" % (oq % 2)))
                for tt in range(TT):
                    if tt or oq:
                        p.wait("tensor", "qfe", p.val("qfe"))
                    for dc in range(DC):
                        p.mm(ps_q[:, :], x2T[:, dc, tt * 128:(tt + 1) * 128],
                             wqb[:, dc, :], dc == 0, dc == DC - 1,
                             sem=None if dc < DC - 1 else "qmm")
                    p.wait("vector", "qmm", p.val("qmm"))
                    p.op("vector", lambda e, o=qtm[:, :], i=ps_q[:, :]:
                         e.tensor_copy(o, i), "qfe")
                    p.wait("vector", "qfe")
                    p.wait("scalar", "qfe", p.val("qfe"))
                    rms_rope(tt, 4, qtm[:, :], sq_scr[:, 0:512], ssq4[:, :],
                             rr4[:, :], wbc[:, :], qbf[:, :], "qfe")
                    p.wait("tensor", "qfe", p.val("qfe"))
                    for h4 in range(4):
                        h = oq * 4 + h4
                        p.wait("tensor", "qtt", p.val("qtt"))
                        p.trans(ps_qt[:, :], qbf[:, h4 * 128:(h4 + 1) * 128],
                                idb[:, :], qT[:, tt, h * 128:(h + 1) * 128], "qtt")

    p.wait("vector", "xrd", 32)
    p.op("vector", lambda e, o=vonesf[:, :], i=vones[:, :]: e.tensor_copy(o, i), "vzm")
    p.wait("vector", "vzm")
    for kbt in range(8):
        p.op("vector", lambda e, o=vaug[:, kbt, :, 0:128], i=vaug[:, kbt, :, 0:128],
             m=vonesf[:, kbt:kbt + 1]: e.tensor_scalar_mul(o, i, m), "vzm")

    with contextlib.ExitStack() as s4:
        p_sb = p.sb("p_sb", [128, 2, 9, 512], BF16, s4)
        dmask = p.sb("dmask", [128, 512], F32, s4)
        bmask = p.sb("bmask", [128, 512], F32, s4)
        rden4 = p.sb("rden4", [1, 2, 512], F32, s4)
        rb4 = p.sb("rb4", [128, 2, 512], F32, s4)
        mbias4 = p.sb("mbias4", [128, 32], F32, s4)
        onescb = p.sb("onescb", [128, 1], BF16, s4)
        p.wait("sync", "qtt", p.val("qtt"))
        p.dma("sync", "ld3", dmask[:, :], dmask_in[:, :])
        p.dma("sync", "ld3", bmask[:, :], bmask_in[:, :])

        with contextlib.ExitStack() as s4m:
            qsq = p.sb("qsq", [128, TT, 128], BF16, s4m)
            m4row = p.sb("m4row", [1, 32], F32, s4m)
            ksq = p.sb("ksq", [128, 2 * T], BF16, s4m)
            mqrow = p.sb("mqrow", [1, 128], F32, s4m)
            mkrow = p.sb("mkrow", [1, KVH, 16], F32, s4m)
            mkwin = p.sb("mkwin", [1, KVH, TT], F32, s4m)
            mrow = p.sb("mrow", [1, 128], F32, s4m)
            ps_nq = p.ps("ps_nq", [1, T], s4m)
            ps_mb = p.ps("ps_mb", [128, 128], s4m)
            for eng in ("scalar", "tensor"):
                p.wait(eng, "xrd", 32)
                p.wait(eng, "qtt", p.val("qtt"))
                p.wait(eng, "vst", p.val("vst"))
                p.wait(eng, "vzm", p.val("vzm"))
            p.wait("vector", "ld", p.val("ld"))
            p.op("vector", lambda e, o=onescb[:, :]: e.memset(o, 1.0), "ocb")
            p.wait("tensor", "ocb", 1)
            for h in range(H1):
                if h:
                    p.wait("scalar", "nqm", 8 * h)
                p.op("scalar", lambda e, o=qsq[:, :, :],
                     i=qT[:, :, h * 128:(h + 1) * 128]:
                     e.activation(o, i, ACT.Square), "sqq")
                p.wait("tensor", "sqq", h + 1)
                if h:
                    p.wait("tensor", "mqr", 8 * h)
                for qt in range(TT):
                    p.mm(ps_nq[0:1, qt * 128:(qt + 1) * 128], onescb[:, :],
                         qsq[:, qt, :], True, True, sem="nqm")
                p.wait("vector", "nqm", 8 * (h + 1))
                for qt in range(TT):
                    p.op("vector", lambda e,
                         o=mqrow[0:1, h * 8 + qt:h * 8 + qt + 1],
                         i=ps_nq[0:1, qt * 128:(qt + 1) * 128]:
                         e.reduce_max(o, i, AX.X), "mqr")
            for kvh in range(KVH):
                if kvh:
                    p.wait("scalar", "nkm", 16 * kvh)
                p.op("scalar", lambda e, o=ksq[:, :], i=kT[:, kvh, :]:
                     e.activation(o, i, ACT.Square), "sqk")
                p.wait("tensor", "sqk", kvh + 1)
                p.wait("tensor", "mqr", 8 * H1 if kvh == 0 else 0)
                for kt in range(16):
                    p.mm(ps_nq[0:1, (kt % 8) * 128:(kt % 8 + 1) * 128],
                         onescb[:, :], ksq[:, kt * 128:(kt + 1) * 128],
                         True, True, sem="nkm")
                    p.wait("vector", "nkm", 16 * kvh + kt + 1)
                    p.op("vector", lambda e,
                         o=mkrow[0:1, kvh, kt:kt + 1],
                         i=ps_nq[0:1, (kt % 8) * 128:(kt % 8 + 1) * 128]:
                         e.reduce_max(o, i, AX.X), "mkr")
                    if kt < 15:
                        p.wait("tensor", "mkr", 16 * kvh + kt + 1)
            p.wait("vector", "mkr", 64)
            for kvh in range(KVH):
                for qt in range(TT):
                    p.op("vector", lambda e, o=mkwin[0:1, kvh, qt:qt + 1],
                         i=mkrow[0:1, kvh, qt:qt + 9]:
                         e.reduce_max(o, i, AX.X), "mkw")
            p.wait("vector", "mkw")
            p.wait("vector", "mqr")
            for h in range(H1):
                p.op("vector", lambda e, o=mrow[0:1, h * 8:(h + 1) * 8],
                     a=mqrow[0:1, h * 8:(h + 1) * 8], b=mkwin[0:1, h // KVH, :]:
                     e.tensor_tensor(o, a, b, ALU.mult), "mcr")
            p.wait("vector", "mcr")
            for kvh in range(KVH):
                for qt in range(TT):
                    j0 = (4 * kvh) * 8 + qt
                    p.op("vector", lambda e,
                         o=m4row[0:1, kvh * 8 + qt:kvh * 8 + qt + 1],
                         i=mrow[0:1, j0:j0 + 25:8]:
                         e.reduce_max(o, i, AX.X), "m4r")
            p.wait("scalar", "m4r", p.val("m4r"))
            p.op("scalar", lambda e, o=m4row[0:1, :], i=m4row[0:1, :]:
                 e.activation(o, i, ACT.Sqrt), "msq")
            p.wait("vector", "msq", 1)
            p.op("vector", lambda e, o=m4row[0:1, :], i=m4row[0:1, :]:
                 e.tensor_scalar_mul(o, i, -SM_SCALE), "mns")
            p.wait("tensor", "mns", 1)
            p.wait("tensor", "ld3", p.val("ld3"))
            p.mm(ps_mb[:, 0:32], onesf[:, :], m4row[0:1, :], True, True, sem="mbm")
            p.wait("vector", "mbm", 1)
            p.op("vector", lambda e, o=mbias4[:, :], i=ps_mb[:, 0:32]:
                 e.tensor_copy(o, i), "mbb")

        ps_qkL = [p.ps("ps_qk", [128, 512], s4), p.ps("ps_qk", [128, 512], s4)]
        ps_aoL = [p.ps("ps_ao", [128, 512], s4), p.ps("ps_ao", [128, 512], s4)]
        ps_denL = [p.ps("ps_den", [1, 512], s4), p.ps("ps_den", [1, 512], s4)]
        ps_rb = p.ps("ps_rb", [128, 512], s4)

        p.wait("scalar", "mbb", 1)
        p.wait("tensor", "mbb", 1)
        NG = TT * KVH
        groups = [(qt, kvh) for qt in range(TT) for kvh in range(KVH)]

        def _qkg(gi):
            qt, kvh = groups[gi]
            for b9 in range(9):
                qb = gi * 9 + b9
                if qb >= 2:
                    p.wait("tensor", "pexp", qb - 1)
                p.mm(ps_qkL[qb % 2][:, :],
                     kT[:, kvh, (qt + b9) * 128:(qt + b9 + 1) * 128],
                     qT[:, qt, kvh * 512:(kvh + 1) * 512], True, True, sem="smm")

        def _pvg(gi):
            qt, kvh = groups[gi]
            p.wait("tensor", "pexp", gi * 9 + 9)
            if gi >= 2:
                p.wait("tensor", "aomul", gi - 1)
                p.wait("tensor", "rcp", gi - 1)
            for b9 in range(9):
                p.mm(ps_denL[gi % 2][:, :], vaug[:, qt + b9, kvh, 128:129],
                     p_sb[:, gi % 2, b9, :], b9 == 0, b9 == 8)
                p.mm(ps_aoL[gi % 2][:, :], vaug[:, qt + b9, kvh, 0:128],
                     p_sb[:, gi % 2, b9, :], b9 == 0, b9 == 8,
                     sem=None if b9 < 8 else "avm")
            p.wait("tensor", "rcp", gi + 1)
            if gi >= 1:
                p.wait("tensor", "rbc", gi)
            p.mm(ps_rb[:, :], onesf[:, :], rden4[0:1, gi % 2, :],
                 True, True, sem="rbm")

        def _mskg(gi):
            b0 = (gi * 9) % 2
            b8 = (gi * 9 + 8) % 2
            p.wait("vector", "smm", gi * 9 + 1)
            p.op("vector", lambda e, o=ps_qkL[b0][:, :], i=ps_qkL[b0][:, :],
                 m=bmask[:, :]: e.tensor_tensor(o, i, m, ALU.add), "mk0")
            p.wait("vector", "smm", gi * 9 + 9)
            p.op("vector", lambda e, o=ps_qkL[b8][:, :], i=ps_qkL[b8][:, :],
                 m=dmask[:, :]: e.tensor_tensor(o, i, m, ALU.add), "mk8")

        def _recipg(gi):
            p.wait("vector", "avm", gi + 1)
            p.op("vector", lambda e, o=rden4[0:1, gi % 2, :],
                 i=ps_denL[gi % 2][:, :]: e.reciprocal(o, i), "rcp")

        def _multg(gi):
            qt, kvh = groups[gi]
            p.wait("vector", "rbc", gi + 1)
            p.op("vector", lambda e,
                 o=aoT[:, qt, kvh * 512:(kvh + 1) * 512],
                 a=ps_aoL[gi % 2][:, :], b=rb4[:, gi % 2, :]:
                 e.tensor_tensor(o, a, b, ALU.mult), "aomul")

        def _expg(gi):
            qt, kvh = groups[gi]
            for b9 in range(9):
                qb = gi * 9 + b9
                if b9 == 0:
                    p.wait("scalar", "mk0", gi + 1)
                    if gi >= 2:
                        p.wait("scalar", "avm", gi - 1)
                elif b9 == 8:
                    p.wait("scalar", "mk8", gi + 1)
                else:
                    p.wait("scalar", "smm", qb + 1)
                p.op("scalar", lambda e, o=p_sb[:, gi % 2, b9, :],
                     i=ps_qkL[qb % 2][:, :],
                     b=mbias4[:, kvh * 8 + qt:kvh * 8 + qt + 1]:
                     e.activation(o, i, ACT.Exp, bias=b, scale=SM_SCALE), "pexp")

        def _copyg(gi):
            p.wait("scalar", "rbm", gi + 1)
            if gi >= 2:
                p.wait("scalar", "aomul", gi - 1)
            p.op("scalar", lambda e, o=rb4[:, gi % 2, :],
                 i=ps_rb[:, :]: e.copy(o, i), "rbc")

        for gi in range(NG):
            _qkg(gi)
            if gi >= 1:
                _pvg(gi - 1)
            _mskg(gi)
            if gi >= 2:
                _multg(gi - 2)
            if gi >= 1:
                _recipg(gi - 1)
            _expg(gi)
            if gi >= 1:
                _copyg(gi - 1)
        _pvg(NG - 1)
        _multg(NG - 2)
        _recipg(NG - 1)
        _copyg(NG - 1)
        _multg(NG - 1)
        p.wait("gpsimd", "smm", 9 * NG)
        p.wait("gpsimd", "avm", NG)
        for q in range(4):
            if q >= 2:
                p.wait("gpsimd", "opr", (q - 1) * TT)
            for ch in range(4):
                r0 = (q * 4 + ch) * 128
                for hp in range(2):
                    p.dma("gpsimd", "wo",
                          woQ[hp * 64:(hp + 1) * 64, q % 2, ch, :],
                          wo1_in[r0 + hp * 64:r0 + (hp + 1) * 64, :])

    with contextlib.ExitStack() as s5:
        s5o = contextlib.ExitStack()
        ps_oL = [p.ps("ps_o", [128, D], s5o), p.ps("ps_o", [128, D], s5o)]
        p.wait("tensor", "aomul", p.val("aomul"))
        for q in range(4):
            p.wait("tensor", "wo", 128 * (q + 1))
            for tt in range(TT):
                gi = q * TT + tt
                ps_o = ps_oL[gi % 2]
                if gi >= 2:
                    p.wait("tensor", "ore", gi - 1)
                for dc4 in range(4):
                    for oc in range(4):
                        p.mm(ps_o[:, oc * 512:(oc + 1) * 512],
                             aoT[:, tt, (q * 4 + dc4) * 128:(q * 4 + dc4 + 1) * 128],
                             woQ[:, q % 2, dc4, oc * 512:(oc + 1) * 512],
                             dc4 == 0, dc4 == 3,
                             sem=None if not (dc4 == 3 and oc == 3) else "opr")
                p.wait("vector", "opr", gi + 1)
                p.op("vector", lambda e, o=x_tm[:, tt, :], i=ps_o[:, :],
                     x=x_tm[:, tt, :]:
                     e.tensor_tensor(o, i, x, ALU.add), "ore")
        s5o.close()
        for tt in range(TT):
            p.wait("vector", "ore", p.val("ore"))
            layer_norm(p, x_tm[:, tt, :], sq_scr[:, :], ssum[:, :], ssq[:, :],
                       mean[:, :], m2[:, :], var[:, :], rstd[:, :], "lne")

        rtr = p.sb("rtr", [128, DC, E1], BF16, s5)
        lgt = p.sb("lgt", [128, TT, E1], F32, s5)
        m1c = p.sb("m1c", [128, 1], F32, s5)
        m2cc = p.sb("m2cc", [128, 1], F32, s5)
        oh1 = p.sb("oh1", [128, E1], F32, s5)
        oh2 = p.sb("oh2", [128, E1], F32, s5)
        scrM = p.sb("scrM", [128, E1], F32, s5)
        cnt_s = p.sb("cnt_s", [1, E1], F32, s5)
        ps_t2 = p.ps("ps_t2", [128, 128], s5)
        ps_lg = p.ps("ps_lg", [128, E1], s5)
        ps_cnt = p.ps("ps_cnt", [1, E1], s5)
        p.wait("gpsimd", "lne", p.val("lne"))
        p.dma("gpsimd", "rtl", rtr[:, :, :], rtr_in.rearrange("(c p) w -> p c w", p=128))
        p.wait("tensor", "lne", p.val("lne"))
        for tt in range(TT):
            for dc in range(DC):
                p.wait("tensor", "x3e", p.val("x3e"))
                p.trans(ps_t2[:, :], x_tm[:, tt, dc * 128:(dc + 1) * 128], idf[:, :],
                        x3T[:, dc, tt * 128:(tt + 1) * 128], "x3e")
        p.wait("tensor", "x3e", p.val("x3e"))
        p.wait("tensor", "rtl", p.val("rtl"))
        for tt in range(TT):
            if tt:
                p.wait("tensor", "lge", p.val("lge"))
            for dc in range(DC):
                p.mm(ps_lg[:, :], x3T[:, dc, tt * 128:(tt + 1) * 128], rtr[:, dc, :],
                     dc == 0, dc == DC - 1, sem=None if dc < DC - 1 else "lgm")
            p.wait("vector", "lgm", p.val("lgm"))
            p.op("vector", lambda e, o=lgt[:, tt, :], i=ps_lg[:, :]:
                 e.tensor_copy(o, i), "lge")
        for tt in range(TT):
            p.wait("vector", "lge", p.val("lge"))
            p.op("vector", lambda e, o=m1c[:, :], i=lgt[:, tt, :]:
                 e.reduce_max(o, i, AX.X), "tp")
            p.wait("vector", "tp")
            p.op("vector", lambda e, o=oh1[:, :], i=lgt[:, tt, :], m=m1c[:, :]:
                 e.tensor_scalar(o, i, m, None, ALU.is_equal), "tp")
            p.wait("vector", "tp")
            p.op("vector", lambda e, o=scrM[:, :], i=oh1[:, :], x=lgt[:, tt, :]:
                 e.scalar_tensor_tensor(o, i, NEG, x, ALU.mult, ALU.add), "tp")
            p.wait("vector", "tp")
            p.op("vector", lambda e, o=m2cc[:, :], i=scrM[:, :]:
                 e.reduce_max(o, i, AX.X), "tp")
            p.wait("vector", "tp")
            p.op("vector", lambda e, o=oh2[:, :], i=scrM[:, :], m=m2cc[:, :]:
                 e.tensor_scalar(o, i, m, None, ALU.is_equal), "tp")
            p.wait("vector", "tp")
            p.op("vector", lambda e, o=oh1[:, :], a=oh1[:, :], b=oh2[:, :]:
                 e.tensor_tensor(o, a, b, ALU.add), "tp")
            p.wait("tensor", "tp", p.val("tp"))
            p.mm(ps_cnt[:, :], onescol[:, :], oh1[:, :], tt == 0, tt == TT - 1,
                 sem="cm")
            p.wait("vector", "cm", p.val("cm"))
        p.wait("vector", "cm", p.val("cm"))
        p.op("vector", lambda e, o=cnt_s[:, :], i=ps_cnt[:, :]:
             e.tensor_copy(o, i), "cme")

    for tt in range(TT):
        p.wait("sync", "lne", 9 * (tt + 1))
        p.dma("sync", "st", x3_out[tt * 128:(tt + 1) * 128, :], x_tm[:, tt, :])
    p.wait("sync", "cme", p.val("cme"))
    p.dma("sync", "st", cnt_out[:, :], cnt_s[:, :])
    p.wait("sync", "st", p.val("st"))
    return p.finish()


def _prep_lB(inputs, x1, s_all, nonce):
    import ml_dtypes
    bfd = ml_dtypes.bfloat16
    neg = np.float32(NEG)
    ii, jj = np.meshgrid(np.arange(128), np.arange(128), indexing="ij")
    shared = {
        "sall_in": np.ascontiguousarray(s_all),
        "ew_in": np.asarray(inputs["L0_ffn_expert_w"], np.float32),
        "eb_in": np.asarray(inputs["L0_ffn_expert_b"], np.float32).reshape(1, D),
        "wq1_in": np.ascontiguousarray(np.asarray(inputs["L1_attn_wq"], np.float32).T),
        "wk1_in": np.ascontiguousarray(np.asarray(inputs["L1_attn_wk"], np.float32).T),
        "wv1_in": np.ascontiguousarray(np.asarray(inputs["L1_attn_wv"], np.float32).T),
        "qnw_in": np.asarray(inputs["L1_attn_q_norm_w"], np.float32).reshape(1, DH),
        "knw_in": np.asarray(inputs["L1_attn_k_norm_w"], np.float32).reshape(1, DH),
        "wo1_in": np.ascontiguousarray(np.asarray(inputs["L1_attn_wo"], np.float32).T),
        "dmask_in": np.ascontiguousarray(np.tile(
            np.where(ii >= jj, 0.0, neg).astype(np.float32).T, (1, 4))),
        "bmask_in": np.ascontiguousarray(np.tile(
            np.where(jj > ii, 0.0, neg).astype(np.float32).T, (1, 4))),
        "idb_in": np.eye(128, dtype=np.float32).astype(bfd),
        "idf_in": np.eye(128, dtype=np.float32),
        "rtr_in": np.asarray(inputs["L1_ffn_router"], np.float32),
        "ones_in": np.ones((1, 128), np.float32),
        "nonce_in": np.array([[nonce]], np.int32),
    }
    in_maps = []
    for c in range(NCORE):
        half = c % 2
        m = dict(shared)
        m["x_in"] = x1[c * T:(c + 1) * T]
        m["xT_in"] = np.ascontiguousarray(x1[c * T:(c + 1) * T].T)
        m["sown_in"] = s_all[c * T:(c + 1) * T]
        cos, sin = _rope_tables(c)
        m["cos_in"] = cos
        m["sin_in"] = sin
        von = np.zeros((2 * T,), np.float32)
        if half == 0:
            von[T:] = 1.0
        else:
            von[:] = 1.0
        m["vones_in"] = np.ascontiguousarray(von.reshape(16, 128).T).astype(bfd)
        in_maps.append(m)
    return in_maps


def run_lB(inputs, x1, s_all, nonce=1234567):
    nc = build_lB()
    in_maps = _prep_lB(inputs, x1, s_all, nonce)
    res = run_bass_kernel_spmd(nc, in_maps, core_ids=list(range(NCORE)))
    x3 = np.concatenate([res.results[c]["x3_out"] for c in range(NCORE)], 0)
    cnt = np.stack([res.results[c]["cnt_out"][0] for c in range(NCORE)], 0)
    return x3, cnt


import time as _time


def kernel(**inputs):
    x1, s = run_l1(inputs)
    x3, cnt = run_lB(inputs, x1, s, nonce=(int(_time.time() * 1000) & 0x7FFFFFFF) | 1)
    x4 = run_l5(inputs, x3, cnt)
    return x4.reshape(4, 2048, D)
